# version 04
# baseline (speedup 1.0000x reference)
import math
from contextlib import ExitStack

import numpy as np
import ml_dtypes
import concourse.bass as bass
import concourse.mybir as mybir
from concourse.bass_utils import run_bass_kernel_spmd

F32 = mybir.dt.float32
BF16 = mybir.dt.bfloat16
AF = mybir.ActivationFunctionType
ALU = mybir.AluOpType
AX = mybir.AxisListType

D = 1024
T = 2048
NT = T // 512
ND = D // 128
NCORES = 8
NEG = -1e9

ENGINES = ("tensor", "vector", "scalar", "sync", "gpsimd")


def cdiv(a, b):
    return -(-a // b)


class Sem:
    __slots__ = ("h", "n", "name", "dma")

    def __init__(self, h, name):
        self.h = h
        self.n = 0
        self.name = name
        self.dma = False


class Prog:

    def __init__(self, nc):
        self.nc = nc
        self.stack = ExitStack()
        self.q = {e: [] for e in ENGINES}
        self.waited = {e: {} for e in ENGINES}
        self._sems = {}
        self.cur_label = None
        self.w_bf16 = False

    def sem(self, name):
        if name not in self._sems:
            h = self.stack.enter_context(self.nc.semaphore(name))
            self._sems[name] = Sem(h, name)
        return self._sems[name]

    def sbuf(self, name, shape, dtype):
        st = self._scopes[-1] if getattr(self, "_scopes", None) else self.stack
        return st.enter_context(self.nc.sbuf_tensor(name, list(shape), dtype))

    def scope(self):
        if not hasattr(self, "_scopes"):
            self._scopes = []
        es = ExitStack()
        self._scopes.append(es)

        class _Scope:
            def __init__(s):
                s.es = es
                s.prog = self

            def close(s):
                s.prog._scopes.remove(s.es)
                s.es.close()
        return _Scope()

    def psum(self, name, shape, dtype):
        return self.stack.enter_context(self.nc.psum_tensor(name, list(shape), dtype))

    def _dedup(self, eng, waits):
        real = []
        for s, v in waits:
            if s.dma:
                v = s.n
            if v > 0 and self.waited[eng].get(s.name, 0) < v:
                real.append((s.h, v))
                self.waited[eng][s.name] = v
        return tuple(real)

    def emit(self, eng, fn, waits=(), inc=None, inc_n=1):
        real = self._dedup(eng, waits)
        if inc is not None:
            inc.n += inc_n
        post = inc.n if inc is not None else None
        ih = inc.h if inc is not None else None

        def closure(engine, fn=fn, real=real, ih=ih, inc_n=inc_n):
            for h, v in real:
                engine.wait_ge(h, v)
            inst = fn(engine)
            if ih is not None:
                inst.then_inc(ih, inc_n)

        self.q[eng].append((self.cur_label, closure))
        return post

    def wdma(self, out, in_, sem, waits=(), **kw):
        if self.w_bf16:
            dt = getattr(in_, "dtype", None)
            assert dt is None or dt == BF16, f"weight src not pre-cast: {dt}"
        return self.dma("sync" if self.w_bf16 else "gpsimd", out, in_, sem,
                        waits=waits, **kw)

    def dma(self, eng, out, in_, sem, waits=(), **kw):
        sem.dma = True
        return self.emit(
            eng, lambda e, o=out, i=in_, kw=kw: e.dma_start(out=o, in_=i, **kw),
            waits=waits, inc=sem, inc_n=16)

    def wait(self, eng, *pairs):
        real = self._dedup(eng, pairs)
        if real:
            def closure(engine, real=real):
                for h, v in real:
                    engine.wait_ge(h, v)
            self.q[eng].append((self.cur_label, closure))

    def barrier(self, engines=ENGINES):
        b = self.sem("_barrier")
        target = b.n + len(engines)
        for e in engines:
            b.n += 1
            self.q[e].append((self.cur_label, lambda eng, h=b.h: eng.sem_inc(h, 1)))
        for e in engines:
            self.wait(e, (b, target))

    def flush(self):
        nc = self.nc

        def body(engine, fns):
            cur = None
            sid = None
            for lbl, f in fns:
                if lbl != cur:
                    if cur is not None:
                        nc.leave_named_scope(cur, sid, False)
                    sid = (nc.enter_named_scope(lbl, False)[0]
                           if lbl is not None else None)
                    cur = lbl
                f(engine)
            if cur is not None:
                nc.leave_named_scope(cur, sid, False)
        with nc.Block() as blk:
            for e in ENGINES:
                fns = self.q[e]
                getattr(blk, e)(lambda engine, fns=fns: body(engine, fns))
        self.q = {e: [] for e in ENGINES}


class Mk:

    def __init__(self, nc):
        self.nc = nc
        self.p = Prog(nc)
        p = self.p
        self.psf = p.psum("psf", (128, 3072), F32)
        self.psf2 = p.psum("psf2", (128, 1024), F32)
        self.s_mm = p.sem("mm")
        self.s_ve = p.sem("ve")
        self.s_ac = p.sem("ac")
        self.s_po = p.sem("po")
        self.s_ld = p.sem("ld")
        self.s_gd = p.sem("gd")
        self.bank_readers = {i: [] for i in range(8)}
        self.ones = p.sbuf("ones", (128, 128), BF16)
        self.onesf = p.sbuf("onesf", (128, 128), F32)
        self.identb = p.sbuf("identb", (128, 128), BF16)
        self.identf = p.sbuf("identf", (128, 128), F32)
        self.x_tm = [p.sbuf("x_tm0", (128, D), F32), p.sbuf("x_tm1", (128, D), F32)]
        self.n_tm = [p.sbuf("n_tm0", (128, D), F32), p.sbuf("n_tm1", (128, D), F32)]
        self.stc = p.sbuf("stc", (128, 16), F32)
        self.uid = 0

    def name(self, pfx):
        self.uid += 1
        return f"{pfx}{self.uid}"

    def spool(self, i):
        return self.p.sem("SP%d" % i)

    def gpool(self, i):
        return self.p.sem("GP%d" % i)

    def fb(self, i, c0=0, w=512):
        return self.psf[:, i * 512 + c0:i * 512 + c0 + w]

    def bb(self, i, c0=0, w=1024):
        return self.psb[:, i * 1024 + c0:i * 1024 + c0 + w]

    def bank_war(self, *banks):
        ws = []
        for b in banks:
            ws += self.bank_readers[b]
        return ws

    def bank_read(self, bank, sem, val):
        lst = self.bank_readers[bank]
        for i, (s, v) in enumerate(lst):
            if s is sem:
                lst[i] = (s, max(v, val))
                break
        else:
            lst.append((sem, val))

    def load_consts(self, ident_dram, ones_dram):
        p = self.p
        s_cn = p.sem("cn")
        s_cg = p.sem("cg")
        p.dma("gpsimd", self.identb[:], ident_dram, s_cg)
        p.dma("sync", self.identf[:], ident_dram, s_cn)
        p.dma("gpsimd", self.ones[:], ones_dram, s_cg)
        p.dma("sync", self.onesf[:], ones_dram, s_cn)
        self.const_rdy = [(s_cn, s_cn.n), (s_cg, s_cg.n)]
        return self.const_rdy

    def bcast_row(self, row_ap, out_sb, width, fbank0=4, wait=(), dtype_note=None):
        p = self.p
        nch = cdiv(width, 512)
        last = None
        for c in range(nch):
            w = min(512, width - c * 512)
            bk = fbank0 + (c % 2)
            mmv = p.emit("tensor", lambda e, c=c, w=w, bk=bk:
                         e.matmul(self.fb(bk, 0, w), self.onesf[0:1, :],
                                  row_ap[:, c * 512:c * 512 + w],
                                  start=True, stop=True),
                         waits=list(wait) + self.bank_war(bk), inc=self.s_mm)
            last = p.emit("vector", lambda e, c=c, w=w, bk=bk:
                          e.tensor_copy(out_sb[:, c * 512:c * 512 + w], self.fb(bk, 0, w)),
                          waits=[(self.s_mm, mmv)], inc=self.s_ve)
            self.bank_read(bk, self.s_ve, last)
        return last

    def transpose_fm_to_tm(self, xT, tt, out_tm, wait_x, banks=(0, 1, 2, 3),
                           w_extra=(), evac_eng="vector"):
        p = self.p
        ev_sem = {"vector": self.s_ve, "scalar": self.s_ac}[evac_eng]
        last = None
        for dd in range(8):
            bk = banks[dd % len(banks)]
            mmv = p.emit("tensor", lambda e, dd=dd, bk=bk:
                         e.transpose(self.fb(bk, 0, 128), xT[dd][:, tt * 128:(tt + 1) * 128],
                                     self.identf[:]),
                         waits=[wait_x[dd]] + list(w_extra) + self.bank_war(bk)
                               + getattr(self, "const_rdy", []),
                         inc=self.s_mm)
            if evac_eng == "vector":
                last = p.emit("vector", lambda e, dd=dd, bk=bk:
                              e.tensor_copy(out_tm[:, dd * 128:(dd + 1) * 128], self.fb(bk, 0, 128)),
                              waits=[(self.s_mm, mmv)], inc=ev_sem)
            else:
                last = p.emit("scalar", lambda e, dd=dd, bk=bk:
                              e.copy(out_tm[:, dd * 128:(dd + 1) * 128], self.fb(bk, 0, 128)),
                              waits=[(self.s_mm, mmv)], inc=ev_sem)
            self.bank_read(bk, ev_sem, last)
        return (ev_sem, last)

    def transpose_tm_to_fm(self, in_tm, tt, outT, wait_in, banks=(4, 5), out_dd=None):
        p = self.p
        outs = []
        dds = out_dd if out_dd is not None else range(8)
        for idx, dd in enumerate(dds):
            bk = banks[idx % len(banks)]
            eng, sem = (("vector", self.s_ve) if bk % 2 == 0 else ("scalar", self.s_ac))
            mmv = p.emit("tensor", lambda e, dd=dd, bk=bk:
                         e.transpose(self.fb(bk, 0, 128), in_tm[:, dd * 128:(dd + 1) * 128],
                                     self.identf[:]),
                         waits=[wait_in] + self.bank_war(bk)
                               + getattr(self, "const_rdy", []),
                         inc=self.s_mm)
            if eng == "vector":
                v = p.emit("vector", lambda e, dd=dd, bk=bk:
                           e.tensor_copy(outT[dd][:, tt * 128:(tt + 1) * 128], self.fb(bk, 0, 128)),
                           waits=[(self.s_mm, mmv)], inc=sem)
            else:
                v = p.emit("scalar", lambda e, dd=dd, bk=bk:
                           e.copy(outT[dd][:, tt * 128:(tt + 1) * 128], self.fb(bk, 0, 128)),
                           waits=[(self.s_mm, mmv)], inc=sem)
            outs.append((sem, v))
            self.bank_read(bk, sem, v)
        return outs

    def ln(self, xT, x_ready, xn_out, eps=1e-5, n_feat=D):
        p = self.p
        if isinstance(x_ready, tuple):
            x_ready = [x_ready] * 8
        stc = self.stc
        for tt in range(16):
            par = tt % 2
            x_tm = self.x_tm[par]
            n_tm = self.n_tm[par]
            wx = [(self.s_ve, getattr(self, "_xtm_free", [0, 0])[par])]
            rdy = self.transpose_fm_to_tm(xT, tt, x_tm, x_ready,
                                          banks=((0, 1) if par == 0 else (2, 3)), w_extra=wx)
            w_ntm = [(self.s_mm, getattr(self, "_ntm_free", [0, 0])[par])]
            a1 = p.emit("scalar", lambda e, x_tm=x_tm, n_tm=n_tm: e.activation(n_tm[:], x_tm[:], AF.Square),
                        waits=[rdy] + w_ntm, inc=self.s_ac)
            r0 = p.emit("vector", lambda e, x_tm=x_tm: e.reduce_sum(stc[:, 0:1], x_tm[:], axis=AX.X),
                        waits=[rdy], inc=self.s_ve)
            r1 = p.emit("vector", lambda e, n_tm=n_tm: e.reduce_sum(stc[:, 1:2], n_tm[:], axis=AX.X),
                        waits=[(self.s_ac, a1)], inc=self.s_ve)
            inv = 1.0 / n_feat
            c1 = p.emit("vector", lambda e: e.tensor_scalar_mul(stc[:, 0:1], stc[:, 0:1], inv),
                        waits=[(self.s_ve, r0)], inc=self.s_ve)
            c2 = p.emit("vector", lambda e: e.tensor_tensor(stc[:, 2:3], stc[:, 0:1], stc[:, 0:1], ALU.mult),
                        waits=[(self.s_ve, c1)], inc=self.s_ve)
            c3 = p.emit("vector", lambda e: e.scalar_tensor_tensor(stc[:, 1:2], stc[:, 1:2], inv,
                                                                  stc[:, 2:3], ALU.mult, ALU.subtract),
                        waits=[(self.s_ve, c2)], inc=self.s_ve)
            c3b = p.emit("vector", lambda e: e.tensor_scalar_add(stc[:, 1:2], stc[:, 1:2], float(eps)),
                         waits=[(self.s_ve, c3)], inc=self.s_ve)
            c4 = p.emit("scalar", lambda e: e.activation(stc[:, 1:2], stc[:, 1:2], AF.Ln),
                        waits=[(self.s_ve, c3b)], inc=self.s_ac)
            c5 = p.emit("scalar", lambda e: e.activation(stc[:, 1:2], stc[:, 1:2], AF.Exp, scale=-0.5),
                        waits=[(self.s_ac, c4)], inc=self.s_ac)
            c6 = p.emit("vector", lambda e: e.tensor_scalar(stc[:, 2:3], stc[:, 1:2], stc[:, 0:1], -1.0,
                                                            ALU.mult, ALU.mult),
                        waits=[(self.s_ac, c5)], inc=self.s_ve)
            a3 = p.emit("vector", lambda e, x_tm=x_tm, n_tm=n_tm:
                        e.tensor_scalar(n_tm[:], x_tm[:], stc[:, 1:2], stc[:, 2:3],
                                        ALU.mult, ALU.add),
                        waits=[(self.s_ve, c6)], inc=self.s_ve)
            free = getattr(self, "_xtm_free", [0, 0])
            free[par] = a3
            self._xtm_free = free
            self.transpose_tm_to_fm(n_tm, tt, xn_out, (self.s_ve, a3),
                                    banks=(4, 5))
            nfree = getattr(self, "_ntm_free", [0, 0])
            nfree[par] = self.s_mm.n
            self._ntm_free = nfree
        return [(self.s_ve, self.s_ve.n), (self.s_ac, self.s_ac.n)]

    def wload(self, wsb, dram_kc, F, KC=8, waits=(), sem=None):
        sem = sem or self.s_gd
        v = None
        for kc in range(KC):
            v = self.p.wdma(wsb[:, kc * F:(kc + 1) * F], dram_kc[kc],
                            sem, waits=waits)
        return (sem, v)

    def colload(self, sb, dram, sem=None, waits=()):
        sem = sem or self.s_ld
        v = self.p.dma("sync", sb, dram, sem, waits=waits)
        return (sem, v)

    def proj(self, wsb, w_rdy, xn, xn_rdy, Fout, evac, banks=(0, 1, 2, 3),
             KC=8, wcol0=0):
        p = self.p
        if isinstance(xn_rdy, tuple):
            xn_rdy = [xn_rdy] * KC
        for fi in range(Fout // 128):
            for c in range(NT):
                bk = banks[(fi * NT + c) % len(banks)]
                war = self.bank_war(bk)
                mmv = None
                for kc in range(KC):
                    mmv = p.emit("tensor", lambda e, fi=fi, c=c, kc=kc, bk=bk:
                                 e.matmul(self.fb(bk, 0, 512),
                                          wsb[:, wcol0 + kc * Fout + fi * 128:
                                              wcol0 + kc * Fout + fi * 128 + 128],
                                          xn[kc][:, c * 512:(c + 1) * 512],
                                          start=(kc == 0), stop=(kc == KC - 1)),
                                 waits=[w_rdy, xn_rdy[kc]] + (war if kc == 0 else []),
                                 inc=self.s_mm if kc == KC - 1 else None)
                rv = evac(fi, c, self.fb(bk, 0, 512), (self.s_mm, mmv))
                self.bank_read(bk, *rv)

    def ev_bias_copy(self, out_tiles, bias_sb=None, eng="vector", dtype_note=None):
        sem = {"vector": self.s_ve, "scalar": self.s_ac}[eng]

        def evac(fi, c, ps, mm_pair):
            if bias_sb is not None:
                v = self.p.emit(eng, lambda e, fi=fi, c=c, ps=ps:
                                e.tensor_scalar_add(out_tiles[fi][:, c * 512:(c + 1) * 512],
                                                    ps, bias_sb[:, fi:fi + 1]),
                                waits=[mm_pair], inc=sem)
            else:
                v = self.p.emit(eng, lambda e, fi=fi, c=c, ps=ps:
                                e.tensor_copy(out_tiles[fi][:, c * 512:(c + 1) * 512], ps),
                                waits=[mm_pair], inc=sem)
            return (sem, v)
        return evac

    def ev_residual(self, xT):
        def evac(fi, c, ps, mm_pair):
            v = self.p.emit("vector", lambda e, fi=fi, c=c, ps=ps:
                            e.tensor_add(xT[fi][:, c * 512:(c + 1) * 512],
                                         xT[fi][:, c * 512:(c + 1) * 512], ps),
                            waits=[mm_pair], inc=self.s_ve)
            return (self.s_ve, v)
        return evac

    def sub_l0_attn_b(self, xT, x_rdy, W, scr, stop_after=None, npairs=8):
        p = self.p
        s_so = [self.spool(0), self.spool(1)]
        s_cl = self.spool(2)

        xn = [p.sbuf(self.name("xn"), (128, T), BF16) for _ in range(8)]
        rdy = self.ln(xT, x_rdy, xn)
        xn_rdy = [rdy[0]] * 8
        p.wait("tensor", rdy[1])

        wbuf = p.sbuf(self.name("wb"), (128, 8192), BF16)
        wbuf2 = p.sbuf(self.name("wb2"), (128, 8192), BF16)
        stag = [p.sbuf(self.name("stg"), (128, T), BF16) for _ in range(2)]
        biasq = p.sbuf(self.name("bq"), (128, 8), F32)
        biask = p.sbuf(self.name("bk"), (128, 8), F32)
        biasv = p.sbuf(self.name("bv"), (128, 8), F32)
        attc = p.sbuf(self.name("attc"), (1, 1280), F32)
        self.colload(biasq[:], W["bq"], s_cl)
        self.colload(biask[:], W["bk"], s_cl)
        self.colload(biasv[:], W["bv"], s_cl)
        self.colload(attc[:], W["attc"], s_cl)
        cst = (s_cl, s_cl.n)
        bq_r = bk_r = bv_r = at_r = cst

        stg_free = [0, 0]

        def scratch_proj(wh, w_rdy, bias_sb, b_rdy, dst):
            def evac(fi, c, ps, mm_pair, bias_sb=bias_sb, dst=dst):
                sl = fi % 2
                sb = stag[sl]
                v = p.emit("vector", lambda e, c=c, ps=ps, sb=sb, fi=fi:
                           e.tensor_scalar_add(sb[:, c * 512:(c + 1) * 512], ps,
                                               bias_sb[:, fi:fi + 1]),
                           waits=[mm_pair, b_rdy, (s_so[sl], stg_free[sl])],
                           inc=self.s_ve)
                if c == NT - 1:
                    p.dma("sync", dst[fi * 128:(fi + 1) * 128, :], sb[:],
                          s_so[sl], waits=[(self.s_ve, v)])
                    stg_free[sl] = s_so[sl].n
                return (self.s_ve, v)
            self.proj(wh, w_rdy, xn, xn_rdy, D, evac, banks=(0, 1, 2, 3))

        s_w1 = self.gpool(0)
        s_w2 = self.gpool(1)
        wq_r = self.wload(wbuf, W["WqT"], D, sem=s_w1)
        scratch_proj(wbuf, wq_r, biasq, bq_r, scr["qd"])
        mm_q = self.s_mm.n
        wk_r = self.wload(wbuf2, W["WkT"], D, sem=s_w2)
        scratch_proj(wbuf2, wk_r, biask, bk_r, scr["kd"])
        mm_k = self.s_mm.n
        wv_r = self.wload(wbuf, W["WvT"], D, waits=[(self.s_mm, mm_q)], sem=s_w1)
        scratch_proj(wbuf, wv_r, biasv, bv_r, scr["vd"])
        wo_r = self.wload(wbuf2, W["WoT"], D, waits=[(self.s_mm, mm_k)], sem=s_w2)
        qkv_done = [(s_so[0], s_so[0].n), (s_so[1], s_so[1].n)]

        qb = [p.sbuf(self.name("qh"), (128, T), BF16) for _ in range(2)]
        kb = [p.sbuf(self.name("kh"), (128, T), BF16) for _ in range(2)]
        vb = [p.sbuf(self.name("vh"), (128, T), F32) for _ in range(2)]
        vtm = p.sbuf(self.name("vtm"), (128, T), BF16)
        expT = p.sbuf(self.name("expT"), (128, T), BF16)
        o_h = p.sbuf(self.name("oh"), (128, 1024), F32)
        oT = xn
        mcols = p.sbuf(self.name("mcols"), (128, 16), F32)
        gcols = p.sbuf(self.name("gcols"), (128, 16), F32)
        srow = self.n_tm[0]
        csc = p.sbuf(self.name("csc"), (1, 256), F32)
        ncB = p.sbuf(self.name("ncB"), (128, 1), F32)

        pair_rdy = []
        pair_free = [0, 0]
        s_hl2 = [self.spool(3), self.spool(4)]
        s_hv2 = [self.gpool(2), self.gpool(3)]

        def load_pair(pr):
            b = pr % 2
            w = qkv_done + [(self.s_mm, pair_free[b])]
            p.dma("sync", qb[b][:], scr["qd"][pr * 128:(pr + 1) * 128, :], s_hl2[b], waits=w)
            v0 = p.dma("sync", kb[b][:], scr["kd"][pr * 128:(pr + 1) * 128, :], s_hl2[b], waits=w)
            v = p.dma("gpsimd", vb[b][:], scr["vd"][pr * 128:(pr + 1) * 128, :], s_hv2[b], waits=w)
            pair_rdy.append([(s_hl2[b], v0), (s_hv2[b], v)])

        if stop_after == "proj":
            p.wait("vector", *qkv_done)
            return [(self.s_ve, self.s_ve.n)]
        load_pair(0)
        oT_w = [0] * 8
        for pr in range(npairs):
            b = pr % 2
            if pr < 7:
                load_pair(pr + 1)
            prdy1, prdy2 = pair_rdy[pr]
            for tt in range(16):
                bk_t = 4 + (tt % 2)
                mmv = p.emit("tensor", lambda e, tt=tt, b=b, bk_t=bk_t:
                             e.transpose(self.fb(bk_t, 0, 128),
                                         vb[b][:, tt * 128:(tt + 1) * 128], self.identf[:]),
                             waits=[prdy2] + self.bank_war(bk_t), inc=self.s_mm)
                v = p.emit("vector", lambda e, tt=tt, bk_t=bk_t:
                           e.tensor_copy(vtm[:, tt * 128:(tt + 1) * 128], self.fb(bk_t, 0, 128)),
                           waits=[(self.s_mm, mmv)], inc=self.s_ve)
                self.bank_read(bk_t, self.s_ve, v)
            vtm_rdy = (self.s_ve, self.s_ve.n)
            if stop_after == "vtm":
                p.wait("vector", vtm_rdy)
                return [(self.s_ve, self.s_ve.n)]

            for hh in range(2):
                h = pr * 2 + hh
                r0 = hh * 64
                import os
                _nomask = bool(os.environ.get("NOMASK"))
                for tqt in range(16):
                    has_mask = (not _nomask) and (tqt in (0, 15))
                    for c in range(4):
                        stop = _nomask or not ((tqt == 0 and c == 3) or (tqt == 15 and c == 0))
                        p.emit("tensor", lambda e, tqt=tqt, c=c, b=b, r0=r0, stop=stop:
                               e.matmul(self.fb(c, 0, 512),
                                        qb[b][r0:r0 + 64, tqt * 128:(tqt + 1) * 128],
                                        kb[b][r0:r0 + 64, c * 512:(c + 1) * 512],
                                        start=True, stop=stop),
                               waits=[prdy1] + self.bank_war(c),
                               inc=None if (c < 3 or has_mask) else self.s_mm)
                    if tqt == 0 and not _nomask:
                        p.emit("tensor", lambda e: e.matmul(self.fb(3, 0, 512), attc[0:1, 0:128],
                                                            attc[0:1, 768:1280], start=False, stop=True),
                               waits=[at_r], inc=self.s_mm)
                    if tqt == 15 and not _nomask:
                        p.emit("tensor", lambda e: e.matmul(self.fb(0, 0, 512), attc[0:1, 128:256],
                                                            attc[0:1, 256:768], start=False, stop=True),
                               waits=[at_r], inc=self.s_mm)
                    mmv = self.s_mm.n
                    v = p.emit("vector", lambda e, tqt=tqt:
                               e.reduce_max(mcols[:, tqt:tqt + 1], self.psf[:, 0:T], axis=AX.X),
                               waits=[(self.s_mm, mmv)], inc=self.s_ve)
                    for bk_i in range(4):
                        self.bank_read(bk_i, self.s_ve, v)
                e1 = p.emit("vector", lambda e: e.reduce_max(gcols[:, 0:1], mcols[:], axis=AX.X),
                            waits=[(self.s_ve, self.s_ve.n)], inc=self.s_ve)
                e2 = p.emit("tensor", lambda e: e.transpose(self.psf[0:1, 2048:2176], gcols[:, 0:1],
                                                            self.identf[:]),
                            waits=[(self.s_ve, e1)] + self.bank_war(4), inc=self.s_mm)
                e3 = p.emit("vector", lambda e: e.tensor_copy(csc[0:1, 4:132], self.psf[0:1, 2048:2176]),
                            waits=[(self.s_mm, e2)], inc=self.s_ve)
                self.bank_read(4, self.s_ve, e3)
                e4 = p.emit("vector", lambda e: e.reduce_max(csc[0:1, 0:1], csc[0:1, 4:132], axis=AX.X),
                            waits=[(self.s_ve, e3)], inc=self.s_ve)
                e5 = p.emit("vector", lambda e: e.tensor_scalar_mul(csc[0:1, 1:2], csc[0:1, 0:1], -0.125),
                            waits=[(self.s_ve, e4)], inc=self.s_ve)
                e6 = p.emit("tensor", lambda e: e.matmul(self.fb(4, 0, 1), self.onesf[0:1, :],
                                                         csc[0:1, 1:2], start=True, stop=True),
                            waits=[(self.s_ve, e5)] + self.bank_war(4), inc=self.s_mm)
                e7 = p.emit("vector", lambda e: e.tensor_copy(ncB[:], self.fb(4, 0, 1)),
                            waits=[(self.s_mm, e6)], inc=self.s_ve)
                self.bank_read(4, self.s_ve, e7)
                ncB_rdy = (self.s_ve, e7)
                if stop_after == "gc":
                    p.wait("vector", (self.s_ve, self.s_ve.n))
                    return [(self.s_ve, self.s_ve.n)]

                for tkt in range(16):
                    for c in range(4):
                        bk_b = 4 + (c % 2)
                        masked = (tkt == 15 and c == 0) or (tkt == 0 and c == 3)
                        p.emit("tensor", lambda e, tkt=tkt, c=c, b=b, r0=r0, bk_b=bk_b, masked=masked:
                               e.matmul(self.fb(bk_b, 0, 512),
                                        kb[b][r0:r0 + 64, tkt * 128:(tkt + 1) * 128],
                                        qb[b][r0:r0 + 64, c * 512:(c + 1) * 512],
                                        start=True, stop=not masked),
                               waits=[prdy1] + self.bank_war(bk_b),
                               inc=None if masked else self.s_mm)
                        if tkt == 15 and c == 0:
                            p.emit("tensor", lambda e, bk_b=bk_b:
                                   e.matmul(self.fb(bk_b, 0, 512), attc[0:1, 128:256],
                                            attc[0:1, 256:768], start=False, stop=True),
                                   inc=self.s_mm)
                        if tkt == 0 and c == 3:
                            p.emit("tensor", lambda e, bk_b=bk_b:
                                   e.matmul(self.fb(bk_b, 0, 512), attc[0:1, 0:128],
                                            attc[0:1, 768:1280], start=False, stop=True),
                                   inc=self.s_mm)
                        mmv = self.s_mm.n
                        a = p.emit("scalar", lambda e, c=c, bk_b=bk_b:
                                   e.activation(expT[:, c * 512:(c + 1) * 512],
                                                self.fb(bk_b, 0, 512), AF.Exp,
                                                bias=ncB[:], scale=0.125),
                                   waits=[(self.s_mm, mmv), ncB_rdy], inc=self.s_ac)
                        self.bank_read(bk_b, self.s_ac, a)
                    exp_rdy = (self.s_ac, self.s_ac.n)
                    for c in range(4):
                        p.emit("tensor", lambda e, tkt=tkt, c=c:
                               e.matmul(self.psf[0:1, c * 512:(c + 1) * 512],
                                        self.ones[:, 0:1], expT[:, c * 512:(c + 1) * 512],
                                        start=(tkt == 0), stop=(tkt == 15)),
                               waits=[exp_rdy] + (self.bank_war(0, 1, 2, 3) if (tkt == 0 and c == 0) else []))
                    for tqt in range(16):
                        p.emit("tensor", lambda e, tkt=tkt, tqt=tqt, hh=hh:
                               e.matmul(self.psf2[:, tqt * 64:(tqt + 1) * 64],
                                        expT[:, tqt * 128:(tqt + 1) * 128],
                                        vtm[:, tkt * 128 + hh * 64:tkt * 128 + hh * 64 + 64],
                                        start=(tkt == 0), stop=(tkt == 15)),
                               waits=[exp_rdy, vtm_rdy] + (self.bank_war(6, 7) if (tkt == 0 and tqt == 0) else []),
                               inc=self.s_mm if (tkt == 15 and tqt == 15) else None)
                av_done = self.s_mm.n
                if stop_after == "passB":
                    p.wait("vector", (self.s_mm, av_done))
                    p.wait("vector", (self.s_ac, self.s_ac.n))
                    return [(self.s_ve, self.s_ve.n)]
                p.emit("vector", lambda e: e.tensor_copy(srow[0:1, 0:1024], self.psf[0:1, 0:1024]),
                       waits=[(self.s_mm, av_done)], inc=self.s_ve)
                g1 = p.emit("vector", lambda e: e.tensor_copy(srow[64:65, 0:1024], self.psf[0:1, 1024:2048]),
                            inc=self.s_ve)
                for bk_i in range(4):
                    self.bank_read(bk_i, self.s_ve, g1)
                p.emit("vector", lambda e: e.tensor_scalar_add(srow[0:1, 0:1024], srow[0:1, 0:1024], 1e-30),
                       waits=[(self.s_ve, g1)], inc=self.s_ve)
                g2 = p.emit("vector", lambda e: e.tensor_scalar_add(srow[64:65, 0:1024], srow[64:65, 0:1024], 1e-30),
                            inc=self.s_ve)
                for tqt in range(16):
                    bk_t = 4 + (tqt % 2)
                    r_s = 64 * (tqt // 8)
                    c_s = (tqt % 8) * 128
                    mmv = p.emit("tensor", lambda e, bk_t=bk_t, r_s=r_s, c_s=c_s:
                                 e.transpose(self.fb(bk_t, 0, 128),
                                             srow[r_s:r_s + 1, c_s:c_s + 128],
                                             self.onesf[r_s:r_s + 1, :]),
                                 waits=[(self.s_ve, g2)] + self.bank_war(bk_t), inc=self.s_mm)
                    v = p.emit("vector", lambda e, tqt=tqt, bk_t=bk_t:
                               e.tensor_copy(gcols[:, tqt:tqt + 1], self.fb(bk_t, 0, 1)),
                               waits=[(self.s_mm, mmv)], inc=self.s_ve)
                    self.bank_read(bk_t, self.s_ve, v)
                g3 = p.emit("vector", lambda e: e.reciprocal(gcols[:], gcols[:]),
                            waits=[(self.s_ve, self.s_ve.n)], inc=self.s_ve)
                for tqt in range(16):
                    p.emit("vector", lambda e, tqt=tqt:
                           e.tensor_scalar_mul(o_h[:, tqt * 64:(tqt + 1) * 64],
                                               self.psf2[:, tqt * 64:(tqt + 1) * 64],
                                               gcols[:, tqt:tqt + 1]),
                           waits=[(self.s_ve, g3), (self.s_mm, av_done)], inc=self.s_ve)
                ohv = self.s_ve.n
                self.bank_read(6, self.s_ve, ohv)
                self.bank_read(7, self.s_ve, ohv)
                dd0 = h // 2
                rr = (h % 2) * 64
                for tqt in range(16):
                    bk_t = 4 + (tqt % 2)
                    mmv = p.emit("tensor", lambda e, tqt=tqt, bk_t=bk_t:
                                 e.transpose(self.psf[0:64, bk_t * 512:bk_t * 512 + 128],
                                             o_h[:, tqt * 64:(tqt + 1) * 64],
                                             self.identf[:]),
                                 waits=[(self.s_ve, ohv)] + self.bank_war(bk_t), inc=self.s_mm)
                    a = p.emit("scalar", lambda e, tqt=tqt, bk_t=bk_t, dd0=dd0, rr=rr:
                               e.copy(oT[dd0][rr:rr + 64, tqt * 128:(tqt + 1) * 128],
                                      self.psf[0:64, bk_t * 512:bk_t * 512 + 128]),
                               waits=[(self.s_mm, mmv)], inc=self.s_ac)
                    self.bank_read(bk_t, self.s_ac, a)
                oT_w[dd0] = self.s_ac.n
            pair_free[b] = self.s_mm.n

        if stop_after == "heads":
            p.wait("vector", (self.s_ac, self.s_ac.n))
            p.wait("vector", (self.s_mm, self.s_mm.n))
            return [(self.s_ve, self.s_ve.n)]
        oT_rdy = [(self.s_ac, oT_w[dd]) for dd in range(8)]
        self.proj(wbuf2, wo_r, oT, oT_rdy, D, self.ev_residual(xT), banks=(0, 1, 2, 3))
        return [(self.s_ve, self.s_ve.n)]


def host_prep_l0_attn_b(inp):
    f32 = np.float32
    d = {}
    for nm, key in (("WqT", "L0_attn_b_Wq"), ("WkT", "L0_attn_b_Wk"),
                    ("WvT", "L0_attn_b_Wv"), ("WoT", "L0_attn_b_Wo")):
        W = np.asarray(inp[key], f32)
        d["l0ab_" + nm] = np.ascontiguousarray(W.T.reshape(8, 128, 1024))
    for nm, key in (("bq", "L0_attn_b_bq"), ("bk", "L0_attn_b_bk"),
                    ("bv", "L0_attn_b_bv")):
        d["l0ab_" + nm] = np.ascontiguousarray(np.asarray(inp[key], f32).reshape(8, 128).T)
    att = np.zeros((1, 1280), f32)
    att[0, 0] = 1.0
    att[0, 128 + 127] = 1.0
    att[0, 256 + 0] = -8e9
    att[0, 768 + 511] = -8e9
    d["l0ab_attc"] = att
    return d

def host_prep_l1_ffn(inp):
    f32 = np.float32
    d = {}
    for nm, key, kc in (("w1T", "L1_ffn_w1_w", 8), ("w2T", "L1_ffn_w2_w", 8),
                        ("h1T", "L1_ffn_h1_w", 32), ("h2T", "L1_ffn_h2_w", 32),
                        ("w3T", "L1_ffn_w3_w", 32)):
        Wm = np.asarray(inp[key], f32)
        nf = Wm.shape[0] // 128
        d["l1f_" + nm] = np.ascontiguousarray(
            Wm.T.reshape(kc, 128, nf, 128).transpose(2, 1, 0, 3))
    for nm, key, nf in (("s1", "L1_ffn_w1_s", 32), ("s2", "L1_ffn_w2_s", 32),
                        ("h1s", "L1_ffn_h1_s", 32), ("h2s", "L1_ffn_h2_s", 32),
                        ("s3", "L1_ffn_w3_s", 8)):
        d["l1f_" + nm] = np.ascontiguousarray(np.asarray(inp[key], f32).reshape(nf, 128).T)
    return d



def sub_l1_ffn(self, xT, x_rdy, W):
    p = self.p
    HID = 4096
    NKH = 32
    xnbig = p.sbuf(self.name("xn"), (128, 8 * T), BF16)
    xn = [xnbig[:, i * T:(i + 1) * T] for i in range(8)]
    rdy = self.ln(xT, x_rdy, xn)
    xn_rdy = [rdy[0]] * 8
    p.wait("tensor", rdy[1])

    s1 = p.sbuf(self.name("s1"), (128, 32), F32)
    s2 = p.sbuf(self.name("s2"), (128, 32), F32)
    hs1 = p.sbuf(self.name("hs1"), (128, 32), F32)
    hs2 = p.sbuf(self.name("hs2"), (128, 32), F32)
    s3 = p.sbuf(self.name("s3"), (128, 8), F32)
    s12 = p.sbuf(self.name("s12"), (128, 32), F32)
    s_cl = self.spool(0)
    self.colload(s1[:], W["s1"], s_cl)
    self.colload(s2[:], W["s2"], s_cl)
    self.colload(hs1[:], W["h1s"], s_cl)
    self.colload(hs2[:], W["h2s"], s_cl)
    self.colload(s3[:], W["s3"], s_cl)
    cst = (s_cl, s_cl.n)
    r1 = r2 = r3 = r4 = r5 = cst
    sv = p.emit("vector", lambda e: e.tensor_tensor(s12[:], s1[:], s2[:], ALU.mult),
                waits=[cst], inc=self.s_ve)

    wA = [p.sbuf(self.name("wA"), (128, 1024), BF16) for _ in range(2)]
    wB = [p.sbuf(self.name("wB"), (128, 1024), BF16) for _ in range(2)]
    wH = [p.sbuf(self.name("wH"), (128, 4096), BF16) for _ in range(2)]
    h_a = [p.sbuf(self.name("ha"), (128, 512), BF16) for _ in range(32)]
    h_b = [p.sbuf(self.name("hb"), (128, 512), BF16) for _ in range(32)]
    t1 = p.sbuf(self.name("t1"), (128, 512), F32)
    t2 = p.sbuf(self.name("t2"), (128, 512), F32)

    wA_free = [0, 0]
    wB_free = [0, 0]
    wH_free = [0, 0]
    swA = [self.gpool(0), self.gpool(1)]
    swB = [self.gpool(2), self.gpool(3)]
    swH = [self.gpool(4), self.gpool(5)]
    self.dbg = {"h_a": h_a, "h_b": h_b, "xn": xn}
    import os
    NTC = int(os.environ.get("L1F_NTC", "4"))
    NSTG = int(os.environ.get("L1F_NSTG", "4"))

    for tc in range(NTC):
        c0 = tc * 512
        for fi in range(NKH):
            bsl = fi % 2
            wa, wb = wA[bsl], wB[bsl]
            wva0 = p.wdma(wa[:], W["w1T"][fi], swA[bsl],
                         waits=[(self.s_mm, wA_free[bsl])])
            wvb0 = p.wdma(wb[:], W["w2T"][fi], swB[bsl],
                         waits=[(self.s_mm, wB_free[bsl])])
            bk1, bk2 = (0, 1) if fi % 2 == 0 else (2, 3)
            for kc in range(8):
                p.emit("tensor", lambda e, kc=kc, fi=fi, bk1=bk1, wa=wa, c0=c0:
                       e.matmul(self.fb(bk1, 0, 512), wa[:, kc * 128:(kc + 1) * 128],
                                xn[kc][:, c0:c0 + 512], start=(kc == 0), stop=(kc == 7)),
                       waits=[(swA[bsl], wva0), xn_rdy[kc]] + (self.bank_war(bk1) if kc == 0 else []))
            mv1 = p.emit("tensor", lambda e, fi=fi, bk2=bk2, wb=wb, c0=c0:
                         e.matmul(self.fb(bk2, 0, 512), wb[:, 0:128],
                                  xn[0][:, c0:c0 + 512], start=True, stop=False),
                         waits=[(swB[bsl], wvb0)] + self.bank_war(bk2))
            for kc in range(1, 8):
                mv1 = p.emit("tensor", lambda e, kc=kc, bk2=bk2, wb=wb, c0=c0:
                             e.matmul(self.fb(bk2, 0, 512), wb[:, kc * 128:(kc + 1) * 128],
                                      xn[kc][:, c0:c0 + 512], start=False, stop=(kc == 7)),
                             inc=self.s_mm if kc == 7 else None)
            wA_free[bsl] = mv1
            wB_free[bsl] = mv1
            a1 = p.emit("scalar", lambda e, bk1=bk1, fi=fi:
                        e.activation(t1[:], self.fb(bk1, 0, 512), AF.Silu,
                                     scale=s1[:, fi:fi + 1]),
                        waits=[(self.s_mm, mv1), (self.s_ve, getattr(self, "_l1t1f", 0)), r1],
                        inc=self.s_ac)
            v0 = p.emit("vector", lambda e, bk2=bk2, fi=fi:
                        e.tensor_scalar_mul(t2[:], self.fb(bk2, 0, 512), s2[:, fi:fi + 1]),
                        waits=[(self.s_mm, mv1), r2, (self.s_ve, getattr(self, "_l1t2f", 0))],
                        inc=self.s_ve)
            v1 = p.emit("vector", lambda e, fi=fi:
                        e.tensor_tensor(h_a[fi][:], t1[:], t2[:], ALU.mult),
                        waits=[(self.s_ac, a1), (self.s_ve, v0)], inc=self.s_ve)
            self._l1t1f = v1
            self._l1t2f = v1
            self.bank_read(bk1, self.s_ac, a1)
            self.bank_read(bk2, self.s_ve, v0)
        ha_rdy = (self.s_ve, self.s_ve.n)

        stages = (("h1T", hs1, h_a, h_b), ("h2T", hs2, h_b, h_a))[:max(0, NSTG - 1)]
        for stage, (wkey, scol, src, dst) in enumerate(stages):
            src_rdy = (self.s_ve, self.s_ve.n) if stage == 0 else (self.s_ac, self.s_ac.n)
            if os.environ.get("L1F_DBG"):
                print(f"tc={tc} stage={stage} src_rdy={src_rdy[0].name}>={src_rdy[1]}")
            for fi in range(NKH):
                bsl = fi % 2
                wh = wH[bsl]
                wv = p.wdma(wh[:], W[wkey][fi], swH[bsl],
                            waits=[(self.s_mm, wH_free[bsl])])
                bk = (0, 1, 2, 3)[fi % 4]
                mv = None
                for kc in range(NKH):
                    mv = p.emit("tensor", lambda e, kc=kc, bk=bk, wh=wh, src=src:
                                e.matmul(self.fb(bk, 0, 512), wh[:, kc * 128:(kc + 1) * 128],
                                         src[kc][:], start=(kc == 0), stop=(kc == NKH - 1)),
                                waits=[(swH[bsl], wv), src_rdy] + (self.bank_war(bk) if kc == 0 else []),
                                inc=self.s_mm if kc == NKH - 1 else None)
                wH_free[bsl] = mv
                a1 = p.emit("scalar", lambda e, bk=bk, fi=fi, scol=scol, dst=dst:
                            e.activation(dst[fi][:], self.fb(bk, 0, 512), AF.Gelu,
                                         scale=scol[:, fi:fi + 1]),
                            waits=[(self.s_mm, mv), r3, r4], inc=self.s_ac)
                if os.environ.get("L1F_DBG") and fi == 0:
                    print(f"  tc={tc} stage={stage} fi0 a1={a1}")
                self.bank_read(bk, self.s_ac, a1)

        h_rdy = (self.s_ac, self.s_ac.n)
        for fi in range(8 if NSTG >= 4 else 0):
            bsl = fi % 2
            wh = wH[bsl]
            wv = p.wdma(wh[:], W["w3T"][fi], swH[bsl],
                        waits=[(self.s_mm, wH_free[bsl])])
            bk = 4 + fi % 2
            mv = None
            for kc in range(NKH):
                mv = p.emit("tensor", lambda e, kc=kc, bk=bk, wh=wh:
                            e.matmul(self.fb(bk, 0, 512), wh[:, kc * 128:(kc + 1) * 128],
                                     h_a[kc][:], start=(kc == 0), stop=(kc == NKH - 1)),
                            waits=[(swH[bsl], wv), h_rdy] + (self.bank_war(bk) if kc == 0 else []),
                            inc=self.s_mm if kc == NKH - 1 else None)
            wH_free[bsl] = mv
            v0 = p.emit("vector", lambda e, fi=fi, bk=bk:
                        e.tensor_scalar_mul(t2[:], self.fb(bk, 0, 512), s3[:, fi:fi + 1]),
                        waits=[(self.s_mm, mv), r5, (self.s_ve, getattr(self, "_l1t2f", 0))],
                        inc=self.s_ve)
            v1 = p.emit("vector", lambda e, fi=fi, c0=c0:
                        e.tensor_add(xT[fi][:, c0:c0 + 512], xT[fi][:, c0:c0 + 512], t2[:]),
                        inc=self.s_ve)
            self._l1t2f = v1
            self.bank_read(bk, self.s_ve, v0)
    return [(self.s_ve, self.s_ve.n)]



Mk.sub_l1_ffn = sub_l1_ffn


def sub_l2_attn_b(self, xT, x_rdy, W, scr):
    p = self.p
    s_og = self.spool(0)
    s_cl = self.spool(1)
    s_w1 = self.gpool(0)
    s_w2 = self.gpool(1)
    xnbig = p.sbuf(self.name("xn"), (128, 8 * T), BF16)
    xn = [xnbig[:, i * T:(i + 1) * T] for i in range(8)]
    rdy = self.ln(xT, x_rdy, xn)
    xn_rdy = [rdy[0]] * 8
    p.wait("tensor", rdy[1])

    cols = p.sbuf(self.name("cols"), (128, 24), F32)
    bqc, bgc, obc = cols[:, 0:8], cols[:, 8:16], cols[:, 16:24]
    e2b = p.sbuf(self.name("e2b"), (2, 128), BF16)
    bkB = p.sbuf(self.name("bkB"), (128, 1024), F32)
    bvB = p.sbuf(self.name("bvB"), (128, 1024), F32)
    s_cl2 = self.gpool(2)
    self.colload(cols[:, 0:8], W["bq"], s_cl)
    self.colload(cols[:, 8:16], W["bg"], s_cl)
    self.colload(cols[:, 16:24], W["outb"], s_cl)
    p.wdma(e2b[:], W["e2"], s_cl2)
    p.dma("sync", bkB[:], W["bkrow"].rearrange("(o f) -> o f", o=1).broadcast_to((128, 1024)), s_cl)
    p.dma("sync", bvB[:], W["bvrow"].rearrange("(o f) -> o f", o=1).broadcast_to((128, 1024)), s_cl)
    cst = (s_cl, s_cl.n)
    cst2 = (s_cl2, s_cl2.n)

    wb1 = p.sbuf(self.name("wb"), (128, 8192), BF16)
    wb2 = p.sbuf(self.name("wb"), (128, 8192), BF16)
    ktile = [p.sbuf(self.name("kt"), (128, 1024), BF16) for _ in range(2)]
    vtile = [p.sbuf(self.name("vt"), (128, 1024), BF16) for _ in range(2)]
    tA = p.sbuf(self.name("tA"), (128, 1024), F32)
    tB = p.sbuf(self.name("tB"), (128, 1024), F32)
    qb2 = [p.sbuf(self.name("qb2"), (128, T), BF16) for _ in range(2)]
    s_q = self.spool(2)
    ogt = [p.sbuf(self.name("og"), (128, T), BF16) for _ in range(2)]
    kvsb = p.sbuf(self.name("kvsb"), (128, 512), BF16)
    kvacc = p.sbuf(self.name("kvacc"), (128, 512), F32)
    ksacc = p.sbuf(self.name("ksacc"), (128, 8), F32)
    ksums = p.sbuf(self.name("ksums"), (128, 8), F32)
    ks2 = p.sbuf(self.name("ks2"), (128, 2), BF16)
    zt = p.sbuf(self.name("zt"), (2, 512), F32)
    ztb = p.sbuf(self.name("ztb"), (2, 512), BF16)
    zbs = p.sbuf(self.name("zbs"), (128, 512), F32)
    gs = p.sbuf(self.name("gs"), (128, 512), F32)

    wk_r = self.wload(wb1, W["wkT"], D, sem=s_w1)
    wv_r = self.wload(wb2, W["wvT"], D, sem=s_w2)
    self._l2b_free = [0, 0]
    for tt in range(16):
        sl = tt % 2
        kt, vt = ktile[sl], vtile[sl]
        for dst, wsb, w_r, bank in ((kt, wb1, wk_r, 0), (vt, wb2, wv_r, 1)):
            for fc in range(2):
                mv = None
                for kc in range(8):
                    mv = p.emit("tensor", lambda e, kc=kc, fc=fc, tt=tt, wsb=wsb, bank=bank:
                                e.matmul(self.fb(bank, 0, 512),
                                         xn[kc][:, tt * 128:(tt + 1) * 128],
                                         wsb[:, kc * D + fc * 512:kc * D + (fc + 1) * 512],
                                         start=(kc == 0), stop=(kc == 7)),
                                waits=[w_r, xn_rdy[kc], rdy[1],
                                       (self.s_ve, self._l2b_free[sl])]
                                      + (self.bank_war(bank) if kc == 0 else []),
                                inc=self.s_mm if kc == 7 else None)
                if bank == 0:
                    v0 = p.emit("vector", lambda e, fc=fc: e.tensor_add(
                        tA[:, fc * 512:(fc + 1) * 512], self.fb(0, 0, 512),
                        bkB[:, fc * 512:(fc + 1) * 512]),
                        waits=[(self.s_mm, mv), cst,
                               (self.s_ve, getattr(self, "_l2b_tf", 0))], inc=self.s_ve)
                    self.bank_read(0, self.s_ve, v0)
                    v1 = p.emit("vector", lambda e, fc=fc: e.tensor_scalar_min(
                        tB[:, fc * 512:(fc + 1) * 512],
                        tA[:, fc * 512:(fc + 1) * 512], 0.0), inc=self.s_ve)
                    a1 = p.emit("scalar", lambda e, fc=fc: e.activation(
                        tB[:, fc * 512:(fc + 1) * 512],
                        tB[:, fc * 512:(fc + 1) * 512], AF.Exp),
                        waits=[(self.s_ve, v1)], inc=self.s_ac)
                    v2 = p.emit("vector", lambda e, fc=fc: e.tensor_scalar_add(
                        tA[:, fc * 512:(fc + 1) * 512],
                        tA[:, fc * 512:(fc + 1) * 512], 1.0), inc=self.s_ve)
                    v3 = p.emit("vector", lambda e, fc=fc, kt=kt: e.tensor_max(
                        kt[:, fc * 512:(fc + 1) * 512],
                        tA[:, fc * 512:(fc + 1) * 512],
                        tB[:, fc * 512:(fc + 1) * 512]),
                        waits=[(self.s_ac, a1)], inc=self.s_ve)
                    self._l2b_tf = v3
                else:
                    v3 = p.emit("vector", lambda e, fc=fc, vt=vt: e.tensor_add(
                        vt[:, fc * 512:(fc + 1) * 512], self.fb(1, 0, 512),
                        bvB[:, fc * 512:(fc + 1) * 512]),
                        waits=[(self.s_mm, mv), cst], inc=self.s_ve)
                    self.bank_read(1, self.s_ve, v3)
        trdy = (self.s_ve, self.s_ve.n)
        for h in range(16):
            hp, par = h // 2, h % 2
            tp = (0, par * 64)
            p.emit("tensor", lambda e, h=h, hp=hp, par=par, kt=kt, vt=vt, tp=tp:
                   e.matmul(self.psf[par * 64:par * 64 + 64,
                                     4 * 512 + hp * 64:4 * 512 + (hp + 1) * 64],
                            kt[:, h * 64:(h + 1) * 64], vt[:, h * 64:(h + 1) * 64],
                            start=True, stop=True, tile_position=tp),
                   waits=[trdy] + (self.bank_war(4) if h == 0 else []),
                   inc=self.s_mm if h == 15 else None)
            p.emit("tensor", lambda e, h=h, hp=hp, par=par, kt=kt, tp=tp:
                   e.matmul(self.psf[par * 64:par * 64 + 64, 5 * 512 + hp:5 * 512 + hp + 1],
                            kt[:, h * 64:(h + 1) * 64], self.ones[:, 0:1],
                            start=True, stop=True, tile_position=tp),
                   waits=[trdy] + (self.bank_war(5) if h == 0 else []),
                   inc=self.s_mm if h == 15 else None)
        mmv = self.s_mm.n
        if tt == 0:
            v1 = p.emit("vector", lambda e: e.tensor_copy(kvacc[:], self.fb(4, 0, 512)),
                        waits=[(self.s_mm, mmv)], inc=self.s_ve)
            v2 = p.emit("vector", lambda e: e.tensor_copy(ksacc[:], self.fb(5, 0, 8)),
                        inc=self.s_ve)
        else:
            v1 = p.emit("vector", lambda e: e.tensor_add(kvacc[:], kvacc[:], self.fb(4, 0, 512)),
                        waits=[(self.s_mm, mmv)], inc=self.s_ve)
            v2 = p.emit("vector", lambda e: e.tensor_add(ksacc[:], ksacc[:], self.fb(5, 0, 8)),
                        inc=self.s_ve)
        self.bank_read(4, self.s_ve, v1)
        self.bank_read(5, self.s_ve, v2)
        self._l2b_free[sl] = self.s_mm.n
    kv_done = self.s_mm.n
    v1 = p.emit("vector", lambda e: e.tensor_copy(kvsb[:, 0:512], kvacc[:]),
                inc=self.s_ve)
    v2 = p.emit("vector", lambda e: e.tensor_copy(ksums[:], ksacc[:]),
                inc=self.s_ve)
    kv_rdy = (self.s_ve, v2)

    wq_r = self.wload(wb1, W["wqT"], D, sem=s_w1, waits=[(self.s_mm, kv_done)])
    self._l2bq = 0
    q_spill = [0, 0]

    def evq(fi, c, ps, mm_pair):
        sl = fi % 2
        v0 = p.emit("vector", lambda e, fi=fi, ps=ps:
                    e.tensor_scalar(tA[:, 0:512], ps, bqc[:, fi:fi + 1], 0.0,
                                    ALU.add, ALU.min),
                    waits=[mm_pair, cst, (self.s_ve, self._l2bq)], inc=self.s_ve)
        a1 = p.emit("scalar", lambda e: e.activation(tA[:, 0:512], tA[:, 0:512], AF.Exp),
                    waits=[(self.s_ve, v0)], inc=self.s_ac)
        v1 = p.emit("vector", lambda e, fi=fi, ps=ps:
                    e.tensor_scalar(tB[:, 0:512], ps, bqc[:, fi:fi + 1], 1.0,
                                    ALU.add, ALU.add), inc=self.s_ve)
        v2 = p.emit("vector", lambda e, fi=fi, c=c, sl=sl:
                    e.tensor_max(ogt[sl][:, c * 512:(c + 1) * 512], tB[:, 0:512], tA[:, 0:512]),
                    waits=[(self.s_ac, a1), (self.s_ve, v1), (s_q, q_spill[sl])],
                    inc=self.s_ve)
        self._l2bq = v2
        if c == NT - 1:
            p.dma("sync", scr["kd"][fi * 128:(fi + 1) * 128, :], ogt[sl][:],
                  s_q, waits=[(self.s_ve, v2)])
            q_spill[sl] = s_q.n
        return (self.s_ve, v2)
    self.proj(wb1, wq_r, xn, xn_rdy, D, evq, banks=(0, 1, 2, 3))
    q_all = (s_q, s_q.n)
    qload = []
    q_free = [0, 0]

    def load_q(dd):
        v = p.dma("sync", qb2[dd % 2][:], scr["kd"][dd * 128:(dd + 1) * 128, :],
                  s_q, waits=[q_all, (self.s_mm, q_free[dd % 2])])
        qload.append((s_q, v))
    load_q(0)

    wg_r = self.wload(wb2, W["wgT"], D, sem=s_w2, waits=[(self.s_mm, kv_done)])
    self._l2bz = 0
    og_spill = [0, 0]
    for dd in range(8):
        if dd < 7:
            load_q(dd + 1)
        q_rdy = qload[dd]
        v0 = p.emit("vector", lambda e: e.memset(ks2[:], 0.0),
                    waits=[(self.s_mm, self._l2bz), kv_rdy], inc=self.s_ve)
        v1 = p.emit("vector", lambda e, dd=dd: e.tensor_copy(ks2[0:64, 0:1], ksums[0:64, dd:dd + 1]),
                    waits=[(self.s_ve, v0)], inc=self.s_ve)
        v2 = p.emit("vector", lambda e, dd=dd: e.tensor_copy(ks2[64:128, 1:2], ksums[64:128, dd:dd + 1]),
                    waits=[(self.s_ve, v1)], inc=self.s_ve)
        for c in range(4):
            bk = 4 + c % 2
            mv = p.emit("tensor", lambda e, dd=dd, c=c, bk=bk:
                        e.matmul(self.fb(bk, 0, 512)[0:2, :], ks2[:],
                                 qb2[dd % 2][:, c * 512:(c + 1) * 512], start=True, stop=True),
                        waits=[(self.s_ve, v2), q_rdy] + self.bank_war(bk), inc=self.s_mm)
            z0 = p.emit("vector", lambda e, bk=bk:
                        e.tensor_scalar_add(zt[0:2, :], self.fb(bk, 0, 512)[0:2, :], 1e-6),
                        waits=[(self.s_mm, mv)], inc=self.s_ve)
            self.bank_read(bk, self.s_ve, z0)
            z2 = p.emit("vector", lambda e: e.reciprocal(zt[0:2, :], zt[0:2, :]),
                        waits=[(self.s_ve, z0)], inc=self.s_ve)
            z3 = p.emit("vector", lambda e: e.tensor_copy(ztb[0:2, :], zt[0:2, :]),
                        waits=[(self.s_ve, z2)], inc=self.s_ve)
            mvz = p.emit("tensor", lambda e, c=c:
                         e.matmul(self.fb(2, 0, 512), e2b[:],
                                  ztb[0:2, :], start=True, stop=True),
                         waits=[(self.s_ve, z3), cst2] + self.bank_war(2), inc=self.s_mm)
            vz = p.emit("vector", lambda e, c=c: e.tensor_copy(zbs[:], self.fb(2, 0, 512)),
                        waits=[(self.s_mm, mvz)], inc=self.s_ve)
            self.bank_read(2, self.s_ve, vz)
            for par in range(2):
                p.emit("tensor", lambda e, dd=dd, c=c, par=par:
                       e.matmul(self.fb(0, 0, 512)[par * 64:par * 64 + 64, :],
                                kvsb[par * 64:par * 64 + 64, dd * 64:(dd + 1) * 64],
                                qb2[dd % 2][par * 64:par * 64 + 64, c * 512:(c + 1) * 512],
                                start=True, stop=True, tile_position=(par * 64, par * 64)),
                       waits=[kv_rdy, q_rdy] + (self.bank_war(0) if par == 0 else []),
                       inc=self.s_mm if par == 1 else None)
            mvo = self.s_mm.n
            mvg = None
            for kc in range(8):
                mvg = p.emit("tensor", lambda e, dd=dd, c=c, kc=kc:
                             e.matmul(self.fb(1, 0, 512),
                                      wb2[:, kc * D + dd * 128:kc * D + dd * 128 + 128],
                                      xn[kc][:, c * 512:(c + 1) * 512],
                                      start=(kc == 0), stop=(kc == 7)),
                             waits=[wg_r, xn_rdy[kc]] + (self.bank_war(1) if kc == 0 else []),
                             inc=self.s_mm if kc == 7 else None)
            ag = p.emit("scalar", lambda e, dd=dd: e.activation(gs[:], self.fb(1, 0, 512),
                                                                AF.Sigmoid, bias=bgc[:, dd:dd + 1]),
                        waits=[(self.s_mm, mvg), cst,
                               (self.s_ve, getattr(self, "_l2bg", 0))], inc=self.s_ac)
            self.bank_read(1, self.s_ac, ag)
            vt1 = p.emit("vector", lambda e: e.tensor_mul(tA[:, 0:512], self.fb(0, 0, 512), zbs[:]),
                         waits=[(self.s_mm, mvo), (self.s_ve, vz),
                                (self.s_ve, getattr(self, "_l2bt", 0))], inc=self.s_ve)
            self.bank_read(0, self.s_ve, vt1)
            vt2 = p.emit("vector", lambda e, dd=dd, c=c: e.tensor_mul(
                ogt[dd % 2][:, c * 512:(c + 1) * 512], tA[:, 0:512], gs[:]),
                waits=[(self.s_ac, ag), (self.s_ve, vt1),
                       (s_og, og_spill[dd % 2])], inc=self.s_ve)
            self._l2bg = vt2
            self._l2bt = vt2
        p.dma("sync", scr["qd"][dd * 128:(dd + 1) * 128, :],
              ogt[dd % 2][:], s_og, waits=[(self.s_ve, self.s_ve.n)])
        og_spill[dd % 2] = s_og.n
        q_free[dd % 2] = self.s_mm.n
        self._l2bz = self.s_mm.n
    og_rdy = (s_og, s_og.n)
    for i in range(8):
        p.dma("sync", xn[i][:], scr["qd"][i * 128:(i + 1) * 128, :], s_og,
              waits=[og_rdy])
    og_rdy = (s_og, s_og.n)
    og_final = xn

    ow_r = self.wload(wb1, W["outw"], D, sem=s_w1, waits=[(self.s_mm, self.s_mm.n)])

    def evout(fi, c, ps, mm_pair):
        v0 = p.emit("vector", lambda e, fi=fi, ps=ps:
                    e.tensor_scalar_add(tB[:, 0:512], ps, obc[:, fi:fi + 1]),
                    waits=[mm_pair, cst, (self.s_ve, getattr(self, "_l2bo", 0))],
                    inc=self.s_ve)
        v1 = p.emit("vector", lambda e, fi=fi, c=c:
                    e.tensor_add(xT[fi][:, c * 512:(c + 1) * 512],
                                 xT[fi][:, c * 512:(c + 1) * 512], tB[:, 0:512]),
                    inc=self.s_ve)
        self._l2bo = v1
        return (self.s_ve, v1)
    self.proj(wb1, ow_r, og_final, [og_rdy] * 8, D, evout, banks=(0, 1, 2, 3))
    return [(self.s_ve, self.s_ve.n)]


Mk.sub_l2_attn_b = sub_l2_attn_b


def host_prep_l2_attn_b(inp):
    f32 = np.float32
    d = {}
    for nm, key in (("wqT", "L2_attn_b_wq_w"), ("wkT", "L2_attn_b_wk_w"),
                    ("wvT", "L2_attn_b_wv_w"), ("wgT", "L2_attn_b_wg_w"),
                    ("outw", "L2_attn_b_out_w")):
        Wm = np.asarray(inp[key], f32)
        d["l2b_" + nm] = np.ascontiguousarray(Wm.reshape(8, 128, 1024))
    for nm, key in (("bq", "L2_attn_b_wq_b"), ("bg", "L2_attn_b_wg_b"),
                    ("outb", "L2_attn_b_out_b")):
        d["l2b_" + nm] = np.ascontiguousarray(np.asarray(inp[key], f32).reshape(8, 128).T)
    d["l2b_bkrow"] = np.asarray(inp["L2_attn_b_wk_b"], f32)
    d["l2b_bvrow"] = np.asarray(inp["L2_attn_b_wv_b"], f32)
    e2 = np.zeros((2, 128), f32)
    e2[0, 0:64] = 1.0
    e2[1, 64:128] = 1.0
    d["l2b_e2"] = e2
    return d


def sub_l3_attn_a(self, xT, x_rdy, W, scr):
    p = self.p
    s_cl = self.spool(0)
    s_w1 = self.gpool(0)
    s_so = [self.spool(1), self.spool(2)]
    s_rl = [self.spool(3), self.spool(4)]
    s_kl = [self.spool(5), self.spool(6)]

    xnbig = p.sbuf(self.name("xn"), (128, 8 * T), BF16)
    xn = [xnbig[:, i * T:(i + 1) * T] for i in range(8)]
    rdy = self.ln(xT, x_rdy, xn)
    xn_rdy = [rdy[0]] * 8
    p.wait("tensor", rdy[1])

    cols = p.sbuf(self.name("cols"), (128, 48), F32)
    rmu, kmu, vmu, wde, uu = (cols[:, 0:8], cols[:, 8:16], cols[:, 16:24],
                              cols[:, 24:32], cols[:, 32:40])
    self.colload(cols[:, 0:8], W["rmu"], s_cl)
    self.colload(cols[:, 8:16], W["kmu"], s_cl)
    self.colload(cols[:, 16:24], W["vmu"], s_cl)
    self.colload(cols[:, 24:32], W["wdec"], s_cl)
    self.colload(cols[:, 32:40], W["uu"], s_cl)
    cB = p.sbuf(self.name("cB"), (128, T), F32)
    p.dma("sync", cB[:], W["cnt"].rearrange("(o f) -> o f", o=1).broadcast_to((128, T)), s_cl)
    cst = (s_cl, s_cl.n)

    wb1 = p.sbuf(self.name("wb"), (128, 8192), BF16)
    tA = p.sbuf(self.name("tA"), (128, T), F32)
    tB = p.sbuf(self.name("tB"), (128, T), F32)

    scope = p.scope()
    wbA = p.sbuf(self.name("wA"), (128, 8192), BF16)
    wbB = p.sbuf(self.name("wB"), (128, 8192), BF16)
    stag = [p.sbuf(self.name("st"), (128, T), BF16) for _ in range(2)]
    stagf = p.sbuf(self.name("sf"), (128, T), F32)

    def sproj(wkey, mucol, dst, fp32=False):
        w_r = self.wload(wb1, W[wkey], D, sem=s_w1, waits=[(self.s_mm, self.s_mm.n)])
        fv = None
        for kc in range(8):
            p.emit("vector", lambda e, kc=kc, mucol=mucol:
                   e.tensor_scalar_mul(wbA[:, kc * D:(kc + 1) * D],
                                       wb1[:, kc * D:(kc + 1) * D], mucol[:, kc:kc + 1]),
                   waits=[w_r, cst, (self.s_mm, getattr(self, "_l3w", 0))], inc=self.s_ve)
            fv = p.emit("vector", lambda e, kc=kc:
                        e.tensor_sub(wbB[:, kc * D:(kc + 1) * D],
                                     wb1[:, kc * D:(kc + 1) * D], wbA[:, kc * D:(kc + 1) * D]),
                        inc=self.s_ve)
        w2r = (self.s_ve, fv)
        stg_free = [0, 0]
        for fi in range(8):
            for c in range(NT):
                bk = (fi * NT + c) % 4
                war = self.bank_war(bk)
                mv = None
                for kc in range(8):
                    p.emit("tensor", lambda e, kc=kc, fi=fi, c=c, bk=bk:
                           e.matmul(self.fb(bk, 0, 512),
                                    wbA[:, kc * D + fi * 128:kc * D + fi * 128 + 128],
                                    xn[kc][:, c * 512:(c + 1) * 512],
                                    start=(kc == 0), stop=False),
                           waits=[w2r, xn_rdy[kc]] + (war if kc == 0 else []))
                for kc in range(8):
                    if c == 0:
                        mv = p.emit("tensor", lambda e, kc=kc, fi=fi, bk=bk:
                                    e.matmul(self.fb(bk, 1, 511),
                                             wbB[:, kc * D + fi * 128:kc * D + fi * 128 + 128],
                                             xn[kc][:, 0:511],
                                             start=False, stop=(kc == 7)),
                                    inc=self.s_mm if kc == 7 else None)
                    else:
                        mv = p.emit("tensor", lambda e, kc=kc, fi=fi, c=c, bk=bk:
                                    e.matmul(self.fb(bk, 0, 512),
                                             wbB[:, kc * D + fi * 128:kc * D + fi * 128 + 128],
                                             xn[kc][:, c * 512 - 1:(c + 1) * 512 - 1],
                                             start=False, stop=(kc == 7)),
                                    inc=self.s_mm if kc == 7 else None)
                sl = fi % 2
                sb = stagf if fp32 else stag[sl]
                sem = s_so[0] if fp32 else s_so[sl]
                vv = p.emit("vector", lambda e, c=c, bk=bk, sb=sb:
                            e.tensor_copy(sb[:, c * 512:(c + 1) * 512], self.fb(bk, 0, 512)),
                            waits=[(self.s_mm, mv), (sem, stg_free[0 if fp32 else sl])],
                            inc=self.s_ve)
                self.bank_read(bk, self.s_ve, vv)
                if c == NT - 1:
                    p.dma("sync", dst[fi * 128:(fi + 1) * 128, :], sb[:], sem,
                          waits=[(self.s_ve, vv)])
                    stg_free[0 if fp32 else sl] = sem.n
        self._l3w = self.s_mm.n

    sproj("WrT", rmu, scr["qd"])
    sproj("WkT", kmu, scr["kd32"], fp32=True)
    sproj("WvT", vmu, scr["vd"])
    allw = [(s_so[0], s_so[0].n), (s_so[1], s_so[1].n)]
    scope.close()

    swkv = xn
    rrow = [p.sbuf(self.name("rr"), (128, T), BF16) for _ in range(2)]
    kr0 = p.sbuf(self.name("kr"), (128, T), F32)
    krow = [kr0, kr0]
    vrow = [p.sbuf(self.name("vr"), (128, T), BF16) for _ in range(2)]
    ek = p.sbuf(self.name("ek"), (128, T), F32)
    As = p.sbuf(self.name("As"), (128, T), F32)
    Bs = p.sbuf(self.name("Bs"), (128, T), F32)
    ecol = p.sbuf(self.name("ec"), (128, 2), F32)
    row_free = [0, 0]
    rdys = []

    def load_rows(dd):
        sl = dd % 2
        w = allw + [(self.s_ve, row_free[sl])]
        wk = allw + [(self.s_ac, getattr(self, "_l3kfree", 0))]
        p.dma("sync", rrow[sl][:], scr["qd"][dd * 128:(dd + 1) * 128, :], s_rl[sl], waits=w)
        p.dma("sync", krow[sl][:], scr["kd32"][dd * 128:(dd + 1) * 128, :], s_kl[sl], waits=wk)
        v = p.dma("sync", vrow[sl][:], scr["vd"][dd * 128:(dd + 1) * 128, :], s_kl[sl], waits=w)
        rdys.append([(s_rl[sl], s_rl[sl].n), (s_kl[sl], s_kl[sl].n)])

    load_rows(0)
    out_rdy = []
    for dd in range(8):
        sl = dd % 2
        rd = rdys[dd]
        prev = (self.s_ve, getattr(self, "_l3s", 0))
        a0 = p.emit("scalar", lambda e, dd=dd: e.activation(ecol[:, 0:1], wde[:, dd:dd + 1],
                                                            AF.Exp, scale=-1.0),
                    waits=[cst, prev], inc=self.s_ac)
        a1 = p.emit("scalar", lambda e, dd=dd: e.activation(ecol[:, 1:2], wde[:, dd:dd + 1],
                                                            AF.Exp),
                    inc=self.s_ac)
        a2 = p.emit("scalar", lambda e, sl=sl: e.activation(ek[:], krow[sl][:], AF.Exp),
                    waits=rd + [prev], inc=self.s_ac)
        v0 = p.emit("vector", lambda e, sl=sl: e.tensor_mul(tA[:], ek[:], vrow[sl][:]),
                    waits=[(self.s_ac, a2), prev], inc=self.s_ve)
        v1 = p.emit("vector", lambda e: e.memset(tA[:, 0:1], 0.0), inc=self.s_ve)
        s1 = p.emit("vector", lambda e: e.tensor_tensor_scan(
            As[:], ecol[:, 0:1].broadcast_to((128, T)), tA[:], 0.0, ALU.mult, ALU.add),
            waits=[(self.s_ve, v1), (self.s_ac, a0)], inc=self.s_ve)
        a3 = p.emit("scalar", lambda e, sl=sl: e.activation(ek[:], krow[sl][:], AF.Exp,
                                                            scale=-1.0),
                    waits=[(self.s_ve, v0)], inc=self.s_ac)
        v2 = p.emit("vector", lambda e: e.memset(ek[:, 0:1], 0.0),
                    waits=[(self.s_ac, a3)], inc=self.s_ve)
        s2 = p.emit("vector", lambda e: e.tensor_tensor_scan(
            Bs[:], ecol[:, 1:2].broadcast_to((128, T)), ek[:], 0.0, ALU.mult, ALU.add),
            waits=[(self.s_ve, v2), (self.s_ac, a1)], inc=self.s_ve)
        v3 = p.emit("vector", lambda e, sl=sl: e.tensor_mul(tB[:, 1:T], vrow[sl][:, 1:T],
                                                            Bs[:, 0:T - 1]),
                    waits=[(self.s_ve, s2)], inc=self.s_ve)
        v4 = p.emit("vector", lambda e: e.memset(tB[:, 0:1], 0.0), inc=self.s_ve)
        v5 = p.emit("vector", lambda e: e.tensor_add(tB[:], tB[:], cB[:]),
                    waits=[cst, (self.s_ve, v4)], inc=self.s_ve)
        a4 = p.emit("scalar", lambda e, sl=sl, dd=dd: e.activation(ek[:], krow[sl][:], AF.Exp,
                                                                   bias=uu[:, dd:dd + 1]),
                    waits=[cst, (self.s_ve, s2)], inc=self.s_ac)
        v6 = p.emit("vector", lambda e: e.tensor_mul(tB[:], tB[:], ek[:]),
                    waits=[(self.s_ac, a4), (self.s_ve, v5)], inc=self.s_ve)
        v7 = p.emit("vector", lambda e: e.tensor_add(tB[:, 1:T], tB[:, 1:T], As[:, 0:T - 1]),
                    waits=[(self.s_ve, s1)], inc=self.s_ve)
        a5 = p.emit("scalar", lambda e, sl=sl: e.activation(tA[:], rrow[sl][:], AF.Sigmoid),
                    waits=[rd[0], (self.s_ve, v3)], inc=self.s_ac)
        self._l3kfree = a4
        v8 = p.emit("vector", lambda e, dd=dd: e.tensor_mul(swkv[dd][:], tA[:], tB[:]),
                    waits=[(self.s_ac, a5), (self.s_ve, v7)], inc=self.s_ve)
        self._l3s = v8
        row_free[sl] = v8
        out_rdy.append((self.s_ve, v8))
        if dd < 7:
            load_rows(dd + 1)

    wo_r = self.wload(wb1, W["WoT"], D, sem=s_w1, waits=[(self.s_mm, self.s_mm.n)])
    self.proj(wb1, wo_r, swkv, out_rdy, D, self.ev_residual(xT), banks=(0, 1, 2, 3))
    return [(self.s_ve, self.s_ve.n)]


Mk.sub_l3_attn_a = sub_l3_attn_a


def host_prep_l3_attn_a(inp):
    f32 = np.float32
    d = {}
    for nm, key in (("WrT", "L3_attn_a_Wr"), ("WkT", "L3_attn_a_Wk"),
                    ("WvT", "L3_attn_a_Wv"), ("WoT", "L3_attn_a_Wo")):
        Wm = np.asarray(inp[key], f32)
        d["l3a_" + nm] = np.ascontiguousarray(Wm.T.reshape(8, 128, 1024))
    for nm, key in (("rmu", "L3_attn_a_r_mu"), ("kmu", "L3_attn_a_k_mu"),
                    ("vmu", "L3_attn_a_v_mu"), ("wdec", "L3_attn_a_w"),
                    ("uu", "L3_attn_a_u")):
        d["l3a_" + nm] = np.ascontiguousarray(np.asarray(inp[key], f32).reshape(8, 128).T)
    d["l3a_cnt"] = np.maximum(np.arange(T, dtype=f32) - 1, 0)
    return d


def glob_c(self, mcols_ap, ncols, scale, dst_col, wait_v, bank=2, waits2=()):
    p = self.p
    g = self.gscr
    c1 = p.emit("vector", lambda e: e.reduce_max(g[:, 0:1], mcols_ap[:, 0:ncols], axis=AX.X),
                waits=[wait_v], inc=self.s_ve)
    c2 = p.emit("tensor", lambda e: e.transpose(self.psf[0:1, bank * 512:bank * 512 + 128],
                                                g[:, 0:1], self.identf[:]),
                waits=[(self.s_ve, c1)] + self.bank_war(bank), inc=self.s_mm)
    c3 = p.emit("vector", lambda e: e.tensor_copy(self.gscr2[0:1, :],
                                                  self.psf[0:1, bank * 512:bank * 512 + 128]),
                waits=[(self.s_mm, c2)], inc=self.s_ve)
    self.bank_read(bank, self.s_ve, c3)
    c4 = p.emit("vector", lambda e: e.reduce_max(self.gscr2[0:1, 0:1], self.gscr2[0:1, :], axis=AX.X),
                waits=[(self.s_ve, c3)], inc=self.s_ve)
    c5 = p.emit("vector", lambda e, scale=scale: e.tensor_scalar_mul(self.gscr2[0:1, 1:2],
                                                                     self.gscr2[0:1, 0:1], -scale),
                waits=[(self.s_ve, c4)], inc=self.s_ve)
    c6 = p.emit("tensor", lambda e: e.matmul(self.psf[0:128, bank * 512:bank * 512 + 1],
                                             self.onesf[0:1, :], self.gscr2[0:1, 1:2],
                                             start=True, stop=True),
                waits=[(self.s_ve, c5)] + self.bank_war(bank), inc=self.s_mm)
    c7 = p.emit("vector", lambda e, dst_col=dst_col: e.tensor_copy(dst_col, self.psf[:, bank * 512:bank * 512 + 1]),
                waits=[(self.s_mm, c6)] + list(waits2), inc=self.s_ve)
    self.bank_read(bank, self.s_ve, c7)
    return (self.s_ve, c7)


Mk.glob_c = glob_c


DBG_SPILL_O = False
LF_STAGE = "full"


def sub_local_ffn(self, xT, x_rdy, W, scr, depth, heads, window, pfx,
                  do_attn=True, do_ffn=True):
    p = self.p
    INNER = heads * 64
    NI = INNER // 128
    NW = T // window
    WT = window // 128
    NH = 22
    NA = 22
    s_cl = self.spool(0)
    s_cg = self.gpool(0)
    s_sp = self.spool(1)
    s_w1 = self.gpool(1)
    s_wh = [self.gpool(2), self.gpool(3)]
    s_qs = [self.spool(2), self.spool(3)]
    s_pl = [self.spool(4), self.spool(5)]
    s_pv = [self.spool(6), self.spool(7)]

    for dd in range(8):
        p.dma("sync", scr["xspill"][dd * 128:(dd + 1) * 128, :], xT[dd][:],
              s_sp, waits=[x_rdy[dd]])
    spilled = (s_sp, s_sp.n)

    xnbig = p.sbuf(self.name("xn"), (128, 8 * T), BF16)
    xn = [xnbig[:, i * T:(i + 1) * T] for i in range(8)]
    rdy = self.ln(xT, x_rdy, xn)
    p.wait("tensor", rdy[1])
    xl_rdy = []
    for dd in range(8):
        v = p.emit("vector", lambda e, dd=dd: e.tensor_copy(xT[dd][:], xn[dd][:]),
                   waits=[rdy[0], spilled], inc=self.s_ve)
        xl_rdy.append((self.s_ve, v))

    ctab = p.sbuf(self.name("cq"), (128, T), BF16)
    stab = p.sbuf(self.name("sq"), (128, T), BF16)
    rotP = p.sbuf(self.name("rp"), (128, 128), BF16)
    p.wdma(rotP[:], W["rotP"], s_cg)
    e2f = p.sbuf(self.name("e2"), (2, 128), F32)
    e64 = p.sbuf(self.name("e64"), (128, 2), F32)
    p.dma("sync", e2f[:], W["e2f"], s_cl)
    p.dma("sync", e64[:], W["e64"], s_cl)
    mskA = p.sbuf(self.name("mA"), (128, (WT - 1) * 128 + 2 * window), BF16)
    mskT = p.sbuf(self.name("mT"), (128, (2 * WT - 1) * 128 + window), BF16)
    for nm, b in (("mskA", mskA), ("mskT", mskT)):
        p.wdma(b[:], W[nm], s_cg)
    cst = (s_cl, s_cl.n)
    cstg = (s_cg, s_cg.n)

    wbuf = xnbig[:, 0:max(8 * INNER, NI * 1024)]
    wh2 = [p.sbuf(self.name("w2"), (128, NH * 128), BF16) for _ in range(2)]
    shs = [p.sbuf(self.name("sh"), (128, T), BF16) for _ in range(8)]
    if NI == 4:
        stag = [xn[4], xn[5]]
        qpr = [xn[6]]
        kpr = [xn[7]]
        vpr = [xn[5]]
    else:
        stag = [p.sbuf(self.name("st"), (128, T), BF16) for _ in range(2)]
        qpr = [stag[1]]
        kpr = [p.sbuf(self.name("kp"), (128, T), BF16)]
        vpr = [stag[0]]
    lcol = p.sbuf(self.name("lc"), (128, 4), F32)
    rB = p.sbuf(self.name("rB"), (128, T), F32)
    tA = p.sbuf(self.name("tA"), (128, T), F32)
    rr8 = rB[0:2, :]
    esc = p.sbuf(self.name("es"), (128, 2 * window), BF16)
    mcols = p.sbuf(self.name("mc"), (128, 16), F32)
    gcols = p.sbuf(self.name("gc"), (128, 16), F32)
    ncB = p.sbuf(self.name("nc"), (128, 1), F32)
    o_h = p.sbuf(self.name("oh"), (128, WT * 64), F32)
    avacc = p.sbuf(self.name("av"), (128, WT * 64), F32)
    srowb = self.n_tm[0][64:65, 0:window]
    _ha_src = ([0, 1, 2, 3, 5, 6, 7] if NI == 4 else [0, 1, 2, 3, 4, 5])
    ha = []
    for j in range(NA):
        t_i = _ha_src[j // 4]
        ha.append(xn[t_i][:, (j % 4) * 512:(j % 4) * 512 + 512])
    lgb = p.sbuf(self.name("lg"), (128, 24), F32)
    b1c = p.sbuf(self.name("b1"), (128, 2 * NA), F32)
    self.gscr = self.stc[:, 8:12]
    self.gscr2 = self.n_tm[0][0:1, 0:128]
    oTf = shs

    self._lf_shf = 0
    self._lf_tA = 0
    self._lf_esc = 0
    self._lf_ha = 0
    wh_free = [0, 0]

    def shift_set(src_rdy):
        outs = []
        for dd in range(8):
            if dd < 4:
                v = p.emit("vector", lambda e, dd=dd: e.tensor_copy(shs[dd][:], xT[dd][:]),
                           waits=[src_rdy[dd], (self.s_mm, self._lf_shf)], inc=self.s_ve)
            else:
                p.emit("vector", lambda e, dd=dd:
                       e.tensor_copy(shs[dd][:, 1:T], xT[dd][:, 0:T - 1]),
                       waits=[src_rdy[dd], (self.s_mm, self._lf_shf)], inc=self.s_ve)
                v = p.emit("vector", lambda e, dd=dd: e.memset(shs[dd][:, 0:1], 0.0),
                           inc=self.s_ve)
            outs.append((self.s_ve, v))
        return outs

    def qk_pipeline(lw, sh_rdy):
        LFQ = int(globals().get("LFQ_LVL", 4))
        wv0 = [(self.s_ve, self.s_ve.n)]
        self.colload(lcol[:, 0:1], lw["qs"], s_cl, waits=wv0)
        qsk_r = self.colload(lcol[:, 1:2], lw["ks"], s_cl, waits=wv0)
        csl = (s_cl, s_cl.n)
        stg_free = [0, 0]
        for which, dst_d, cnm, snm, scol in (
                (0, scr["qd"], "cosq", "sinq", lcol[:, 0:1]),
                (1, scr["kd"], "cosk", "sink", lcol[:, 1:2])):
            p.wdma(ctab[:], W[cnm], s_cg,
                  waits=[(self.s_ve, self.s_ve.n)])
            p.wdma(stab[:], W[snm], s_cg,
                  waits=[(self.s_ve, self.s_ve.n)])
            cstg2 = (s_cg, s_cg.n)
            w_r = self.wload(wbuf, lw["wqkvT"][which], INNER, sem=s_w1,
                             waits=[(self.s_mm, self.s_mm.n)])
            for fi in range(NI):
                sl = fi % 2
                sg = stag[sl]
                for c in range(4):
                    bk = c % 2
                    mv = None
                    for kc in range(8):
                        mv = p.emit("tensor", lambda e, kc=kc, fi=fi, c=c, bk=bk:
                                    e.matmul(self.fb(bk, 0, 512),
                                             wbuf[:, kc * INNER + fi * 128:kc * INNER + fi * 128 + 128],
                                             shs[kc][:, c * 512:(c + 1) * 512],
                                             start=(kc == 0), stop=(kc == 7)),
                                    waits=[w_r, sh_rdy[kc]] + (self.bank_war(bk) if kc == 0 else []),
                                    inc=self.s_mm if kc == 7 else None)
                    v0 = p.emit("vector", lambda e, c=c, bk=bk, sg=sg:
                                e.tensor_copy(sg[:, c * 512:(c + 1) * 512], self.fb(bk, 0, 512)),
                                waits=[(self.s_mm, mv), (s_qs[sl], stg_free[sl])], inc=self.s_ve)
                    self.bank_read(bk, self.s_ve, v0)
                    if LFQ < 1:
                        continue
                    a0 = p.emit("scalar", lambda e, bk=bk: e.activation(tA[:, 0:512],
                                                                        self.fb(bk, 0, 512), AF.Square),
                                waits=[(self.s_mm, mv), (self.s_ve, v0),
                                       (self.s_ve, self._lf_tA)], inc=self.s_ac)
                    self.bank_read(bk, self.s_ac, a0)
                    self._lf_tA = self.s_ac.n
                    if LFQ == 10:
                        continue
                    mv2 = p.emit("tensor", lambda e: e.matmul(self.fb(2, 0, 512)[0:2, :],
                                                              e64[:], tA[:, 0:512],
                                                              start=True, stop=True),
                                 waits=[(self.s_ac, a0), cst] + self.bank_war(2), inc=self.s_mm)
                    if LFQ == 11:
                        continue
                    v1 = p.emit("vector", lambda e, c=c:
                                e.tensor_copy(rr8[0:2, c * 512:(c + 1) * 512],
                                              self.fb(2, 0, 512)[0:2, :]),
                                waits=[(self.s_mm, mv2)], inc=self.s_ve)
                    self.bank_read(2, self.s_ve, v1)
                if LFQ < 2:
                    continue
                z0 = p.emit("vector", lambda e: e.tensor_scalar_add(rr8[0:2, :], rr8[0:2, :], 1e-24),
                            waits=[(self.s_ve, self.s_ve.n)], inc=self.s_ve)
                z1 = p.emit("scalar", lambda e: e.activation(rr8[0:2, :], rr8[0:2, :], AF.Ln),
                            waits=[(self.s_ve, z0)], inc=self.s_ac)
                z2 = p.emit("scalar", lambda e: e.activation(rr8[0:2, :], rr8[0:2, :], AF.Exp,
                                                             scale=-0.5),
                            waits=[(self.s_ac, z1)], inc=self.s_ac)
                for c in range(4):
                    mv = p.emit("tensor", lambda e, c=c: e.matmul(self.fb(2, 0, 512), e2f[:],
                                                                  rr8[0:2, c * 512:(c + 1) * 512],
                                                                  start=True, stop=True),
                                waits=[(self.s_ac, z2)] + self.bank_war(2), inc=self.s_mm)
                    v1 = p.emit("vector", lambda e, c=c: e.tensor_copy(rB[:, c * 512:(c + 1) * 512],
                                                                       self.fb(2, 0, 512)),
                                waits=[(self.s_mm, mv)], inc=self.s_ve)
                    self.bank_read(2, self.s_ve, v1)
                v2 = p.emit("vector", lambda e, sg=sg, scol=scol:
                            e.tensor_scalar_mul(tA[:], sg[:], scol),
                            waits=[(self.s_ve, self.s_ve.n), csl], inc=self.s_ve)
                v3 = p.emit("vector", lambda e, sg=sg: e.tensor_mul(sg[:], tA[:], rB[:]),
                            inc=self.s_ve)
                self._lf_tA = v3
                if LFQ < 3:
                    continue
                for c in range(4):
                    mv = p.emit("tensor", lambda e, c=c, sg=sg:
                                e.matmul(self.fb(2, 0, 512), rotP[:],
                                         sg[:, c * 512:(c + 1) * 512], start=True, stop=True),
                                waits=[(self.s_ve, v3), cstg] + self.bank_war(2), inc=self.s_mm)
                    w0 = p.emit("vector", lambda e, c=c:
                                e.tensor_mul(tA[:, 0:512], self.fb(2, 0, 512),
                                             stab[:, c * 512:(c + 1) * 512]),
                                waits=[(self.s_mm, mv), cstg2], inc=self.s_ve)
                    self.bank_read(2, self.s_ve, w0)
                    w1 = p.emit("vector", lambda e, c=c, sg=sg:
                                e.tensor_mul(tA[:, 512:1024], sg[:, c * 512:(c + 1) * 512],
                                             ctab[:, c * 512:(c + 1) * 512]),
                                inc=self.s_ve)
                    w2 = p.emit("vector", lambda e, c=c, sg=sg:
                                e.tensor_add(sg[:, c * 512:(c + 1) * 512],
                                             tA[:, 0:512], tA[:, 512:1024]),
                                inc=self.s_ve)
                self._lf_tA = self.s_ve.n
                if LFQ < 4:
                    continue
                p.dma("sync", dst_d[fi * 128:(fi + 1) * 128, :], sg[:], s_qs[sl],
                      waits=[(self.s_ve, self.s_ve.n)])
                stg_free[sl] = s_qs[sl].n

    def v_pipeline(lw, sh_rdy):
        w_r = self.wload(wbuf, lw["wqkvT"][2], INNER, sem=s_w1,
                         waits=[(self.s_mm, self.s_mm.n)])
        stg_free = [0, 0]
        for tt in range(16):
            for pr in range(NI):
                sl = (tt * NI + pr) % 2
                mv = None
                for kc in range(8):
                    mv = p.emit("tensor", lambda e, kc=kc, tt=tt, pr=pr:
                                e.matmul(self.fb(4, 0, 128),
                                         shs[kc][:, tt * 128:(tt + 1) * 128],
                                         wbuf[:, kc * INNER + pr * 128:kc * INNER + (pr + 1) * 128],
                                         start=(kc == 0), stop=(kc == 7)),
                                waits=[w_r, sh_rdy[kc]] + (self.bank_war(4) if kc == 0 else []),
                                inc=self.s_mm if kc == 7 else None)
                vv = p.emit("vector", lambda e, sl=sl: e.tensor_copy(stag[sl][:, 0:128],
                                                                     self.fb(4, 0, 128)),
                            waits=[(self.s_mm, mv), (s_qs[sl], stg_free[sl])], inc=self.s_ve)
                self.bank_read(4, self.s_ve, vv)
                p.dma("sync", scr["vd2"][pr][:, tt * 128:(tt + 1) * 128], stag[sl][:, 0:128],
                      s_qs[sl], waits=[(self.s_ve, vv)])
                stg_free[sl] = s_qs[sl].n
        return [(s_qs[0], s_qs[0].n), (s_qs[1], s_qs[1].n)]

    def attn_core(lw, qkv_done):
        wo_r = self.wload(wbuf, lw["woT"], 1024, KC=NI, sem=s_w1,
                          waits=[(self.s_mm, self.s_mm.n)])
        pair_free = [0, 0]
        prdys = []

        def load_pair(pr):
            sl = 0
            w = qkv_done + [(self.s_mm, pair_free[sl])]
            p.dma("sync", qpr[sl][:], scr["qd"][pr * 128:(pr + 1) * 128, :], s_pl[sl], waits=w)
            p.dma("sync", kpr[sl][:], scr["kd"][pr * 128:(pr + 1) * 128, :], s_pl[sl], waits=w)
            p.dma("sync", vpr[sl][:], scr["vd2"][pr], s_pv[sl], waits=w)
            prdys.append([(s_pl[sl], s_pl[sl].n), (s_pv[sl], s_pv[sl].n)])

        oT_w = {}
        for pr in range(NI):
            sl = 0
            load_pair(pr)
            prdy = prdys[pr]
            for hh in range(2):
                h = pr * 2 + hh
                r0 = hh * 64
                for w in range(NW):
                    nk = WT if w == 0 else 2 * WT
                    kb0 = (w - 1) * window if w > 0 else 0
                    pass
                    for ti in range(WT):
                        tq0 = w * window + ti * 128
                        nck = cdiv(nk * 128, 512)
                        for ck in range(nck):
                            wd = min(512, nk * 128 - ck * 512)
                            bk = ck % 2
                            mv = p.emit("tensor", lambda e, r0=r0, tq0=tq0, ck=ck, bk=bk, kb0=kb0, sl=sl, wd=wd:
                                        e.matmul(self.fb(bk, 0, wd),
                                                 qpr[sl][r0:r0 + 64, tq0:tq0 + 128],
                                                 kpr[sl][r0:r0 + 64, kb0 + ck * 512:kb0 + ck * 512 + wd],
                                                 start=True, stop=True),
                                        waits=[prdy[0]] + self.bank_war(bk), inc=self.s_mm)
                            moff = ((window if w == 0 else 0) + ck * 512) + (WT - 1 - ti) * 128
                            vv = p.emit("vector", lambda e, ck=ck, bk=bk, wd=wd, moff=moff:
                                        e.tensor_add(tA[:, ck * 512:ck * 512 + wd], self.fb(bk, 0, wd),
                                                     mskA[:, moff:moff + wd]),
                                        waits=[(self.s_mm, mv), cstg,
                                               (self.s_ac, self._lf_esc)], inc=self.s_ve)
                            self.bank_read(bk, self.s_ve, vv)
                        p.emit("vector", lambda e, ti=ti, nk=nk:
                               e.reduce_max(mcols[:, ti:ti + 1], tA[:, 0:nk * 128], axis=AX.X),
                               inc=self.s_ve)
                    crdy = self.glob_c(mcols[:], WT, 8.0, ncB[:], (self.s_ve, self.s_ve.n), bank=2,
                                       waits2=[(self.s_ac, self._lf_esc)])
                    esc_free = getattr(self, "_lf_escfree", [0, 0])
                    for tk in range(nk):
                        bk = 3 + tk % 2
                        slo = tk % 2
                        mv = p.emit("tensor", lambda e, r0=r0, tk=tk, w=w, bk=bk, kb0=kb0, sl=sl:
                                    e.matmul(self.fb(bk, 0, window),
                                             kpr[sl][r0:r0 + 64, kb0 + tk * 128:kb0 + (tk + 1) * 128],
                                             qpr[sl][r0:r0 + 64, w * window:(w + 1) * window],
                                             start=True, stop=True),
                                    waits=[prdy[0]] + self.bank_war(bk), inc=self.s_mm)
                        tkoff = ((2 * WT - 1) - (tk + (WT if w == 0 else 0))) * 128
                        vv = p.emit("vector", lambda e, bk=bk, tkoff=tkoff:
                                    e.tensor_add(tA[:, 0:window], self.fb(bk, 0, window),
                                                 mskT[:, tkoff:tkoff + window]),
                                    waits=[(self.s_mm, mv), cstg,
                                           (self.s_ac, self._lf_esc)], inc=self.s_ve)
                        self.bank_read(bk, self.s_ve, vv)
                        aa = p.emit("scalar", lambda e, slo=slo:
                                    e.activation(esc[:, slo * window:(slo + 1) * window],
                                                 tA[:, 0:window], AF.Exp, scale=8.0, bias=ncB[:]),
                                    waits=[(self.s_ve, vv), crdy,
                                           (self.s_mm, esc_free[slo])], inc=self.s_ac)
                        self._lf_esc = aa
                        erdy = (self.s_ac, aa)
                        p.emit("tensor", lambda e, tk=tk, nk=nk, slo=slo:
                               e.matmul(self.psf[0:1, 2 * 512:2 * 512 + window],
                                        self.ones[:, 0:1], esc[:, slo * window:(slo + 1) * window],
                                        start=(tk == 0), stop=(tk == nk - 1)),
                               waits=[erdy] + (self.bank_war(2) if tk == 0 else []),
                               inc=self.s_mm if tk == nk - 1 else None)
                        for ti in range(WT):
                            p.emit("tensor", lambda e, tk=tk, ti=ti, hh=hh, kb0=kb0, sl=sl, slo=slo:
                                   e.matmul(self.psf2[:, ti * 64:(ti + 1) * 64],
                                            esc[:, slo * window + ti * 128:slo * window + (ti + 1) * 128],
                                            vpr[sl][:, (kb0 // 128 + tk) * 128 + hh * 64:
                                                    (kb0 // 128 + tk) * 128 + hh * 64 + 64],
                                            start=True, stop=True),
                                   waits=[erdy, prdy[1]] + (self.bank_war(6, 7) if ti == 0 else []),
                                   inc=self.s_mm if ti == WT - 1 else None)
                        mvav = self.s_mm.n
                        if tk == 0:
                            vav = p.emit("vector", lambda e: e.tensor_copy(
                                avacc[:, 0:WT * 64], self.psf2[:, 0:WT * 64]),
                                waits=[(self.s_mm, mvav)], inc=self.s_ve)
                        else:
                            vav = p.emit("vector", lambda e: e.tensor_add(
                                avacc[:, 0:WT * 64], avacc[:, 0:WT * 64], self.psf2[:, 0:WT * 64]),
                                waits=[(self.s_mm, mvav)], inc=self.s_ve)
                        self.bank_read(6, self.s_ve, vav)
                        self.bank_read(7, self.s_ve, vav)
                        esc_free[slo] = self.s_mm.n
                    self._lf_escfree = esc_free
                    av_done = self.s_mm.n
                    g1 = p.emit("vector", lambda e: e.tensor_scalar_add(
                        srowb, self.psf[0:1, 2 * 512:2 * 512 + window], 1e-30),
                        waits=[(self.s_mm, av_done)], inc=self.s_ve)
                    self.bank_read(2, self.s_ve, g1)
                    g2 = p.emit("vector", lambda e: e.reciprocal(srowb, srowb),
                                waits=[(self.s_ve, g1)], inc=self.s_ve)
                    for ti in range(WT):
                        mv = p.emit("tensor", lambda e, ti=ti:
                                    e.matmul(self.fb(2, 0, 1),
                                             self.n_tm[0][64:65, ti * 128:(ti + 1) * 128],
                                             self.identf[64:65, 64:65],
                                             start=True, stop=True),
                                    waits=[(self.s_ve, g2)] + self.bank_war(2), inc=self.s_mm)
                        vv = p.emit("vector", lambda e, ti=ti:
                                    e.tensor_copy(gcols[:, ti:ti + 1], self.fb(2, 0, 1)),
                                    waits=[(self.s_mm, mv)], inc=self.s_ve)
                        self.bank_read(2, self.s_ve, vv)
                    for ti in range(WT):
                        p.emit("vector", lambda e, ti=ti:
                               e.tensor_scalar_mul(o_h[:, ti * 64:(ti + 1) * 64],
                                                   avacc[:, ti * 64:(ti + 1) * 64],
                                                   gcols[:, ti:ti + 1]),
                               waits=[(self.s_ve, self.s_ve.n)], inc=self.s_ve)
                    ohv = self.s_ve.n
                    self.bank_read(7, self.s_ve, ohv)
                    dd0, rr = h // 2, (h % 2) * 64
                    for ti in range(WT):
                        mv = p.emit("tensor", lambda e, ti=ti:
                                    e.transpose(self.psf[0:64, 2 * 512:2 * 512 + 128],
                                                o_h[:, ti * 64:(ti + 1) * 64], self.identf[:]),
                                    waits=[(self.s_ve, ohv)] + self.bank_war(2), inc=self.s_mm)
                        aa = p.emit("scalar", lambda e, ti=ti, dd0=dd0, rr=rr, w=w:
                                    e.copy(oTf[dd0][rr:rr + 64,
                                                    w * window + ti * 128:w * window + (ti + 1) * 128],
                                           self.psf[0:64, 2 * 512:2 * 512 + 128]),
                                    waits=[(self.s_mm, mv)], inc=self.s_ac)
                        self.bank_read(2, self.s_ac, aa)
                    oT_w[dd0] = self.s_ac.n
            pair_free[sl] = self.s_mm.n
        o_rdy = [(self.s_ac, oT_w[dd]) for dd in range(NI)]
        if globals().get('DBG_SPILL_O'):
            for dd in range(NI):
                p.dma("sync", scr["vd"][dd * 128:(dd + 1) * 128, :], oTf[dd][:],
                      self.s_ld, waits=[o_rdy[dd]])
        self.proj(wbuf, wo_r, oTf, o_rdy, 1024, self.ev_residual(xT), banks=(0, 1, 3, 4), KC=NI)
        return [(self.s_ve, self.s_ve.n)] * 8

    def ffn_block(lw, sh_rdy):
        wv = [(self.s_ve, self.s_ve.n)]
        self.colload(lgb[:, 0:8], lw["lg"], s_cl, waits=wv)
        self.colload(lgb[:, 8:16], lw["lb"], s_cl, waits=wv)
        self.colload(b1c[:, 0:NA], lw["b1a"], s_cl, waits=wv)
        b1_r = self.colload(b1c[:, NA:2 * NA], lw["b1g"], s_cl, waits=wv)
        csl = (s_cl, s_cl.n)
        for dd in range(8):
            for c in range(4):
                p.emit("tensor", lambda e, dd=dd, c=c:
                       e.matmul(self.psf[0:1, c * 512:(c + 1) * 512], self.ones[:, 0:1],
                                shs[dd][:, c * 512:(c + 1) * 512],
                                start=(dd == 0), stop=(dd == 7)),
                       waits=[sh_rdy[dd]] + (self.bank_war(0, 1, 2, 3) if (dd == 0 and c == 0) else []),
                       inc=self.s_mm if (dd == 7 and c == 3) else None)
        smm = self.s_mm.n
        g0 = p.emit("vector", lambda e: e.tensor_copy(rB[0:1, :], self.psf[0:1, 0:T]),
                    waits=[(self.s_mm, smm)], inc=self.s_ve)
        for b in range(4):
            self.bank_read(b, self.s_ve, g0)
        for dd in range(8):
            a0 = p.emit("scalar", lambda e, dd=dd: e.activation(tA[:], shs[dd][:], AF.Square),
                        waits=[sh_rdy[dd], (self.s_mm, getattr(self, "_lf_sq", 0))], inc=self.s_ac)
            for c in range(4):
                p.emit("tensor", lambda e, dd=dd, c=c:
                       e.matmul(self.psf[0:1, c * 512:(c + 1) * 512], self.onesf[:, 0:1],
                                tA[:, c * 512:(c + 1) * 512],
                                start=(dd == 0), stop=(dd == 7)),
                       waits=[(self.s_ac, a0)] + (self.bank_war(0, 1, 2, 3) if (dd == 0 and c == 0) else []),
                       inc=self.s_mm if c == 3 else None)
            self._lf_sq = self.s_mm.n
        smm2 = self.s_mm.n
        g1 = p.emit("vector", lambda e: e.tensor_copy(tA[0:1, :], self.psf[0:1, 0:T]),
                    waits=[(self.s_mm, smm2)], inc=self.s_ve)
        for b in range(4):
            self.bank_read(b, self.s_ve, g1)
        inv = 1.0 / 1024.0
        c1 = p.emit("vector", lambda e: e.tensor_scalar_mul(rB[0:1, :], rB[0:1, :], inv),
                    waits=[(self.s_ve, g1)], inc=self.s_ve)
        c2 = p.emit("vector", lambda e: e.tensor_scalar_mul(tA[0:1, :], tA[0:1, :], inv),
                    waits=[(self.s_ve, c1)], inc=self.s_ve)
        c3 = p.emit("vector", lambda e: e.tensor_mul(stag[0][0:1, :], rB[0:1, :], rB[0:1, :]),
                    waits=[(self.s_ve, c2)], inc=self.s_ve)
        c4 = p.emit("vector", lambda e: e.tensor_sub(tA[0:1, :], tA[0:1, :], stag[0][0:1, :]),
                    waits=[(self.s_ve, c3)], inc=self.s_ve)
        c5 = p.emit("vector", lambda e: e.tensor_scalar_add(tA[0:1, :], tA[0:1, :], 1e-5),
                    waits=[(self.s_ve, c4)], inc=self.s_ve)
        c6 = p.emit("scalar", lambda e: e.activation(tA[0:1, :], tA[0:1, :], AF.Ln),
                    waits=[(self.s_ve, c5)], inc=self.s_ac)
        c7 = p.emit("scalar", lambda e: e.activation(tA[0:1, :], tA[0:1, :], AF.Exp,
                                                     scale=-0.5),
                    waits=[(self.s_ac, c6)], inc=self.s_ac)
        c8 = p.emit("vector", lambda e: e.tensor_mul(stag[0][0:1, :], tA[0:1, :], rB[0:1, :]),
                    waits=[(self.s_ac, c7)], inc=self.s_ve)
        c9 = p.emit("vector", lambda e: e.tensor_scalar_mul(stag[0][0:1, :], stag[0][0:1, :], -1.0),
                    waits=[(self.s_ve, c8)], inc=self.s_ve)
        for c in range(4):
            mv = p.emit("tensor", lambda e, c=c: e.matmul(self.fb(c, 0, 512), self.onesf[0:1, :],
                                                          tA[0:1, c * 512:(c + 1) * 512],
                                                          start=True, stop=True),
                        waits=[(self.s_ve, c9)] + self.bank_war(c), inc=self.s_mm)
            vv = p.emit("vector", lambda e, c=c: e.tensor_copy(rB[:, c * 512:(c + 1) * 512],
                                                               self.fb(c, 0, 512)),
                        waits=[(self.s_mm, mv)], inc=self.s_ve)
            self.bank_read(c, self.s_ve, vv)
        for c in range(4):
            mv = p.emit("tensor", lambda e, c=c: e.matmul(self.fb(c, 0, 512), self.ones[0:1, :],
                                                          stag[0][0:1, c * 512:(c + 1) * 512],
                                                          start=True, stop=True),
                        waits=[(self.s_ve, self.s_ve.n)] + self.bank_war(c), inc=self.s_mm)
            vv = p.emit("vector", lambda e, c=c: e.tensor_copy(tA[:, c * 512:(c + 1) * 512],
                                                               self.fb(c, 0, 512)),
                        waits=[(self.s_mm, mv)], inc=self.s_ve)
            self.bank_read(c, self.s_ve, vv)
        nsh = []
        for dd in range(8):
            u0 = p.emit("vector", lambda e, dd=dd: e.tensor_mul(stag[0][:], shs[dd][:], rB[:]),
                        waits=[(self.s_ve, self.s_ve.n)], inc=self.s_ve)
            u1 = p.emit("vector", lambda e: e.tensor_add(stag[0][:], stag[0][:], tA[:]),
                        inc=self.s_ve)
            u2 = p.emit("vector", lambda e, dd=dd:
                        e.tensor_scalar(shs[dd][:], stag[0][:], lgb[:, dd:dd + 1],
                                        lgb[:, 8 + dd:8 + dd + 1], ALU.mult, ALU.add),
                        waits=[csl], inc=self.s_ve)
            nsh.append((self.s_ve, u2))
        for tc in range(4):
            c0 = tc * 512
            for fi in range(NA):
                sl = fi % 2
                wva = p.wdma(wh2[sl][:, 0:1024], lw["w1a"][fi], s_wh[sl],
                            waits=[(self.s_mm, wh_free[sl])])
                wvb = p.wdma(wh2[sl][:, 1024:2048], lw["w1g"][fi], s_wh[sl],
                            waits=[(self.s_mm, wh_free[sl])])
                mva = None
                for kc in range(8):
                    p.emit("tensor", lambda e, kc=kc, sl=sl, c0=c0:
                           e.matmul(self.fb(0, 0, 512), wh2[sl][:, kc * 128:(kc + 1) * 128],
                                    shs[kc][:, c0:c0 + 512], start=(kc == 0), stop=(kc == 7)),
                           waits=[(s_wh[sl], wvb), nsh[kc]] + (self.bank_war(0) if kc == 0 else []))
                for kc in range(8):
                    mva = p.emit("tensor", lambda e, kc=kc, sl=sl, c0=c0:
                                 e.matmul(self.fb(1, 0, 512), wh2[sl][:, 1024 + kc * 128:1024 + (kc + 1) * 128],
                                          shs[kc][:, c0:c0 + 512], start=(kc == 0), stop=(kc == 7)),
                                 waits=[(s_wh[sl], wvb)] + (self.bank_war(1) if kc == 0 else []),
                                 inc=self.s_mm if kc == 7 else None)
                wh_free[sl] = mva
                ag = p.emit("scalar", lambda e, fi=fi: e.activation(tA[:, 0:512], self.fb(1, 0, 512),
                                                                    AF.Gelu,
                                                                    bias=b1c[:, NA + fi:NA + fi + 1]),
                            waits=[(self.s_mm, mva), b1_r, (self.s_ve, self._lf_ha)], inc=self.s_ac)
                self.bank_read(1, self.s_ac, ag)
                v0 = p.emit("vector", lambda e, fi=fi: e.tensor_scalar_add(tA[:, 512:1024],
                                                                           self.fb(0, 0, 512),
                                                                           b1c[:, fi:fi + 1]),
                            waits=[(self.s_mm, mva), b1_r], inc=self.s_ve)
                self.bank_read(0, self.s_ve, v0)
                v1 = p.emit("vector", lambda e, fi=fi: e.tensor_mul(ha[fi], tA[:, 512:1024],
                                                                    tA[:, 0:512]),
                            waits=[(self.s_ac, ag), (self.s_ve, v0)], inc=self.s_ve)
                self._lf_ha = v1
            ha_rdy = (self.s_ve, self.s_ve.n)
            for fi in range(8):
                sl = fi % 2
                wv = p.wdma(wh2[sl][:], lw["w2T"][fi], s_wh[sl],
                           waits=[(self.s_mm, wh_free[sl])])
                mv = None
                for kc in range(NH):
                    mv = p.emit("tensor", lambda e, kc=kc, sl=sl:
                                e.matmul(self.fb(4, 0, 512), wh2[sl][:, kc * 128:(kc + 1) * 128],
                                         ha[kc], start=(kc == 0), stop=(kc == NH - 1)),
                                waits=[(s_wh[sl], wv), ha_rdy] + (self.bank_war(4) if kc == 0 else []),
                                inc=self.s_mm if kc == NH - 1 else None)
                wh_free[sl] = mv
                v0 = p.emit("vector", lambda e, fi=fi, c0=c0:
                            e.tensor_add(xT[fi][:, c0:c0 + 512], xT[fi][:, c0:c0 + 512],
                                         self.fb(4, 0, 512)),
                            waits=[(self.s_mm, mv)], inc=self.s_ve)
                self.bank_read(4, self.s_ve, v0)
        return [(self.s_ve, self.s_ve.n)] * 8

    self._lf_sq = 0
    cur_rdy = xl_rdy
    for l in range(depth):
        lw = {k: W[f"{l}_{k}"] for k in
              ("wqkvT", "qs", "ks", "woT", "lg", "lb", "w1a", "w1g", "w2T",
               "b1a", "b1g", "b2")}
        if do_attn:
            stage = globals().get("LF_STAGE", "full")
            sh_rdy = shift_set(cur_rdy)
            self._lf_shf = self.s_mm.n
            qk_pipeline(lw, sh_rdy)
            if stage != "qk":
                qkv_done = v_pipeline(lw, sh_rdy)
                if stage != "v":
                    cur_rdy = attn_core(lw, qkv_done)
        if do_ffn:
            sh_rdy = shift_set(cur_rdy)
            self._lf_shf = self.s_mm.n
            cur_rdy = ffn_block(lw, sh_rdy)
            b2_r = self.colload(lgb[:, 16:24], W[f"{l}_b2"], s_cl,
                                waits=[(self.s_ve, self.s_ve.n)])
            nr = []
            for dd in range(8):
                v = p.emit("vector", lambda e, dd=dd:
                           e.tensor_scalar_add(xT[dd][:], xT[dd][:], lgb[:, 16 + dd:16 + dd + 1]),
                           waits=[b2_r, cur_rdy[dd]], inc=self.s_ve)
                nr.append((self.s_ve, v))
            cur_rdy = nr

    rel = [tA, rB]
    rfree = [self.s_ve.n, self.s_ve.n]
    s_rl = [self.spool(8), self.spool(9)]
    for dd in range(8):
        sl = dd % 2
        v = p.dma("sync", rel[sl][:], scr["xspill"][dd * 128:(dd + 1) * 128, :], s_rl[sl],
                  waits=[(self.s_ve, rfree[sl])])
        vv = p.emit("vector", lambda e, dd=dd, sl=sl: e.tensor_add(xT[dd][:], xT[dd][:], rel[sl][:]),
                    waits=[(s_rl[sl], v), cur_rdy[dd]], inc=self.s_ve)
        rfree[sl] = vv
    return [(self.s_ve, self.s_ve.n)]


Mk.sub_local_ffn = sub_local_ffn



def sub_l2_attn_a(self, xT, x_rdy, W, scr):
    p = self.p
    s_cl = self.spool(0)
    s_w = self.gpool(0)
    s_sp = [self.spool(1), self.spool(2)]
    s_ts = [self.spool(3), self.spool(4)]
    s_pl = self.spool(5)
    s_ug = self.gpool(1)

    xn = [p.sbuf(self.name("xn"), (128, T), BF16) for _ in range(8)]
    rdy = self.ln(xT, x_rdy, xn)
    p.wait("tensor", rdy[1])
    xn_rdy = [rdy[0]] * 8

    wbuf = p.sbuf(self.name("wb"), (128, 8192), BF16)
    utm = p.sbuf(self.name("ut"), (128, 128), BF16)
    p.wdma(utm[:], W["utm"], s_ug)
    cst = (s_cl, s_cl.n)
    ftmp = p.sbuf(self.name("ft"), (128, 1024), F32)
    qsg = [p.sbuf(self.name("qg"), (128, T), BF16) for _ in range(2)]
    tsg = [p.sbuf(self.name("tg"), (128, 512), BF16) for _ in range(2)]
    vd3 = scr["vd"].rearrange("(a p) t -> a p t", p=128)
    kd3 = scr["kd2"]

    self._a2_ft = 0
    sg_free = [0, 0]
    ts_free = [0, 0]

    def fmap_evac(dst_bf, ps, mm_pair, extra=(), w=512):
        g1 = p.emit("vector", lambda e, ps=ps, w=w:
                    e.tensor_scalar_min(ftmp[:, 0:w], ps, 0.0),
                    waits=[mm_pair, (self.s_ac, self._a2_ft)] + list(extra),
                    inc=self.s_ve)
        g2 = p.emit("scalar", lambda e, w=w: e.activation(ftmp[:, 0:w],
                                                          ftmp[:, 0:w], AF.Exp),
                    waits=[(self.s_ve, g1)], inc=self.s_ac)
        g3 = p.emit("vector", lambda e, ps=ps, w=w:
                    e.tensor_scalar_add(ftmp[:, 512:512 + w], ps, 1.0),
                    waits=[(self.s_ve, g1)], inc=self.s_ve)
        g4 = p.emit("vector", lambda e, dst_bf=dst_bf, w=w:
                    e.tensor_max(dst_bf, ftmp[:, 0:w], ftmp[:, 512:512 + w]),
                    waits=[(self.s_ac, g2), (self.s_ve, g3)], inc=self.s_ve)
        self._a2_ft = g2
        return (self.s_ve, g4)

    def stage_fmap_spill(w_rdy, dst_d):
        def evac(fi, c, ps, mm_pair):
            sl = fi % 2
            dst = qsg[sl][:, c * 512:(c + 1) * 512]
            extra = [(s_sp[sl], sg_free[sl])] if c == 0 else []
            rv = fmap_evac(dst, ps, mm_pair, extra)
            if c == NT - 1:
                p.dma("sync", dst_d[fi * 128:(fi + 1) * 128, :], qsg[sl][:],
                      s_sp[sl], waits=[rv])
                sg_free[sl] = s_sp[sl].n
            return rv
        return evac

    def token_proj(w_rdy, dst3, fmap):
        for tt in range(16):
            for pb in range(2):
                sl = (tt * 2 + pb) % 2
                bk = 4 + sl
                mv = None
                for kc in range(8):
                    mv = p.emit("tensor", lambda e, kc=kc, tt=tt, pb=pb, bk=bk:
                                e.matmul(self.fb(bk, 0, 512),
                                         xn[kc][:, tt * 128:(tt + 1) * 128],
                                         wbuf[:, kc * 1024 + pb * 512:
                                              kc * 1024 + pb * 512 + 512],
                                         start=(kc == 0), stop=(kc == 7)),
                                waits=[w_rdy, xn_rdy[kc]] +
                                      (self.bank_war(bk) if kc == 0 else []),
                                inc=self.s_mm if kc == 7 else None)
                extra = [(s_ts[sl], ts_free[sl])]
                if fmap:
                    rv = fmap_evac(tsg[sl][:], self.fb(bk, 0, 512),
                                   (self.s_mm, mv), extra)
                else:
                    v = p.emit("vector", lambda e, sl=sl, bk=bk:
                               e.tensor_copy(tsg[sl][:], self.fb(bk, 0, 512)),
                               waits=[(self.s_mm, mv)] + extra, inc=self.s_ve)
                    rv = (self.s_ve, v)
                self.bank_read(bk, *rv)
                for j in range(4):
                    pr = pb * 4 + j
                    p.dma("sync", dst3[pr][:, tt * 128:(tt + 1) * 128],
                          tsg[sl][:, j * 128:(j + 1) * 128], s_ts[sl], waits=[rv])
                ts_free[sl] = s_ts[sl].n

    wq_r = self.wload(wbuf, W["wqT"], 1024, sem=s_w)
    self.proj(wbuf, wq_r, xn, xn_rdy, 1024, stage_fmap_spill(wq_r, scr["qd"]))
    wk_r = self.wload(wbuf, W["wkT"], 1024, sem=s_w, waits=[(self.s_mm, self.s_mm.n)])
    self.proj(wbuf, wk_r, xn, xn_rdy, 1024, stage_fmap_spill(wk_r, scr["kd"]))
    token_proj(wk_r, kd3, True)
    wv_r = self.wload(wbuf, W["wvT"], 1024, sem=s_w, waits=[(self.s_mm, self.s_mm.n)])
    token_proj(wv_r, vd3, False)
    qkv_done = [(s_sp[0], s_sp[0].n), (s_sp[1], s_sp[1].n),
                (s_ts[0], s_ts[0].n), (s_ts[1], s_ts[1].n)]

    qTp = p.sbuf(self.name("qp"), (128, T), BF16)
    kTp = p.sbuf(self.name("kp"), (128, T), BF16)
    ktm = p.sbuf(self.name("km"), (128, T), BF16)
    vtm = p.sbuf(self.name("vm"), (128, T), BF16)
    KVb = p.sbuf(self.name("kvb"), (128, 64), BF16)
    KVa = p.sbuf(self.name("kva"), (128, 128), F32)
    KsB = p.sbuf(self.name("ksb"), (128, 1), BF16)
    KsA = p.sbuf(self.name("ksa"), (128, 2), F32)
    sTm = p.sbuf(self.name("stm"), (128, 128), BF16)
    numB = p.sbuf(self.name("nb"), (128, 128), F32)
    tmpo = p.sbuf(self.name("to"), (128, 128), BF16)
    denr = p.sbuf(self.name("dr"), (1, 128), F32)
    oTf = xn

    st_free = 0
    tmo_free = 0
    otw = {}
    for pr in range(8):
        ldw = qkv_done + [(self.s_mm, self.s_mm.n)]
        p.dma("sync", qTp[:], scr["qd"][pr * 128:(pr + 1) * 128, :], s_pl, waits=ldw)
        p.dma("sync", kTp[:], scr["kd"][pr * 128:(pr + 1) * 128, :], s_pl, waits=ldw)
        p.dma("sync", ktm[:], kd3[pr], s_pl, waits=ldw)
        p.dma("sync", vtm[:], vd3[pr], s_pl, waits=ldw)
        prdy = (s_pl, s_pl.n)
        z0 = p.emit("vector", lambda e: e.memset(KVa[:], 0.0),
                    waits=[(self.s_mm, self.s_mm.n), (self.s_ac, self.s_ac.n)],
                    inc=self.s_ve)
        p.emit("vector", lambda e: e.memset(KVb[:], 0.0), inc=self.s_ve)
        p.emit("vector", lambda e: e.memset(KsA[:], 0.0), inc=self.s_ve)
        z3 = p.emit("vector", lambda e: e.memset(KsB[:], 0.0), inc=self.s_ve)
        kvb_rdy = (self.s_ve, z3)
        ksb_rdy = (self.s_ve, z3)
        kva_w = (self.s_ve, z3)
        for ch in range(16):
            for hh in range(2):
                r0 = hh * 64
                cc = ch * 128
                m1 = p.emit("tensor", lambda e, r0=r0, cc=cc:
                            e.matmul(self.fb(0, 0, 128),
                                     kTp[r0:r0 + 64, cc:cc + 128],
                                     qTp[r0:r0 + 64, cc:cc + 128],
                                     start=True, stop=True),
                            waits=[prdy] + self.bank_war(0), inc=self.s_mm)
                e1 = p.emit("vector", lambda e:
                            e.tensor_mul(sTm[:], self.fb(0, 0, 128), utm[:]),
                            waits=[(self.s_mm, m1), (self.s_mm, st_free), cst],
                            inc=self.s_ve)
                self.bank_read(0, self.s_ve, e1)
                p.emit("tensor", lambda e, r0=r0, cc=cc:
                       e.matmul(self.fb(1, 0, 128)[0:64, :],
                                KVb[r0:r0 + 64, 0:64],
                                qTp[r0:r0 + 64, cc:cc + 128],
                                start=True, stop=False),
                       waits=[kvb_rdy, prdy] + self.bank_war(1))
                m3 = p.emit("tensor", lambda e, cc=cc, hh=hh:
                            e.matmul(self.fb(1, 0, 128)[0:64, :],
                                     vtm[:, cc + hh * 64:cc + hh * 64 + 64],
                                     sTm[:], start=False, stop=True),
                            waits=[(self.s_ve, e1)], inc=self.s_mm)
                p.emit("tensor", lambda e, r0=r0, cc=cc:
                       e.matmul(self.psf[0:1, 2 * 512:2 * 512 + 128],
                                KsB[r0:r0 + 64, 0:1],
                                qTp[r0:r0 + 64, cc:cc + 128],
                                start=True, stop=False),
                       waits=[ksb_rdy] + self.bank_war(2))
                m5 = p.emit("tensor", lambda e:
                            e.matmul(self.psf[0:1, 2 * 512:2 * 512 + 128],
                                     self.ones[:, 0:1], sTm[:],
                                     start=False, stop=True),
                            waits=[(self.s_ve, e1)], inc=self.s_mm)
                st_free = self.s_mm.n
                e2 = p.emit("vector", lambda e:
                            e.tensor_scalar_add(denr[0:1, :],
                                                self.psf[0:1, 2 * 512:2 * 512 + 128],
                                                1e-8),
                            waits=[(self.s_mm, m5)], inc=self.s_ve)
                self.bank_read(2, self.s_ve, e2)
                e3 = p.emit("vector", lambda e: e.reciprocal(denr[0:1, :],
                                                             denr[0:1, :]),
                            inc=self.s_ve)
                m6 = p.emit("tensor", lambda e:
                            e.matmul(self.fb(3, 0, 128)[0:64, :],
                                     self.onesf[0:1, 0:64], denr[0:1, :],
                                     start=True, stop=True),
                            waits=[(self.s_ve, e3)] + self.bank_war(3),
                            inc=self.s_mm)
                e4 = p.emit("vector", lambda e:
                            e.tensor_copy(numB[0:64, :], self.fb(1, 0, 128)[0:64, :]),
                            waits=[(self.s_mm, m3)], inc=self.s_ve)
                self.bank_read(1, self.s_ve, e4)
                e5 = p.emit("vector", lambda e:
                            e.tensor_mul(tmpo[0:64, :], numB[0:64, :],
                                         self.fb(3, 0, 128)[0:64, :]),
                            waits=[(self.s_mm, m6), (self.s_ve, e4),
                                   (self.s_ac, tmo_free)], inc=self.s_ve)
                self.bank_read(3, self.s_ve, e5)
                e6 = p.emit("scalar", lambda e, pr=pr, r0=r0, cc=cc:
                            e.copy(oTf[pr][r0:r0 + 64, cc:cc + 128],
                                   tmpo[0:64, 0:128]),
                            waits=[(self.s_ve, e5)], inc=self.s_ac)
                tmo_free = e6
                otw[pr] = e6
                m7 = p.emit("tensor", lambda e, cc=cc, hh=hh:
                            e.matmul(self.fb(4, 0, 64)[0:64, :],
                                     ktm[:, cc + hh * 64:cc + hh * 64 + 64],
                                     vtm[:, cc + hh * 64:cc + hh * 64 + 64],
                                     start=True, stop=True),
                            waits=[prdy] + self.bank_war(4), inc=self.s_mm)
                e7 = p.emit("vector", lambda e, hh=hh:
                            e.tensor_add(KVa[0:64, hh * 64:hh * 64 + 64],
                                         KVa[0:64, hh * 64:hh * 64 + 64],
                                         self.fb(4, 0, 64)[0:64, :]),
                            waits=[(self.s_mm, m7), (self.s_ac, self.s_ac.n)],
                            inc=self.s_ve)
                self.bank_read(4, self.s_ve, e7)
                m8 = p.emit("tensor", lambda e, cc=cc, hh=hh:
                            e.matmul(self.fb(5, 0, 1)[0:64, :],
                                     ktm[:, cc + hh * 64:cc + hh * 64 + 64],
                                     self.ones[:, 0:1],
                                     start=True, stop=True),
                            waits=[prdy] + self.bank_war(5), inc=self.s_mm)
                e8 = p.emit("vector", lambda e, hh=hh:
                            e.tensor_add(KsA[0:64, hh:hh + 1],
                                         KsA[0:64, hh:hh + 1],
                                         self.fb(5, 0, 1)[0:64, :]),
                            waits=[(self.s_mm, m8), (self.s_ac, self.s_ac.n)],
                            inc=self.s_ve)
                self.bank_read(5, self.s_ve, e8)
                c1 = p.emit("scalar", lambda e, r0=r0, hh=hh:
                            e.copy(KVb[r0:r0 + 64, 0:64],
                                   KVa[0:64, hh * 64:hh * 64 + 64]),
                            waits=[(self.s_ve, e7), (self.s_mm, m3)],
                            inc=self.s_ac)
                c2 = p.emit("scalar", lambda e, r0=r0, hh=hh:
                            e.copy(KsB[r0:r0 + 64, 0:1], KsA[0:64, hh:hh + 1]),
                            waits=[(self.s_ve, e8), (self.s_mm, m5)],
                            inc=self.s_ac)
                kvb_rdy = (self.s_ac, c1)
                ksb_rdy = (self.s_ac, c2)

    wo_r = self.wload(wbuf, W["woT"], 1024, sem=s_w, waits=[(self.s_mm, self.s_mm.n)])
    o_rdy = [(self.s_ac, otw[pr]) for pr in range(8)]
    self.proj(wbuf, wo_r, oTf, o_rdy, 1024, self.ev_residual(xT), banks=(0, 1, 2, 3))
    return [(self.s_ve, self.s_ve.n)] * 8


Mk.sub_l2_attn_a = sub_l2_attn_a


def host_prep_l2_attn_a(inp):
    f32 = np.float32
    d = {}
    for nm, key in (("wqT", "L2_attn_a_Wq"), ("wkT", "L2_attn_a_Wk"),
                    ("wvT", "L2_attn_a_Wv"), ("woT", "L2_attn_a_Wo")):
        w = np.asarray(inp[key], f32)
        d["l2a_" + nm] = np.ascontiguousarray(w.T.reshape(8, 128, 1024))
    jj = np.arange(128)[:, None]
    ii = np.arange(128)[None, :]
    d["l2a_utm"] = (jj <= ii).astype(f32)
    return d



def sub_mamba(self, xT, x_rdy, W, scr, d_state, rms, pfx):
    p = self.p
    R = 64 + 2 * d_state
    S = d_state
    eps_in = 1e-6 if rms else 1e-5
    s_ld = self.spool(0)
    s_sp = [self.spool(1), self.spool(2)]
    s_w = self.gpool(0)
    s_bl = self.spool(3)

    xn = [p.sbuf(self.name("xn"), (128, T), BF16) for _ in range(8)]
    rdy = self.ln(xT, x_rdy, xn)
    p.wait("tensor", rdy[1])
    xn_rdy = [rdy[0]] * 8

    mzr = scr["mzd"].rearrange("(a p) t -> a p t", p=128)
    mcr = scr["mcd"].rearrange("(a p) t -> a p t", p=128)
    dtr = scr["dtd"].rearrange("(a p) t -> a p t", p=128)
    dur = scr["dtu2"].rearrange("(a p) t -> a p t", p=128)
    myr = scr["myd"].rearrange("(a p) t -> a p t", p=128)

    colsA = p.sbuf(self.name("ca"), (128, 16 * S), F32)
    colsD = p.sbuf(self.name("cd"), (128, 16), F32)
    colsCW = p.sbuf(self.name("cw"), (128, 64), F32)
    colsCB = p.sbuf(self.name("cb"), (128, 16), F32)
    colsDT = p.sbuf(self.name("ct"), (128, 16), F32)
    colsG = p.sbuf(self.name("cg"), (128, 16), F32)
    p.dma("sync", colsA[:], W["alog"].rearrange("p a s -> p (a s)"), s_ld)
    p.dma("sync", colsD[:], W["dcol"], s_ld)
    p.dma("sync", colsCW[:], W["cw"].rearrange("p a k -> p (a k)"), s_ld)
    p.dma("sync", colsCB[:], W["cb"], s_ld)
    p.dma("sync", colsDT[:], W["dtb"], s_ld)
    p.dma("sync", colsG[:, 0:8], W["g"], s_ld)
    if not rms:
        p.dma("sync", colsG[:, 8:16], W["b"], s_ld)
    crdy = (s_ld, s_ld.n)
    a1 = p.emit("scalar", lambda e: e.activation(colsA[:], colsA[:], AF.Exp),
                waits=[crdy], inc=self.s_ac)
    a2 = p.emit("vector", lambda e: e.tensor_scalar_mul(colsA[:], colsA[:], -1.0),
                waits=[(self.s_ac, a1)], inc=self.s_ve)
    ardy = (self.s_ve, a2)

    scA = p.scope()
    xg = [p.sbuf(self.name("xg"), (128, T), BF16) for _ in range(8)]
    rstdB = p.sbuf(self.name("rs"), (128, T), F32)
    nmrB = p.sbuf(self.name("nm"), (128, T), F32)
    sq = p.sbuf(self.name("sqq"), (128, T), F32)
    if not rms:
        for dd in range(8):
            for c in range(NT):
                p.emit("tensor", lambda e, dd=dd, c=c:
                       e.matmul(self.psf[0:1, c * 512:(c + 1) * 512], self.ones[:, 0:1],
                                xn[dd][:, c * 512:(c + 1) * 512],
                                start=(dd == 0), stop=(dd == 7)),
                       waits=[xn_rdy[dd]] + (self.bank_war(0, 1, 2, 3) if (dd == 0 and c == 0) else []),
                       inc=self.s_mm if (dd == 7 and c == 3) else None)
        g0 = p.emit("vector", lambda e: e.tensor_copy(rstdB[0:1, :], self.psf[0:1, 0:T]),
                    waits=[(self.s_mm, self.s_mm.n)], inc=self.s_ve)
        for b in range(4):
            self.bank_read(b, self.s_ve, g0)
    sqg = 0
    for dd in range(8):
        a0 = p.emit("scalar", lambda e, dd=dd: e.activation(sq[:], xn[dd][:], AF.Square),
                    waits=[xn_rdy[dd], (self.s_mm, sqg)], inc=self.s_ac)
        for c in range(NT):
            p.emit("tensor", lambda e, dd=dd, c=c:
                   e.matmul(self.psf[0:1, c * 512:(c + 1) * 512], self.onesf[:, 0:1],
                            sq[:, c * 512:(c + 1) * 512],
                            start=(dd == 0), stop=(dd == 7)),
                   waits=[(self.s_ac, a0)] + (self.bank_war(0, 1, 2, 3) if (dd == 0 and c == 0) else []),
                   inc=self.s_mm if c == 3 else None)
        sqg = self.s_mm.n
    g1 = p.emit("vector", lambda e: e.tensor_copy(nmrB[0:1, :], self.psf[0:1, 0:T]),
                waits=[(self.s_mm, sqg)], inc=self.s_ve)
    for b in range(4):
        self.bank_read(b, self.s_ve, g1)
    inv = 1.0 / 1024.0
    c2 = p.emit("vector", lambda e: e.tensor_scalar_mul(nmrB[0:1, :], nmrB[0:1, :], inv),
                waits=[(self.s_ve, g1)], inc=self.s_ve)
    if not rms:
        c1 = p.emit("vector", lambda e: e.tensor_scalar_mul(rstdB[0:1, :], rstdB[0:1, :], inv),
                    waits=[(self.s_ve, c2)], inc=self.s_ve)
        c3 = p.emit("vector", lambda e: e.tensor_mul(sq[0:1, :], rstdB[0:1, :], rstdB[0:1, :]),
                    waits=[(self.s_ve, c1)], inc=self.s_ve)
        c4 = p.emit("vector", lambda e: e.tensor_sub(nmrB[0:1, :], nmrB[0:1, :], sq[0:1, :]),
                    waits=[(self.s_ve, c3)], inc=self.s_ve)
    c5 = p.emit("vector", lambda e, eps_in=eps_in:
                e.tensor_scalar_add(nmrB[0:1, :], nmrB[0:1, :], eps_in),
                waits=[(self.s_ve, self.s_ve.n)], inc=self.s_ve)
    c6 = p.emit("scalar", lambda e: e.activation(nmrB[0:1, :], nmrB[0:1, :], AF.Ln),
                waits=[(self.s_ve, c5)], inc=self.s_ac)
    c7 = p.emit("scalar", lambda e: e.activation(sq[0:1, :], nmrB[0:1, :], AF.Exp, scale=-0.5),
                waits=[(self.s_ac, c6)], inc=self.s_ac)
    if not rms:
        c8 = p.emit("vector", lambda e: e.tensor_mul(nmrB[0:1, :], sq[0:1, :], rstdB[0:1, :]),
                    waits=[(self.s_ac, c7)], inc=self.s_ve)
        c9 = p.emit("vector", lambda e: e.tensor_scalar_mul(nmrB[0:1, :], nmrB[0:1, :], -1.0),
                    waits=[(self.s_ve, c8)], inc=self.s_ve)
    for c in range(NT):
        mv = p.emit("tensor", lambda e, c=c: e.matmul(self.fb(c, 0, 512), self.onesf[0:1, :],
                                                      sq[0:1, c * 512:(c + 1) * 512],
                                                      start=True, stop=True),
                    waits=[(self.s_ac, c7), (self.s_ve, self.s_ve.n)] + self.bank_war(c),
                    inc=self.s_mm)
        vv = p.emit("vector", lambda e, c=c: e.tensor_copy(rstdB[:, c * 512:(c + 1) * 512],
                                                           self.fb(c, 0, 512)),
                    waits=[(self.s_mm, mv)], inc=self.s_ve)
        self.bank_read(c, self.s_ve, vv)
    if not rms:
        for c in range(NT):
            mv = p.emit("tensor", lambda e, c=c: e.matmul(self.fb(c, 0, 512), self.onesf[0:1, :],
                                                          nmrB[0:1, c * 512:(c + 1) * 512],
                                                          start=True, stop=True),
                        waits=[(self.s_ve, self.s_ve.n)] + self.bank_war(c), inc=self.s_mm)
            vv = p.emit("vector", lambda e, c=c: e.tensor_copy(sq[:, c * 512:(c + 1) * 512],
                                                               self.fb(c, 0, 512)),
                        waits=[(self.s_mm, mv)], inc=self.s_ve)
            self.bank_read(c, self.s_ve, vv)
    xg_rdy = []
    for dd in range(8):
        u0 = p.emit("vector", lambda e, dd=dd: e.tensor_mul(nmrB[:], xn[dd][:], rstdB[:]),
                    waits=[(self.s_ve, self.s_ve.n), crdy], inc=self.s_ve)
        if not rms:
            p.emit("vector", lambda e: e.tensor_add(nmrB[:], nmrB[:], sq[:]), inc=self.s_ve)
            u2 = p.emit("vector", lambda e, dd=dd:
                        e.tensor_scalar(xg[dd][:], nmrB[:], colsG[:, dd:dd + 1],
                                        colsG[:, 8 + dd:8 + dd + 1], ALU.mult, ALU.add),
                        inc=self.s_ve)
        else:
            u2 = p.emit("vector", lambda e, dd=dd:
                        e.tensor_scalar_mul(xg[dd][:], nmrB[:], colsG[:, dd:dd + 1]),
                        inc=self.s_ve)
        xg_rdy.append((self.s_ve, u2))
    wbuf = p.sbuf(self.name("wb"), (128, 8192), BF16)
    qsg = [p.sbuf(self.name("qg"), (128, T), BF16) for _ in range(2)]
    sg_free = [0, 0]
    for ch in range(4):
        w_r = self.wload(wbuf, W["ipw"][ch], 1024, sem=s_w,
                         waits=[(self.s_mm, self.s_mm.n)])

        def evac(fi, c, ps, mm_pair, ch=ch):
            sl = fi % 2
            extra = [(s_sp[sl], sg_free[sl])] if c == 0 else []
            v = p.emit("vector", lambda e, fi=fi, c=c, ps=ps, sl=sl:
                       e.tensor_copy(qsg[sl][:, c * 512:(c + 1) * 512], ps),
                       waits=[mm_pair] + extra, inc=self.s_ve)
            if c == NT - 1:
                p.dma("sync", mzr[ch * 8 + fi], qsg[sl][:], s_sp[sl],
                      waits=[(self.s_ve, v)])
                sg_free[sl] = s_sp[sl].n
            return (self.s_ve, v)
        self.proj(wbuf, w_r, xg, xg_rdy, 1024, evac)
    p.wait("vector", (s_sp[0], s_sp[0].n), (s_sp[1], s_sp[1].n))
    p.barrier()
    scA.close()

    scB = p.scope()
    cvb = [p.sbuf(self.name("cv"), (128, T), BF16) for _ in range(2)]
    acc = p.sbuf(self.name("acc"), (128, T), F32)
    ctmp = p.sbuf(self.name("ctp"), (128, T), F32)
    xst = [p.sbuf(self.name("xs"), (128, T), BF16) for _ in range(2)]
    cv_free = [0, 0]
    st_free = [0, 0]
    for dd in range(16):
        sl = dd % 2
        lv = p.dma("sync", cvb[sl][:], mzr[dd], s_bl,
                   waits=[(self.s_ve, cv_free[sl])])
        v0 = p.emit("vector", lambda e, sl=sl, dd=dd:
                    e.tensor_scalar_mul(acc[:], cvb[sl][:], colsCW[:, dd * 4 + 3:dd * 4 + 4]),
                    waits=[(s_bl, lv), crdy, (self.s_ac, getattr(self, "_mm_cva", 0))],
                    inc=self.s_ve)
        for k in range(3):
            dlt = 3 - k
            p.emit("vector", lambda e, sl=sl, dd=dd, k=k, dlt=dlt:
                   e.tensor_scalar_mul(ctmp[:, 0:T - dlt], cvb[sl][:, 0:T - dlt],
                                       colsCW[:, dd * 4 + k:dd * 4 + k + 1]),
                   inc=self.s_ve)
            p.emit("vector", lambda e, dlt=dlt:
                   e.tensor_add(acc[:, dlt:T], acc[:, dlt:T], ctmp[:, 0:T - dlt]),
                   inc=self.s_ve)
        vz = p.emit("vector", lambda e: e.memset(ctmp[:, 0:4], 0.0), inc=self.s_ve)
        cv_free[sl] = self.s_ve.n
        aa = p.emit("scalar", lambda e, sl=sl, dd=dd:
                    e.activation(xst[sl][:], acc[:], AF.Silu, bias=colsCB[:, dd:dd + 1]),
                    waits=[(self.s_ve, vz), (s_bl, st_free[sl])], inc=self.s_ac)
        self._mm_cva = aa
        p.dma("sync", mcr[dd], xst[sl][:], s_bl, waits=[(self.s_ac, aa)])
        st_free[sl] = s_bl.n
    p.wait("vector", (s_bl, s_bl.n))
    p.barrier()
    scB.close()

    dblb = p.sbuf(self.name("db"), (64, T), BF16)
    bcb = p.sbuf(self.name("bc"), (2 * S, T), F32)
    scC = p.scope()
    wz = p.sbuf(self.name("wz"), (128, 16 * R), BF16)
    xcb = [p.sbuf(self.name("xc"), (128, T), BF16) for _ in range(2)]
    wz_r = self.wload(wz, W["xpw"], R, KC=16, sem=s_w, waits=[(self.s_mm, self.s_mm.n)])
    xc_mm = [0, 0]
    scC_mm = None
    for kc in range(16):
        sl = kc % 2
        lv = p.dma("sync", xcb[sl][:], mcr[kc], s_bl, waits=[(self.s_mm, xc_mm[sl])])
        for c in range(NT):
            p.emit("tensor", lambda e, kc=kc, c=c, sl=sl:
                   e.matmul(self.psf[0:64, c * 512:(c + 1) * 512],
                            wz[:, kc * R:kc * R + 64],
                            xcb[sl][:, c * 512:(c + 1) * 512],
                            start=(kc == 0), stop=(kc == 15)),
                   waits=[wz_r, (s_bl, lv)] +
                         (self.bank_war(0, 1, 2, 3, 4, 5, 6, 7) if (kc == 0 and c == 0) else []),
                   inc=None)
            bcps = (self.fb(4 + c, 0, 512)[0:2 * S, :] if c < 2 else
                    self.psf2[0:2 * S, (c - 2) * 512:(c - 1) * 512])
            scC_mm = p.emit("tensor", lambda e, kc=kc, c=c, sl=sl, bcps=bcps:
                            e.matmul(bcps, wz[:, kc * R + 64:kc * R + R],
                                     xcb[sl][:, c * 512:(c + 1) * 512],
                                     start=(kc == 0), stop=(kc == 15)),
                            inc=self.s_mm if c == NT - 1 else None)
        xc_mm[sl] = scC_mm
    for c in range(NT):
        v1 = p.emit("vector", lambda e, c=c:
                    e.tensor_copy(dblb[:, c * 512:(c + 1) * 512],
                                  self.psf[0:64, c * 512:(c + 1) * 512]),
                    waits=[(self.s_mm, scC_mm)], inc=self.s_ve)
        self.bank_read(c, self.s_ve, v1)
        bcps = (self.fb(4 + c, 0, 512)[0:2 * S, :] if c < 2 else
                self.psf2[0:2 * S, (c - 2) * 512:(c - 1) * 512])
        v2 = p.emit("vector", lambda e, c=c, bcps=bcps:
                    e.tensor_copy(bcb[:, c * 512:(c + 1) * 512], bcps),
                    inc=self.s_ve)
        self.bank_read(4 + c if c < 2 else 6 + (c - 2), self.s_ve, v2)
    p.wait("vector", (s_bl, s_bl.n))
    p.barrier()
    scC.close()

    scD = p.scope()
    dtwb = p.sbuf(self.name("dw"), (64, 2048), BF16)
    xcd = [p.sbuf(self.name("xd"), (128, T), BF16) for _ in range(2)]
    dst = [p.sbuf(self.name("ds"), (128, T), BF16) for _ in range(2)]
    ust = [p.sbuf(self.name("us"), (128, T), BF16) for _ in range(2)]
    dwv = p.wdma(dtwb[0:64, :], W["dtw"], s_w,
                waits=[(self.s_mm, self.s_mm.n)])
    dw_r = (s_w, dwv)
    spt = p.sbuf(self.name("sp"), (128, 1536), F32)
    st_free = [0, 0]
    dtv = (self.s_ve, self.s_ve.n)
    for fi in range(16):
        sl = fi % 2
        lv = p.dma("sync", xcd[sl][:], mcr[fi], s_bl, waits=[(self.s_ve, st_free[sl])])
        for c in range(NT):
            bk = c % 2
            mv = p.emit("tensor", lambda e, fi=fi, c=c, bk=bk:
                        e.matmul(self.fb(bk, 0, 512), dtwb[0:64, fi * 128:(fi + 1) * 128],
                                 dblb[:, c * 512:(c + 1) * 512],
                                 start=True, stop=True),
                        waits=[dw_r, dtv] + self.bank_war(bk), inc=self.s_mm)
            x0 = spt[:, 0:512]
            mm_ = spt[:, 512:1024]
            e2 = spt[:, 1024:1536]
            h1 = p.emit("vector", lambda e, fi=fi, bk=bk:
                        e.tensor_scalar_add(x0, self.fb(bk, 0, 512),
                                            colsDT[:, fi:fi + 1]),
                        waits=[(self.s_mm, mv), crdy,
                               (self.s_ac, getattr(self, "_mm_spa", 0)),
                               (s_sp[sl], getattr(self, "_mm_dsp", 0))],
                        inc=self.s_ve)
            self.bank_read(bk, self.s_ve, h1)
            h2 = p.emit("vector", lambda e: e.tensor_scalar_max(mm_, x0, 0.0),
                        inc=self.s_ve)
            h3 = p.emit("vector", lambda e: e.tensor_sub(x0, x0, mm_), inc=self.s_ve)
            h4 = p.emit("scalar", lambda e: e.activation(x0, x0, AF.Exp),
                        waits=[(self.s_ve, h3)], inc=self.s_ac)
            h5 = p.emit("scalar", lambda e: e.activation(e2, mm_, AF.Exp, scale=-1.0),
                        inc=self.s_ac)
            h6 = p.emit("vector", lambda e: e.tensor_add(x0, x0, e2),
                        waits=[(self.s_ac, h5)], inc=self.s_ve)
            h7 = p.emit("scalar", lambda e: e.activation(x0, x0, AF.Ln),
                        waits=[(self.s_ve, h6)], inc=self.s_ac)
            aa = p.emit("vector", lambda e, c=c, sl=sl:
                        e.tensor_add(dst[sl][:, c * 512:(c + 1) * 512], x0, mm_),
                        waits=[(self.s_ac, h7)], inc=self.s_ve)
            self._mm_spa = h7
            vv = p.emit("vector", lambda e, c=c, sl=sl:
                        e.tensor_mul(ust[sl][:, c * 512:(c + 1) * 512],
                                     dst[sl][:, c * 512:(c + 1) * 512],
                                     xcd[sl][:, c * 512:(c + 1) * 512]),
                        waits=[(self.s_ve, aa), (s_bl, lv), (s_sp[sl], 1)],
                        inc=self.s_ve)
        p.dma("sync", dtr[fi], dst[sl][:], s_sp[sl], waits=[(self.s_ac, self.s_ac.n)])
        p.dma("sync", dur[fi], ust[sl][:], s_sp[sl], waits=[(self.s_ve, self.s_ve.n)])
        self._mm_dsp = s_sp[sl].n
        st_free[sl] = self.s_ve.n
    p.wait("vector", (s_sp[0], s_sp[0].n), (s_sp[1], s_sp[1].n), (s_bl, s_bl.n))
    p.barrier()
    scD.close()

    scF = p.scope()
    BLK = 2
    dtt = [p.sbuf(self.name("dt"), (128, T), BF16) for _ in range(BLK)]
    dtu = [p.sbuf(self.name("du"), (128, T), BF16) for _ in range(BLK)]
    yac = [p.sbuf(self.name("ya"), (128, T), F32) for _ in range(BLK)]
    d0b = p.sbuf(self.name("d0"), (128, T), F32)
    d1b = p.sbuf(self.name("d1"), (128, T), F32)
    hb = p.sbuf(self.name("hb"), (128, T), F32)
    tmpb = p.sbuf(self.name("tp"), (128, T), F32)
    brB = p.sbuf(self.name("bB"), (1, T), BF16)
    brC = p.sbuf(self.name("bC"), (1, T), BF16)
    zst = p.sbuf(self.name("zs"), (128, T), BF16)
    bcv = (self.s_ve, self.s_ve.n)
    for blk in range(16 // BLK):
        lvv = []
        for j in range(BLK):
            dd = blk * BLK + j
            p.dma("sync", dtt[j][:], dtr[dd], s_bl, waits=[(self.s_ac, getattr(self, "_mm_sc", 0))])
            lv = p.dma("sync", dtu[j][:], dur[dd], s_bl,
                       waits=[(self.s_ve, getattr(self, "_mm_scv", 0))])
            lvv.append(lv)
        blrdy = (s_bl, s_bl.n)
        yz = [p.emit("vector", lambda e, j=j: e.memset(yac[j][:], 0.0),
                     waits=[(self.s_ve, getattr(self, "_mm_scv", 0))], inc=self.s_ve)
              for j in range(BLK)]
        for s in range(S):
            for c in range(NT):
                mvb = p.emit("tensor", lambda e, s=s, c=c:
                             e.matmul(self.psf[0:1, 5 * 512:5 * 512 + 512],
                                      self.identf[0:2 * S, s:s + 1],
                                      bcb[:, c * 512:(c + 1) * 512],
                                      start=True, stop=True),
                             waits=[bcv] + self.bank_war(5), inc=self.s_mm)
                vb = p.emit("vector", lambda e, c=c:
                            e.tensor_copy(brB[0:1, c * 512:(c + 1) * 512],
                                          self.psf[0:1, 5 * 512:5 * 512 + 512]),
                            waits=[(self.s_mm, mvb)], inc=self.s_ve)
                self.bank_read(5, self.s_ve, vb)
                mvc = p.emit("tensor", lambda e, s=s, c=c:
                             e.matmul(self.psf[0:1, 5 * 512:5 * 512 + 512],
                                      self.identf[0:2 * S, S + s:S + s + 1],
                                      bcb[:, c * 512:(c + 1) * 512],
                                      start=True, stop=True),
                             waits=[(self.s_ve, vb)] + self.bank_war(5), inc=self.s_mm)
                vc = p.emit("vector", lambda e, c=c:
                            e.tensor_copy(brC[0:1, c * 512:(c + 1) * 512],
                                          self.psf[0:1, 5 * 512:5 * 512 + 512]),
                            waits=[(self.s_mm, mvc)], inc=self.s_ve)
                self.bank_read(5, self.s_ve, vc)
            brv = (self.s_ve, self.s_ve.n)
            for c in range(NT):
                p.emit("tensor", lambda e, c=c:
                       e.matmul(self.fb(c, 0, 512), self.ones[0:1, :],
                                brB[0:1, c * 512:(c + 1) * 512], start=True, stop=True),
                       waits=[brv] + self.bank_war(c), inc=self.s_mm)
            for c in range(NT):
                cps = (self.fb(4 + c, 0, 512) if c < 2 else
                       self.psf2[:, (c - 2) * 512:(c - 1) * 512])
                cbk = 4 + c if c < 2 else 6 + (c - 2)
                p.emit("tensor", lambda e, c=c, cps=cps: e.matmul(
                       cps, self.ones[0:1, :],
                       brC[0:1, c * 512:(c + 1) * 512],
                       start=True, stop=True),
                       waits=[brv] + self.bank_war(cbk), inc=self.s_mm)
            bcast_mm = self.s_mm.n
            for j in range(BLK):
                dd = blk * BLK + j
                a0 = p.emit("scalar", lambda e, j=j, dd=dd, s=s:
                            e.activation(d0b[:], dtt[j][:], AF.Exp,
                                         scale=colsA[:, dd * S + s:dd * S + s + 1]),
                            waits=[blrdy, ardy, (self.s_ve, getattr(self, "_mm_d0", 0))],
                            inc=self.s_ac)
                for c in range(NT):
                    p.emit("vector", lambda e, j=j, c=c:
                           e.tensor_mul(d1b[:, c * 512:(c + 1) * 512],
                                        dtu[j][:, c * 512:(c + 1) * 512],
                                        self.fb(c, 0, 512)),
                           waits=[(self.s_mm, bcast_mm), blrdy,
                                  (self.s_ve, getattr(self, "_mm_d1", 0))], inc=self.s_ve)
                    self.bank_read(c, self.s_ve, self.s_ve.n)
                sc = p.emit("vector", lambda e: e.tensor_tensor_scan(
                    hb[:], d0b[:], d1b[:], 0.0, ALU.mult, ALU.add),
                    waits=[(self.s_ac, a0)], inc=self.s_ve)
                self._mm_d0 = sc
                self._mm_d1 = sc
                for c in range(NT):
                    cps = (self.fb(4 + c, 0, 512) if c < 2 else
                           self.psf2[:, (c - 2) * 512:(c - 1) * 512])
                    v = p.emit("vector", lambda e, c=c, cps=cps:
                               e.tensor_mul(tmpb[:, c * 512:(c + 1) * 512],
                                            hb[:, c * 512:(c + 1) * 512], cps),
                               waits=[(self.s_ve, sc)], inc=self.s_ve)
                    self.bank_read(4 + c if c < 2 else 6 + (c - 2), self.s_ve, v)
                p.emit("vector", lambda e, j=j: e.tensor_add(yac[j][:], yac[j][:], tmpb[:]),
                       waits=[(self.s_ve, yz[j])], inc=self.s_ve)
        for j in range(BLK):
            dd = blk * BLK + j
            lv1 = p.dma("sync", dtt[j][:], mcr[dd], s_bl, waits=[(self.s_ac, self.s_ac.n),
                                                                 (self.s_ve, self.s_ve.n)])
            lv2 = p.dma("sync", dtu[j][:], mzr[16 + dd], s_bl)
            v1 = p.emit("vector", lambda e, j=j, dd=dd:
                        e.tensor_scalar_mul(tmpb[:], dtt[j][:], colsD[:, dd:dd + 1]),
                        waits=[(s_bl, lv2), crdy, (self.s_ve, self.s_ve.n)], inc=self.s_ve)
            v2 = p.emit("vector", lambda e, j=j: e.tensor_add(yac[j][:], yac[j][:], tmpb[:]),
                        inc=self.s_ve)
            a1_ = p.emit("scalar", lambda e, j=j: e.activation(d0b[:], dtu[j][:], AF.Silu),
                         waits=[(s_bl, lv2), (self.s_ve, v2)], inc=self.s_ac)
            v3 = p.emit("vector", lambda e, j=j: e.tensor_mul(zst[:], yac[j][:], d0b[:]),
                        waits=[(self.s_ac, a1_)], inc=self.s_ve)
            p.dma("sync", myr[dd], zst[:], s_sp[0], waits=[(self.s_ve, v3)])
            p.wait("vector", (s_sp[0], s_sp[0].n))
            self._mm_sc = self.s_ac.n
            self._mm_scv = self.s_ve.n
    p.wait("vector", (s_sp[0], s_sp[0].n), (s_bl, s_bl.n))
    p.barrier()
    scF.close()

    for half in range(2):
        scG = p.scope()
        wbuf = p.sbuf(self.name("wo"), (128, 8192), BF16)
        ysrc = [p.sbuf(self.name("ys"), (128, T), BF16) for _ in range(8)]
        w_r = self.wload(wbuf, W["opw"][half * 8:(half + 1) * 8], 1024, sem=s_w,
                         waits=[(self.s_mm, self.s_mm.n)])
        yrdy = []
        for j in range(8):
            lv = p.dma("sync", ysrc[j][:], myr[half * 8 + j], s_bl,
                       waits=[(self.s_mm, self.s_mm.n)])
            yrdy.append((s_bl, lv))
        self.proj(wbuf, w_r, ysrc, yrdy, 1024, self.ev_residual(xT))
        p.wait("vector", (s_bl, s_bl.n))
        p.barrier()
        scG.close()
    out_rdy = []
    for dd in range(8):
        v = p.emit("vector", lambda e, dd=dd:
                   e.tensor_add(xT[dd][:], xT[dd][:], xn[dd][:]),
                   waits=[(self.s_ve, self.s_ve.n)], inc=self.s_ve)
        out_rdy.append((self.s_ve, v))
    return [(self.s_ve, self.s_ve.n)] * 8


Mk.sub_mamba = sub_mamba


def host_prep_mamba(inp, pfx, keys, d_state):
    f32 = np.float32
    S = d_state
    R = 64 + 2 * S
    d = {}
    ipw = np.asarray(inp[keys["ipw"]], f32)
    d[pfx + "ipw"] = np.ascontiguousarray(
        ipw.T.reshape(8, 128, 4, 1024).transpose(2, 0, 1, 3))
    xpw = np.asarray(inp[keys["xpw"]], f32)
    d[pfx + "xpw"] = np.ascontiguousarray(xpw.T.reshape(16, 128, R))
    d[pfx + "dtw"] = np.ascontiguousarray(np.asarray(inp[keys["dtw"]], f32).T)
    d[pfx + "dtb"] = np.ascontiguousarray(np.asarray(inp[keys["dtb"]], f32).reshape(16, 128).T)
    d[pfx + "cw"] = np.ascontiguousarray(
        np.asarray(inp[keys["cw"]], f32).reshape(16, 128, 4).transpose(1, 0, 2))
    d[pfx + "cb"] = np.ascontiguousarray(np.asarray(inp[keys["cb"]], f32).reshape(16, 128).T)
    d[pfx + "alog"] = np.ascontiguousarray(
        np.asarray(inp[keys["alog"]], f32).reshape(16, 128, S).transpose(1, 0, 2))
    d[pfx + "dcol"] = np.ascontiguousarray(np.asarray(inp[keys["dcol"]], f32).reshape(16, 128).T)
    opw = np.asarray(inp[keys["opw"]], f32)
    d[pfx + "opw"] = np.ascontiguousarray(opw.T.reshape(16, 128, 1024))
    d[pfx + "g"] = np.ascontiguousarray(np.asarray(inp[keys["g"]], f32).reshape(8, 128).T)
    if "b" in keys:
        d[pfx + "b"] = np.ascontiguousarray(np.asarray(inp[keys["b"]], f32).reshape(8, 128).T)
    return d


def host_prep_l0_attn_a(inp):
    return host_prep_mamba(inp, "l0a_", dict(
        ipw="L0_attn_a_in_proj_w", xpw="L0_attn_a_x_proj_w", dtw="L0_attn_a_dt_proj_w",
        dtb="L0_attn_a_dt_proj_b", cw="L0_attn_a_conv_w", cb="L0_attn_a_conv_b",
        alog="L0_attn_a_A_log", dcol="L0_attn_a_D", opw="L0_attn_a_out_proj_w",
        g="L0_attn_a_ln_gamma", b="L0_attn_a_ln_beta"), 16)


def host_prep_l3_attn_b(inp):
    return host_prep_mamba(inp, "l3b_", dict(
        ipw="L3_attn_b_in_proj_w", xpw="L3_attn_b_x_proj_w", dtw="L3_attn_b_dt_proj_w",
        dtb="L3_attn_b_dt_proj_b", cw="L3_attn_b_conv_w", cb="L3_attn_b_conv_b",
        alog="L3_attn_b_A_log", dcol="L3_attn_b_D_param", opw="L3_attn_b_out_proj_w",
        g="L3_attn_b_norm_w"), 32)


def sub_l1_attn_b(self, xT, x_rdy, W, scr):
    p = self.p
    K = 17
    s_cl = self.spool(0)
    s_cg = self.gpool(0)
    s_w = self.gpool(1)
    s_sp = [self.spool(1), self.spool(2)]
    s_pl = self.spool(3)

    wbuf = p.sbuf(self.name("wb"), (128, 8192), BF16)
    cols = p.sbuf(self.name("cl"), (128, 8), F32)
    p.dma("sync", cols[:, 0:1], W["qg"], s_cl)
    p.dma("sync", cols[:, 1:2], W["qb"], s_cl)
    p.dma("sync", cols[:, 2:3], W["kg"], s_cl)
    p.dma("sync", cols[:, 3:4], W["kb"], s_cl)
    p.dma("sync", cols[0:32, 4:5], W["bwin"], s_cl)
    p.dma("sync", cols[0:32, 5:6], W["affs"], s_cl)
    p.dma("sync", cols[0:32, 6:7], W["affb"], s_cl)
    selw = p.sbuf(self.name("sw"), (32, 8 * 68), F32)
    p.dma("sync", selw[:], W["selw"], s_cl)
    e34 = p.sbuf(self.name("e3"), (128, K * 34), BF16)
    p.wdma(e34[:], W["e34sc"], s_cg)
    sel342 = p.sbuf(self.name("s42"), (34, 2), BF16)
    p.wdma(sel342[:], W["sel342"], s_cg)
    sel234 = p.sbuf(self.name("s24"), (2, 34), F32)
    p.dma("sync", sel234[:], W["sel234"], s_cl)
    selav = p.sbuf(self.name("sav"), (34, K * 128), BF16)
    p.wdma(selav[:], W["selav"], s_cg)
    rel34 = p.sbuf(self.name("r34"), (34, 1), F32)
    p.dma("sync", rel34[:], W["rel34"], s_cl)
    e64 = p.sbuf(self.name("e64"), (128, 2), F32)
    p.dma("sync", e64[:], W["e64b"], s_cl)
    e2f = p.sbuf(self.name("e2f"), (2, 128), F32)
    p.dma("sync", e2f[:], W["e2fb"], s_cl)
    cst = (s_cl, s_cl.n)
    cstg = (s_cg, s_cg.n)

    qsg = [p.sbuf(self.name("qg"), (128, T), BF16) for _ in range(2)]
    sq2 = p.sbuf(self.name("sq"), (128, T), F32)
    wsb = p.sbuf(self.name("ws"), (32, T), F32)
    wwin = p.sbuf(self.name("ww"), (128, 8 * 32), BF16)
    scX = p.scope()
    xn = [p.sbuf(self.name("xn"), (128, T), BF16) for _ in range(8)]
    rdy = self.ln(xT, x_rdy, xn)
    p.wait("tensor", rdy[1])
    xn_rdy = [rdy[0]] * 8
    scP = p.scope()
    r2 = p.sbuf(self.name("r2"), (2, 3 * T), F32)
    rb = p.sbuf(self.name("rb"), (128, T), F32)
    rb2 = p.sbuf(self.name("rb2"), (128, T), F32)
    sg_free = [0, 0]

    def headln_spill(gcol, bcol, dst_d):
        def evac(fi, c, ps, mm_pair):
            sl = fi % 2
            extra = [(s_sp[sl], sg_free[sl])] if c == 0 else []
            a = p.emit("scalar", lambda e, fi=fi, c=c, ps=ps, sl=sl:
                       e.activation(sq2[:, c * 512:(c + 1) * 512], ps, AF.Silu),
                       waits=[mm_pair, (self.s_ve, getattr(self, "_1b_sq", 0))] + extra,
                       inc=self.s_ac)
            if c < NT - 1:
                return (self.s_ac, a)
            m1 = None
            for cc in range(NT):
                m1 = p.emit("tensor", lambda e, cc=cc:
                            e.matmul(self.fb(4, 0, 512)[0:2, :], e64[:],
                                     sq2[:, cc * 512:(cc + 1) * 512],
                                     start=True, stop=True),
                            waits=[(self.s_ac, a), cst] + self.bank_war(4), inc=self.s_mm)
                v1 = p.emit("vector", lambda e, cc=cc:
                            e.tensor_copy(r2[0:2, cc * 512:(cc + 1) * 512],
                                          self.fb(4, 0, 512)[0:2, :]),
                            waits=[(self.s_mm, m1)], inc=self.s_ve)
                self.bank_read(4, self.s_ve, v1)
            aq = p.emit("scalar", lambda e: e.activation(rb[:], sq2[:], AF.Square),
                        waits=[(self.s_ac, a)], inc=self.s_ac)
            for cc in range(NT):
                m1 = p.emit("tensor", lambda e, cc=cc:
                            e.matmul(self.fb(4, 0, 512)[0:2, :], e64[:],
                                     rb[:, cc * 512:(cc + 1) * 512],
                                     start=True, stop=True),
                            waits=[(self.s_ac, aq)] + self.bank_war(4), inc=self.s_mm)
                v1 = p.emit("vector", lambda e, cc=cc:
                            e.tensor_copy(r2[0:2, T + cc * 512:T + (cc + 1) * 512],
                                          self.fb(4, 0, 512)[0:2, :]),
                            waits=[(self.s_mm, m1)], inc=self.s_ve)
                self.bank_read(4, self.s_ve, v1)
            i64 = 1.0 / 64.0
            c1 = p.emit("vector", lambda e: e.tensor_scalar_mul(r2[0:2, 0:T], r2[0:2, 0:T], i64),
                        waits=[(self.s_ve, self.s_ve.n)], inc=self.s_ve)
            c2 = p.emit("vector", lambda e: e.tensor_scalar_mul(r2[0:2, T:2 * T],
                                                                r2[0:2, T:2 * T], i64),
                        inc=self.s_ve)
            c3 = p.emit("vector", lambda e: e.tensor_mul(r2[0:2, 2 * T:3 * T],
                                                         r2[0:2, 0:T], r2[0:2, 0:T]),
                        inc=self.s_ve)
            c4 = p.emit("vector", lambda e: e.tensor_sub(r2[0:2, T:2 * T],
                                                         r2[0:2, T:2 * T],
                                                         r2[0:2, 2 * T:3 * T]),
                        inc=self.s_ve)
            c5 = p.emit("vector", lambda e: e.tensor_scalar_add(r2[0:2, T:2 * T],
                                                                r2[0:2, T:2 * T], 1e-5),
                        inc=self.s_ve)
            c6 = p.emit("scalar", lambda e: e.activation(r2[0:2, T:2 * T],
                                                         r2[0:2, T:2 * T], AF.Ln),
                        waits=[(self.s_ve, c5)], inc=self.s_ac)
            c7 = p.emit("scalar", lambda e: e.activation(r2[0:2, T:2 * T],
                                                         r2[0:2, T:2 * T], AF.Exp, scale=-0.5),
                        waits=[(self.s_ac, c6)], inc=self.s_ac)
            c8 = p.emit("vector", lambda e: e.tensor_mul(r2[0:2, 0:T], r2[0:2, 0:T],
                                                         r2[0:2, T:2 * T]),
                        waits=[(self.s_ac, c7)], inc=self.s_ve)
            c9 = p.emit("vector", lambda e: e.tensor_scalar_mul(r2[0:2, 0:T],
                                                                r2[0:2, 0:T], -1.0),
                        inc=self.s_ve)
            for cc in range(NT):
                m1 = p.emit("tensor", lambda e, cc=cc:
                            e.matmul(self.fb(4, 0, 512), e2f[:],
                                     r2[0:2, T + cc * 512:T + (cc + 1) * 512],
                                     start=True, stop=True),
                            waits=[(self.s_ve, c9)] + self.bank_war(4), inc=self.s_mm)
                v1 = p.emit("vector", lambda e, cc=cc:
                            e.tensor_copy(rb[:, cc * 512:(cc + 1) * 512], self.fb(4, 0, 512)),
                            waits=[(self.s_mm, m1)], inc=self.s_ve)
                self.bank_read(4, self.s_ve, v1)
                m2 = p.emit("tensor", lambda e, cc=cc:
                            e.matmul(self.fb(5, 0, 512), e2f[:],
                                     r2[0:2, cc * 512:(cc + 1) * 512],
                                     start=True, stop=True),
                            waits=[(self.s_ve, c9)] + self.bank_war(5), inc=self.s_mm)
                v2 = p.emit("vector", lambda e, cc=cc:
                            e.tensor_copy(rb2[:, cc * 512:(cc + 1) * 512], self.fb(5, 0, 512)),
                            waits=[(self.s_mm, m2)], inc=self.s_ve)
                self.bank_read(5, self.s_ve, v2)
            u0 = p.emit("vector", lambda e: e.tensor_mul(rb[:], sq2[:], rb[:]),
                        waits=[(self.s_ve, self.s_ve.n)], inc=self.s_ve)
            u1 = p.emit("vector", lambda e: e.tensor_add(rb[:], rb[:], rb2[:]),
                        inc=self.s_ve)
            u2 = p.emit("vector", lambda e, gcol=gcol, bcol=bcol, sl=sl:
                        e.tensor_scalar(qsg[sl][:], rb[:], gcol, bcol, ALU.mult, ALU.add),
                        waits=[cst], inc=self.s_ve)
            self._1b_sq = u2
            p.dma("sync", dst_d[fi * 128:(fi + 1) * 128, :], qsg[sl][:], s_sp[sl],
                  waits=[(self.s_ve, u2)])
            sg_free[sl] = s_sp[sl].n
            return (self.s_ac, a)
        return evac

    def silu_spill(dst_d, roff=0):
        def evac(fi, c, ps, mm_pair):
            sl = fi % 2
            extra = [(s_sp[sl], sg_free[sl])] if c == 0 else []
            a = p.emit("scalar", lambda e, c=c, ps=ps, sl=sl:
                       e.activation(qsg[sl][:, c * 512:(c + 1) * 512], ps, AF.Silu),
                       waits=[mm_pair] + extra, inc=self.s_ac)
            if c == NT - 1:
                p.dma("sync", dst_d[(roff + fi) * 128:(roff + fi + 1) * 128, :],
                      qsg[sl][:], s_sp[sl], waits=[(self.s_ac, a)])
                sg_free[sl] = s_sp[sl].n
            return (self.s_ac, a)
        return evac

    wq_r = self.wload(wbuf, W["wq"], 1024, sem=s_w)
    self.proj(wbuf, wq_r, xn, xn_rdy, 1024,
              headln_spill(cols[:, 0:1], cols[:, 1:2], scr["qd"]), banks=(0, 1, 2, 3))
    wk_r = self.wload(wbuf, W["wkv"][0:8], 1024, sem=s_w, waits=[(self.s_mm, self.s_mm.n)])
    self.proj(wbuf, wk_r, xn, xn_rdy, 1024,
              headln_spill(cols[:, 2:3], cols[:, 3:4], scr["kd"]), banks=(0, 1, 2, 3))
    p.wait("vector", (s_sp[0], s_sp[0].n), (s_sp[1], s_sp[1].n))
    p.barrier()
    scP.close()
    wv_r = self.wload(wbuf, W["wkv"][8:16], 1024, sem=s_w, waits=[(self.s_mm, self.s_mm.n)])
    self.proj(wbuf, wv_r, xn, xn_rdy, 1024, silu_spill(scr["vd"]), banks=(0, 1, 2, 3))
    ww_r = self.wload(wwin, W["wwin"], 32, sem=s_w, waits=[(self.s_mm, self.s_mm.n)])
    for c in range(NT):
        mv = None
        for kc in range(8):
            mv = p.emit("tensor", lambda e, kc=kc, c=c:
                        e.matmul(self.fb(4, 0, 512)[0:32, :], wwin[:, kc * 32:(kc + 1) * 32],
                                 xn[kc][:, c * 512:(c + 1) * 512],
                                 start=(kc == 0), stop=(kc == 7)),
                        waits=[ww_r, xn_rdy[kc]] + (self.bank_war(4) if kc == 0 else []),
                        inc=self.s_mm if kc == 7 else None)
        a1 = p.emit("scalar", lambda e, c=c:
                    e.activation(wsb[:, c * 512:(c + 1) * 512], self.fb(4, 0, 512)[0:32, :],
                                 AF.Silu, bias=cols[0:32, 4:5]),
                    waits=[(self.s_mm, mv), cst], inc=self.s_ac)
        self.bank_read(4, self.s_ac, a1)
    aw = p.emit("scalar", lambda e: e.activation(wsb[:], wsb[:], AF.Sigmoid),
                waits=[(self.s_ac, self.s_ac.n)], inc=self.s_ac)
    vw = p.emit("vector", lambda e: e.tensor_scalar(wsb[:], wsb[:], cols[0:32, 5:6],
                                                    cols[0:32, 6:7], ALU.mult, ALU.add),
                waits=[(self.s_ac, aw), cst], inc=self.s_ve)
    qkv_done = [(s_sp[0], s_sp[0].n), (s_sp[1], s_sp[1].n), (self.s_ve, vw)]
    p.wait("vector", (s_sp[0], s_sp[0].n), (s_sp[1], s_sp[1].n))
    p.barrier()
    scX.close()

    scA = p.scope()
    qTp = p.sbuf(self.name("qp"), (128, T), BF16)
    kTp = p.sbuf(self.name("kp"), (128, T), BF16)
    vTp = p.sbuf(self.name("vp"), (128, T), BF16)
    pb = [p.sbuf(self.name("pb"), (128, T), BF16) for _ in range(2)]
    pensb = p.sbuf(self.name("pn"), (34, T), F32)
    t34 = p.sbuf(self.name("t34"), (34, T), F32)
    esc = p.sbuf(self.name("es"), (34, T), BF16)
    atts = p.sbuf(self.name("at"), (34, T), BF16)
    dnr = p.sbuf(self.name("dn"), (2, T), F32)
    oacc = p.sbuf(self.name("oa"), (128, T), F32)
    tmps = p.sbuf(self.name("tm"), (128, 512), F32)
    SCALE = 64.0 ** (-0.5)
    pair_free = 0
    for ti in range(8):
        ldw = qkv_done + [(self.s_mm, pair_free), (self.s_ac, self.s_ac.n),
                          (self.s_ve, self.s_ve.n)]
        p.dma("sync", qTp[:], scr["qd"][ti * 128:(ti + 1) * 128, :], s_pl, waits=ldw)
        p.dma("sync", kTp[:], scr["kd"][ti * 128:(ti + 1) * 128, :], s_pl, waits=ldw)
        p.dma("sync", vTp[:], scr["vd"][ti * 128:(ti + 1) * 128, :], s_pl, waits=ldw)
        prdy = (s_pl, s_pl.n)
        for c in range(NT):
            m1 = p.emit("tensor", lambda e, ti=ti, c=c:
                        e.matmul(self.psf2[0:34, 0:512], selw[:, ti * 68:ti * 68 + 34],
                                 wsb[0:32, c * 512:(c + 1) * 512], start=True, stop=True),
                        waits=[(self.s_ve, vw)] + self.bank_war(6), inc=self.s_mm)
            v1 = p.emit("vector", lambda e, c=c:
                        e.tensor_scalar_sub(t34[:, c * 512:(c + 1) * 512],
                                            self.psf2[0:34, 0:512], rel34[:, 0:1]),
                        waits=[(self.s_mm, m1), cst,
                               (self.s_ac, getattr(self, "_1b_pen", 0))], inc=self.s_ve)
            self.bank_read(6, self.s_ve, v1)
            m2 = p.emit("tensor", lambda e, ti=ti, c=c:
                        e.matmul(self.psf2[0:34, 512:1024], selw[:, ti * 68 + 34:ti * 68 + 68],
                                 wsb[0:32, c * 512:(c + 1) * 512], start=True, stop=True),
                        waits=[(self.s_ve, vw)] + self.bank_war(7), inc=self.s_mm)
            v2 = p.emit("vector", lambda e, c=c:
                        e.tensor_mul(pensb[:, c * 512:(c + 1) * 512],
                                     t34[:, c * 512:(c + 1) * 512],
                                     self.psf2[0:34, 512:1024]),
                        waits=[(self.s_mm, m2), (self.s_ve, v1)], inc=self.s_ve)
            self.bank_read(7, self.s_ve, v2)
        ap = p.emit("scalar", lambda e: e.activation(pensb[:], pensb[:], AF.Sigmoid),
                    waits=[(self.s_ve, self.s_ve.n)], inc=self.s_ac)
        vp2 = p.emit("vector", lambda e: e.tensor_scalar(pensb[:], pensb[:], 1e4, -1e4,
                                                         ALU.mult, ALU.add),
                     waits=[(self.s_ac, ap)], inc=self.s_ve)
        self._1b_pen = ap
        pen_rdy = (self.s_ve, vp2)
        pb_free = [0, 0]
        for w in range(K):
            dlt = w - 8
            sl = w % 2
            a0 = max(0, -dlt)
            b0 = T - max(0, dlt)
            ve = p.emit("vector", lambda e, sl=sl, a0=a0, b0=b0, dlt=dlt:
                        e.tensor_mul(pb[sl][:, a0:b0], qTp[:, a0:b0],
                                     kTp[:, a0 + dlt:b0 + dlt]),
                        waits=[prdy, (self.s_mm, pb_free[sl])], inc=self.s_ve)
            if a0 > 0:
                ve = p.emit("vector", lambda e, sl=sl, a0=a0:
                            e.memset(pb[sl][:, 0:a0], 0.0), inc=self.s_ve)
            if b0 < T:
                ve = p.emit("vector", lambda e, sl=sl, b0=b0:
                            e.memset(pb[sl][:, b0:T], 0.0), inc=self.s_ve)
            for c in range(NT):
                mv = p.emit("tensor", lambda e, w=w, c=c, sl=sl:
                            e.matmul(self.psf[0:34, c * 512:(c + 1) * 512],
                                     e34[:, w * 34:(w + 1) * 34],
                                     pb[sl][:, c * 512:(c + 1) * 512],
                                     start=(w == 0), stop=(w == K - 1)),
                            waits=[(self.s_ve, ve), cstg] +
                                  (self.bank_war(0, 1, 2, 3) if (w == 0 and c == 0) else []),
                            inc=self.s_mm if c == NT - 1 else None)
            pb_free[sl] = self.s_mm.n
        smm = self.s_mm.n
        for c in range(NT):
            v1 = p.emit("vector", lambda e, c=c:
                        e.tensor_add(t34[:, c * 512:(c + 1) * 512],
                                     pensb[:, c * 512:(c + 1) * 512],
                                     self.psf[0:34, c * 512:(c + 1) * 512]),
                        waits=[(self.s_mm, smm), pen_rdy], inc=self.s_ve)
            self.bank_read(c, self.s_ve, v1)
            a1 = p.emit("scalar", lambda e, c=c:
                        e.activation(esc[:, c * 512:(c + 1) * 512],
                                     t34[:, c * 512:(c + 1) * 512], AF.Exp),
                        waits=[(self.s_ve, v1)], inc=self.s_ac)
        erdy = (self.s_ac, self.s_ac.n)
        for c in range(NT):
            m1 = p.emit("tensor", lambda e, c=c:
                        e.matmul(self.fb(4, 0, 512)[0:2, :], sel342[:],
                                 esc[:, c * 512:(c + 1) * 512], start=True, stop=True),
                        waits=[erdy, cstg] + self.bank_war(4), inc=self.s_mm)
            v1 = p.emit("vector", lambda e, c=c:
                        e.tensor_scalar_add(dnr[0:2, c * 512:(c + 1) * 512],
                                            self.fb(4, 0, 512)[0:2, :], 1e-30),
                        waits=[(self.s_mm, m1)], inc=self.s_ve)
            self.bank_read(4, self.s_ve, v1)
        vr = p.emit("vector", lambda e: e.reciprocal(dnr[0:2, :], dnr[0:2, :]),
                    waits=[(self.s_ve, self.s_ve.n)], inc=self.s_ve)
        for c in range(NT):
            m1 = p.emit("tensor", lambda e, c=c:
                        e.matmul(self.fb(4, 0, 512)[0:34, :], sel234[:],
                                 dnr[0:2, c * 512:(c + 1) * 512], start=True, stop=True),
                        waits=[(self.s_ve, vr)] + self.bank_war(4), inc=self.s_mm)
            v1 = p.emit("vector", lambda e, c=c:
                        e.tensor_mul(atts[:, c * 512:(c + 1) * 512],
                                     esc[:, c * 512:(c + 1) * 512],
                                     self.fb(4, 0, 512)[0:34, :]),
                        waits=[(self.s_mm, m1)], inc=self.s_ve)
            self.bank_read(4, self.s_ve, v1)
        at_rdy = (self.s_ve, self.s_ve.n)
        z0 = p.emit("vector", lambda e: e.memset(oacc[:], 0.0),
                    waits=[(self.s_ac, getattr(self, "_1b_oa", 0))], inc=self.s_ve)
        for w in range(K):
            dlt = w - 8
            for c in range(NT):
                lo = max(c * 512, -dlt)
                hi = min((c + 1) * 512, T - dlt)
                if lo >= hi:
                    continue
                bk = 4 + (w % 2)
                m1 = p.emit("tensor", lambda e, w=w, c=c, bk=bk:
                            e.matmul(self.fb(bk, 0, 512), selav[:, w * 128:(w + 1) * 128],
                                     atts[:, c * 512:(c + 1) * 512], start=True, stop=True),
                            waits=[at_rdy, cstg] + self.bank_war(bk), inc=self.s_mm)
                v1 = p.emit("vector", lambda e, lo=lo, hi=hi, c=c, dlt=dlt, bk=bk:
                            e.tensor_mul(tmps[:, lo - c * 512:hi - c * 512],
                                         vTp[:, lo + dlt:hi + dlt],
                                         self.fb(bk, 0, 512)[:, lo - c * 512:hi - c * 512]),
                            waits=[(self.s_mm, m1), (self.s_ve, self.s_ve.n)], inc=self.s_ve)
                self.bank_read(bk, self.s_ve, v1)
                v2 = p.emit("vector", lambda e, lo=lo, hi=hi:
                            e.tensor_add(oacc[:, lo:hi], oacc[:, lo:hi],
                                         tmps[:, lo - (lo // 512) * 512:
                                              lo - (lo // 512) * 512 + (hi - lo)]),
                            inc=self.s_ve)
        av = p.emit("scalar", lambda e, ti=ti: e.copy(qsg[0][:], oacc[:]),
                    waits=[(self.s_ve, self.s_ve.n), (s_sp[0], sg_free[0])], inc=self.s_ac)
        self._1b_oa = av
        p.dma("sync", scr["od"][ti * 128:(ti + 1) * 128, :], qsg[0][:], s_sp[0],
              waits=[(self.s_ac, av)])
        sg_free[0] = s_sp[0].n
        pair_free = self.s_mm.n

    p.wait("vector", (s_sp[0], s_sp[0].n), (s_pl, s_pl.n))
    p.barrier()
    scA.close()
    wo_r = self.wload(wbuf, W["wout"], 1024, sem=s_w, waits=[(self.s_mm, self.s_mm.n)])
    osrc = [p.sbuf(self.name("os"), (128, T), BF16) for _ in range(8)]
    ordy = []
    for j in range(8):
        lv = p.dma("sync", osrc[j][:], scr["od"][j * 128:(j + 1) * 128, :], s_pl,
                   waits=[(s_sp[0], s_sp[0].n)])
        ordy.append((s_pl, lv))

    def ev_silu_res(fi, c, ps, mm_pair):
        a = p.emit("scalar", lambda e, fi=fi, c=c, ps=ps:
                   e.activation(sq2[:, c * 512:(c + 1) * 512], ps, AF.Silu),
                   waits=[mm_pair, (self.s_ve, getattr(self, "_1b_fr", 0))], inc=self.s_ac)
        v = p.emit("vector", lambda e, fi=fi, c=c:
                   e.tensor_add(xT[fi][:, c * 512:(c + 1) * 512],
                                xT[fi][:, c * 512:(c + 1) * 512],
                                sq2[:, c * 512:(c + 1) * 512]),
                   waits=[(self.s_ac, a)], inc=self.s_ve)
        self._1b_fr = v
        return (self.s_ac, a)
    self.proj(wbuf, wo_r, osrc, ordy, 1024, ev_silu_res, banks=(0, 1, 2, 3))
    p.wait("vector", (self.s_ve, self.s_ve.n))
    return [(self.s_ve, self.s_ve.n)] * 8


Mk.sub_l1_attn_b = sub_l1_attn_b


def host_prep_l1_attn_b(inp):
    f32 = np.float32
    K = 17
    d = {}
    d["l1b_wq"] = np.ascontiguousarray(np.asarray(inp["L1_attn_b_Wq"], f32).reshape(8, 128, 1024))
    d["l1b_wkv"] = np.ascontiguousarray(np.asarray(inp["L1_attn_b_Wkv"], f32)
                                        .reshape(8, 128, 2, 1024).transpose(2, 0, 1, 3)
                                        .reshape(16, 128, 1024))
    d["l1b_wwin"] = np.ascontiguousarray(np.asarray(inp["L1_attn_b_Wwin"], f32).reshape(8, 128, 32))
    d["l1b_wout"] = np.ascontiguousarray(np.asarray(inp["L1_attn_b_Wout"], f32).reshape(8, 128, 1024))
    d["l1b_bwin"] = np.ascontiguousarray(np.asarray(inp["L1_attn_b_bwin"], f32).reshape(32, 1))
    for nm, key in (("qg", "L1_attn_b_q_g"), ("qb", "L1_attn_b_q_b"),
                    ("kg", "L1_attn_b_k_g"), ("kb", "L1_attn_b_k_b")):
        d["l1b_" + nm] = np.ascontiguousarray(np.tile(np.asarray(inp[key], f32), 2).reshape(128, 1))
    SCALE = 64.0 ** (-0.5)
    sw = np.zeros((32, 8, 68), f32)
    for ti in range(8):
        for w in range(K):
            for hh in range(2):
                sw[2 * ti + hh, ti, 2 * w + hh] = 1.0
                sw[16 + 2 * ti + hh, ti, 34 + 2 * w + hh] = 1.0
    d["l1b_selw"] = np.ascontiguousarray(sw.reshape(32, 8 * 68))
    d["l1b_affs"] = np.concatenate([np.full(16, 8.0, f32), np.full(16, 9.5, f32)]).reshape(32, 1)
    d["l1b_affb"] = np.full((32, 1), 0.5, f32)
    e34 = np.zeros((128, K, 34), f32)
    for w in range(K):
        for hh in range(2):
            e34[hh * 64:(hh + 1) * 64, w, 2 * w + hh] = SCALE
    d["l1b_e34sc"] = np.ascontiguousarray(e34.reshape(128, K * 34))
    s342 = np.zeros((34, 2), f32)
    s234 = np.zeros((2, 34), f32)
    for w in range(K):
        for hh in range(2):
            s342[2 * w + hh, hh] = 1.0
            s234[hh, 2 * w + hh] = 1.0
    d["l1b_sel342"] = s342
    d["l1b_sel234"] = s234
    sav = np.zeros((34, K, 128), f32)
    for w in range(K):
        for hh in range(2):
            sav[2 * w + hh, w, hh * 64:(hh + 1) * 64] = 1.0
    d["l1b_selav"] = np.ascontiguousarray(sav.reshape(34, K * 128))
    r34 = np.zeros((34, 1), f32)
    for w in range(K):
        r34[2 * w, 0] = r34[2 * w + 1, 0] = abs(w - 8)
    d["l1b_rel34"] = r34
    e64 = np.zeros((128, 2), f32)
    e64[0:64, 0] = 1.0
    e64[64:128, 1] = 1.0
    d["l1b_e64b"] = e64
    e2 = np.zeros((2, 128), f32)
    e2[0, 0:64] = 1.0
    e2[1, 64:128] = 1.0
    d["l1b_e2fb"] = e2
    return d

def sub_l1_attn_a(self, xT, x_rdy, W, scr):
    p = self.p
    SC = (3.0 * 1024 * 64 * 16) ** (-0.25)
    SC2 = (1024.0 * 1024.0) ** (-0.25)
    s_cl = self.spool(0)
    s_cg = self.gpool(0)
    s_w = self.gpool(1)
    s_so = [self.spool(1), self.spool(2)]
    s_hl = [self.spool(3), self.spool(4)]
    s_hv = [self.gpool(2), self.gpool(3)]
    s_pp = [self.gpool(4), self.gpool(5)]

    xn = [p.sbuf(self.name("xn"), (128, T), BF16) for _ in range(8)]
    rdy = self.ln(xT, x_rdy, xn)
    xn_rdy = [rdy[0]] * 8
    p.wait("tensor", rdy[1])

    wbuf = p.sbuf(self.name("wb"), (128, 8192), BF16)
    stag = [p.sbuf(self.name("st"), (128, T), BF16) for _ in range(2)]
    biasq = p.sbuf(self.name("bq"), (128, 16), F32)
    self.colload(biasq[:, 0:8], W["bq"], s_cl)
    self.colload(biasq[:, 8:16], W["obias"], s_cl)
    utm = p.sbuf(self.name("ut"), (128, 128), BF16)
    p.wdma(utm[:], W["utm"], s_cg)
    ckT = [p.sbuf(self.name("ck"), (128, T), BF16) for _ in range(2)]
    skT = [p.sbuf(self.name("sk"), (128, T), BF16) for _ in range(2)]
    snT = [p.sbuf(self.name("sn"), (128, T), BF16) for _ in range(2)]
    sn2T = [p.sbuf(self.name("s2"), (128, T), BF16) for _ in range(2)]
    for ft in range(2):
        p.wdma(ckT[ft][:], W["coskT"][ft], s_cg)
        p.wdma(skT[ft][:], W["sinkT"][ft], s_cg)
        p.wdma(snT[ft][:], W["sinsT"][ft], s_cg)
        p.wdma(sn2T[ft][:], W["sins2T"][ft], s_cg)
    posT = [p.sbuf(self.name("po"), (128, 256), BF16) for _ in range(2)]
    cst = (s_cl, s_cl.n)
    cstg = (s_cg, s_cg.n)
    stg_free = [0, 0]

    def scratch_proj(w_rdy, bias_col, dst):
        def evac(fi, c, ps, mm_pair):
            sl = fi % 2
            sb = stag[sl]
            if bias_col is not None:
                v = p.emit("vector", lambda e, c=c, ps=ps, sb=sb, fi=fi:
                           e.tensor_scalar(sb[:, c * 512:(c + 1) * 512], ps, SC,
                                           biasq[:, fi:fi + 1], ALU.mult, ALU.add),
                           waits=[mm_pair, cst, (s_so[sl], stg_free[sl])],
                           inc=self.s_ve)
            else:
                v = p.emit("vector", lambda e, c=c, ps=ps, sb=sb:
                           e.tensor_scalar_mul(sb[:, c * 512:(c + 1) * 512], ps, SC),
                           waits=[mm_pair, (s_so[sl], stg_free[sl])], inc=self.s_ve)
            if c == NT - 1:
                p.dma("sync", dst[fi * 128:(fi + 1) * 128, :], sb[:],
                      s_so[sl], waits=[(self.s_ve, v)])
                stg_free[sl] = s_so[sl].n
            return (self.s_ve, v)
        return evac

    wq_r = self.wload(wbuf, W["wq"], D, sem=s_w)
    self.proj(wbuf, wq_r, xn, xn_rdy, D, scratch_proj(wq_r, biasq, scr["qd"]))
    wk_r = self.wload(wbuf, W["wk"], D, sem=s_w, waits=[(self.s_mm, self.s_mm.n)])
    self.proj(wbuf, wk_r, xn, xn_rdy, D, scratch_proj(wk_r, None, scr["kd"]))
    wv_r = self.wload(wbuf, W["wv"], D, sem=s_w, waits=[(self.s_mm, self.s_mm.n)])
    self.proj(wbuf, wv_r, xn, xn_rdy, D, scratch_proj(wv_r, None, scr["vd"]))
    qkv_done = [(s_so[0], s_so[0].n), (s_so[1], s_so[1].n)]

    qb = [p.sbuf(self.name("qh"), (128, T), BF16) for _ in range(2)]
    kb = [p.sbuf(self.name("kh"), (128, T), BF16) for _ in range(2)]
    vb = [p.sbuf(self.name("vh"), (128, T), BF16) for _ in range(2)]
    vtm = p.sbuf(self.name("vt"), (128, T), BF16)
    expT = p.sbuf(self.name("ex"), (128, T), BF16)
    Ub = stag
    Vb = [wbuf[:, 0:2048], wbuf[:, 2048:4096]]
    o_h = p.sbuf(self.name("oh"), (128, 1024), F32)
    oT = xn
    mcols = p.sbuf(self.name("mc"), (128, 16), F32)
    gcols = p.sbuf(self.name("gc"), (128, 16), F32)
    srow = self.n_tm[0]
    csc = self.n_tm[1][0:1, 0:256]
    oaccB = self.n_tm[1]
    ncB = p.sbuf(self.name("nc"), (128, 1), F32)

    pair_rdy = []
    pos_rdy = []
    pair_free = [0, 0]

    def load_pair(pr):
        b = pr % 2
        w = qkv_done + [(self.s_mm, pair_free[b])]
        p.dma("sync", qb[b][:], scr["qd"][pr * 128:(pr + 1) * 128, :], s_hl[b], waits=w)
        v0 = p.dma("sync", kb[b][:], scr["kd"][pr * 128:(pr + 1) * 128, :], s_hl[b], waits=w)
        v = p.dma("gpsimd", vb[b][:], scr["vd"][pr * 128:(pr + 1) * 128, :], s_hv[b], waits=w)
        v2 = p.wdma(posT[b][:], W["posT"][pr], s_pp[b],
                   waits=[(self.s_mm, pair_free[b])])
        pair_rdy.append([(s_hl[b], v0), (s_hv[b], v)])
        pos_rdy.append((s_pp[b], v2))

    load_pair(0)
    oT_w = [0] * 8
    uv_free = 0
    for pr in range(8):
        b = pr % 2
        if pr < 7:
            load_pair(pr + 1)
        prdy1, prdy2 = pair_rdy[pr]
        posr = pos_rdy[pr]
        for tt in range(16):
            bk_t = 4 + (tt % 2)
            mmv = p.emit("tensor", lambda e, tt=tt, b=b, bk_t=bk_t:
                         e.matmul(self.fb(bk_t, 0, 128),
                                  vb[b][:, tt * 128:(tt + 1) * 128], self.identb[:],
                                  start=True, stop=True),
                         waits=[prdy2] + self.bank_war(bk_t), inc=self.s_mm)
            v = p.emit("vector", lambda e, tt=tt, bk_t=bk_t:
                       e.tensor_copy(vtm[:, tt * 128:(tt + 1) * 128], self.fb(bk_t, 0, 128)),
                       waits=[(self.s_mm, mmv)], inc=self.s_ve)
            self.bank_read(bk_t, self.s_ve, v)
        vtm_rdy = (self.s_ve, self.s_ve.n)

        for hh in range(2):
            h = pr * 2 + hh
            r0 = hh * 64
            for ft in range(2):
                for c in range(4):
                    bk = c % 2
                    mg = p.emit("tensor", lambda e, ft=ft, c=c, b=b, r0=r0, bk=bk:
                                e.matmul(self.fb(bk, 0, 512),
                                         posT[b][r0:r0 + 64, ft * 128:(ft + 1) * 128],
                                         qb[b][r0:r0 + 64, c * 512:(c + 1) * 512],
                                         start=True, stop=True),
                                waits=[posr, prdy1] + self.bank_war(bk), inc=self.s_mm)
                    u1 = p.emit("vector", lambda e, ft=ft, c=c, bk=bk:
                                e.tensor_mul(Ub[ft][:, c * 512:(c + 1) * 512],
                                             self.fb(bk, 0, 512),
                                             snT[ft][:, c * 512:(c + 1) * 512]),
                                waits=[(self.s_mm, mg), cstg,
                                       (self.s_mm, uv_free)], inc=self.s_ve)
                    u2 = p.emit("vector", lambda e, ft=ft, c=c, bk=bk:
                                e.tensor_mul(Vb[ft][:, c * 512:(c + 1) * 512],
                                             self.fb(bk, 0, 512),
                                             sn2T[ft][:, c * 512:(c + 1) * 512]),
                                inc=self.s_ve)
                    self.bank_read(bk, self.s_ve, u2)
            uv_rdy = (self.s_ve, self.s_ve.n)
            for tqt in range(16):
                for c in range(4):
                    p.emit("tensor", lambda e, tqt=tqt, c=c, b=b, r0=r0:
                           e.matmul(self.fb(c, 0, 512),
                                    qb[b][r0:r0 + 64, tqt * 128:(tqt + 1) * 128],
                                    kb[b][r0:r0 + 64, c * 512:(c + 1) * 512],
                                    start=True, stop=False),
                           waits=[prdy1] + self.bank_war(c))
                    for ft in range(2):
                        p.emit("tensor", lambda e, tqt=tqt, c=c, ft=ft:
                               e.matmul(self.fb(c, 0, 512),
                                        Ub[ft][:, tqt * 128:(tqt + 1) * 128],
                                        ckT[ft][:, c * 512:(c + 1) * 512],
                                        start=False, stop=False),
                               waits=[uv_rdy, cstg])
                        p.emit("tensor", lambda e, tqt=tqt, c=c, ft=ft:
                               e.matmul(self.fb(c, 0, 512),
                                        Vb[ft][:, tqt * 128:(tqt + 1) * 128],
                                        skT[ft][:, c * 512:(c + 1) * 512],
                                        start=False, stop=(ft == 1)),
                               inc=self.s_mm if (ft == 1 and c == 3) else None)
                mmv = self.s_mm.n
                v = p.emit("vector", lambda e, tqt=tqt:
                           e.reduce_max(mcols[:, tqt:tqt + 1], self.psf[:, 0:T], axis=AX.X),
                           waits=[(self.s_mm, mmv)], inc=self.s_ve)
                for bk_i in range(4):
                    self.bank_read(bk_i, self.s_ve, v)
            e1 = p.emit("vector", lambda e: e.reduce_max(gcols[:, 0:1], mcols[:], axis=AX.X),
                        waits=[(self.s_ve, self.s_ve.n)], inc=self.s_ve)
            e2 = p.emit("tensor", lambda e: e.transpose(self.psf[0:1, 2048:2176], gcols[:, 0:1],
                                                        self.identf[:]),
                        waits=[(self.s_ve, e1)] + self.bank_war(4), inc=self.s_mm)
            e3 = p.emit("vector", lambda e: e.tensor_copy(csc[0:1, 4:132], self.psf[0:1, 2048:2176]),
                        waits=[(self.s_mm, e2)], inc=self.s_ve)
            self.bank_read(4, self.s_ve, e3)
            e4 = p.emit("vector", lambda e: e.reduce_max(csc[0:1, 0:1], csc[0:1, 4:132], axis=AX.X),
                        waits=[(self.s_ve, e3)], inc=self.s_ve)
            e5 = p.emit("vector", lambda e: e.tensor_scalar_mul(csc[0:1, 1:2], csc[0:1, 0:1], -0.125),
                        waits=[(self.s_ve, e4)], inc=self.s_ve)
            e6 = p.emit("tensor", lambda e: e.matmul(self.fb(4, 0, 1), self.onesf[0:1, :],
                                                     csc[0:1, 1:2], start=True, stop=True),
                        waits=[(self.s_ve, e5)] + self.bank_war(4), inc=self.s_mm)
            e7 = p.emit("vector", lambda e: e.tensor_copy(ncB[:], self.fb(4, 0, 1)),
                        waits=[(self.s_mm, e6)], inc=self.s_ve)
            self.bank_read(4, self.s_ve, e7)
            ncB_rdy = (self.s_ve, e7)

            for tkt in range(16):
                c_lo = tkt // 4
                for c in range(c_lo, 4):
                    bk_b = 4 + (c % 2)
                    p.emit("tensor", lambda e, tkt=tkt, c=c, b=b, r0=r0, bk_b=bk_b:
                           e.matmul(self.fb(bk_b, 0, 512),
                                    kb[b][r0:r0 + 64, tkt * 128:(tkt + 1) * 128],
                                    qb[b][r0:r0 + 64, c * 512:(c + 1) * 512],
                                    start=True, stop=False),
                           waits=[prdy1] + self.bank_war(bk_b))
                    for ft in range(2):
                        p.emit("tensor", lambda e, tkt=tkt, c=c, ft=ft, bk_b=bk_b:
                               e.matmul(self.fb(bk_b, 0, 512),
                                        ckT[ft][:, tkt * 128:(tkt + 1) * 128],
                                        Ub[ft][:, c * 512:(c + 1) * 512],
                                        start=False, stop=False),
                               waits=[uv_rdy])
                        mmv = p.emit("tensor", lambda e, tkt=tkt, c=c, ft=ft, bk_b=bk_b:
                                     e.matmul(self.fb(bk_b, 0, 512),
                                              skT[ft][:, tkt * 128:(tkt + 1) * 128],
                                              Vb[ft][:, c * 512:(c + 1) * 512],
                                              start=False, stop=(ft == 1)),
                                     inc=self.s_mm if ft == 1 else None)
                    a = p.emit("scalar", lambda e, c=c, bk_b=bk_b:
                               e.activation(expT[:, c * 512:(c + 1) * 512],
                                            self.fb(bk_b, 0, 512), AF.Exp,
                                            bias=ncB[:], scale=0.125),
                               waits=[(self.s_mm, mmv), ncB_rdy,
                                      (self.s_mm, getattr(self, "_1a_ex", 0))],
                               inc=self.s_ac)
                    self.bank_read(bk_b, self.s_ac, a)
                    if c == c_lo:
                        dq = tkt * 128 - c * 512
                        a = p.emit("vector", lambda e, c=c, dq=dq:
                                   e.tensor_mul(expT[:, c * 512 + dq:c * 512 + dq + 128],
                                                expT[:, c * 512 + dq:c * 512 + dq + 128],
                                                utm[:]),
                                   waits=[(self.s_ac, a), cstg], inc=self.s_ve)
                        if dq > 0:
                            a = p.emit("vector", lambda e, c=c, dq=dq:
                                       e.memset(expT[:, c * 512:c * 512 + dq], 0.0),
                                       inc=self.s_ve)
                exp_rdy = (self.s_ve, self.s_ve.n)
                exp_rdy2 = (self.s_ac, self.s_ac.n)
                for c in range(c_lo, 4):
                    p.emit("tensor", lambda e, tkt=tkt, c=c:
                           e.matmul(self.psf[0:1, c * 512:(c + 1) * 512],
                                    self.ones[:, 0:1], expT[:, c * 512:(c + 1) * 512],
                                    start=(tkt == 0), stop=(tkt == (4 * c + 3)),
                                    ),
                           waits=[exp_rdy, exp_rdy2] +
                                 (self.bank_war(0, 1, 2, 3) if (tkt == 0 and c == 0) else []))
                for tqt in range(tkt, 16):
                    p.emit("tensor", lambda e, tkt=tkt, tqt=tqt, hh=hh:
                           e.matmul(self.psf2[:, tqt * 64:(tqt + 1) * 64],
                                    expT[:, tqt * 128:(tqt + 1) * 128],
                                    vtm[:, tkt * 128 + hh * 64:tkt * 128 + hh * 64 + 64],
                                    start=True, stop=True),
                           waits=[exp_rdy, exp_rdy2, vtm_rdy] + self.bank_war(6, 7),
                           inc=self.s_mm if tqt == 15 else None)
                mvav = self.s_mm.n
                if tkt == 0:
                    vav = p.emit("vector", lambda e: e.tensor_copy(
                        oaccB[:, 0:1024], self.psf2[:, 0:1024]),
                        waits=[(self.s_mm, mvav)], inc=self.s_ve)
                else:
                    vav = p.emit("vector", lambda e, tkt=tkt: e.tensor_add(
                        oaccB[:, tkt * 64:1024], oaccB[:, tkt * 64:1024],
                        self.psf2[:, tkt * 64:1024]),
                        waits=[(self.s_mm, mvav)], inc=self.s_ve)
                self.bank_read(6, self.s_ve, vav)
                self.bank_read(7, self.s_ve, vav)
                self._1a_ex = self.s_mm.n
            av_done = self.s_mm.n
            av_done_v = self.s_ve.n
            p.emit("vector", lambda e: e.tensor_copy(srow[0:1, 0:1024], self.psf[0:1, 0:1024]),
                   waits=[(self.s_mm, av_done)], inc=self.s_ve)
            g1 = p.emit("vector", lambda e: e.tensor_copy(srow[64:65, 0:1024], self.psf[0:1, 1024:2048]),
                        inc=self.s_ve)
            for bk_i in range(4):
                self.bank_read(bk_i, self.s_ve, g1)
            p.emit("vector", lambda e: e.tensor_scalar_add(srow[0:1, 0:1024], srow[0:1, 0:1024], 1e-30),
                   waits=[(self.s_ve, g1)], inc=self.s_ve)
            g2 = p.emit("vector", lambda e: e.tensor_scalar_add(srow[64:65, 0:1024], srow[64:65, 0:1024], 1e-30),
                        inc=self.s_ve)
            for tqt in range(16):
                bk_t = 4 + (tqt % 2)
                r_s = 64 * (tqt // 8)
                c_s = (tqt % 8) * 128
                mmv = p.emit("tensor", lambda e, bk_t=bk_t, r_s=r_s, c_s=c_s:
                             e.matmul(self.fb(bk_t, 0, 1),
                                      srow[r_s:r_s + 1, c_s:c_s + 128],
                                      self.identf[r_s:r_s + 1, r_s:r_s + 1],
                                      start=True, stop=True),
                             waits=[(self.s_ve, g2)] + self.bank_war(bk_t), inc=self.s_mm)
                v = p.emit("vector", lambda e, tqt=tqt, bk_t=bk_t:
                           e.tensor_copy(gcols[:, tqt:tqt + 1], self.fb(bk_t, 0, 1)),
                           waits=[(self.s_mm, mmv)], inc=self.s_ve)
                self.bank_read(bk_t, self.s_ve, v)
            g3 = p.emit("vector", lambda e: e.reciprocal(gcols[:], gcols[:]),
                        waits=[(self.s_ve, self.s_ve.n)], inc=self.s_ve)
            for tqt in range(16):
                p.emit("vector", lambda e, tqt=tqt:
                       e.tensor_scalar_mul(o_h[:, tqt * 64:(tqt + 1) * 64],
                                           oaccB[:, tqt * 64:(tqt + 1) * 64],
                                           gcols[:, tqt:tqt + 1]),
                       waits=[(self.s_ve, g3), (self.s_ve, av_done_v)], inc=self.s_ve)
            ohv = self.s_ve.n
            dd0 = h // 2
            rr = (h % 2) * 64
            for tqt in range(16):
                bk_t = 4 + (tqt % 2)
                mmv = p.emit("tensor", lambda e, tqt=tqt, bk_t=bk_t:
                             e.transpose(self.psf[0:64, bk_t * 512:bk_t * 512 + 128],
                                         o_h[:, tqt * 64:(tqt + 1) * 64],
                                         self.identf[:]),
                             waits=[(self.s_ve, ohv)] + self.bank_war(bk_t), inc=self.s_mm)
                a = p.emit("scalar", lambda e, tqt=tqt, bk_t=bk_t, dd0=dd0, rr=rr:
                           e.copy(oT[dd0][rr:rr + 64, tqt * 128:(tqt + 1) * 128],
                                  self.psf[0:64, bk_t * 512:bk_t * 512 + 128]),
                           waits=[(self.s_mm, mmv)], inc=self.s_ac)
                self.bank_read(bk_t, self.s_ac, a)
            oT_w[dd0] = self.s_ac.n
            uv_free = self.s_mm.n
        pair_free[b] = self.s_mm.n

    wo_r = self.wload(wbuf, W["wo"], D, sem=s_w, waits=[(self.s_mm, self.s_mm.n)])
    oT_rdy = [(self.s_ac, oT_w[dd]) for dd in range(8)]

    def ev_out(fi, c, ps, mm_pair):
        v = p.emit("vector", lambda e, fi=fi, c=c, ps=ps:
                   e.tensor_scalar(stag[0][:, c * 512:(c + 1) * 512], ps, SC2,
                                   biasq[:, 8 + fi:8 + fi + 1], ALU.mult, ALU.add),
                   waits=[mm_pair, cst, (self.s_ve, getattr(self, "_1a_fo", 0))],
                   inc=self.s_ve)
        v2 = p.emit("vector", lambda e, fi=fi, c=c:
                    e.tensor_add(xT[fi][:, c * 512:(c + 1) * 512],
                                 xT[fi][:, c * 512:(c + 1) * 512],
                                 stag[0][:, c * 512:(c + 1) * 512]),
                    inc=self.s_ve)
        self._1a_fo = v2
        return (self.s_ve, v)
    self.proj(wbuf, wo_r, oT, oT_rdy, D, ev_out, banks=(0, 1, 2, 3))
    return [(self.s_ve, self.s_ve.n)] * 8


Mk.sub_l1_attn_a = sub_l1_attn_a


def host_prep_l1_attn_a(inp):
    f32 = np.float32
    d = {}
    qkv = np.asarray(inp["L1_attn_a_qkv"], f32).reshape(1024, 3, 1024)
    for i, nm in enumerate(("wq", "wk", "wv")):
        d["l1a_" + nm] = np.ascontiguousarray(qkv[:, i, :].reshape(8, 128, 1024))
    d["l1a_wo"] = np.ascontiguousarray(
        np.asarray(inp["L1_attn_a_out_kernel"], f32).reshape(8, 128, 1024))
    d["l1a_bq"] = np.ascontiguousarray(
        np.asarray(inp["L1_attn_a_q_bias"], f32).reshape(1024).reshape(8, 128).T)
    d["l1a_obias"] = np.ascontiguousarray(
        np.asarray(inp["L1_attn_a_out_bias"], f32).reshape(8, 128).T)
    pos = np.asarray(inp["L1_attn_a_positional"], f32)
    pT = np.zeros((8, 128, 256), f32)
    for pr in range(8):
        for hhh in range(2):
            pT[pr, hhh * 64:(hhh + 1) * 64, :] = pos[:, 2 * pr + hhh, :].T
    d["l1a_posT"] = pT
    FR, MP = 256, 10000.0
    t = np.arange(T, dtype=np.float64)
    idx = np.arange(FR)
    om = MP ** (-(idx // 2 * 2) / FR)
    ang = t[:, None] * om[None, :]
    even = (idx % 2 == 0)
    c0 = np.sqrt(2.0) * FR ** (-0.5)
    sins = np.where(even, np.sin(ang), np.cos(ang)) * c0
    sins2 = np.where(even, -np.cos(ang), np.sin(ang)) * c0
    cosk = np.cos(ang)
    sink = np.sin(ang)
    pack = lambda a: np.ascontiguousarray(a.T.reshape(2, 128, T).astype(f32))
    d["l1a_sinsT"] = pack(sins)
    d["l1a_sins2T"] = pack(sins2)
    d["l1a_coskT"] = pack(cosk)
    d["l1a_sinkT"] = pack(sink)
    jj = np.arange(128)[:, None]
    ii = np.arange(128)[None, :]
    d["l1a_utm"] = (jj <= ii).astype(f32)
    return d
def _rotary_tables(window):
    half = np.arange(0, 64, 2, dtype=np.float64)
    inv_freq = 1.0 / 10000.0 ** (half / 64)
    t = np.arange(T, dtype=np.float64)
    freqs = t[:, None] * inv_freq[None, :]
    freqs = np.concatenate([freqs, freqs], axis=-1)
    base = (half + 0.4 * 64) / (1.4 * 64)
    power = (t - T // 2) / float(window // 2)
    scale = base[None, :] ** power[:, None]
    scale = np.concatenate([scale, scale], axis=-1)
    cos, sin = np.cos(freqs), np.sin(freqs)
    cq = (cos * scale).T.astype(np.float32)
    sq = (sin * scale).T.astype(np.float32)
    ck = (cos / scale).T.astype(np.float32)
    sk = (sin / scale).T.astype(np.float32)
    rep = lambda a: np.ascontiguousarray(np.tile(a, (2, 1)))
    return rep(cq), rep(sq), rep(ck), rep(sk)


def host_prep_local(inp, pfx, keys, depth, heads, window):
    f32 = np.float32
    INNER = heads * 64
    NI = INNER // 128
    d = {}
    cq, sq, ck, sk = _rotary_tables(window)
    d[pfx + "cosq"], d[pfx + "sinq"] = cq, sq
    d[pfx + "cosk"], d[pfx + "sink"] = ck, sk
    rot = np.zeros((64, 64), f32)
    rot[0:32, 32:64] = -np.eye(32, dtype=f32)
    rot[32:64, 0:32] = np.eye(32, dtype=f32)
    P = np.zeros((128, 128), f32)
    P[0:64, 0:64] = rot.T
    P[64:128, 64:128] = rot.T
    d[pfx + "rotP"] = P
    e2 = np.zeros((2, 128), f32)
    e2[0, 0:64] = 1.0
    e2[1, 64:128] = 1.0
    d[pfx + "e2f"] = e2
    e64 = np.zeros((128, 2), f32)
    e64[0:64, 0] = 1.0
    e64[64:128, 1] = 1.0
    d[pfx + "e64"] = e64
    WT = window // 128
    NEGM = -1e5
    p_ = np.arange(128)[:, None]
    u = np.arange((WT - 1) * 128 + 2 * window)[None, :] - (WT - 1) * 128
    badA = (u > p_ + window) | (u < p_)
    d[pfx + "mskA"] = np.where(badA, NEGM, 0.0).astype(f32)
    u = np.arange((2 * WT - 1) * 128 + window)[None, :] - (2 * WT - 1) * 128
    badT = (u < p_ - window) | (u > p_)
    d[pfx + "mskT"] = np.where(badT, NEGM, 0.0).astype(f32)

    wqkv = np.asarray(inp[keys["wqkv"]], f32)
    wo = np.asarray(inp[keys["wo"]], f32)
    qs = np.asarray(inp[keys["qs"]], f32)
    ks = np.asarray(inp[keys["ks"]], f32)
    lg = np.asarray(inp[keys["lg"]], f32)
    lb = np.asarray(inp[keys["lb"]], f32)
    w1 = np.asarray(inp[keys["w1"]], f32)
    b1 = np.asarray(inp[keys["b1"]], f32)
    w2 = np.asarray(inp[keys["w2"]], f32)
    b2 = np.asarray(inp[keys["b2"]], f32)
    for l in range(depth):
        q = wqkv[l]
        d[f"{pfx}{l}_wqkvT"] = np.ascontiguousarray(
            q.reshape(8, 128, 3, INNER).transpose(2, 0, 1, 3))
        d[f"{pfx}{l}_qs"] = np.ascontiguousarray(np.tile(qs[l], 2).reshape(128, 1))
        d[f"{pfx}{l}_ks"] = np.ascontiguousarray(np.tile(ks[l], 2).reshape(128, 1))
        d[f"{pfx}{l}_woT"] = np.ascontiguousarray(wo[l].reshape(NI, 128, 1024))
        d[f"{pfx}{l}_lg"] = np.ascontiguousarray(lg[l].reshape(8, 128).T)
        d[f"{pfx}{l}_lb"] = np.ascontiguousarray(lb[l].reshape(8, 128).T)
        w1a = w1[l][:, :2730]
        w1g = w1[l][:, 2730:]
        pad = np.zeros((1024, 86), f32)
        w1a = np.concatenate([w1a, pad], 1)
        w1g = np.concatenate([w1g, pad], 1)
        d[f"{pfx}{l}_w1a"] = np.ascontiguousarray(
            w1a.reshape(8, 128, 22, 128).transpose(2, 1, 0, 3))
        d[f"{pfx}{l}_w1g"] = np.ascontiguousarray(
            w1g.reshape(8, 128, 22, 128).transpose(2, 1, 0, 3))
        b1a = np.concatenate([b1[l][:2730], np.zeros(86, f32)]).reshape(22, 128).T
        b1g = np.concatenate([b1[l][2730:], np.zeros(86, f32)]).reshape(22, 128).T
        d[f"{pfx}{l}_b1a"] = np.ascontiguousarray(b1a)
        d[f"{pfx}{l}_b1g"] = np.ascontiguousarray(b1g)
        w2p = np.concatenate([w2[l], np.zeros((86, 1024), f32)], 0)
        d[f"{pfx}{l}_w2T"] = np.ascontiguousarray(
            w2p.reshape(22, 128, 8, 128).transpose(2, 1, 0, 3))
        d[f"{pfx}{l}_b2"] = np.ascontiguousarray(b2[l].reshape(8, 128).T)
    return d


def host_prep_l0_ffn(inp):
    return host_prep_local(inp, "l0f_", dict(
        wqkv="L0_ffn_wqkv", wo="L0_ffn_wo", qs="L0_ffn_q_scale", ks="L0_ffn_k_scale",
        lg="L0_ffn_ln_g", lb="L0_ffn_ln_b", w1="L0_ffn_w1", b1="L0_ffn_b1",
        w2="L0_ffn_w2", b2="L0_ffn_b2"), 2, 8, 512)


def host_prep_l3_ffn(inp):
    return host_prep_local(inp, "l3f_", dict(
        wqkv="L3_ffn_wqkv", wo="L3_ffn_wo", qs="L3_ffn_q_scale", ks="L3_ffn_k_scale",
        lg="L3_ffn_ln_g", lb="L3_ffn_ln_b", w1="L3_ffn_w1", b1="L3_ffn_b1",
        w2="L3_ffn_w2", b2="L3_ffn_b2"), 2, 8, 512)


def host_prep_l2_ffn(inp):
    return host_prep_local(inp, "l2f_", dict(
        wqkv="L2_ffn_Wqkv", wo="L2_ffn_Wo", qs="L2_ffn_q_scale", ks="L2_ffn_k_scale",
        lg="L2_ffn_ln_g", lb="L2_ffn_ln_b", w1="L2_ffn_W1", b1="L2_ffn_b1",
        w2="L2_ffn_W2", b2="L2_ffn_b2"), 6, 16, 256)



CHAIN = [
    ("mamba", dict(d_state=16, rms=False), "l0a_", host_prep_l0_attn_a),
    ("local", dict(depth=2, heads=8, window=512), "l0f_", host_prep_l0_ffn),
    ("l0ab", {}, "l0ab_", host_prep_l0_attn_b),
    ("local", dict(depth=2, heads=8, window=512), "l0f_", None),
    ("l1aa", {}, "l1a_", host_prep_l1_attn_a),
    ("l1f", {}, "l1f_", host_prep_l1_ffn),
    ("l1ab", {}, "l1b_", host_prep_l1_attn_b),
    ("l1f", {}, "l1f_", None),
    ("l2aa", {}, "l2a_", host_prep_l2_attn_a),
    ("local", dict(depth=6, heads=16, window=256), "l2f_", host_prep_l2_ffn),
    ("l2ab", {}, "l2b_", host_prep_l2_attn_b),
    ("local", dict(depth=6, heads=16, window=256), "l2f_", None),
    ("l3aa", {}, "l3a_", host_prep_l3_attn_a),
    ("local", dict(depth=2, heads=8, window=512), "l3f_", None),
    ("mamba", dict(d_state=32, rms=True), "l3b_", host_prep_l3_attn_b),
    ("local", dict(depth=2, heads=8, window=512), "l3f_", None),
]
CHAIN_PREPS = {"l0f_": host_prep_l0_ffn, "l3f_": host_prep_l3_ffn}


def _build_sub_program(kind, kw, pfx, host):
    nc = bass.Bass("TRN2", target_bir_lowering=False)
    x_d = nc.dram_tensor("x", [D, T], F32, kind="ExternalInput").ap()
    id_d = nc.dram_tensor("c_ident", [128, 128], F32, kind="ExternalInput").ap()
    on_d = nc.dram_tensor("c_ones", [128, 128], F32, kind="ExternalInput").ap()
    W = {}
    wkeys = []
    for k, arr in host.items():
        if not k.startswith(pfx):
            continue
        W[k[len(pfx):]] = nc.dram_tensor(k, list(arr.shape), F32,
                                         kind="ExternalInput").ap()
        wkeys.append(k)
    xo_d = nc.dram_tensor("xo", [D, T], F32, kind="ExternalOutput").ap()
    mk = Mk(nc)
    p = mk.p
    xT = [p.sbuf(f"xT{i}", (128, T), F32) for i in range(8)]
    mk.load_consts(id_d, on_d)
    for i in range(8):
        p.dma("sync", xT[i][:], x_d[i * 128:(i + 1) * 128, :], mk.s_ld)
    xrdy = [(mk.s_ld, mk.s_ld.n)] * 8
    scr = None
    if kind != "l1f":
        scr = {k: nc.dram_tensor("scr_" + k, [D, T], BF16, kind="Internal").ap()
               for k in ("qd", "kd", "vd")}
        scr["kd32"] = nc.dram_tensor("scr_kd32", [D, T], F32, kind="Internal").ap()
        scr["xspill"] = nc.dram_tensor("scr_xsp", [D, T], F32, kind="Internal").ap()
        if kind == "local":
            ni = kw["heads"] * 64 // 128
            scr["vd2"] = nc.dram_tensor("scr_vd2", [ni, 128, T], BF16, kind="Internal").ap()
        if kind == "l2aa":
            scr["kd2"] = nc.dram_tensor("scr_kd2", [8, 128, T], BF16, kind="Internal").ap()
        if kind == "l1ab":
            scr["od"] = nc.dram_tensor("scr_od", [1024, T], BF16, kind="Internal").ap()
        if kind == "mamba":
            scr["mzd"] = nc.dram_tensor("scr_mzd", [4096, T], BF16, kind="Internal").ap()
            for nm in ("mcd", "dtd", "dtu2", "myd"):
                scr[nm] = nc.dram_tensor("scr_" + nm, [2048, T], BF16, kind="Internal").ap()
    if kind == "mamba":
        rdy = mk.sub_mamba(xT, xrdy, W, scr, kw["d_state"], kw["rms"], pfx)
    elif kind == "local":
        rdy = mk.sub_local_ffn(xT, xrdy, W, scr, kw["depth"], kw["heads"],
                               kw["window"], pfx)
    elif kind == "l0ab":
        rdy = mk.sub_l0_attn_b(xT, xrdy, W, scr)
    elif kind == "l1aa":
        rdy = mk.sub_l1_attn_a(xT, xrdy, W, scr)
    elif kind == "l1f":
        rdy = mk.sub_l1_ffn(xT, xrdy, W)
    elif kind == "l1ab":
        rdy = mk.sub_l1_attn_b(xT, xrdy, W, scr)
    elif kind == "l2aa":
        rdy = mk.sub_l2_attn_a(xT, xrdy, W, scr)
    elif kind == "l2ab":
        rdy = mk.sub_l2_attn_b(xT, xrdy, W, scr)
    elif kind == "l3aa":
        rdy = mk.sub_l3_attn_a(xT, xrdy, W, scr)
    else:
        raise ValueError(kind)
    st = p.sem("stout")
    for i in range(8):
        p.dma("gpsimd", xo_d[i * 128:(i + 1) * 128, :], xT[i][:], st, waits=rdy)
    p.wait("gpsimd", (st, st.n))
    p.flush()
    return nc, wkeys


def kernel(**inputs):
    return kernel_fused(inputs)


def kernel_unfused(**inputs):
    host = {}
    done = set()
    for kind, kw, pfx, prep in CHAIN:
        if prep is None:
            prep = CHAIN_PREPS.get(pfx)
        if prep is not None and pfx not in done:
            host.update(prep(inputs))
            done.add(pfx)
    x = np.asarray(inputs["L0_attn_a_x"], np.float32)
    cur = [np.ascontiguousarray(x[c].T) for c in range(NCORES)]
    ident = np.eye(128, dtype=np.float32)
    ones = np.ones((128, 128), np.float32)
    progs = {}
    for kind, kw, pfx, _prep in CHAIN:
        key = (kind, pfx, tuple(sorted(kw.items())))
        if key not in progs:
            progs[key] = _build_sub_program(kind, kw, pfx, host)
        nc, wkeys = progs[key]
        base = {k: host[k] for k in wkeys}
        base["c_ident"] = ident
        base["c_ones"] = ones
        im = [dict(base, x=cur[c]) for c in range(NCORES)]
        res = run_bass_kernel_spmd(nc, im, core_ids=list(range(NCORES)))
        cur = [np.ascontiguousarray(np.asarray(res.results[c]["xo"]))
               for c in range(NCORES)]
    return np.stack([cur[c].T for c in range(NCORES)]).astype(np.float32)


def _build_fused_program(host):
    nc = bass.Bass("TRN2", target_bir_lowering=False)
    x_d = nc.dram_tensor("x", [D, T], F32, kind="ExternalInput").ap()
    id_d = nc.dram_tensor("c_ident", [128, 128], F32, kind="ExternalInput").ap()
    on_d = nc.dram_tensor("c_ones", [128, 128], F32, kind="ExternalInput").ap()
    W = {}
    for k, arr in host.items():
        W[k] = nc.dram_tensor(k, list(arr.shape), F32, kind="ExternalInput").ap()
    xo_d = nc.dram_tensor("xo", [D, T], F32, kind="ExternalOutput").ap()
    mk = Mk(nc)
    p = mk.p
    xT = [p.sbuf(f"xT{i}", (128, T), F32) for i in range(8)]
    mk.load_consts(id_d, on_d)
    for i in range(8):
        p.dma("sync", xT[i][:], x_d[i * 128:(i + 1) * 128, :], mk.s_ld)
    xrdy = [(mk.s_ld, mk.s_ld.n)] * 8
    scr = {k: nc.dram_tensor("scr_" + k, [D, T], BF16, kind="Internal").ap()
           for k in ("qd", "kd", "vd", "od")}
    scr["kd32"] = nc.dram_tensor("scr_kd32", [D, T], F32, kind="Internal").ap()
    scr["xspill"] = nc.dram_tensor("scr_xsp", [D, T], F32, kind="Internal").ap()
    scr["vd2"] = nc.dram_tensor("scr_vd2", [8, 128, T], BF16, kind="Internal").ap()
    scr["kd2"] = scr["vd2"]
    scr["mzd"] = nc.dram_tensor("scr_mzd", [4096, T], BF16, kind="Internal").ap()
    for nm in ("mcd", "dtd", "dtu2", "myd"):
        scr[nm] = nc.dram_tensor("scr_" + nm, [2048, T], BF16, kind="Internal").ap()

    def bf16_keys(kind, kw, pfx):
        if kind == "local":
            ks = ["cosq", "sinq", "cosk", "sink", "rotP", "mskA", "mskT"]
            for l in range(kw["depth"]):
                ks += [f"{l}_{n}" for n in ("wqkvT", "woT", "w1a", "w1g", "w2T")]
        elif kind == "mamba":
            ks = ["ipw", "xpw", "dtw", "opw"]
        elif kind == "l0ab":
            ks = ["WqT", "WkT", "WvT", "WoT"]
        elif kind == "l1aa":
            ks = ["wq", "wk", "wv", "wo", "posT", "utm", "coskT", "sinkT",
                  "sinsT", "sins2T"]
        elif kind == "l1f":
            ks = ["w1T", "w2T", "h1T", "h2T", "w3T"]
        elif kind == "l1ab":
            ks = ["wq", "wkv", "wwin", "wout", "e34sc", "sel342", "selav"]
        elif kind == "l2aa":
            ks = ["wqT", "wkT", "wvT", "woT", "utm"]
        elif kind == "l2ab":
            ks = ["wqT", "wkT", "wvT", "wgT", "outw", "e2"]
        elif kind == "l3aa":
            ks = ["WrT", "WkT", "WvT", "WoT"]
        else:
            ks = []
        return [pfx + k for k in ks]

    allb = []
    for kind, kw, pfx, _prep in CHAIN:
        for k in bf16_keys(kind, kw, pfx):
            if k not in allb:
                assert k in host, k
                allb.append(k)
    p.w_bf16 = True
    p.cur_label = "S_cast"
    Wb = {}
    s_ca = p.sem("wcast")
    csc = p.scope()
    stf = [p.sbuf("cstf%d" % i, (128, 2048), F32) for i in range(2)]
    stb = [p.sbuf("cstb%d" % i, (128, 2048), BF16) for i in range(2)]
    slot_rd = [0, 0]
    slot_st = [0, 0]
    csi = 0
    for key in allb:
        shape = list(host[key].shape)
        ne = 1
        for s in shape:
            ne *= s
        r = 128
        while ne % r:
            r //= 2
        C = ne // r
        axes = " ".join("abcdef"[i] for i in range(len(shape)))
        flat_in = W[key].rearrange(f"{axes} -> ({axes})").rearrange(
            "(p q) -> p q", p=r)
        bt = nc.dram_tensor("wb16_" + key, shape, BF16, kind="Internal").ap()
        flat_out = bt.rearrange(f"{axes} -> ({axes})").rearrange(
            "(p q) -> p q", p=r)
        off = 0
        while off < C:
            cw = min(2048, C - off)
            sl = csi % 2
            lv = p.dma("sync", stf[sl][0:r, 0:cw], flat_in[:, off:off + cw],
                       s_ca, waits=[(self_sve := p.sem("ve"), slot_rd[sl])])
            v = p.emit("vector", lambda e, sl=sl, r=r, cw=cw:
                       e.tensor_copy(stb[sl][0:r, 0:cw], stf[sl][0:r, 0:cw]),
                       waits=[(s_ca, lv), (s_ca, slot_st[sl])], inc=p.sem("ve"))
            p.dma("sync", flat_out[:, off:off + cw], stb[sl][0:r, 0:cw],
                  s_ca, waits=[(p.sem("ve"), v)])
            slot_rd[sl] = v
            slot_st[sl] = s_ca.n
            csi += 1
            off += cw
        Wb[key] = bt
    p.wait("sync", (s_ca, s_ca.n))
    p.wait("vector", (s_ca, s_ca.n))
    p.barrier()
    csc.close()
    p.cur_label = None

    def subW(pfx):
        d = {k[len(pfx):]: v for k, v in W.items() if k.startswith(pfx)}
        for k, v in Wb.items():
            if k.startswith(pfx):
                d[k[len(pfx):]] = v
        return d

    for si, (kind, kw, pfx, _prep) in enumerate(CHAIN):
        sc = p.scope()
        p.cur_label = "S%02d_%s%s" % (si, kind, pfx)
        Ws = subW(pfx)
        if kind == "mamba":
            mk.sub_mamba(xT, xrdy, Ws, scr, kw["d_state"], kw["rms"], pfx)
        elif kind == "local":
            mk._lf_escfree = [0, 0]
            mk.sub_local_ffn(xT, xrdy, Ws, scr, kw["depth"], kw["heads"],
                             kw["window"], pfx)
        elif kind == "l0ab":
            mk.sub_l0_attn_b(xT, xrdy, Ws, scr)
        elif kind == "l1aa":
            mk.sub_l1_attn_a(xT, xrdy, Ws, scr)
        elif kind == "l1f":
            mk.sub_l1_ffn(xT, xrdy, Ws)
        elif kind == "l1ab":
            mk.sub_l1_attn_b(xT, xrdy, Ws, scr)
        elif kind == "l2aa":
            mk.sub_l2_attn_a(xT, xrdy, Ws, scr)
        elif kind == "l2ab":
            mk.sub_l2_attn_b(xT, xrdy, Ws, scr)
        elif kind == "l3aa":
            mk.sub_l3_attn_a(xT, xrdy, Ws, scr)
        drains = [(s, s.n) for s in p._sems.values() if s.dma and s.n > 0]
        for eng in ("sync", "gpsimd", "vector"):
            p.wait(eng, *drains)
        p.barrier()
        sc.close()
        p.cur_label = None
        xrdy = [(p.sem("_barrier"), p.sem("_barrier").n)] * 8
    st = p.sem("stout")
    for i in range(8):
        p.dma("gpsimd", xo_d[i * 128:(i + 1) * 128, :], xT[i][:], st, waits=xrdy)
    p.wait("gpsimd", (st, st.n))
    p.flush()
    return nc


def kernel_fused(inputs):
    host = {}
    done = set()
    for kind, kw, pfx, prep in CHAIN:
        if prep is None:
            prep = CHAIN_PREPS.get(pfx)
        if prep is not None and pfx not in done:
            host.update(prep(inputs))
            done.add(pfx)
    x = np.asarray(inputs["L0_attn_a_x"], np.float32)
    ident = np.eye(128, dtype=np.float32)
    ones = np.ones((128, 128), np.float32)
    nc = _build_fused_program(host)
    base = dict(host)
    base["c_ident"] = ident
    base["c_ones"] = ones
    im = [dict(base, x=np.ascontiguousarray(x[c].T)) for c in range(NCORES)]
    res = run_bass_kernel_spmd(nc, im, core_ids=list(range(NCORES)))
    out = np.stack([np.asarray(res.results[c]["xo"]).T for c in range(NCORES)])
    return out.astype(np.float32)


# version 06 (final)
# speedup vs baseline: 1.1970x; 1.1970x over previous
import math
from contextlib import ExitStack

import numpy as np
import ml_dtypes
import concourse.bass as bass
import concourse.mybir as mybir
from concourse.bass_utils import run_bass_kernel_spmd

F32 = mybir.dt.float32
BF16 = mybir.dt.bfloat16
AF = mybir.ActivationFunctionType
ALU = mybir.AluOpType
AX = mybir.AxisListType

D = 1024
T = 2048
NT = T // 512
ND = D // 128
NCORES = 8
NEG = -1e9

ENGINES = ("tensor", "vector", "scalar", "sync", "gpsimd")


def cdiv(a, b):
    return -(-a // b)


class Sem:
    __slots__ = ("h", "n", "name", "dma")

    def __init__(self, h, name):
        self.h = h
        self.n = 0
        self.name = name
        self.dma = False


class Prog:

    def __init__(self, nc):
        self.nc = nc
        self.stack = ExitStack()
        self.q = {e: [] for e in ENGINES}
        self.waited = {e: {} for e in ENGINES}
        self._sems = {}
        self.cur_label = None
        self.w_bf16 = False

    def sem(self, name):
        if name not in self._sems:
            h = self.stack.enter_context(self.nc.semaphore(name))
            self._sems[name] = Sem(h, name)
        return self._sems[name]

    def sbuf(self, name, shape, dtype):
        st = self._scopes[-1] if getattr(self, "_scopes", None) else self.stack
        return st.enter_context(self.nc.sbuf_tensor(name, list(shape), dtype))

    def scope(self):
        if not hasattr(self, "_scopes"):
            self._scopes = []
        es = ExitStack()
        self._scopes.append(es)

        class _Scope:
            def __init__(s):
                s.es = es
                s.prog = self

            def close(s):
                s.prog._scopes.remove(s.es)
                s.es.close()
        return _Scope()

    def psum(self, name, shape, dtype):
        return self.stack.enter_context(self.nc.psum_tensor(name, list(shape), dtype))

    def _dedup(self, eng, waits):
        real = []
        for s, v in waits:
            if s.dma:
                v = s.n
            if v > 0 and self.waited[eng].get(s.name, 0) < v:
                real.append((s.h, v))
                self.waited[eng][s.name] = v
        return tuple(real)

    def emit(self, eng, fn, waits=(), inc=None, inc_n=1):
        real = self._dedup(eng, waits)
        if inc is not None:
            inc.n += inc_n
        post = inc.n if inc is not None else None
        ih = inc.h if inc is not None else None

        def closure(engine, fn=fn, real=real, ih=ih, inc_n=inc_n):
            for h, v in real:
                engine.wait_ge(h, v)
            inst = fn(engine)
            if ih is not None:
                inst.then_inc(ih, inc_n)

        self.q[eng].append((self.cur_label, closure))
        return post

    def wdma(self, out, in_, sem, waits=(), **kw):
        if self.w_bf16:
            dt = getattr(in_, "dtype", None)
            assert dt is None or dt == BF16, f"weight src not pre-cast: {dt}"
        return self.dma("sync" if self.w_bf16 else "gpsimd", out, in_, sem,
                        waits=waits, **kw)

    def dma(self, eng, out, in_, sem, waits=(), **kw):
        sem.dma = True
        return self.emit(
            eng, lambda e, o=out, i=in_, kw=kw: e.dma_start(out=o, in_=i, **kw),
            waits=waits, inc=sem, inc_n=16)

    def wait(self, eng, *pairs):
        real = self._dedup(eng, pairs)
        if real:
            def closure(engine, real=real):
                for h, v in real:
                    engine.wait_ge(h, v)
            self.q[eng].append((self.cur_label, closure))

    def barrier(self, engines=ENGINES):
        b = self.sem("_barrier")
        target = b.n + len(engines)
        for e in engines:
            b.n += 1
            self.q[e].append((self.cur_label, lambda eng, h=b.h: eng.sem_inc(h, 1)))
        for e in engines:
            self.wait(e, (b, target))

    def flush(self):
        nc = self.nc

        def body(engine, fns):
            cur = None
            sid = None
            for lbl, f in fns:
                if lbl != cur:
                    if cur is not None:
                        nc.leave_named_scope(cur, sid, False)
                    sid = (nc.enter_named_scope(lbl, False)[0]
                           if lbl is not None else None)
                    cur = lbl
                f(engine)
            if cur is not None:
                nc.leave_named_scope(cur, sid, False)
        with nc.Block() as blk:
            for e in ENGINES:
                fns = self.q[e]
                getattr(blk, e)(lambda engine, fns=fns: body(engine, fns))
        self.q = {e: [] for e in ENGINES}


class Mk:

    def __init__(self, nc):
        self.nc = nc
        self.p = Prog(nc)
        p = self.p
        self.psf = p.psum("psf", (128, 3072), F32)
        self.psf2 = p.psum("psf2", (128, 1024), F32)
        self.s_mm = p.sem("mm")
        self.s_ve = p.sem("ve")
        self.s_ac = p.sem("ac")
        self.s_po = p.sem("po")
        self.s_ld = p.sem("ld")
        self.s_gd = p.sem("gd")
        self.bank_readers = {i: [] for i in range(8)}
        self.ones = p.sbuf("ones", (128, 128), BF16)
        self.onesf = p.sbuf("onesf", (128, 128), F32)
        self.identb = p.sbuf("identb", (128, 128), BF16)
        self.identf = p.sbuf("identf", (128, 128), F32)
        self.x_tm = [p.sbuf("x_tm0", (128, D), F32), p.sbuf("x_tm1", (128, D), F32)]
        self.n_tm = [p.sbuf("n_tm0", (128, D), F32), p.sbuf("n_tm1", (128, D), F32)]
        self.stc = p.sbuf("stc", (128, 16), F32)
        self.uid = 0

    def name(self, pfx):
        self.uid += 1
        return f"{pfx}{self.uid}"

    def spool(self, i):
        return self.p.sem("SP%d" % i)

    def gpool(self, i):
        return self.p.sem("GP%d" % i)

    def fb(self, i, c0=0, w=512):
        return self.psf[:, i * 512 + c0:i * 512 + c0 + w]

    def bb(self, i, c0=0, w=1024):
        return self.psb[:, i * 1024 + c0:i * 1024 + c0 + w]

    def bank_war(self, *banks):
        ws = []
        for b in banks:
            ws += self.bank_readers[b]
        return ws

    def bank_read(self, bank, sem, val):
        lst = self.bank_readers[bank]
        for i, (s, v) in enumerate(lst):
            if s is sem:
                lst[i] = (s, max(v, val))
                break
        else:
            lst.append((sem, val))

    def load_consts(self, ident_dram, ones_dram):
        p = self.p
        s_cn = p.sem("cn")
        s_cg = p.sem("cg")
        p.dma("gpsimd", self.identb[:], ident_dram, s_cg)
        p.dma("sync", self.identf[:], ident_dram, s_cn)
        p.dma("gpsimd", self.ones[:], ones_dram, s_cg)
        p.dma("sync", self.onesf[:], ones_dram, s_cn)
        self.const_rdy = [(s_cn, s_cn.n), (s_cg, s_cg.n)]
        return self.const_rdy

    def bcast_row(self, row_ap, out_sb, width, fbank0=4, wait=(), dtype_note=None):
        p = self.p
        nch = cdiv(width, 512)
        last = None
        for c in range(nch):
            w = min(512, width - c * 512)
            bk = fbank0 + (c % 2)
            mmv = p.emit("tensor", lambda e, c=c, w=w, bk=bk:
                         e.matmul(self.fb(bk, 0, w), self.onesf[0:1, :],
                                  row_ap[:, c * 512:c * 512 + w],
                                  start=True, stop=True),
                         waits=list(wait) + self.bank_war(bk), inc=self.s_mm)
            last = p.emit("vector", lambda e, c=c, w=w, bk=bk:
                          e.tensor_copy(out_sb[:, c * 512:c * 512 + w], self.fb(bk, 0, w)),
                          waits=[(self.s_mm, mmv)], inc=self.s_ve)
            self.bank_read(bk, self.s_ve, last)
        return last

    def transpose_fm_to_tm(self, xT, tt, out_tm, wait_x, banks=(0, 1, 2, 3),
                           w_extra=(), evac_eng="vector"):
        p = self.p
        ev_sem = {"vector": self.s_ve, "scalar": self.s_ac}[evac_eng]
        last = None
        for dd in range(8):
            bk = banks[dd % len(banks)]
            mmv = p.emit("tensor", lambda e, dd=dd, bk=bk:
                         e.transpose(self.fb(bk, 0, 128), xT[dd][:, tt * 128:(tt + 1) * 128],
                                     self.identf[:]),
                         waits=[wait_x[dd]] + list(w_extra) + self.bank_war(bk)
                               + getattr(self, "const_rdy", []),
                         inc=self.s_mm)
            if evac_eng == "vector":
                last = p.emit("vector", lambda e, dd=dd, bk=bk:
                              e.tensor_copy(out_tm[:, dd * 128:(dd + 1) * 128], self.fb(bk, 0, 128)),
                              waits=[(self.s_mm, mmv)], inc=ev_sem)
            else:
                last = p.emit("scalar", lambda e, dd=dd, bk=bk:
                              e.copy(out_tm[:, dd * 128:(dd + 1) * 128], self.fb(bk, 0, 128)),
                              waits=[(self.s_mm, mmv)], inc=ev_sem)
            self.bank_read(bk, ev_sem, last)
        return (ev_sem, last)

    def transpose_tm_to_fm(self, in_tm, tt, outT, wait_in, banks=(4, 5), out_dd=None):
        p = self.p
        outs = []
        dds = out_dd if out_dd is not None else range(8)
        for idx, dd in enumerate(dds):
            bk = banks[idx % len(banks)]
            eng, sem = (("vector", self.s_ve) if bk % 2 == 0 else ("scalar", self.s_ac))
            mmv = p.emit("tensor", lambda e, dd=dd, bk=bk:
                         e.transpose(self.fb(bk, 0, 128), in_tm[:, dd * 128:(dd + 1) * 128],
                                     self.identf[:]),
                         waits=[wait_in] + self.bank_war(bk)
                               + getattr(self, "const_rdy", []),
                         inc=self.s_mm)
            if eng == "vector":
                v = p.emit("vector", lambda e, dd=dd, bk=bk:
                           e.tensor_copy(outT[dd][:, tt * 128:(tt + 1) * 128], self.fb(bk, 0, 128)),
                           waits=[(self.s_mm, mmv)], inc=sem)
            else:
                v = p.emit("scalar", lambda e, dd=dd, bk=bk:
                           e.copy(outT[dd][:, tt * 128:(tt + 1) * 128], self.fb(bk, 0, 128)),
                           waits=[(self.s_mm, mmv)], inc=sem)
            outs.append((sem, v))
            self.bank_read(bk, sem, v)
        return outs

    def ln(self, xT, x_ready, xn_out, eps=1e-5, n_feat=D):
        p = self.p
        if isinstance(x_ready, tuple):
            x_ready = [x_ready] * 8
        stc = self.stc
        for tt in range(16):
            par = tt % 2
            x_tm = self.x_tm[par]
            n_tm = self.n_tm[par]
            wx = [(self.s_ve, getattr(self, "_xtm_free", [0, 0])[par])]
            rdy = self.transpose_fm_to_tm(xT, tt, x_tm, x_ready,
                                          banks=((0, 1) if par == 0 else (2, 3)), w_extra=wx)
            w_ntm = [(self.s_mm, getattr(self, "_ntm_free", [0, 0])[par])]
            a1 = p.emit("scalar", lambda e, x_tm=x_tm, n_tm=n_tm: e.activation(n_tm[:], x_tm[:], AF.Square),
                        waits=[rdy] + w_ntm, inc=self.s_ac)
            r0 = p.emit("vector", lambda e, x_tm=x_tm: e.reduce_sum(stc[:, 0:1], x_tm[:], axis=AX.X),
                        waits=[rdy], inc=self.s_ve)
            r1 = p.emit("vector", lambda e, n_tm=n_tm: e.reduce_sum(stc[:, 1:2], n_tm[:], axis=AX.X),
                        waits=[(self.s_ac, a1)], inc=self.s_ve)
            inv = 1.0 / n_feat
            c1 = p.emit("vector", lambda e: e.tensor_scalar_mul(stc[:, 0:1], stc[:, 0:1], inv),
                        waits=[(self.s_ve, r0)], inc=self.s_ve)
            c2 = p.emit("vector", lambda e: e.tensor_tensor(stc[:, 2:3], stc[:, 0:1], stc[:, 0:1], ALU.mult),
                        waits=[(self.s_ve, c1)], inc=self.s_ve)
            c3 = p.emit("vector", lambda e: e.scalar_tensor_tensor(stc[:, 1:2], stc[:, 1:2], inv,
                                                                  stc[:, 2:3], ALU.mult, ALU.subtract),
                        waits=[(self.s_ve, c2)], inc=self.s_ve)
            c3b = p.emit("vector", lambda e: e.tensor_scalar_add(stc[:, 1:2], stc[:, 1:2], float(eps)),
                         waits=[(self.s_ve, c3)], inc=self.s_ve)
            c4 = p.emit("scalar", lambda e: e.activation(stc[:, 1:2], stc[:, 1:2], AF.Ln),
                        waits=[(self.s_ve, c3b)], inc=self.s_ac)
            c5 = p.emit("scalar", lambda e: e.activation(stc[:, 1:2], stc[:, 1:2], AF.Exp, scale=-0.5),
                        waits=[(self.s_ac, c4)], inc=self.s_ac)
            c6 = p.emit("vector", lambda e: e.tensor_scalar(stc[:, 2:3], stc[:, 1:2], stc[:, 0:1], -1.0,
                                                            ALU.mult, ALU.mult),
                        waits=[(self.s_ac, c5)], inc=self.s_ve)
            a3 = p.emit("vector", lambda e, x_tm=x_tm, n_tm=n_tm:
                        e.tensor_scalar(n_tm[:], x_tm[:], stc[:, 1:2], stc[:, 2:3],
                                        ALU.mult, ALU.add),
                        waits=[(self.s_ve, c6)], inc=self.s_ve)
            free = getattr(self, "_xtm_free", [0, 0])
            free[par] = a3
            self._xtm_free = free
            self.transpose_tm_to_fm(n_tm, tt, xn_out, (self.s_ve, a3),
                                    banks=(4, 5))
            nfree = getattr(self, "_ntm_free", [0, 0])
            nfree[par] = self.s_mm.n
            self._ntm_free = nfree
        return [(self.s_ve, self.s_ve.n), (self.s_ac, self.s_ac.n)]

    def wload(self, wsb, dram_kc, F, KC=8, waits=(), sem=None):
        sem = sem or self.s_gd
        v = None
        for kc in range(KC):
            v = self.p.wdma(wsb[:, kc * F:(kc + 1) * F], dram_kc[kc],
                            sem, waits=waits)
        return (sem, v)

    def colload(self, sb, dram, sem=None, waits=()):
        sem = sem or self.s_ld
        v = self.p.dma("sync", sb, dram, sem, waits=waits)
        return (sem, v)

    def proj(self, wsb, w_rdy, xn, xn_rdy, Fout, evac, banks=(0, 1, 2, 3),
             KC=8, wcol0=0):
        p = self.p
        if isinstance(xn_rdy, tuple):
            xn_rdy = [xn_rdy] * KC
        for fi in range(Fout // 128):
            for c in range(NT):
                bk = banks[(fi * NT + c) % len(banks)]
                war = self.bank_war(bk)
                mmv = None
                for kc in range(KC):
                    mmv = p.emit("tensor", lambda e, fi=fi, c=c, kc=kc, bk=bk:
                                 e.matmul(self.fb(bk, 0, 512),
                                          wsb[:, wcol0 + kc * Fout + fi * 128:
                                              wcol0 + kc * Fout + fi * 128 + 128],
                                          xn[kc][:, c * 512:(c + 1) * 512],
                                          start=(kc == 0), stop=(kc == KC - 1)),
                                 waits=[w_rdy, xn_rdy[kc]] + (war if kc == 0 else []),
                                 inc=self.s_mm if kc == KC - 1 else None)
                rv = evac(fi, c, self.fb(bk, 0, 512), (self.s_mm, mmv))
                self.bank_read(bk, *rv)

    def ev_bias_copy(self, out_tiles, bias_sb=None, eng="vector", dtype_note=None):
        sem = {"vector": self.s_ve, "scalar": self.s_ac}[eng]

        def evac(fi, c, ps, mm_pair):
            if bias_sb is not None:
                v = self.p.emit(eng, lambda e, fi=fi, c=c, ps=ps:
                                e.tensor_scalar_add(out_tiles[fi][:, c * 512:(c + 1) * 512],
                                                    ps, bias_sb[:, fi:fi + 1]),
                                waits=[mm_pair], inc=sem)
            else:
                v = self.p.emit(eng, lambda e, fi=fi, c=c, ps=ps:
                                e.tensor_copy(out_tiles[fi][:, c * 512:(c + 1) * 512], ps),
                                waits=[mm_pair], inc=sem)
            return (sem, v)
        return evac

    def ev_residual(self, xT):
        def evac(fi, c, ps, mm_pair):
            v = self.p.emit("vector", lambda e, fi=fi, c=c, ps=ps:
                            e.tensor_add(xT[fi][:, c * 512:(c + 1) * 512],
                                         xT[fi][:, c * 512:(c + 1) * 512], ps),
                            waits=[mm_pair], inc=self.s_ve)
            return (self.s_ve, v)
        return evac

    def sub_l0_attn_b(self, xT, x_rdy, W, scr, stop_after=None, npairs=8):
        p = self.p
        s_so = [self.spool(0), self.spool(1)]
        s_cl = self.spool(2)

        xn = [p.sbuf(self.name("xn"), (128, T), BF16) for _ in range(8)]
        rdy = self.ln(xT, x_rdy, xn)
        xn_rdy = [rdy[0]] * 8
        p.wait("tensor", rdy[1])

        wbuf = p.sbuf(self.name("wb"), (128, 8192), BF16)
        wbuf2 = p.sbuf(self.name("wb2"), (128, 8192), BF16)
        stag = [p.sbuf(self.name("stg"), (128, T), BF16) for _ in range(2)]
        biasq = p.sbuf(self.name("bq"), (128, 8), F32)
        biask = p.sbuf(self.name("bk"), (128, 8), F32)
        biasv = p.sbuf(self.name("bv"), (128, 8), F32)
        attc = p.sbuf(self.name("attc"), (1, 1280), F32)
        self.colload(biasq[:], W["bq"], s_cl)
        self.colload(biask[:], W["bk"], s_cl)
        self.colload(biasv[:], W["bv"], s_cl)
        self.colload(attc[:], W["attc"], s_cl)
        cst = (s_cl, s_cl.n)
        bq_r = bk_r = bv_r = at_r = cst

        stg_free = [0, 0]

        def scratch_proj(wh, w_rdy, bias_sb, b_rdy, dst):
            def evac(fi, c, ps, mm_pair, bias_sb=bias_sb, dst=dst):
                sl = fi % 2
                sb = stag[sl]
                v = p.emit("vector", lambda e, c=c, ps=ps, sb=sb, fi=fi:
                           e.tensor_scalar_add(sb[:, c * 512:(c + 1) * 512], ps,
                                               bias_sb[:, fi:fi + 1]),
                           waits=[mm_pair, b_rdy, (s_so[sl], stg_free[sl])],
                           inc=self.s_ve)
                if c == NT - 1:
                    p.dma("sync", dst[fi * 128:(fi + 1) * 128, :], sb[:],
                          s_so[sl], waits=[(self.s_ve, v)])
                    stg_free[sl] = s_so[sl].n
                return (self.s_ve, v)
            self.proj(wh, w_rdy, xn, xn_rdy, D, evac, banks=(0, 1, 2, 3))

        s_w1 = self.gpool(0)
        s_w2 = self.gpool(1)
        wq_r = self.wload(wbuf, W["WqT"], D, sem=s_w1)
        scratch_proj(wbuf, wq_r, biasq, bq_r, scr["qd"])
        mm_q = self.s_mm.n
        wk_r = self.wload(wbuf2, W["WkT"], D, sem=s_w2)
        scratch_proj(wbuf2, wk_r, biask, bk_r, scr["kd"])
        mm_k = self.s_mm.n
        wv_r = self.wload(wbuf, W["WvT"], D, waits=[(self.s_mm, mm_q)], sem=s_w1)
        scratch_proj(wbuf, wv_r, biasv, bv_r, scr["vd"])
        wo_r = self.wload(wbuf2, W["WoT"], D, waits=[(self.s_mm, mm_k)], sem=s_w2)
        qkv_done = [(s_so[0], s_so[0].n), (s_so[1], s_so[1].n)]

        qb = [p.sbuf(self.name("qh"), (128, T), BF16) for _ in range(2)]
        kb = [p.sbuf(self.name("kh"), (128, T), BF16) for _ in range(2)]
        vb = [p.sbuf(self.name("vh"), (128, T), F32) for _ in range(2)]
        vtm = p.sbuf(self.name("vtm"), (128, T), BF16)
        expT = p.sbuf(self.name("expT"), (128, T), BF16)
        o_h = p.sbuf(self.name("oh"), (128, 1024), F32)
        oT = xn
        mcols = p.sbuf(self.name("mcols"), (128, 16), F32)
        gcols = p.sbuf(self.name("gcols"), (128, 16), F32)
        srow = self.n_tm[0]
        csc = p.sbuf(self.name("csc"), (1, 256), F32)
        ncB = p.sbuf(self.name("ncB"), (128, 1), F32)

        pair_rdy = []
        pair_free = [0, 0]
        s_hl2 = [self.spool(3), self.spool(4)]
        s_hv2 = [self.gpool(2), self.gpool(3)]

        def load_pair(pr):
            b = pr % 2
            w = qkv_done + [(self.s_mm, pair_free[b])]
            p.dma("sync", qb[b][:], scr["qd"][pr * 128:(pr + 1) * 128, :], s_hl2[b], waits=w)
            v0 = p.dma("sync", kb[b][:], scr["kd"][pr * 128:(pr + 1) * 128, :], s_hl2[b], waits=w)
            v = p.dma("gpsimd", vb[b][:], scr["vd"][pr * 128:(pr + 1) * 128, :], s_hv2[b], waits=w)
            pair_rdy.append([(s_hl2[b], v0), (s_hv2[b], v)])

        if stop_after == "proj":
            p.wait("vector", *qkv_done)
            return [(self.s_ve, self.s_ve.n)]
        load_pair(0)
        oT_w = [0] * 8
        for pr in range(npairs):
            b = pr % 2
            if pr < 7:
                load_pair(pr + 1)
            prdy1, prdy2 = pair_rdy[pr]
            for tt in range(16):
                bk_t = 4 + (tt % 2)
                mmv = p.emit("tensor", lambda e, tt=tt, b=b, bk_t=bk_t:
                             e.transpose(self.fb(bk_t, 0, 128),
                                         vb[b][:, tt * 128:(tt + 1) * 128], self.identf[:]),
                             waits=[prdy2] + self.bank_war(bk_t), inc=self.s_mm)
                v = p.emit("vector", lambda e, tt=tt, bk_t=bk_t:
                           e.tensor_copy(vtm[:, tt * 128:(tt + 1) * 128], self.fb(bk_t, 0, 128)),
                           waits=[(self.s_mm, mmv)], inc=self.s_ve)
                self.bank_read(bk_t, self.s_ve, v)
            vtm_rdy = (self.s_ve, self.s_ve.n)
            if stop_after == "vtm":
                p.wait("vector", vtm_rdy)
                return [(self.s_ve, self.s_ve.n)]

            for hh in range(2):
                h = pr * 2 + hh
                r0 = hh * 64
                import os
                _nomask = bool(os.environ.get("NOMASK"))
                for tqt in range(16):
                    has_mask = (not _nomask) and (tqt in (0, 15))
                    for c in range(4):
                        stop = _nomask or not ((tqt == 0 and c == 3) or (tqt == 15 and c == 0))
                        p.emit("tensor", lambda e, tqt=tqt, c=c, b=b, r0=r0, stop=stop:
                               e.matmul(self.fb(c, 0, 512),
                                        qb[b][r0:r0 + 64, tqt * 128:(tqt + 1) * 128],
                                        kb[b][r0:r0 + 64, c * 512:(c + 1) * 512],
                                        start=True, stop=stop),
                               waits=[prdy1] + self.bank_war(c),
                               inc=None if (c < 3 or has_mask) else self.s_mm)
                    if tqt == 0 and not _nomask:
                        p.emit("tensor", lambda e: e.matmul(self.fb(3, 0, 512), attc[0:1, 0:128],
                                                            attc[0:1, 768:1280], start=False, stop=True),
                               waits=[at_r], inc=self.s_mm)
                    if tqt == 15 and not _nomask:
                        p.emit("tensor", lambda e: e.matmul(self.fb(0, 0, 512), attc[0:1, 128:256],
                                                            attc[0:1, 256:768], start=False, stop=True),
                               waits=[at_r], inc=self.s_mm)
                    mmv = self.s_mm.n
                    v = p.emit("vector", lambda e, tqt=tqt:
                               e.reduce_max(mcols[:, tqt:tqt + 1], self.psf[:, 0:T], axis=AX.X),
                               waits=[(self.s_mm, mmv)], inc=self.s_ve)
                    for bk_i in range(4):
                        self.bank_read(bk_i, self.s_ve, v)
                e1 = p.emit("vector", lambda e: e.reduce_max(gcols[:, 0:1], mcols[:], axis=AX.X),
                            waits=[(self.s_ve, self.s_ve.n)], inc=self.s_ve)
                e2 = p.emit("tensor", lambda e: e.transpose(self.psf[0:1, 2048:2176], gcols[:, 0:1],
                                                            self.identf[:]),
                            waits=[(self.s_ve, e1)] + self.bank_war(4), inc=self.s_mm)
                e3 = p.emit("vector", lambda e: e.tensor_copy(csc[0:1, 4:132], self.psf[0:1, 2048:2176]),
                            waits=[(self.s_mm, e2)], inc=self.s_ve)
                self.bank_read(4, self.s_ve, e3)
                e4 = p.emit("vector", lambda e: e.reduce_max(csc[0:1, 0:1], csc[0:1, 4:132], axis=AX.X),
                            waits=[(self.s_ve, e3)], inc=self.s_ve)
                e5 = p.emit("vector", lambda e: e.tensor_scalar_mul(csc[0:1, 1:2], csc[0:1, 0:1], -0.125),
                            waits=[(self.s_ve, e4)], inc=self.s_ve)
                e6 = p.emit("tensor", lambda e: e.matmul(self.fb(4, 0, 1), self.onesf[0:1, :],
                                                         csc[0:1, 1:2], start=True, stop=True),
                            waits=[(self.s_ve, e5)] + self.bank_war(4), inc=self.s_mm)
                e7 = p.emit("vector", lambda e: e.tensor_copy(ncB[:], self.fb(4, 0, 1)),
                            waits=[(self.s_mm, e6)], inc=self.s_ve)
                self.bank_read(4, self.s_ve, e7)
                ncB_rdy = (self.s_ve, e7)
                if stop_after == "gc":
                    p.wait("vector", (self.s_ve, self.s_ve.n))
                    return [(self.s_ve, self.s_ve.n)]

                for tkt in range(16):
                    for c in range(4):
                        bk_b = 4 + (c % 2)
                        masked = (tkt == 15 and c == 0) or (tkt == 0 and c == 3)
                        p.emit("tensor", lambda e, tkt=tkt, c=c, b=b, r0=r0, bk_b=bk_b, masked=masked:
                               e.matmul(self.fb(bk_b, 0, 512),
                                        kb[b][r0:r0 + 64, tkt * 128:(tkt + 1) * 128],
                                        qb[b][r0:r0 + 64, c * 512:(c + 1) * 512],
                                        start=True, stop=not masked),
                               waits=[prdy1] + self.bank_war(bk_b),
                               inc=None if masked else self.s_mm)
                        if tkt == 15 and c == 0:
                            p.emit("tensor", lambda e, bk_b=bk_b:
                                   e.matmul(self.fb(bk_b, 0, 512), attc[0:1, 128:256],
                                            attc[0:1, 256:768], start=False, stop=True),
                                   inc=self.s_mm)
                        if tkt == 0 and c == 3:
                            p.emit("tensor", lambda e, bk_b=bk_b:
                                   e.matmul(self.fb(bk_b, 0, 512), attc[0:1, 0:128],
                                            attc[0:1, 768:1280], start=False, stop=True),
                                   inc=self.s_mm)
                        mmv = self.s_mm.n
                        a = p.emit("scalar", lambda e, c=c, bk_b=bk_b:
                                   e.activation(expT[:, c * 512:(c + 1) * 512],
                                                self.fb(bk_b, 0, 512), AF.Exp,
                                                bias=ncB[:], scale=0.125),
                                   waits=[(self.s_mm, mmv), ncB_rdy], inc=self.s_ac)
                        self.bank_read(bk_b, self.s_ac, a)
                    exp_rdy = (self.s_ac, self.s_ac.n)
                    for c in range(4):
                        p.emit("tensor", lambda e, tkt=tkt, c=c:
                               e.matmul(self.psf[0:1, c * 512:(c + 1) * 512],
                                        self.ones[:, 0:1], expT[:, c * 512:(c + 1) * 512],
                                        start=(tkt == 0), stop=(tkt == 15)),
                               waits=[exp_rdy] + (self.bank_war(0, 1, 2, 3) if (tkt == 0 and c == 0) else []))
                    for tqt in range(16):
                        p.emit("tensor", lambda e, tkt=tkt, tqt=tqt, hh=hh:
                               e.matmul(self.psf2[:, tqt * 64:(tqt + 1) * 64],
                                        expT[:, tqt * 128:(tqt + 1) * 128],
                                        vtm[:, tkt * 128 + hh * 64:tkt * 128 + hh * 64 + 64],
                                        start=(tkt == 0), stop=(tkt == 15)),
                               waits=[exp_rdy, vtm_rdy] + (self.bank_war(6, 7) if (tkt == 0 and tqt == 0) else []),
                               inc=self.s_mm if (tkt == 15 and tqt == 15) else None)
                av_done = self.s_mm.n
                if stop_after == "passB":
                    p.wait("vector", (self.s_mm, av_done))
                    p.wait("vector", (self.s_ac, self.s_ac.n))
                    return [(self.s_ve, self.s_ve.n)]
                p.emit("vector", lambda e: e.tensor_copy(srow[0:1, 0:1024], self.psf[0:1, 0:1024]),
                       waits=[(self.s_mm, av_done)], inc=self.s_ve)
                g1 = p.emit("vector", lambda e: e.tensor_copy(srow[64:65, 0:1024], self.psf[0:1, 1024:2048]),
                            inc=self.s_ve)
                for bk_i in range(4):
                    self.bank_read(bk_i, self.s_ve, g1)
                p.emit("vector", lambda e: e.tensor_scalar_add(srow[0:1, 0:1024], srow[0:1, 0:1024], 1e-30),
                       waits=[(self.s_ve, g1)], inc=self.s_ve)
                g2 = p.emit("vector", lambda e: e.tensor_scalar_add(srow[64:65, 0:1024], srow[64:65, 0:1024], 1e-30),
                            inc=self.s_ve)
                for tqt in range(16):
                    bk_t = 4 + (tqt % 2)
                    r_s = 64 * (tqt // 8)
                    c_s = (tqt % 8) * 128
                    mmv = p.emit("tensor", lambda e, bk_t=bk_t, r_s=r_s, c_s=c_s:
                                 e.transpose(self.fb(bk_t, 0, 128),
                                             srow[r_s:r_s + 1, c_s:c_s + 128],
                                             self.onesf[r_s:r_s + 1, :]),
                                 waits=[(self.s_ve, g2)] + self.bank_war(bk_t), inc=self.s_mm)
                    v = p.emit("vector", lambda e, tqt=tqt, bk_t=bk_t:
                               e.tensor_copy(gcols[:, tqt:tqt + 1], self.fb(bk_t, 0, 1)),
                               waits=[(self.s_mm, mmv)], inc=self.s_ve)
                    self.bank_read(bk_t, self.s_ve, v)
                g3 = p.emit("vector", lambda e: e.reciprocal(gcols[:], gcols[:]),
                            waits=[(self.s_ve, self.s_ve.n)], inc=self.s_ve)
                for tqt in range(16):
                    p.emit("vector", lambda e, tqt=tqt:
                           e.tensor_scalar_mul(o_h[:, tqt * 64:(tqt + 1) * 64],
                                               self.psf2[:, tqt * 64:(tqt + 1) * 64],
                                               gcols[:, tqt:tqt + 1]),
                           waits=[(self.s_ve, g3), (self.s_mm, av_done)], inc=self.s_ve)
                ohv = self.s_ve.n
                self.bank_read(6, self.s_ve, ohv)
                self.bank_read(7, self.s_ve, ohv)
                dd0 = h // 2
                rr = (h % 2) * 64
                for tqt in range(16):
                    bk_t = 4 + (tqt % 2)
                    mmv = p.emit("tensor", lambda e, tqt=tqt, bk_t=bk_t:
                                 e.transpose(self.psf[0:64, bk_t * 512:bk_t * 512 + 128],
                                             o_h[:, tqt * 64:(tqt + 1) * 64],
                                             self.identf[:]),
                                 waits=[(self.s_ve, ohv)] + self.bank_war(bk_t), inc=self.s_mm)
                    a = p.emit("scalar", lambda e, tqt=tqt, bk_t=bk_t, dd0=dd0, rr=rr:
                               e.copy(oT[dd0][rr:rr + 64, tqt * 128:(tqt + 1) * 128],
                                      self.psf[0:64, bk_t * 512:bk_t * 512 + 128]),
                               waits=[(self.s_mm, mmv)], inc=self.s_ac)
                    self.bank_read(bk_t, self.s_ac, a)
                oT_w[dd0] = self.s_ac.n
            pair_free[b] = self.s_mm.n

        if stop_after == "heads":
            p.wait("vector", (self.s_ac, self.s_ac.n))
            p.wait("vector", (self.s_mm, self.s_mm.n))
            return [(self.s_ve, self.s_ve.n)]
        oT_rdy = [(self.s_ac, oT_w[dd]) for dd in range(8)]
        self.proj(wbuf2, wo_r, oT, oT_rdy, D, self.ev_residual(xT), banks=(0, 1, 2, 3))
        return [(self.s_ve, self.s_ve.n)]


def host_prep_l0_attn_b(inp):
    f32 = np.float32
    d = {}
    for nm, key in (("WqT", "L0_attn_b_Wq"), ("WkT", "L0_attn_b_Wk"),
                    ("WvT", "L0_attn_b_Wv"), ("WoT", "L0_attn_b_Wo")):
        W = np.asarray(inp[key], f32)
        d["l0ab_" + nm] = np.ascontiguousarray(W.T.reshape(8, 128, 1024))
    for nm, key in (("bq", "L0_attn_b_bq"), ("bk", "L0_attn_b_bk"),
                    ("bv", "L0_attn_b_bv")):
        d["l0ab_" + nm] = np.ascontiguousarray(np.asarray(inp[key], f32).reshape(8, 128).T)
    att = np.zeros((1, 1280), f32)
    att[0, 0] = 1.0
    att[0, 128 + 127] = 1.0
    att[0, 256 + 0] = -8e9
    att[0, 768 + 511] = -8e9
    d["l0ab_attc"] = att
    return d

def host_prep_l1_ffn(inp):
    f32 = np.float32
    d = {}
    for nm, key, kc in (("w1T", "L1_ffn_w1_w", 8), ("w2T", "L1_ffn_w2_w", 8),
                        ("h1T", "L1_ffn_h1_w", 32), ("h2T", "L1_ffn_h2_w", 32),
                        ("w3T", "L1_ffn_w3_w", 32)):
        Wm = np.asarray(inp[key], f32)
        nf = Wm.shape[0] // 128
        d["l1f_" + nm] = np.ascontiguousarray(
            Wm.T.reshape(kc, 128, nf, 128).transpose(2, 1, 0, 3))
    for nm, key, nf in (("s1", "L1_ffn_w1_s", 32), ("s2", "L1_ffn_w2_s", 32),
                        ("h1s", "L1_ffn_h1_s", 32), ("h2s", "L1_ffn_h2_s", 32),
                        ("s3", "L1_ffn_w3_s", 8)):
        d["l1f_" + nm] = np.ascontiguousarray(np.asarray(inp[key], f32).reshape(nf, 128).T)
    return d



def sub_l1_ffn(self, xT, x_rdy, W):
    p = self.p
    HID = 4096
    NKH = 32
    xnbig = p.sbuf(self.name("xn"), (128, 8 * T), BF16)
    xn = [xnbig[:, i * T:(i + 1) * T] for i in range(8)]
    rdy = self.ln(xT, x_rdy, xn)
    xn_rdy = [rdy[0]] * 8
    p.wait("tensor", rdy[1])

    s1 = p.sbuf(self.name("s1"), (128, 32), F32)
    s2 = p.sbuf(self.name("s2"), (128, 32), F32)
    hs1 = p.sbuf(self.name("hs1"), (128, 32), F32)
    hs2 = p.sbuf(self.name("hs2"), (128, 32), F32)
    s3 = p.sbuf(self.name("s3"), (128, 8), F32)
    s12 = p.sbuf(self.name("s12"), (128, 32), F32)
    s_cl = self.spool(0)
    self.colload(s1[:], W["s1"], s_cl)
    self.colload(s2[:], W["s2"], s_cl)
    self.colload(hs1[:], W["h1s"], s_cl)
    self.colload(hs2[:], W["h2s"], s_cl)
    self.colload(s3[:], W["s3"], s_cl)
    cst = (s_cl, s_cl.n)
    r1 = r2 = r3 = r4 = r5 = cst
    sv = p.emit("vector", lambda e: e.tensor_tensor(s12[:], s1[:], s2[:], ALU.mult),
                waits=[cst], inc=self.s_ve)

    wA = [p.sbuf(self.name("wA"), (128, 1024), BF16) for _ in range(2)]
    wB = [p.sbuf(self.name("wB"), (128, 1024), BF16) for _ in range(2)]
    wH = [p.sbuf(self.name("wH"), (128, 4096), BF16) for _ in range(2)]
    h_a = [p.sbuf(self.name("ha"), (128, 512), BF16) for _ in range(32)]
    h_b = [p.sbuf(self.name("hb"), (128, 512), BF16) for _ in range(32)]
    t1 = p.sbuf(self.name("t1"), (128, 512), F32)
    t2 = p.sbuf(self.name("t2"), (128, 512), F32)

    wA_free = [0, 0]
    wB_free = [0, 0]
    wH_free = [0, 0]
    swA = [self.gpool(0), self.gpool(1)]
    swB = [self.gpool(2), self.gpool(3)]
    swH = [self.gpool(4), self.gpool(5)]
    self.dbg = {"h_a": h_a, "h_b": h_b, "xn": xn}
    import os
    NTC = int(os.environ.get("L1F_NTC", "4"))
    NSTG = int(os.environ.get("L1F_NSTG", "4"))

    for tc in range(NTC):
        c0 = tc * 512
        for fi in range(NKH):
            bsl = fi % 2
            wa, wb = wA[bsl], wB[bsl]
            wva0 = p.wdma(wa[:], W["w1T"][fi], swA[bsl],
                         waits=[(self.s_mm, wA_free[bsl])])
            wvb0 = p.wdma(wb[:], W["w2T"][fi], swB[bsl],
                         waits=[(self.s_mm, wB_free[bsl])])
            bk1, bk2 = (0, 1) if fi % 2 == 0 else (2, 3)
            for kc in range(8):
                p.emit("tensor", lambda e, kc=kc, fi=fi, bk1=bk1, wa=wa, c0=c0:
                       e.matmul(self.fb(bk1, 0, 512), wa[:, kc * 128:(kc + 1) * 128],
                                xn[kc][:, c0:c0 + 512], start=(kc == 0), stop=(kc == 7)),
                       waits=[(swA[bsl], wva0), xn_rdy[kc]] + (self.bank_war(bk1) if kc == 0 else []))
            mv1 = p.emit("tensor", lambda e, fi=fi, bk2=bk2, wb=wb, c0=c0:
                         e.matmul(self.fb(bk2, 0, 512), wb[:, 0:128],
                                  xn[0][:, c0:c0 + 512], start=True, stop=False),
                         waits=[(swB[bsl], wvb0)] + self.bank_war(bk2))
            for kc in range(1, 8):
                mv1 = p.emit("tensor", lambda e, kc=kc, bk2=bk2, wb=wb, c0=c0:
                             e.matmul(self.fb(bk2, 0, 512), wb[:, kc * 128:(kc + 1) * 128],
                                      xn[kc][:, c0:c0 + 512], start=False, stop=(kc == 7)),
                             inc=self.s_mm if kc == 7 else None)
            wA_free[bsl] = mv1
            wB_free[bsl] = mv1
            a1 = p.emit("scalar", lambda e, bk1=bk1, fi=fi:
                        e.activation(t1[:], self.fb(bk1, 0, 512), AF.Silu,
                                     scale=s1[:, fi:fi + 1]),
                        waits=[(self.s_mm, mv1), (self.s_ve, getattr(self, "_l1t1f", 0)), r1],
                        inc=self.s_ac)
            v0 = p.emit("vector", lambda e, bk2=bk2, fi=fi:
                        e.tensor_scalar_mul(t2[:], self.fb(bk2, 0, 512), s2[:, fi:fi + 1]),
                        waits=[(self.s_mm, mv1), r2, (self.s_ve, getattr(self, "_l1t2f", 0))],
                        inc=self.s_ve)
            v1 = p.emit("vector", lambda e, fi=fi:
                        e.tensor_tensor(h_a[fi][:], t1[:], t2[:], ALU.mult),
                        waits=[(self.s_ac, a1), (self.s_ve, v0)], inc=self.s_ve)
            self._l1t1f = v1
            self._l1t2f = v1
            self.bank_read(bk1, self.s_ac, a1)
            self.bank_read(bk2, self.s_ve, v0)
        ha_rdy = (self.s_ve, self.s_ve.n)

        stages = (("h1T", hs1, h_a, h_b), ("h2T", hs2, h_b, h_a))[:max(0, NSTG - 1)]
        for stage, (wkey, scol, src, dst) in enumerate(stages):
            src_rdy = (self.s_ve, self.s_ve.n) if stage == 0 else (self.s_ac, self.s_ac.n)
            if os.environ.get("L1F_DBG"):
                print(f"tc={tc} stage={stage} src_rdy={src_rdy[0].name}>={src_rdy[1]}")
            for fi in range(NKH):
                bsl = fi % 2
                wh = wH[bsl]
                wv = p.wdma(wh[:], W[wkey][fi], swH[bsl],
                            waits=[(self.s_mm, wH_free[bsl])])
                bk = (0, 1, 2, 3)[fi % 4]
                mv = None
                for kc in range(NKH):
                    mv = p.emit("tensor", lambda e, kc=kc, bk=bk, wh=wh, src=src:
                                e.matmul(self.fb(bk, 0, 512), wh[:, kc * 128:(kc + 1) * 128],
                                         src[kc][:], start=(kc == 0), stop=(kc == NKH - 1)),
                                waits=[(swH[bsl], wv), src_rdy] + (self.bank_war(bk) if kc == 0 else []),
                                inc=self.s_mm if kc == NKH - 1 else None)
                wH_free[bsl] = mv
                a1 = p.emit("scalar", lambda e, bk=bk, fi=fi, scol=scol, dst=dst:
                            e.activation(dst[fi][:], self.fb(bk, 0, 512), AF.Gelu,
                                         scale=scol[:, fi:fi + 1]),
                            waits=[(self.s_mm, mv), r3, r4], inc=self.s_ac)
                if os.environ.get("L1F_DBG") and fi == 0:
                    print(f"  tc={tc} stage={stage} fi0 a1={a1}")
                self.bank_read(bk, self.s_ac, a1)

        h_rdy = (self.s_ac, self.s_ac.n)
        for fi in range(8 if NSTG >= 4 else 0):
            bsl = fi % 2
            wh = wH[bsl]
            wv = p.wdma(wh[:], W["w3T"][fi], swH[bsl],
                        waits=[(self.s_mm, wH_free[bsl])])
            bk = 4 + fi % 2
            mv = None
            for kc in range(NKH):
                mv = p.emit("tensor", lambda e, kc=kc, bk=bk, wh=wh:
                            e.matmul(self.fb(bk, 0, 512), wh[:, kc * 128:(kc + 1) * 128],
                                     h_a[kc][:], start=(kc == 0), stop=(kc == NKH - 1)),
                            waits=[(swH[bsl], wv), h_rdy] + (self.bank_war(bk) if kc == 0 else []),
                            inc=self.s_mm if kc == NKH - 1 else None)
            wH_free[bsl] = mv
            v0 = p.emit("vector", lambda e, fi=fi, bk=bk:
                        e.tensor_scalar_mul(t2[:], self.fb(bk, 0, 512), s3[:, fi:fi + 1]),
                        waits=[(self.s_mm, mv), r5, (self.s_ve, getattr(self, "_l1t2f", 0))],
                        inc=self.s_ve)
            v1 = p.emit("vector", lambda e, fi=fi, c0=c0:
                        e.tensor_add(xT[fi][:, c0:c0 + 512], xT[fi][:, c0:c0 + 512], t2[:]),
                        inc=self.s_ve)
            self._l1t2f = v1
            self.bank_read(bk, self.s_ve, v0)
    return [(self.s_ve, self.s_ve.n)]



Mk.sub_l1_ffn = sub_l1_ffn


def sub_l2_attn_b(self, xT, x_rdy, W, scr):
    p = self.p
    s_og = self.spool(0)
    s_cl = self.spool(1)
    s_w1 = self.gpool(0)
    s_w2 = self.gpool(1)
    xnbig = p.sbuf(self.name("xn"), (128, 8 * T), BF16)
    xn = [xnbig[:, i * T:(i + 1) * T] for i in range(8)]
    rdy = self.ln(xT, x_rdy, xn)
    xn_rdy = [rdy[0]] * 8
    p.wait("tensor", rdy[1])

    cols = p.sbuf(self.name("cols"), (128, 24), F32)
    bqc, bgc, obc = cols[:, 0:8], cols[:, 8:16], cols[:, 16:24]
    e2b = p.sbuf(self.name("e2b"), (2, 128), BF16)
    bkB = p.sbuf(self.name("bkB"), (128, 1024), F32)
    bvB = p.sbuf(self.name("bvB"), (128, 1024), F32)
    s_cl2 = self.gpool(2)
    self.colload(cols[:, 0:8], W["bq"], s_cl)
    self.colload(cols[:, 8:16], W["bg"], s_cl)
    self.colload(cols[:, 16:24], W["outb"], s_cl)
    p.wdma(e2b[:], W["e2"], s_cl2)
    p.dma("sync", bkB[:], W["bkrow"].rearrange("(o f) -> o f", o=1).broadcast_to((128, 1024)), s_cl)
    p.dma("sync", bvB[:], W["bvrow"].rearrange("(o f) -> o f", o=1).broadcast_to((128, 1024)), s_cl)
    cst = (s_cl, s_cl.n)
    cst2 = (s_cl2, s_cl2.n)

    wb1 = p.sbuf(self.name("wb"), (128, 8192), BF16)
    wb2 = p.sbuf(self.name("wb"), (128, 8192), BF16)
    ktile = [p.sbuf(self.name("kt"), (128, 1024), BF16) for _ in range(2)]
    vtile = [p.sbuf(self.name("vt"), (128, 1024), BF16) for _ in range(2)]
    tA = p.sbuf(self.name("tA"), (128, 1024), F32)
    tB = p.sbuf(self.name("tB"), (128, 1024), F32)
    qb2 = [p.sbuf(self.name("qb2"), (128, T), BF16) for _ in range(2)]
    s_q = self.spool(2)
    ogt = [p.sbuf(self.name("og"), (128, T), BF16) for _ in range(2)]
    kvsb = p.sbuf(self.name("kvsb"), (128, 512), BF16)
    kvacc = p.sbuf(self.name("kvacc"), (128, 512), F32)
    ksacc = p.sbuf(self.name("ksacc"), (128, 8), F32)
    ksums = p.sbuf(self.name("ksums"), (128, 8), F32)
    ks2 = p.sbuf(self.name("ks2"), (128, 2), BF16)
    zt = p.sbuf(self.name("zt"), (2, 512), F32)
    ztb = p.sbuf(self.name("ztb"), (2, 512), BF16)
    zbs = p.sbuf(self.name("zbs"), (128, 512), F32)
    gs = p.sbuf(self.name("gs"), (128, 512), F32)

    wk_r = self.wload(wb1, W["wkT"], D, sem=s_w1)
    wv_r = self.wload(wb2, W["wvT"], D, sem=s_w2)
    self._l2b_free = [0, 0]
    for tt in range(16):
        sl = tt % 2
        kt, vt = ktile[sl], vtile[sl]
        for dst, wsb, w_r, bank in ((kt, wb1, wk_r, 0), (vt, wb2, wv_r, 1)):
            for fc in range(2):
                mv = None
                for kc in range(8):
                    mv = p.emit("tensor", lambda e, kc=kc, fc=fc, tt=tt, wsb=wsb, bank=bank:
                                e.matmul(self.fb(bank, 0, 512),
                                         xn[kc][:, tt * 128:(tt + 1) * 128],
                                         wsb[:, kc * D + fc * 512:kc * D + (fc + 1) * 512],
                                         start=(kc == 0), stop=(kc == 7)),
                                waits=[w_r, xn_rdy[kc], rdy[1],
                                       (self.s_ve, self._l2b_free[sl])]
                                      + (self.bank_war(bank) if kc == 0 else []),
                                inc=self.s_mm if kc == 7 else None)
                if bank == 0:
                    v0 = p.emit("vector", lambda e, fc=fc: e.tensor_add(
                        tA[:, fc * 512:(fc + 1) * 512], self.fb(0, 0, 512),
                        bkB[:, fc * 512:(fc + 1) * 512]),
                        waits=[(self.s_mm, mv), cst,
                               (self.s_ve, getattr(self, "_l2b_tf", 0))], inc=self.s_ve)
                    self.bank_read(0, self.s_ve, v0)
                    v1 = p.emit("vector", lambda e, fc=fc: e.tensor_scalar_min(
                        tB[:, fc * 512:(fc + 1) * 512],
                        tA[:, fc * 512:(fc + 1) * 512], 0.0), inc=self.s_ve)
                    a1 = p.emit("scalar", lambda e, fc=fc: e.activation(
                        tB[:, fc * 512:(fc + 1) * 512],
                        tB[:, fc * 512:(fc + 1) * 512], AF.Exp),
                        waits=[(self.s_ve, v1)], inc=self.s_ac)
                    v2 = p.emit("vector", lambda e, fc=fc: e.tensor_scalar_add(
                        tA[:, fc * 512:(fc + 1) * 512],
                        tA[:, fc * 512:(fc + 1) * 512], 1.0), inc=self.s_ve)
                    v3 = p.emit("vector", lambda e, fc=fc, kt=kt: e.tensor_max(
                        kt[:, fc * 512:(fc + 1) * 512],
                        tA[:, fc * 512:(fc + 1) * 512],
                        tB[:, fc * 512:(fc + 1) * 512]),
                        waits=[(self.s_ac, a1)], inc=self.s_ve)
                    self._l2b_tf = v3
                else:
                    v3 = p.emit("vector", lambda e, fc=fc, vt=vt: e.tensor_add(
                        vt[:, fc * 512:(fc + 1) * 512], self.fb(1, 0, 512),
                        bvB[:, fc * 512:(fc + 1) * 512]),
                        waits=[(self.s_mm, mv), cst], inc=self.s_ve)
                    self.bank_read(1, self.s_ve, v3)
        trdy = (self.s_ve, self.s_ve.n)
        for h in range(16):
            hp, par = h // 2, h % 2
            tp = (0, par * 64)
            p.emit("tensor", lambda e, h=h, hp=hp, par=par, kt=kt, vt=vt, tp=tp:
                   e.matmul(self.psf[par * 64:par * 64 + 64,
                                     4 * 512 + hp * 64:4 * 512 + (hp + 1) * 64],
                            kt[:, h * 64:(h + 1) * 64], vt[:, h * 64:(h + 1) * 64],
                            start=True, stop=True, tile_position=tp),
                   waits=[trdy] + (self.bank_war(4) if h == 0 else []),
                   inc=self.s_mm if h == 15 else None)
            p.emit("tensor", lambda e, h=h, hp=hp, par=par, kt=kt, tp=tp:
                   e.matmul(self.psf[par * 64:par * 64 + 64, 5 * 512 + hp:5 * 512 + hp + 1],
                            kt[:, h * 64:(h + 1) * 64], self.ones[:, 0:1],
                            start=True, stop=True, tile_position=tp),
                   waits=[trdy] + (self.bank_war(5) if h == 0 else []),
                   inc=self.s_mm if h == 15 else None)
        mmv = self.s_mm.n
        if tt == 0:
            v1 = p.emit("vector", lambda e: e.tensor_copy(kvacc[:], self.fb(4, 0, 512)),
                        waits=[(self.s_mm, mmv)], inc=self.s_ve)
            v2 = p.emit("vector", lambda e: e.tensor_copy(ksacc[:], self.fb(5, 0, 8)),
                        inc=self.s_ve)
        else:
            v1 = p.emit("vector", lambda e: e.tensor_add(kvacc[:], kvacc[:], self.fb(4, 0, 512)),
                        waits=[(self.s_mm, mmv)], inc=self.s_ve)
            v2 = p.emit("vector", lambda e: e.tensor_add(ksacc[:], ksacc[:], self.fb(5, 0, 8)),
                        inc=self.s_ve)
        self.bank_read(4, self.s_ve, v1)
        self.bank_read(5, self.s_ve, v2)
        self._l2b_free[sl] = self.s_mm.n
    kv_done = self.s_mm.n
    v1 = p.emit("vector", lambda e: e.tensor_copy(kvsb[:, 0:512], kvacc[:]),
                inc=self.s_ve)
    v2 = p.emit("vector", lambda e: e.tensor_copy(ksums[:], ksacc[:]),
                inc=self.s_ve)
    kv_rdy = (self.s_ve, v2)

    wq_r = self.wload(wb1, W["wqT"], D, sem=s_w1, waits=[(self.s_mm, kv_done)])
    self._l2bq = 0
    q_spill = [0, 0]

    def evq(fi, c, ps, mm_pair):
        sl = fi % 2
        v0 = p.emit("vector", lambda e, fi=fi, ps=ps:
                    e.tensor_scalar(tA[:, 0:512], ps, bqc[:, fi:fi + 1], 0.0,
                                    ALU.add, ALU.min),
                    waits=[mm_pair, cst, (self.s_ve, self._l2bq)], inc=self.s_ve)
        a1 = p.emit("scalar", lambda e: e.activation(tA[:, 0:512], tA[:, 0:512], AF.Exp),
                    waits=[(self.s_ve, v0)], inc=self.s_ac)
        v1 = p.emit("vector", lambda e, fi=fi, ps=ps:
                    e.tensor_scalar(tB[:, 0:512], ps, bqc[:, fi:fi + 1], 1.0,
                                    ALU.add, ALU.add), inc=self.s_ve)
        v2 = p.emit("vector", lambda e, fi=fi, c=c, sl=sl:
                    e.tensor_max(ogt[sl][:, c * 512:(c + 1) * 512], tB[:, 0:512], tA[:, 0:512]),
                    waits=[(self.s_ac, a1), (self.s_ve, v1), (s_q, q_spill[sl])],
                    inc=self.s_ve)
        self._l2bq = v2
        if c == NT - 1:
            p.dma("sync", scr["kd"][fi * 128:(fi + 1) * 128, :], ogt[sl][:],
                  s_q, waits=[(self.s_ve, v2)])
            q_spill[sl] = s_q.n
        return (self.s_ve, v2)
    self.proj(wb1, wq_r, xn, xn_rdy, D, evq, banks=(0, 1, 2, 3))
    q_all = (s_q, s_q.n)
    qload = []
    q_free = [0, 0]

    def load_q(dd):
        v = p.dma("sync", qb2[dd % 2][:], scr["kd"][dd * 128:(dd + 1) * 128, :],
                  s_q, waits=[q_all, (self.s_mm, q_free[dd % 2])])
        qload.append((s_q, v))
    load_q(0)

    wg_r = self.wload(wb2, W["wgT"], D, sem=s_w2, waits=[(self.s_mm, kv_done)])
    self._l2bz = 0
    og_spill = [0, 0]
    for dd in range(8):
        if dd < 7:
            load_q(dd + 1)
        q_rdy = qload[dd]
        v0 = p.emit("vector", lambda e: e.memset(ks2[:], 0.0),
                    waits=[(self.s_mm, self._l2bz), kv_rdy], inc=self.s_ve)
        v1 = p.emit("vector", lambda e, dd=dd: e.tensor_copy(ks2[0:64, 0:1], ksums[0:64, dd:dd + 1]),
                    waits=[(self.s_ve, v0)], inc=self.s_ve)
        v2 = p.emit("vector", lambda e, dd=dd: e.tensor_copy(ks2[64:128, 1:2], ksums[64:128, dd:dd + 1]),
                    waits=[(self.s_ve, v1)], inc=self.s_ve)
        for c in range(4):
            bk = 4 + c % 2
            mv = p.emit("tensor", lambda e, dd=dd, c=c, bk=bk:
                        e.matmul(self.fb(bk, 0, 512)[0:2, :], ks2[:],
                                 qb2[dd % 2][:, c * 512:(c + 1) * 512], start=True, stop=True),
                        waits=[(self.s_ve, v2), q_rdy] + self.bank_war(bk), inc=self.s_mm)
            z0 = p.emit("vector", lambda e, bk=bk:
                        e.tensor_scalar_add(zt[0:2, :], self.fb(bk, 0, 512)[0:2, :], 1e-6),
                        waits=[(self.s_mm, mv)], inc=self.s_ve)
            self.bank_read(bk, self.s_ve, z0)
            z2 = p.emit("vector", lambda e: e.reciprocal(zt[0:2, :], zt[0:2, :]),
                        waits=[(self.s_ve, z0)], inc=self.s_ve)
            z3 = p.emit("vector", lambda e: e.tensor_copy(ztb[0:2, :], zt[0:2, :]),
                        waits=[(self.s_ve, z2)], inc=self.s_ve)
            mvz = p.emit("tensor", lambda e, c=c:
                         e.matmul(self.fb(2, 0, 512), e2b[:],
                                  ztb[0:2, :], start=True, stop=True),
                         waits=[(self.s_ve, z3), cst2] + self.bank_war(2), inc=self.s_mm)
            vz = p.emit("vector", lambda e, c=c: e.tensor_copy(zbs[:], self.fb(2, 0, 512)),
                        waits=[(self.s_mm, mvz)], inc=self.s_ve)
            self.bank_read(2, self.s_ve, vz)
            for par in range(2):
                p.emit("tensor", lambda e, dd=dd, c=c, par=par:
                       e.matmul(self.fb(0, 0, 512)[par * 64:par * 64 + 64, :],
                                kvsb[par * 64:par * 64 + 64, dd * 64:(dd + 1) * 64],
                                qb2[dd % 2][par * 64:par * 64 + 64, c * 512:(c + 1) * 512],
                                start=True, stop=True, tile_position=(par * 64, par * 64)),
                       waits=[kv_rdy, q_rdy] + (self.bank_war(0) if par == 0 else []),
                       inc=self.s_mm if par == 1 else None)
            mvo = self.s_mm.n
            mvg = None
            for kc in range(8):
                mvg = p.emit("tensor", lambda e, dd=dd, c=c, kc=kc:
                             e.matmul(self.fb(1, 0, 512),
                                      wb2[:, kc * D + dd * 128:kc * D + dd * 128 + 128],
                                      xn[kc][:, c * 512:(c + 1) * 512],
                                      start=(kc == 0), stop=(kc == 7)),
                             waits=[wg_r, xn_rdy[kc]] + (self.bank_war(1) if kc == 0 else []),
                             inc=self.s_mm if kc == 7 else None)
            ag = p.emit("scalar", lambda e, dd=dd: e.activation(gs[:], self.fb(1, 0, 512),
                                                                AF.Sigmoid, bias=bgc[:, dd:dd + 1]),
                        waits=[(self.s_mm, mvg), cst,
                               (self.s_ve, getattr(self, "_l2bg", 0))], inc=self.s_ac)
            self.bank_read(1, self.s_ac, ag)
            vt1 = p.emit("vector", lambda e: e.tensor_mul(tA[:, 0:512], self.fb(0, 0, 512), zbs[:]),
                         waits=[(self.s_mm, mvo), (self.s_ve, vz),
                                (self.s_ve, getattr(self, "_l2bt", 0))], inc=self.s_ve)
            self.bank_read(0, self.s_ve, vt1)
            vt2 = p.emit("vector", lambda e, dd=dd, c=c: e.tensor_mul(
                ogt[dd % 2][:, c * 512:(c + 1) * 512], tA[:, 0:512], gs[:]),
                waits=[(self.s_ac, ag), (self.s_ve, vt1),
                       (s_og, og_spill[dd % 2])], inc=self.s_ve)
            self._l2bg = vt2
            self._l2bt = vt2
        p.dma("sync", scr["qd"][dd * 128:(dd + 1) * 128, :],
              ogt[dd % 2][:], s_og, waits=[(self.s_ve, self.s_ve.n)])
        og_spill[dd % 2] = s_og.n
        q_free[dd % 2] = self.s_mm.n
        self._l2bz = self.s_mm.n
    og_rdy = (s_og, s_og.n)
    for i in range(8):
        p.dma("sync", xn[i][:], scr["qd"][i * 128:(i + 1) * 128, :], s_og,
              waits=[og_rdy])
    og_rdy = (s_og, s_og.n)
    og_final = xn

    ow_r = self.wload(wb1, W["outw"], D, sem=s_w1, waits=[(self.s_mm, self.s_mm.n)])

    def evout(fi, c, ps, mm_pair):
        v0 = p.emit("vector", lambda e, fi=fi, ps=ps:
                    e.tensor_scalar_add(tB[:, 0:512], ps, obc[:, fi:fi + 1]),
                    waits=[mm_pair, cst, (self.s_ve, getattr(self, "_l2bo", 0))],
                    inc=self.s_ve)
        v1 = p.emit("vector", lambda e, fi=fi, c=c:
                    e.tensor_add(xT[fi][:, c * 512:(c + 1) * 512],
                                 xT[fi][:, c * 512:(c + 1) * 512], tB[:, 0:512]),
                    inc=self.s_ve)
        self._l2bo = v1
        return (self.s_ve, v1)
    self.proj(wb1, ow_r, og_final, [og_rdy] * 8, D, evout, banks=(0, 1, 2, 3))
    return [(self.s_ve, self.s_ve.n)]


Mk.sub_l2_attn_b = sub_l2_attn_b


def host_prep_l2_attn_b(inp):
    f32 = np.float32
    d = {}
    for nm, key in (("wqT", "L2_attn_b_wq_w"), ("wkT", "L2_attn_b_wk_w"),
                    ("wvT", "L2_attn_b_wv_w"), ("wgT", "L2_attn_b_wg_w"),
                    ("outw", "L2_attn_b_out_w")):
        Wm = np.asarray(inp[key], f32)
        d["l2b_" + nm] = np.ascontiguousarray(Wm.reshape(8, 128, 1024))
    for nm, key in (("bq", "L2_attn_b_wq_b"), ("bg", "L2_attn_b_wg_b"),
                    ("outb", "L2_attn_b_out_b")):
        d["l2b_" + nm] = np.ascontiguousarray(np.asarray(inp[key], f32).reshape(8, 128).T)
    d["l2b_bkrow"] = np.asarray(inp["L2_attn_b_wk_b"], f32)
    d["l2b_bvrow"] = np.asarray(inp["L2_attn_b_wv_b"], f32)
    e2 = np.zeros((2, 128), f32)
    e2[0, 0:64] = 1.0
    e2[1, 64:128] = 1.0
    d["l2b_e2"] = e2
    return d


def sub_l3_attn_a(self, xT, x_rdy, W, scr):
    p = self.p
    s_cl = self.spool(0)
    s_w1 = self.gpool(0)
    s_so = [self.spool(1), self.spool(2)]
    s_rl = [self.spool(3), self.spool(4)]
    s_kl = [self.spool(5), self.spool(6)]

    xnbig = p.sbuf(self.name("xn"), (128, 8 * T), BF16)
    xn = [xnbig[:, i * T:(i + 1) * T] for i in range(8)]
    rdy = self.ln(xT, x_rdy, xn)
    xn_rdy = [rdy[0]] * 8
    p.wait("tensor", rdy[1])

    cols = p.sbuf(self.name("cols"), (128, 48), F32)
    rmu, kmu, vmu, wde, uu = (cols[:, 0:8], cols[:, 8:16], cols[:, 16:24],
                              cols[:, 24:32], cols[:, 32:40])
    self.colload(cols[:, 0:8], W["rmu"], s_cl)
    self.colload(cols[:, 8:16], W["kmu"], s_cl)
    self.colload(cols[:, 16:24], W["vmu"], s_cl)
    self.colload(cols[:, 24:32], W["wdec"], s_cl)
    self.colload(cols[:, 32:40], W["uu"], s_cl)
    cB = p.sbuf(self.name("cB"), (128, T), F32)
    p.dma("sync", cB[:], W["cnt"].rearrange("(o f) -> o f", o=1).broadcast_to((128, T)), s_cl)
    cst = (s_cl, s_cl.n)

    wb1 = p.sbuf(self.name("wb"), (128, 8192), BF16)
    tA = p.sbuf(self.name("tA"), (128, T), F32)
    tB = p.sbuf(self.name("tB"), (128, T), F32)

    scope = p.scope()
    wbA = p.sbuf(self.name("wA"), (128, 8192), BF16)
    wbB = p.sbuf(self.name("wB"), (128, 8192), BF16)
    stag = [p.sbuf(self.name("st"), (128, T), BF16) for _ in range(2)]
    stagf = p.sbuf(self.name("sf"), (128, T), F32)

    def sproj(wkey, mucol, dst, fp32=False):
        w_r = self.wload(wb1, W[wkey], D, sem=s_w1, waits=[(self.s_mm, self.s_mm.n)])
        fv = None
        for kc in range(8):
            p.emit("vector", lambda e, kc=kc, mucol=mucol:
                   e.tensor_scalar_mul(wbA[:, kc * D:(kc + 1) * D],
                                       wb1[:, kc * D:(kc + 1) * D], mucol[:, kc:kc + 1]),
                   waits=[w_r, cst, (self.s_mm, getattr(self, "_l3w", 0))], inc=self.s_ve)
            fv = p.emit("vector", lambda e, kc=kc:
                        e.tensor_sub(wbB[:, kc * D:(kc + 1) * D],
                                     wb1[:, kc * D:(kc + 1) * D], wbA[:, kc * D:(kc + 1) * D]),
                        inc=self.s_ve)
        w2r = (self.s_ve, fv)
        stg_free = [0, 0]
        for fi in range(8):
            for c in range(NT):
                bk = (fi * NT + c) % 4
                war = self.bank_war(bk)
                mv = None
                for kc in range(8):
                    p.emit("tensor", lambda e, kc=kc, fi=fi, c=c, bk=bk:
                           e.matmul(self.fb(bk, 0, 512),
                                    wbA[:, kc * D + fi * 128:kc * D + fi * 128 + 128],
                                    xn[kc][:, c * 512:(c + 1) * 512],
                                    start=(kc == 0), stop=False),
                           waits=[w2r, xn_rdy[kc]] + (war if kc == 0 else []))
                for kc in range(8):
                    if c == 0:
                        mv = p.emit("tensor", lambda e, kc=kc, fi=fi, bk=bk:
                                    e.matmul(self.fb(bk, 1, 511),
                                             wbB[:, kc * D + fi * 128:kc * D + fi * 128 + 128],
                                             xn[kc][:, 0:511],
                                             start=False, stop=(kc == 7)),
                                    inc=self.s_mm if kc == 7 else None)
                    else:
                        mv = p.emit("tensor", lambda e, kc=kc, fi=fi, c=c, bk=bk:
                                    e.matmul(self.fb(bk, 0, 512),
                                             wbB[:, kc * D + fi * 128:kc * D + fi * 128 + 128],
                                             xn[kc][:, c * 512 - 1:(c + 1) * 512 - 1],
                                             start=False, stop=(kc == 7)),
                                    inc=self.s_mm if kc == 7 else None)
                sl = fi % 2
                sb = stagf if fp32 else stag[sl]
                sem = s_so[0] if fp32 else s_so[sl]
                vv = p.emit("vector", lambda e, c=c, bk=bk, sb=sb:
                            e.tensor_copy(sb[:, c * 512:(c + 1) * 512], self.fb(bk, 0, 512)),
                            waits=[(self.s_mm, mv), (sem, stg_free[0 if fp32 else sl])],
                            inc=self.s_ve)
                self.bank_read(bk, self.s_ve, vv)
                if c == NT - 1:
                    p.dma("sync", dst[fi * 128:(fi + 1) * 128, :], sb[:], sem,
                          waits=[(self.s_ve, vv)])
                    stg_free[0 if fp32 else sl] = sem.n
        self._l3w = self.s_mm.n

    sproj("WrT", rmu, scr["qd"])
    sproj("WkT", kmu, scr["kd32"], fp32=True)
    sproj("WvT", vmu, scr["vd"])
    allw = [(s_so[0], s_so[0].n), (s_so[1], s_so[1].n)]
    scope.close()

    swkv = xn
    rrow = [p.sbuf(self.name("rr"), (128, T), BF16) for _ in range(2)]
    kr0 = p.sbuf(self.name("kr"), (128, T), F32)
    krow = [kr0, kr0]
    vrow = [p.sbuf(self.name("vr"), (128, T), BF16) for _ in range(2)]
    ek = p.sbuf(self.name("ek"), (128, T), F32)
    As = p.sbuf(self.name("As"), (128, T), F32)
    Bs = p.sbuf(self.name("Bs"), (128, T), F32)
    ecol = p.sbuf(self.name("ec"), (128, 2), F32)
    row_free = [0, 0]
    rdys = []

    def load_rows(dd):
        sl = dd % 2
        w = allw + [(self.s_ve, row_free[sl])]
        wk = allw + [(self.s_ac, getattr(self, "_l3kfree", 0))]
        p.dma("sync", rrow[sl][:], scr["qd"][dd * 128:(dd + 1) * 128, :], s_rl[sl], waits=w)
        p.dma("sync", krow[sl][:], scr["kd32"][dd * 128:(dd + 1) * 128, :], s_kl[sl], waits=wk)
        v = p.dma("sync", vrow[sl][:], scr["vd"][dd * 128:(dd + 1) * 128, :], s_kl[sl], waits=w)
        rdys.append([(s_rl[sl], s_rl[sl].n), (s_kl[sl], s_kl[sl].n)])

    load_rows(0)
    out_rdy = []
    for dd in range(8):
        sl = dd % 2
        rd = rdys[dd]
        prev = (self.s_ve, getattr(self, "_l3s", 0))
        a0 = p.emit("scalar", lambda e, dd=dd: e.activation(ecol[:, 0:1], wde[:, dd:dd + 1],
                                                            AF.Exp, scale=-1.0),
                    waits=[cst, prev], inc=self.s_ac)
        a1 = p.emit("scalar", lambda e, dd=dd: e.activation(ecol[:, 1:2], wde[:, dd:dd + 1],
                                                            AF.Exp),
                    inc=self.s_ac)
        a2 = p.emit("scalar", lambda e, sl=sl: e.activation(ek[:], krow[sl][:], AF.Exp),
                    waits=rd + [prev], inc=self.s_ac)
        v0 = p.emit("vector", lambda e, sl=sl: e.tensor_mul(tA[:], ek[:], vrow[sl][:]),
                    waits=[(self.s_ac, a2), prev], inc=self.s_ve)
        v1 = p.emit("vector", lambda e: e.memset(tA[:, 0:1], 0.0), inc=self.s_ve)
        s1 = p.emit("vector", lambda e: e.tensor_tensor_scan(
            As[:], ecol[:, 0:1].broadcast_to((128, T)), tA[:], 0.0, ALU.mult, ALU.add),
            waits=[(self.s_ve, v1), (self.s_ac, a0)], inc=self.s_ve)
        a3 = p.emit("scalar", lambda e, sl=sl: e.activation(ek[:], krow[sl][:], AF.Exp,
                                                            scale=-1.0),
                    waits=[(self.s_ve, v0)], inc=self.s_ac)
        v2 = p.emit("vector", lambda e: e.memset(ek[:, 0:1], 0.0),
                    waits=[(self.s_ac, a3)], inc=self.s_ve)
        s2 = p.emit("vector", lambda e: e.tensor_tensor_scan(
            Bs[:], ecol[:, 1:2].broadcast_to((128, T)), ek[:], 0.0, ALU.mult, ALU.add),
            waits=[(self.s_ve, v2), (self.s_ac, a1)], inc=self.s_ve)
        v3 = p.emit("vector", lambda e, sl=sl: e.tensor_mul(tB[:, 1:T], vrow[sl][:, 1:T],
                                                            Bs[:, 0:T - 1]),
                    waits=[(self.s_ve, s2)], inc=self.s_ve)
        v4 = p.emit("vector", lambda e: e.memset(tB[:, 0:1], 0.0), inc=self.s_ve)
        v5 = p.emit("vector", lambda e: e.tensor_add(tB[:], tB[:], cB[:]),
                    waits=[cst, (self.s_ve, v4)], inc=self.s_ve)
        a4 = p.emit("scalar", lambda e, sl=sl, dd=dd: e.activation(ek[:], krow[sl][:], AF.Exp,
                                                                   bias=uu[:, dd:dd + 1]),
                    waits=[cst, (self.s_ve, s2)], inc=self.s_ac)
        v6 = p.emit("vector", lambda e: e.tensor_mul(tB[:], tB[:], ek[:]),
                    waits=[(self.s_ac, a4), (self.s_ve, v5)], inc=self.s_ve)
        v7 = p.emit("vector", lambda e: e.tensor_add(tB[:, 1:T], tB[:, 1:T], As[:, 0:T - 1]),
                    waits=[(self.s_ve, s1)], inc=self.s_ve)
        a5 = p.emit("scalar", lambda e, sl=sl: e.activation(tA[:], rrow[sl][:], AF.Sigmoid),
                    waits=[rd[0], (self.s_ve, v3)], inc=self.s_ac)
        self._l3kfree = a4
        v8 = p.emit("vector", lambda e, dd=dd: e.tensor_mul(swkv[dd][:], tA[:], tB[:]),
                    waits=[(self.s_ac, a5), (self.s_ve, v7)], inc=self.s_ve)
        self._l3s = v8
        row_free[sl] = v8
        out_rdy.append((self.s_ve, v8))
        if dd < 7:
            load_rows(dd + 1)

    wo_r = self.wload(wb1, W["WoT"], D, sem=s_w1, waits=[(self.s_mm, self.s_mm.n)])
    self.proj(wb1, wo_r, swkv, out_rdy, D, self.ev_residual(xT), banks=(0, 1, 2, 3))
    return [(self.s_ve, self.s_ve.n)]


Mk.sub_l3_attn_a = sub_l3_attn_a


def host_prep_l3_attn_a(inp):
    f32 = np.float32
    d = {}
    for nm, key in (("WrT", "L3_attn_a_Wr"), ("WkT", "L3_attn_a_Wk"),
                    ("WvT", "L3_attn_a_Wv"), ("WoT", "L3_attn_a_Wo")):
        Wm = np.asarray(inp[key], f32)
        d["l3a_" + nm] = np.ascontiguousarray(Wm.T.reshape(8, 128, 1024))
    for nm, key in (("rmu", "L3_attn_a_r_mu"), ("kmu", "L3_attn_a_k_mu"),
                    ("vmu", "L3_attn_a_v_mu"), ("wdec", "L3_attn_a_w"),
                    ("uu", "L3_attn_a_u")):
        d["l3a_" + nm] = np.ascontiguousarray(np.asarray(inp[key], f32).reshape(8, 128).T)
    d["l3a_cnt"] = np.maximum(np.arange(T, dtype=f32) - 1, 0)
    return d


def glob_c(self, mcols_ap, ncols, scale, dst_col, wait_v, bank=2, waits2=()):
    p = self.p
    g = self.gscr
    c1 = p.emit("vector", lambda e: e.reduce_max(g[:, 0:1], mcols_ap[:, 0:ncols], axis=AX.X),
                waits=[wait_v], inc=self.s_ve)
    c2 = p.emit("tensor", lambda e: e.transpose(self.psf[0:1, bank * 512:bank * 512 + 128],
                                                g[:, 0:1], self.identf[:]),
                waits=[(self.s_ve, c1)] + self.bank_war(bank), inc=self.s_mm)
    c3 = p.emit("vector", lambda e: e.tensor_copy(self.gscr2[0:1, :],
                                                  self.psf[0:1, bank * 512:bank * 512 + 128]),
                waits=[(self.s_mm, c2)], inc=self.s_ve)
    self.bank_read(bank, self.s_ve, c3)
    c4 = p.emit("vector", lambda e: e.reduce_max(self.gscr2[0:1, 0:1], self.gscr2[0:1, :], axis=AX.X),
                waits=[(self.s_ve, c3)], inc=self.s_ve)
    c5 = p.emit("vector", lambda e, scale=scale: e.tensor_scalar_mul(self.gscr2[0:1, 1:2],
                                                                     self.gscr2[0:1, 0:1], -scale),
                waits=[(self.s_ve, c4)], inc=self.s_ve)
    c6 = p.emit("tensor", lambda e: e.matmul(self.psf[0:128, bank * 512:bank * 512 + 1],
                                             self.onesf[0:1, :], self.gscr2[0:1, 1:2],
                                             start=True, stop=True),
                waits=[(self.s_ve, c5)] + self.bank_war(bank), inc=self.s_mm)
    c7 = p.emit("vector", lambda e, dst_col=dst_col: e.tensor_copy(dst_col, self.psf[:, bank * 512:bank * 512 + 1]),
                waits=[(self.s_mm, c6)] + list(waits2), inc=self.s_ve)
    self.bank_read(bank, self.s_ve, c7)
    return (self.s_ve, c7)


Mk.glob_c = glob_c


DBG_SPILL_O = False
LF_STAGE = "full"


def sub_local_ffn(self, xT, x_rdy, W, scr, depth, heads, window, pfx,
                  do_attn=True, do_ffn=True):
    p = self.p
    INNER = heads * 64
    NI = INNER // 128
    NW = T // window
    WT = window // 128
    NH = 22
    NA = 22
    s_cl = self.spool(0)
    s_cg = self.gpool(0)
    s_sp = self.spool(1)
    s_w1 = self.gpool(1)
    s_wh = [self.gpool(2), self.gpool(3)]
    s_qs = [self.spool(2), self.spool(3)]
    s_pl = [self.spool(4), self.spool(5)]
    s_pv = [self.spool(6), self.spool(7)]

    for dd in range(8):
        p.dma("sync", scr["xspill"][dd * 128:(dd + 1) * 128, :], xT[dd][:],
              s_sp, waits=[x_rdy[dd]])
    spilled = (s_sp, s_sp.n)

    xnbig = p.sbuf(self.name("xn"), (128, 8 * T), BF16)
    xn = [xnbig[:, i * T:(i + 1) * T] for i in range(8)]
    rdy = self.ln(xT, x_rdy, xn)
    p.wait("tensor", rdy[1])
    xl_rdy = []
    for dd in range(8):
        v = p.emit("vector", lambda e, dd=dd: e.tensor_copy(xT[dd][:], xn[dd][:]),
                   waits=[rdy[0], spilled], inc=self.s_ve)
        xl_rdy.append((self.s_ve, v))

    ctab = p.sbuf(self.name("cq"), (128, T), BF16)
    stab = p.sbuf(self.name("sq"), (128, T), BF16)
    rotP = p.sbuf(self.name("rp"), (128, 128), BF16)
    p.wdma(rotP[:], W["rotP"], s_cg)
    e2f = p.sbuf(self.name("e2"), (2, 128), F32)
    e64 = p.sbuf(self.name("e64"), (128, 2), F32)
    p.dma("sync", e2f[:], W["e2f"], s_cl)
    p.dma("sync", e64[:], W["e64"], s_cl)
    mskT = p.sbuf(self.name("mT"), (128, (2 * WT - 1) * 128 + window), BF16)
    p.wdma(mskT[:], W["mskT"], s_cg)
    cst = (s_cl, s_cl.n)
    cstg = (s_cg, s_cg.n)

    wbuf = xnbig[:, 0:max(8 * INNER, NI * 1024)]
    wh2 = [p.sbuf(self.name("w2"), (128, NH * 128), BF16) for _ in range(2)]
    shs = [p.sbuf(self.name("sh"), (128, T), BF16) for _ in range(8)]
    if NI == 4:
        stag = [xn[4], xn[5]]
        qpr = [xn[6]]
        kpr = [xn[7]]
        vpr = [xn[5]]
    else:
        stag = [p.sbuf(self.name("st"), (128, T), BF16) for _ in range(2)]
        qpr = [stag[1]]
        kpr = [p.sbuf(self.name("kp"), (128, T), BF16)]
        vpr = [stag[0]]
    lcol = p.sbuf(self.name("lc"), (128, 4), F32)
    rB = p.sbuf(self.name("rB"), (128, T), F32)
    tA = p.sbuf(self.name("tA"), (128, T), F32)
    rr8 = rB[0:2, :]
    esc = p.sbuf(self.name("es"), (128, 2 * window), BF16)
    mcols = p.sbuf(self.name("mc"), (128, 16), F32)
    gcols = p.sbuf(self.name("gc"), (128, 16), F32)
    ncB = p.sbuf(self.name("nc"), (128, 1), F32)
    o_h = p.sbuf(self.name("oh"), (128, WT * 64), F32)
    avacc = p.sbuf(self.name("av"), (128, WT * 64), F32)
    srowb = self.n_tm[0][64:65, 0:window]
    _ha_src = ([0, 1, 2, 3, 5, 6, 7] if NI == 4 else [0, 1, 2, 3, 4, 5])
    ha = []
    for j in range(NA):
        t_i = _ha_src[j // 4]
        ha.append(xn[t_i][:, (j % 4) * 512:(j % 4) * 512 + 512])
    lgb = p.sbuf(self.name("lg"), (128, 24), F32)
    b1c = p.sbuf(self.name("b1"), (128, 2 * NA), F32)
    self.gscr = self.stc[:, 8:12]
    self.gscr2 = self.n_tm[0][0:1, 0:128]
    oTf = shs

    self._lf_shf = 0
    self._lf_tA = 0
    self._lf_esc = 0
    self._lf_ha = 0
    wh_free = [0, 0]

    def shift_set(src_rdy):
        outs = []
        for dd in range(8):
            if dd < 4:
                v = p.emit("vector", lambda e, dd=dd: e.tensor_copy(shs[dd][:], xT[dd][:]),
                           waits=[src_rdy[dd], (self.s_mm, self._lf_shf)], inc=self.s_ve)
            else:
                p.emit("vector", lambda e, dd=dd:
                       e.tensor_copy(shs[dd][:, 1:T], xT[dd][:, 0:T - 1]),
                       waits=[src_rdy[dd], (self.s_mm, self._lf_shf)], inc=self.s_ve)
                v = p.emit("vector", lambda e, dd=dd: e.memset(shs[dd][:, 0:1], 0.0),
                           inc=self.s_ve)
            outs.append((self.s_ve, v))
        return outs

    def qk_pipeline(lw, sh_rdy):
        LFQ = int(globals().get("LFQ_LVL", 4))
        wv0 = [(self.s_ve, self.s_ve.n)]
        self.colload(lcol[:, 0:1], lw["qs"], s_cl, waits=wv0)
        qsk_r = self.colload(lcol[:, 1:2], lw["ks"], s_cl, waits=wv0)
        csl = (s_cl, s_cl.n)
        stg_free = [0, 0]
        for which, dst_d, cnm, snm, scol in (
                (0, scr["qd"], "cosq", "sinq", lcol[:, 0:1]),
                (1, scr["kd"], "cosk", "sink", lcol[:, 1:2])):
            p.wdma(ctab[:], W[cnm], s_cg,
                  waits=[(self.s_ve, self.s_ve.n)])
            p.wdma(stab[:], W[snm], s_cg,
                  waits=[(self.s_ve, self.s_ve.n)])
            cstg2 = (s_cg, s_cg.n)
            w_r = self.wload(wbuf, lw["wqkvT"][which], INNER, sem=s_w1,
                             waits=[(self.s_mm, self.s_mm.n)])
            for fi in range(NI):
                sl = fi % 2
                sg = stag[sl]
                for c in range(4):
                    bk = c % 2
                    mv = None
                    for kc in range(8):
                        mv = p.emit("tensor", lambda e, kc=kc, fi=fi, c=c, bk=bk:
                                    e.matmul(self.fb(bk, 0, 512),
                                             wbuf[:, kc * INNER + fi * 128:kc * INNER + fi * 128 + 128],
                                             shs[kc][:, c * 512:(c + 1) * 512],
                                             start=(kc == 0), stop=(kc == 7)),
                                    waits=[w_r, sh_rdy[kc]] + (self.bank_war(bk) if kc == 0 else []),
                                    inc=self.s_mm if kc == 7 else None)
                    v0 = p.emit("vector", lambda e, c=c, bk=bk, sg=sg:
                                e.tensor_copy(sg[:, c * 512:(c + 1) * 512], self.fb(bk, 0, 512)),
                                waits=[(self.s_mm, mv), (s_qs[sl], stg_free[sl])], inc=self.s_ve)
                    self.bank_read(bk, self.s_ve, v0)
                    if LFQ < 1:
                        continue
                    a0 = p.emit("scalar", lambda e, bk=bk: e.activation(tA[:, 0:512],
                                                                        self.fb(bk, 0, 512), AF.Square),
                                waits=[(self.s_mm, mv), (self.s_ve, v0),
                                       (self.s_ve, self._lf_tA)], inc=self.s_ac)
                    self.bank_read(bk, self.s_ac, a0)
                    self._lf_tA = self.s_ac.n
                    if LFQ == 10:
                        continue
                    mv2 = p.emit("tensor", lambda e: e.matmul(self.fb(2, 0, 512)[0:2, :],
                                                              e64[:], tA[:, 0:512],
                                                              start=True, stop=True),
                                 waits=[(self.s_ac, a0), cst] + self.bank_war(2), inc=self.s_mm)
                    if LFQ == 11:
                        continue
                    v1 = p.emit("vector", lambda e, c=c:
                                e.tensor_copy(rr8[0:2, c * 512:(c + 1) * 512],
                                              self.fb(2, 0, 512)[0:2, :]),
                                waits=[(self.s_mm, mv2)], inc=self.s_ve)
                    self.bank_read(2, self.s_ve, v1)
                if LFQ < 2:
                    continue
                z0 = p.emit("vector", lambda e: e.tensor_scalar_add(rr8[0:2, :], rr8[0:2, :], 1e-24),
                            waits=[(self.s_ve, self.s_ve.n)], inc=self.s_ve)
                z1 = p.emit("scalar", lambda e: e.activation(rr8[0:2, :], rr8[0:2, :], AF.Ln),
                            waits=[(self.s_ve, z0)], inc=self.s_ac)
                z2 = p.emit("scalar", lambda e: e.activation(rr8[0:2, :], rr8[0:2, :], AF.Exp,
                                                             scale=-0.5),
                            waits=[(self.s_ac, z1)], inc=self.s_ac)
                for c in range(4):
                    mv = p.emit("tensor", lambda e, c=c: e.matmul(self.fb(2, 0, 512), e2f[:],
                                                                  rr8[0:2, c * 512:(c + 1) * 512],
                                                                  start=True, stop=True),
                                waits=[(self.s_ac, z2)] + self.bank_war(2), inc=self.s_mm)
                    v1 = p.emit("vector", lambda e, c=c: e.tensor_copy(rB[:, c * 512:(c + 1) * 512],
                                                                       self.fb(2, 0, 512)),
                                waits=[(self.s_mm, mv)], inc=self.s_ve)
                    self.bank_read(2, self.s_ve, v1)
                v2 = p.emit("vector", lambda e, sg=sg, scol=scol:
                            e.tensor_scalar_mul(tA[:], sg[:], scol),
                            waits=[(self.s_ve, self.s_ve.n), csl], inc=self.s_ve)
                v3 = p.emit("vector", lambda e, sg=sg: e.tensor_mul(sg[:], tA[:], rB[:]),
                            inc=self.s_ve)
                self._lf_tA = v3
                if LFQ < 3:
                    continue
                for c in range(4):
                    mv = p.emit("tensor", lambda e, c=c, sg=sg:
                                e.matmul(self.fb(2, 0, 512), rotP[:],
                                         sg[:, c * 512:(c + 1) * 512], start=True, stop=True),
                                waits=[(self.s_ve, v3), cstg] + self.bank_war(2), inc=self.s_mm)
                    w0 = p.emit("vector", lambda e, c=c:
                                e.tensor_mul(tA[:, 0:512], self.fb(2, 0, 512),
                                             stab[:, c * 512:(c + 1) * 512]),
                                waits=[(self.s_mm, mv), cstg2], inc=self.s_ve)
                    self.bank_read(2, self.s_ve, w0)
                    w1 = p.emit("vector", lambda e, c=c, sg=sg:
                                e.tensor_mul(tA[:, 512:1024], sg[:, c * 512:(c + 1) * 512],
                                             ctab[:, c * 512:(c + 1) * 512]),
                                inc=self.s_ve)
                    w2 = p.emit("vector", lambda e, c=c, sg=sg:
                                e.tensor_add(sg[:, c * 512:(c + 1) * 512],
                                             tA[:, 0:512], tA[:, 512:1024]),
                                inc=self.s_ve)
                self._lf_tA = self.s_ve.n
                if LFQ < 4:
                    continue
                p.dma("sync", dst_d[fi * 128:(fi + 1) * 128, :], sg[:], s_qs[sl],
                      waits=[(self.s_ve, self.s_ve.n)])
                stg_free[sl] = s_qs[sl].n

    def v_pipeline(lw, sh_rdy):
        w_r = self.wload(wbuf, lw["wqkvT"][2], INNER, sem=s_w1,
                         waits=[(self.s_mm, self.s_mm.n)])
        stg_free = [0, 0]
        npb = max(1, NI // 4)
        pw = min(512, NI * 128)
        for tt in range(16):
            for pb in range(npb):
                sl = (tt * npb + pb) % 2
                mv = None
                for kc in range(8):
                    mv = p.emit("tensor", lambda e, kc=kc, tt=tt, pb=pb, pw=pw:
                                e.matmul(self.fb(4, 0, pw),
                                         shs[kc][:, tt * 128:(tt + 1) * 128],
                                         wbuf[:, kc * INNER + pb * 512:kc * INNER + pb * 512 + pw],
                                         start=(kc == 0), stop=(kc == 7)),
                                waits=[w_r, sh_rdy[kc]] + (self.bank_war(4) if kc == 0 else []),
                                inc=self.s_mm if kc == 7 else None)
                vv = p.emit("vector", lambda e, sl=sl, pw=pw: e.tensor_copy(
                            stag[sl][:, 0:pw], self.fb(4, 0, pw)),
                            waits=[(self.s_mm, mv), (s_qs[sl], stg_free[sl])], inc=self.s_ve)
                self.bank_read(4, self.s_ve, vv)
                for j in range(pw // 128):
                    pr = pb * 4 + j
                    p.dma("sync", scr["vd2"][pr][:, tt * 128:(tt + 1) * 128],
                          stag[sl][:, j * 128:(j + 1) * 128],
                          s_qs[sl], waits=[(self.s_ve, vv)])
                stg_free[sl] = s_qs[sl].n
        return [(s_qs[0], s_qs[0].n), (s_qs[1], s_qs[1].n)]

    def attn_core(lw, qkv_done):
        wo_r = self.wload(wbuf, lw["woT"], 1024, KC=NI, sem=s_w1,
                          waits=[(self.s_mm, self.s_mm.n)])
        pair_free = [0, 0]
        prdys = []

        def load_pair(pr):
            sl = 0
            w = qkv_done + [(self.s_mm, pair_free[sl])]
            p.dma("sync", qpr[sl][:], scr["qd"][pr * 128:(pr + 1) * 128, :], s_pl[sl], waits=w)
            p.dma("sync", kpr[sl][:], scr["kd"][pr * 128:(pr + 1) * 128, :], s_pl[sl], waits=w)
            p.dma("sync", vpr[sl][:], scr["vd2"][pr], s_pv[sl], waits=w)
            prdys.append([(s_pl[sl], s_pl[sl].n), (s_pv[sl], s_pv[sl].n)])

        oT_w = {}
        for pr in range(NI):
            sl = 0
            load_pair(pr)
            prdy = prdys[pr]
            for hh in range(2):
                h = pr * 2 + hh
                r0 = hh * 64
                for w in range(NW):
                    nk = WT if w == 0 else 2 * WT
                    kb0 = (w - 1) * window if w > 0 else 0
                    esc_free = getattr(self, "_lf_escfree", [0, 0])
                    for tk in range(nk):
                        bk = 3 + tk % 2
                        slo = tk % 2
                        mv = p.emit("tensor", lambda e, r0=r0, tk=tk, w=w, bk=bk, kb0=kb0, sl=sl:
                                    e.matmul(self.fb(bk, 0, window),
                                             kpr[sl][r0:r0 + 64, kb0 + tk * 128:kb0 + (tk + 1) * 128],
                                             qpr[sl][r0:r0 + 64, w * window:(w + 1) * window],
                                             start=True, stop=True),
                                    waits=[prdy[0]] + self.bank_war(bk), inc=self.s_mm)
                        tkoff = ((2 * WT - 1) - (tk + (WT if w == 0 else 0))) * 128
                        vv = p.emit("vector", lambda e, bk=bk, tkoff=tkoff:
                                    e.tensor_add(tA[:, 0:window], self.fb(bk, 0, window),
                                                 mskT[:, tkoff:tkoff + window]),
                                    waits=[(self.s_mm, mv), cstg,
                                           (self.s_ac, self._lf_esc)], inc=self.s_ve)
                        self.bank_read(bk, self.s_ve, vv)
                        aa = p.emit("scalar", lambda e, slo=slo:
                                    e.activation(esc[:, slo * window:(slo + 1) * window],
                                                 tA[:, 0:window], AF.Exp, scale=8.0),
                                    waits=[(self.s_ve, vv),
                                           (self.s_mm, esc_free[slo])], inc=self.s_ac)
                        self._lf_esc = aa
                        erdy = (self.s_ac, aa)
                        p.emit("tensor", lambda e, tk=tk, nk=nk, slo=slo:
                               e.matmul(self.psf[0:1, 2 * 512:2 * 512 + window],
                                        self.ones[:, 0:1], esc[:, slo * window:(slo + 1) * window],
                                        start=(tk == 0), stop=(tk == nk - 1)),
                               waits=[erdy] + (self.bank_war(2) if tk == 0 else []),
                               inc=self.s_mm if tk == nk - 1 else None)
                        for ti in range(WT):
                            p.emit("tensor", lambda e, tk=tk, ti=ti, hh=hh, kb0=kb0, sl=sl, slo=slo:
                                   e.matmul(self.psf2[:, ti * 64:(ti + 1) * 64],
                                            esc[:, slo * window + ti * 128:slo * window + (ti + 1) * 128],
                                            vpr[sl][:, (kb0 // 128 + tk) * 128 + hh * 64:
                                                    (kb0 // 128 + tk) * 128 + hh * 64 + 64],
                                            start=True, stop=True),
                                   waits=[erdy, prdy[1]] + (self.bank_war(6, 7) if ti == 0 else []),
                                   inc=self.s_mm if ti == WT - 1 else None)
                        mvav = self.s_mm.n
                        if tk == 0:
                            vav = p.emit("vector", lambda e: e.tensor_copy(
                                avacc[:, 0:WT * 64], self.psf2[:, 0:WT * 64]),
                                waits=[(self.s_mm, mvav)], inc=self.s_ve)
                        else:
                            vav = p.emit("vector", lambda e: e.tensor_add(
                                avacc[:, 0:WT * 64], avacc[:, 0:WT * 64], self.psf2[:, 0:WT * 64]),
                                waits=[(self.s_mm, mvav)], inc=self.s_ve)
                        self.bank_read(6, self.s_ve, vav)
                        self.bank_read(7, self.s_ve, vav)
                        esc_free[slo] = self.s_mm.n
                    self._lf_escfree = esc_free
                    av_done = self.s_mm.n
                    g1 = p.emit("vector", lambda e: e.tensor_scalar_add(
                        srowb, self.psf[0:1, 2 * 512:2 * 512 + window], 1e-30),
                        waits=[(self.s_mm, av_done)], inc=self.s_ve)
                    self.bank_read(2, self.s_ve, g1)
                    g2 = p.emit("vector", lambda e: e.reciprocal(srowb, srowb),
                                waits=[(self.s_ve, g1)], inc=self.s_ve)
                    for ti in range(WT):
                        mv = p.emit("tensor", lambda e, ti=ti:
                                    e.matmul(self.fb(2, 0, 1),
                                             self.n_tm[0][64:65, ti * 128:(ti + 1) * 128],
                                             self.identf[64:65, 64:65],
                                             start=True, stop=True),
                                    waits=[(self.s_ve, g2)] + self.bank_war(2), inc=self.s_mm)
                        vv = p.emit("vector", lambda e, ti=ti:
                                    e.tensor_copy(gcols[:, ti:ti + 1], self.fb(2, 0, 1)),
                                    waits=[(self.s_mm, mv)], inc=self.s_ve)
                        self.bank_read(2, self.s_ve, vv)
                    for ti in range(WT):
                        p.emit("vector", lambda e, ti=ti:
                               e.tensor_scalar_mul(o_h[:, ti * 64:(ti + 1) * 64],
                                                   avacc[:, ti * 64:(ti + 1) * 64],
                                                   gcols[:, ti:ti + 1]),
                               waits=[(self.s_ve, self.s_ve.n)], inc=self.s_ve)
                    ohv = self.s_ve.n
                    self.bank_read(7, self.s_ve, ohv)
                    dd0, rr = h // 2, (h % 2) * 64
                    for ti in range(WT):
                        mv = p.emit("tensor", lambda e, ti=ti:
                                    e.transpose(self.psf[0:64, 2 * 512:2 * 512 + 128],
                                                o_h[:, ti * 64:(ti + 1) * 64], self.identf[:]),
                                    waits=[(self.s_ve, ohv)] + self.bank_war(2), inc=self.s_mm)
                        aa = p.emit("scalar", lambda e, ti=ti, dd0=dd0, rr=rr, w=w:
                                    e.copy(oTf[dd0][rr:rr + 64,
                                                    w * window + ti * 128:w * window + (ti + 1) * 128],
                                           self.psf[0:64, 2 * 512:2 * 512 + 128]),
                                    waits=[(self.s_mm, mv)], inc=self.s_ac)
                        self.bank_read(2, self.s_ac, aa)
                    oT_w[dd0] = self.s_ac.n
            pair_free[sl] = self.s_mm.n
        o_rdy = [(self.s_ac, oT_w[dd]) for dd in range(NI)]
        if globals().get('DBG_SPILL_O'):
            for dd in range(NI):
                p.dma("sync", scr["vd"][dd * 128:(dd + 1) * 128, :], oTf[dd][:],
                      self.s_ld, waits=[o_rdy[dd]])
        self.proj(wbuf, wo_r, oTf, o_rdy, 1024, self.ev_residual(xT), banks=(0, 1, 3, 4), KC=NI)
        return [(self.s_ve, self.s_ve.n)] * 8

    def ffn_block(lw, sh_rdy):
        wv = [(self.s_ve, self.s_ve.n)]
        self.colload(lgb[:, 0:8], lw["lg"], s_cl, waits=wv)
        self.colload(lgb[:, 8:16], lw["lb"], s_cl, waits=wv)
        self.colload(b1c[:, 0:NA], lw["b1a"], s_cl, waits=wv)
        b1_r = self.colload(b1c[:, NA:2 * NA], lw["b1g"], s_cl, waits=wv)
        csl = (s_cl, s_cl.n)
        for dd in range(8):
            for c in range(4):
                p.emit("tensor", lambda e, dd=dd, c=c:
                       e.matmul(self.psf[0:1, c * 512:(c + 1) * 512], self.ones[:, 0:1],
                                shs[dd][:, c * 512:(c + 1) * 512],
                                start=(dd == 0), stop=(dd == 7)),
                       waits=[sh_rdy[dd]] + (self.bank_war(0, 1, 2, 3) if (dd == 0 and c == 0) else []),
                       inc=self.s_mm if (dd == 7 and c == 3) else None)
        smm = self.s_mm.n
        g0 = p.emit("vector", lambda e: e.tensor_copy(rB[0:1, :], self.psf[0:1, 0:T]),
                    waits=[(self.s_mm, smm)], inc=self.s_ve)
        for b in range(4):
            self.bank_read(b, self.s_ve, g0)
        for dd in range(8):
            a0 = p.emit("scalar", lambda e, dd=dd: e.activation(tA[:], shs[dd][:], AF.Square),
                        waits=[sh_rdy[dd], (self.s_mm, getattr(self, "_lf_sq", 0))], inc=self.s_ac)
            for c in range(4):
                p.emit("tensor", lambda e, dd=dd, c=c:
                       e.matmul(self.psf[0:1, c * 512:(c + 1) * 512], self.onesf[:, 0:1],
                                tA[:, c * 512:(c + 1) * 512],
                                start=(dd == 0), stop=(dd == 7)),
                       waits=[(self.s_ac, a0)] + (self.bank_war(0, 1, 2, 3) if (dd == 0 and c == 0) else []),
                       inc=self.s_mm if c == 3 else None)
            self._lf_sq = self.s_mm.n
        smm2 = self.s_mm.n
        g1 = p.emit("vector", lambda e: e.tensor_copy(tA[0:1, :], self.psf[0:1, 0:T]),
                    waits=[(self.s_mm, smm2)], inc=self.s_ve)
        for b in range(4):
            self.bank_read(b, self.s_ve, g1)
        inv = 1.0 / 1024.0
        c1 = p.emit("vector", lambda e: e.tensor_scalar_mul(rB[0:1, :], rB[0:1, :], inv),
                    waits=[(self.s_ve, g1)], inc=self.s_ve)
        c2 = p.emit("vector", lambda e: e.tensor_scalar_mul(tA[0:1, :], tA[0:1, :], inv),
                    waits=[(self.s_ve, c1)], inc=self.s_ve)
        c3 = p.emit("vector", lambda e: e.tensor_mul(stag[0][0:1, :], rB[0:1, :], rB[0:1, :]),
                    waits=[(self.s_ve, c2)], inc=self.s_ve)
        c4 = p.emit("vector", lambda e: e.tensor_sub(tA[0:1, :], tA[0:1, :], stag[0][0:1, :]),
                    waits=[(self.s_ve, c3)], inc=self.s_ve)
        c5 = p.emit("vector", lambda e: e.tensor_scalar_add(tA[0:1, :], tA[0:1, :], 1e-5),
                    waits=[(self.s_ve, c4)], inc=self.s_ve)
        c6 = p.emit("scalar", lambda e: e.activation(tA[0:1, :], tA[0:1, :], AF.Ln),
                    waits=[(self.s_ve, c5)], inc=self.s_ac)
        c7 = p.emit("scalar", lambda e: e.activation(tA[0:1, :], tA[0:1, :], AF.Exp,
                                                     scale=-0.5),
                    waits=[(self.s_ac, c6)], inc=self.s_ac)
        c8 = p.emit("vector", lambda e: e.tensor_mul(stag[0][0:1, :], tA[0:1, :], rB[0:1, :]),
                    waits=[(self.s_ac, c7)], inc=self.s_ve)
        c9 = p.emit("vector", lambda e: e.tensor_scalar_mul(stag[0][0:1, :], stag[0][0:1, :], -1.0),
                    waits=[(self.s_ve, c8)], inc=self.s_ve)
        for c in range(4):
            mv = p.emit("tensor", lambda e, c=c: e.matmul(self.fb(c, 0, 512), self.onesf[0:1, :],
                                                          tA[0:1, c * 512:(c + 1) * 512],
                                                          start=True, stop=True),
                        waits=[(self.s_ve, c9)] + self.bank_war(c), inc=self.s_mm)
            vv = p.emit("vector", lambda e, c=c: e.tensor_copy(rB[:, c * 512:(c + 1) * 512],
                                                               self.fb(c, 0, 512)),
                        waits=[(self.s_mm, mv)], inc=self.s_ve)
            self.bank_read(c, self.s_ve, vv)
        for c in range(4):
            mv = p.emit("tensor", lambda e, c=c: e.matmul(self.fb(c, 0, 512), self.ones[0:1, :],
                                                          stag[0][0:1, c * 512:(c + 1) * 512],
                                                          start=True, stop=True),
                        waits=[(self.s_ve, self.s_ve.n)] + self.bank_war(c), inc=self.s_mm)
            vv = p.emit("vector", lambda e, c=c: e.tensor_copy(tA[:, c * 512:(c + 1) * 512],
                                                               self.fb(c, 0, 512)),
                        waits=[(self.s_mm, mv)], inc=self.s_ve)
            self.bank_read(c, self.s_ve, vv)
        nsh = []
        for dd in range(8):
            u0 = p.emit("vector", lambda e, dd=dd: e.tensor_mul(stag[0][:], shs[dd][:], rB[:]),
                        waits=[(self.s_ve, self.s_ve.n)], inc=self.s_ve)
            u1 = p.emit("vector", lambda e: e.tensor_add(stag[0][:], stag[0][:], tA[:]),
                        inc=self.s_ve)
            u2 = p.emit("vector", lambda e, dd=dd:
                        e.tensor_scalar(shs[dd][:], stag[0][:], lgb[:, dd:dd + 1],
                                        lgb[:, 8 + dd:8 + dd + 1], ALU.mult, ALU.add),
                        waits=[csl], inc=self.s_ve)
            nsh.append((self.s_ve, u2))
        for tc in range(4):
            c0 = tc * 512
            for fi in range(NA):
                sl = fi % 2
                wva = p.wdma(wh2[sl][:, 0:1024], lw["w1a"][fi], s_wh[sl],
                            waits=[(self.s_mm, wh_free[sl])])
                wvb = p.wdma(wh2[sl][:, 1024:2048], lw["w1g"][fi], s_wh[sl],
                            waits=[(self.s_mm, wh_free[sl])])
                mva = None
                for kc in range(8):
                    p.emit("tensor", lambda e, kc=kc, sl=sl, c0=c0:
                           e.matmul(self.fb(0, 0, 512), wh2[sl][:, kc * 128:(kc + 1) * 128],
                                    shs[kc][:, c0:c0 + 512], start=(kc == 0), stop=(kc == 7)),
                           waits=[(s_wh[sl], wvb), nsh[kc]] + (self.bank_war(0) if kc == 0 else []))
                for kc in range(8):
                    mva = p.emit("tensor", lambda e, kc=kc, sl=sl, c0=c0:
                                 e.matmul(self.fb(1, 0, 512), wh2[sl][:, 1024 + kc * 128:1024 + (kc + 1) * 128],
                                          shs[kc][:, c0:c0 + 512], start=(kc == 0), stop=(kc == 7)),
                                 waits=[(s_wh[sl], wvb)] + (self.bank_war(1) if kc == 0 else []),
                                 inc=self.s_mm if kc == 7 else None)
                wh_free[sl] = mva
                ag = p.emit("scalar", lambda e, fi=fi: e.activation(tA[:, 0:512], self.fb(1, 0, 512),
                                                                    AF.Gelu,
                                                                    bias=b1c[:, NA + fi:NA + fi + 1]),
                            waits=[(self.s_mm, mva), b1_r, (self.s_ve, self._lf_ha)], inc=self.s_ac)
                self.bank_read(1, self.s_ac, ag)
                v0 = p.emit("vector", lambda e, fi=fi: e.tensor_scalar_add(tA[:, 512:1024],
                                                                           self.fb(0, 0, 512),
                                                                           b1c[:, fi:fi + 1]),
                            waits=[(self.s_mm, mva), b1_r], inc=self.s_ve)
                self.bank_read(0, self.s_ve, v0)
                v1 = p.emit("vector", lambda e, fi=fi: e.tensor_mul(ha[fi], tA[:, 512:1024],
                                                                    tA[:, 0:512]),
                            waits=[(self.s_ac, ag), (self.s_ve, v0)], inc=self.s_ve)
                self._lf_ha = v1
            ha_rdy = (self.s_ve, self.s_ve.n)
            for fi in range(8):
                sl = fi % 2
                wv = p.wdma(wh2[sl][:], lw["w2T"][fi], s_wh[sl],
                           waits=[(self.s_mm, wh_free[sl])])
                mv = None
                for kc in range(NH):
                    mv = p.emit("tensor", lambda e, kc=kc, sl=sl:
                                e.matmul(self.fb(4, 0, 512), wh2[sl][:, kc * 128:(kc + 1) * 128],
                                         ha[kc], start=(kc == 0), stop=(kc == NH - 1)),
                                waits=[(s_wh[sl], wv), ha_rdy] + (self.bank_war(4) if kc == 0 else []),
                                inc=self.s_mm if kc == NH - 1 else None)
                wh_free[sl] = mv
                v0 = p.emit("vector", lambda e, fi=fi, c0=c0:
                            e.tensor_add(xT[fi][:, c0:c0 + 512], xT[fi][:, c0:c0 + 512],
                                         self.fb(4, 0, 512)),
                            waits=[(self.s_mm, mv)], inc=self.s_ve)
                self.bank_read(4, self.s_ve, v0)
        return [(self.s_ve, self.s_ve.n)] * 8

    self._lf_sq = 0
    cur_rdy = xl_rdy
    for l in range(depth):
        lw = {k: W[f"{l}_{k}"] for k in
              ("wqkvT", "qs", "ks", "woT", "lg", "lb", "w1a", "w1g", "w2T",
               "b1a", "b1g", "b2")}
        if do_attn:
            stage = globals().get("LF_STAGE", "full")
            sh_rdy = shift_set(cur_rdy)
            self._lf_shf = self.s_mm.n
            qk_pipeline(lw, sh_rdy)
            if stage != "qk":
                qkv_done = v_pipeline(lw, sh_rdy)
                if stage != "v":
                    cur_rdy = attn_core(lw, qkv_done)
        if do_ffn:
            sh_rdy = shift_set(cur_rdy)
            self._lf_shf = self.s_mm.n
            cur_rdy = ffn_block(lw, sh_rdy)
            b2_r = self.colload(lgb[:, 16:24], W[f"{l}_b2"], s_cl,
                                waits=[(self.s_ve, self.s_ve.n)])
            nr = []
            for dd in range(8):
                v = p.emit("vector", lambda e, dd=dd:
                           e.tensor_scalar_add(xT[dd][:], xT[dd][:], lgb[:, 16 + dd:16 + dd + 1]),
                           waits=[b2_r, cur_rdy[dd]], inc=self.s_ve)
                nr.append((self.s_ve, v))
            cur_rdy = nr

    rel = [tA, rB]
    rfree = [self.s_ve.n, self.s_ve.n]
    s_rl = [self.spool(8), self.spool(9)]
    for dd in range(8):
        sl = dd % 2
        v = p.dma("sync", rel[sl][:], scr["xspill"][dd * 128:(dd + 1) * 128, :], s_rl[sl],
                  waits=[(self.s_ve, rfree[sl])])
        vv = p.emit("vector", lambda e, dd=dd, sl=sl: e.tensor_add(xT[dd][:], xT[dd][:], rel[sl][:]),
                    waits=[(s_rl[sl], v), cur_rdy[dd]], inc=self.s_ve)
        rfree[sl] = vv
    return [(self.s_ve, self.s_ve.n)]


Mk.sub_local_ffn = sub_local_ffn



def sub_l2_attn_a(self, xT, x_rdy, W, scr):
    p = self.p
    s_cl = self.spool(0)
    s_w = self.gpool(0)
    s_sp = [self.spool(1), self.spool(2)]
    s_ts = [self.spool(3), self.spool(4)]
    s_pl = self.spool(5)
    s_ug = self.gpool(1)

    xn = [p.sbuf(self.name("xn"), (128, T), BF16) for _ in range(8)]
    rdy = self.ln(xT, x_rdy, xn)
    p.wait("tensor", rdy[1])
    xn_rdy = [rdy[0]] * 8

    wbuf = p.sbuf(self.name("wb"), (128, 8192), BF16)
    utm = p.sbuf(self.name("ut"), (128, 128), BF16)
    p.wdma(utm[:], W["utm"], s_ug)
    cst = (s_cl, s_cl.n)
    ftmp = p.sbuf(self.name("ft"), (128, 1024), F32)
    qsg = [p.sbuf(self.name("qg"), (128, T), BF16) for _ in range(2)]
    tsg = [p.sbuf(self.name("tg"), (128, 512), BF16) for _ in range(2)]
    vd3 = scr["vd"].rearrange("(a p) t -> a p t", p=128)
    kd3 = scr["kd2"]

    self._a2_ft = 0
    sg_free = [0, 0]
    ts_free = [0, 0]

    def fmap_evac(dst_bf, ps, mm_pair, extra=(), w=512):
        g1 = p.emit("vector", lambda e, ps=ps, w=w:
                    e.tensor_scalar_min(ftmp[:, 0:w], ps, 0.0),
                    waits=[mm_pair, (self.s_ac, self._a2_ft)] + list(extra),
                    inc=self.s_ve)
        g2 = p.emit("scalar", lambda e, w=w: e.activation(ftmp[:, 0:w],
                                                          ftmp[:, 0:w], AF.Exp),
                    waits=[(self.s_ve, g1)], inc=self.s_ac)
        g3 = p.emit("vector", lambda e, ps=ps, w=w:
                    e.tensor_scalar_add(ftmp[:, 512:512 + w], ps, 1.0),
                    waits=[(self.s_ve, g1)], inc=self.s_ve)
        g4 = p.emit("vector", lambda e, dst_bf=dst_bf, w=w:
                    e.tensor_max(dst_bf, ftmp[:, 0:w], ftmp[:, 512:512 + w]),
                    waits=[(self.s_ac, g2), (self.s_ve, g3)], inc=self.s_ve)
        self._a2_ft = g2
        return (self.s_ve, g4)

    def stage_fmap_spill(w_rdy, dst_d):
        def evac(fi, c, ps, mm_pair):
            sl = fi % 2
            dst = qsg[sl][:, c * 512:(c + 1) * 512]
            extra = [(s_sp[sl], sg_free[sl])] if c == 0 else []
            rv = fmap_evac(dst, ps, mm_pair, extra)
            if c == NT - 1:
                p.dma("sync", dst_d[fi * 128:(fi + 1) * 128, :], qsg[sl][:],
                      s_sp[sl], waits=[rv])
                sg_free[sl] = s_sp[sl].n
            return rv
        return evac

    def token_proj(w_rdy, dst3, fmap):
        for tt in range(16):
            for pb in range(2):
                sl = (tt * 2 + pb) % 2
                bk = 4 + sl
                mv = None
                for kc in range(8):
                    mv = p.emit("tensor", lambda e, kc=kc, tt=tt, pb=pb, bk=bk:
                                e.matmul(self.fb(bk, 0, 512),
                                         xn[kc][:, tt * 128:(tt + 1) * 128],
                                         wbuf[:, kc * 1024 + pb * 512:
                                              kc * 1024 + pb * 512 + 512],
                                         start=(kc == 0), stop=(kc == 7)),
                                waits=[w_rdy, xn_rdy[kc]] +
                                      (self.bank_war(bk) if kc == 0 else []),
                                inc=self.s_mm if kc == 7 else None)
                extra = [(s_ts[sl], ts_free[sl])]
                if fmap:
                    rv = fmap_evac(tsg[sl][:], self.fb(bk, 0, 512),
                                   (self.s_mm, mv), extra)
                else:
                    v = p.emit("vector", lambda e, sl=sl, bk=bk:
                               e.tensor_copy(tsg[sl][:], self.fb(bk, 0, 512)),
                               waits=[(self.s_mm, mv)] + extra, inc=self.s_ve)
                    rv = (self.s_ve, v)
                self.bank_read(bk, *rv)
                for j in range(4):
                    pr = pb * 4 + j
                    p.dma("sync", dst3[pr][:, tt * 128:(tt + 1) * 128],
                          tsg[sl][:, j * 128:(j + 1) * 128], s_ts[sl], waits=[rv])
                ts_free[sl] = s_ts[sl].n

    wq_r = self.wload(wbuf, W["wqT"], 1024, sem=s_w)
    self.proj(wbuf, wq_r, xn, xn_rdy, 1024, stage_fmap_spill(wq_r, scr["qd"]))
    wk_r = self.wload(wbuf, W["wkT"], 1024, sem=s_w, waits=[(self.s_mm, self.s_mm.n)])
    self.proj(wbuf, wk_r, xn, xn_rdy, 1024, stage_fmap_spill(wk_r, scr["kd"]))
    token_proj(wk_r, kd3, True)
    wv_r = self.wload(wbuf, W["wvT"], 1024, sem=s_w, waits=[(self.s_mm, self.s_mm.n)])
    token_proj(wv_r, vd3, False)
    qkv_done = [(s_sp[0], s_sp[0].n), (s_sp[1], s_sp[1].n),
                (s_ts[0], s_ts[0].n), (s_ts[1], s_ts[1].n)]

    qTp = p.sbuf(self.name("qp"), (128, T), BF16)
    kTp = p.sbuf(self.name("kp"), (128, T), BF16)
    ktm = p.sbuf(self.name("km"), (128, T), BF16)
    vtm = p.sbuf(self.name("vm"), (128, T), BF16)
    KVb = p.sbuf(self.name("kvb"), (128, 64), BF16)
    KVa = p.sbuf(self.name("kva"), (128, 128), F32)
    KsB = p.sbuf(self.name("ksb"), (128, 1), BF16)
    KsA = p.sbuf(self.name("ksa"), (128, 2), F32)
    sTm = p.sbuf(self.name("stm"), (128, 128), BF16)
    numB = p.sbuf(self.name("nb"), (128, 128), F32)
    tmpo = p.sbuf(self.name("to"), (128, 128), BF16)
    denr = p.sbuf(self.name("dr"), (1, 128), F32)
    oTf = xn

    st_free = 0
    tmo_free = 0
    otw = {}
    for pr in range(8):
        ldw = qkv_done + [(self.s_mm, self.s_mm.n)]
        p.dma("sync", qTp[:], scr["qd"][pr * 128:(pr + 1) * 128, :], s_pl, waits=ldw)
        p.dma("sync", kTp[:], scr["kd"][pr * 128:(pr + 1) * 128, :], s_pl, waits=ldw)
        p.dma("sync", ktm[:], kd3[pr], s_pl, waits=ldw)
        p.dma("sync", vtm[:], vd3[pr], s_pl, waits=ldw)
        prdy = (s_pl, s_pl.n)
        z0 = p.emit("vector", lambda e: e.memset(KVa[:], 0.0),
                    waits=[(self.s_mm, self.s_mm.n), (self.s_ac, self.s_ac.n)],
                    inc=self.s_ve)
        p.emit("vector", lambda e: e.memset(KVb[:], 0.0), inc=self.s_ve)
        p.emit("vector", lambda e: e.memset(KsA[:], 0.0), inc=self.s_ve)
        z3 = p.emit("vector", lambda e: e.memset(KsB[:], 0.0), inc=self.s_ve)
        kvb_rdy = (self.s_ve, z3)
        ksb_rdy = (self.s_ve, z3)
        kva_w = (self.s_ve, z3)
        for ch in range(16):
            for hh in range(2):
                r0 = hh * 64
                cc = ch * 128
                m1 = p.emit("tensor", lambda e, r0=r0, cc=cc:
                            e.matmul(self.fb(0, 0, 128),
                                     kTp[r0:r0 + 64, cc:cc + 128],
                                     qTp[r0:r0 + 64, cc:cc + 128],
                                     start=True, stop=True),
                            waits=[prdy] + self.bank_war(0), inc=self.s_mm)
                e1 = p.emit("vector", lambda e:
                            e.tensor_mul(sTm[:], self.fb(0, 0, 128), utm[:]),
                            waits=[(self.s_mm, m1), (self.s_mm, st_free), cst],
                            inc=self.s_ve)
                self.bank_read(0, self.s_ve, e1)
                p.emit("tensor", lambda e, r0=r0, cc=cc:
                       e.matmul(self.fb(1, 0, 128)[0:64, :],
                                KVb[r0:r0 + 64, 0:64],
                                qTp[r0:r0 + 64, cc:cc + 128],
                                start=True, stop=False),
                       waits=[kvb_rdy, prdy] + self.bank_war(1))
                m3 = p.emit("tensor", lambda e, cc=cc, hh=hh:
                            e.matmul(self.fb(1, 0, 128)[0:64, :],
                                     vtm[:, cc + hh * 64:cc + hh * 64 + 64],
                                     sTm[:], start=False, stop=True),
                            waits=[(self.s_ve, e1)], inc=self.s_mm)
                p.emit("tensor", lambda e, r0=r0, cc=cc:
                       e.matmul(self.psf[0:1, 2 * 512:2 * 512 + 128],
                                KsB[r0:r0 + 64, 0:1],
                                qTp[r0:r0 + 64, cc:cc + 128],
                                start=True, stop=False),
                       waits=[ksb_rdy] + self.bank_war(2))
                m5 = p.emit("tensor", lambda e:
                            e.matmul(self.psf[0:1, 2 * 512:2 * 512 + 128],
                                     self.ones[:, 0:1], sTm[:],
                                     start=False, stop=True),
                            waits=[(self.s_ve, e1)], inc=self.s_mm)
                st_free = self.s_mm.n
                e2 = p.emit("vector", lambda e:
                            e.tensor_scalar_add(denr[0:1, :],
                                                self.psf[0:1, 2 * 512:2 * 512 + 128],
                                                1e-8),
                            waits=[(self.s_mm, m5)], inc=self.s_ve)
                self.bank_read(2, self.s_ve, e2)
                e3 = p.emit("vector", lambda e: e.reciprocal(denr[0:1, :],
                                                             denr[0:1, :]),
                            inc=self.s_ve)
                m6 = p.emit("tensor", lambda e:
                            e.matmul(self.fb(3, 0, 128)[0:64, :],
                                     self.onesf[0:1, 0:64], denr[0:1, :],
                                     start=True, stop=True),
                            waits=[(self.s_ve, e3)] + self.bank_war(3),
                            inc=self.s_mm)
                e4 = p.emit("vector", lambda e:
                            e.tensor_copy(numB[0:64, :], self.fb(1, 0, 128)[0:64, :]),
                            waits=[(self.s_mm, m3)], inc=self.s_ve)
                self.bank_read(1, self.s_ve, e4)
                e5 = p.emit("vector", lambda e:
                            e.tensor_mul(tmpo[0:64, :], numB[0:64, :],
                                         self.fb(3, 0, 128)[0:64, :]),
                            waits=[(self.s_mm, m6), (self.s_ve, e4),
                                   (self.s_ac, tmo_free)], inc=self.s_ve)
                self.bank_read(3, self.s_ve, e5)
                e6 = p.emit("scalar", lambda e, pr=pr, r0=r0, cc=cc:
                            e.copy(oTf[pr][r0:r0 + 64, cc:cc + 128],
                                   tmpo[0:64, 0:128]),
                            waits=[(self.s_ve, e5)], inc=self.s_ac)
                tmo_free = e6
                otw[pr] = e6
                m7 = p.emit("tensor", lambda e, cc=cc, hh=hh:
                            e.matmul(self.fb(4, 0, 64)[0:64, :],
                                     ktm[:, cc + hh * 64:cc + hh * 64 + 64],
                                     vtm[:, cc + hh * 64:cc + hh * 64 + 64],
                                     start=True, stop=True),
                            waits=[prdy] + self.bank_war(4), inc=self.s_mm)
                e7 = p.emit("vector", lambda e, hh=hh:
                            e.tensor_add(KVa[0:64, hh * 64:hh * 64 + 64],
                                         KVa[0:64, hh * 64:hh * 64 + 64],
                                         self.fb(4, 0, 64)[0:64, :]),
                            waits=[(self.s_mm, m7), (self.s_ac, self.s_ac.n)],
                            inc=self.s_ve)
                self.bank_read(4, self.s_ve, e7)
                m8 = p.emit("tensor", lambda e, cc=cc, hh=hh:
                            e.matmul(self.fb(5, 0, 1)[0:64, :],
                                     ktm[:, cc + hh * 64:cc + hh * 64 + 64],
                                     self.ones[:, 0:1],
                                     start=True, stop=True),
                            waits=[prdy] + self.bank_war(5), inc=self.s_mm)
                e8 = p.emit("vector", lambda e, hh=hh:
                            e.tensor_add(KsA[0:64, hh:hh + 1],
                                         KsA[0:64, hh:hh + 1],
                                         self.fb(5, 0, 1)[0:64, :]),
                            waits=[(self.s_mm, m8), (self.s_ac, self.s_ac.n)],
                            inc=self.s_ve)
                self.bank_read(5, self.s_ve, e8)
                c1 = p.emit("scalar", lambda e, r0=r0, hh=hh:
                            e.copy(KVb[r0:r0 + 64, 0:64],
                                   KVa[0:64, hh * 64:hh * 64 + 64]),
                            waits=[(self.s_ve, e7), (self.s_mm, m3)],
                            inc=self.s_ac)
                c2 = p.emit("scalar", lambda e, r0=r0, hh=hh:
                            e.copy(KsB[r0:r0 + 64, 0:1], KsA[0:64, hh:hh + 1]),
                            waits=[(self.s_ve, e8), (self.s_mm, m5)],
                            inc=self.s_ac)
                kvb_rdy = (self.s_ac, c1)
                ksb_rdy = (self.s_ac, c2)

    wo_r = self.wload(wbuf, W["woT"], 1024, sem=s_w, waits=[(self.s_mm, self.s_mm.n)])
    o_rdy = [(self.s_ac, otw[pr]) for pr in range(8)]
    self.proj(wbuf, wo_r, oTf, o_rdy, 1024, self.ev_residual(xT), banks=(0, 1, 2, 3))
    return [(self.s_ve, self.s_ve.n)] * 8


Mk.sub_l2_attn_a = sub_l2_attn_a


def host_prep_l2_attn_a(inp):
    f32 = np.float32
    d = {}
    for nm, key in (("wqT", "L2_attn_a_Wq"), ("wkT", "L2_attn_a_Wk"),
                    ("wvT", "L2_attn_a_Wv"), ("woT", "L2_attn_a_Wo")):
        w = np.asarray(inp[key], f32)
        d["l2a_" + nm] = np.ascontiguousarray(w.T.reshape(8, 128, 1024))
    jj = np.arange(128)[:, None]
    ii = np.arange(128)[None, :]
    d["l2a_utm"] = (jj <= ii).astype(f32)
    return d



def sub_mamba(self, xT, x_rdy, W, scr, d_state, rms, pfx):
    p = self.p
    R = 64 + 2 * d_state
    S = d_state
    eps_in = 1e-6 if rms else 1e-5
    s_ld = self.spool(0)
    s_sp = [self.spool(1), self.spool(2)]
    s_w = self.gpool(0)
    s_bl = self.spool(3)

    xn = [p.sbuf(self.name("xn"), (128, T), BF16) for _ in range(8)]
    rdy = self.ln(xT, x_rdy, xn)
    p.wait("tensor", rdy[1])
    xn_rdy = [rdy[0]] * 8

    mzr = scr["mzd"].rearrange("(a p) t -> a p t", p=128)
    mcr = scr["mcd"].rearrange("(a p) t -> a p t", p=128)
    dtr = scr["dtd"].rearrange("(a p) t -> a p t", p=128)
    dur = scr["dtu2"].rearrange("(a p) t -> a p t", p=128)
    myr = scr["myd"].rearrange("(a p) t -> a p t", p=128)

    colsA = p.sbuf(self.name("ca"), (128, 16 * S), F32)
    colsD = p.sbuf(self.name("cd"), (128, 16), F32)
    colsCW = p.sbuf(self.name("cw"), (128, 64), F32)
    colsCB = p.sbuf(self.name("cb"), (128, 16), F32)
    colsDT = p.sbuf(self.name("ct"), (128, 16), F32)
    colsG = p.sbuf(self.name("cg"), (128, 16), F32)
    p.dma("sync", colsA[:], W["alog"].rearrange("p a s -> p (a s)"), s_ld)
    p.dma("sync", colsD[:], W["dcol"], s_ld)
    p.dma("sync", colsCW[:], W["cw"].rearrange("p a k -> p (a k)"), s_ld)
    p.dma("sync", colsCB[:], W["cb"], s_ld)
    p.dma("sync", colsDT[:], W["dtb"], s_ld)
    p.dma("sync", colsG[:, 0:8], W["g"], s_ld)
    if not rms:
        p.dma("sync", colsG[:, 8:16], W["b"], s_ld)
    crdy = (s_ld, s_ld.n)
    a1 = p.emit("scalar", lambda e: e.activation(colsA[:], colsA[:], AF.Exp),
                waits=[crdy], inc=self.s_ac)
    a2 = p.emit("vector", lambda e: e.tensor_scalar_mul(colsA[:], colsA[:], -1.0),
                waits=[(self.s_ac, a1)], inc=self.s_ve)
    ardy = (self.s_ve, a2)

    scA = p.scope()
    xg = [p.sbuf(self.name("xg"), (128, T), BF16) for _ in range(8)]
    rstdB = p.sbuf(self.name("rs"), (128, T), F32)
    nmrB = p.sbuf(self.name("nm"), (128, T), F32)
    sq = p.sbuf(self.name("sqq"), (128, T), F32)
    if not rms:
        for dd in range(8):
            for c in range(NT):
                p.emit("tensor", lambda e, dd=dd, c=c:
                       e.matmul(self.psf[0:1, c * 512:(c + 1) * 512], self.ones[:, 0:1],
                                xn[dd][:, c * 512:(c + 1) * 512],
                                start=(dd == 0), stop=(dd == 7)),
                       waits=[xn_rdy[dd]] + (self.bank_war(0, 1, 2, 3) if (dd == 0 and c == 0) else []),
                       inc=self.s_mm if (dd == 7 and c == 3) else None)
        g0 = p.emit("vector", lambda e: e.tensor_copy(rstdB[0:1, :], self.psf[0:1, 0:T]),
                    waits=[(self.s_mm, self.s_mm.n)], inc=self.s_ve)
        for b in range(4):
            self.bank_read(b, self.s_ve, g0)
    sqg = 0
    for dd in range(8):
        a0 = p.emit("scalar", lambda e, dd=dd: e.activation(sq[:], xn[dd][:], AF.Square),
                    waits=[xn_rdy[dd], (self.s_mm, sqg)], inc=self.s_ac)
        for c in range(NT):
            p.emit("tensor", lambda e, dd=dd, c=c:
                   e.matmul(self.psf[0:1, c * 512:(c + 1) * 512], self.onesf[:, 0:1],
                            sq[:, c * 512:(c + 1) * 512],
                            start=(dd == 0), stop=(dd == 7)),
                   waits=[(self.s_ac, a0)] + (self.bank_war(0, 1, 2, 3) if (dd == 0 and c == 0) else []),
                   inc=self.s_mm if c == 3 else None)
        sqg = self.s_mm.n
    g1 = p.emit("vector", lambda e: e.tensor_copy(nmrB[0:1, :], self.psf[0:1, 0:T]),
                waits=[(self.s_mm, sqg)], inc=self.s_ve)
    for b in range(4):
        self.bank_read(b, self.s_ve, g1)
    inv = 1.0 / 1024.0
    c2 = p.emit("vector", lambda e: e.tensor_scalar_mul(nmrB[0:1, :], nmrB[0:1, :], inv),
                waits=[(self.s_ve, g1)], inc=self.s_ve)
    if not rms:
        c1 = p.emit("vector", lambda e: e.tensor_scalar_mul(rstdB[0:1, :], rstdB[0:1, :], inv),
                    waits=[(self.s_ve, c2)], inc=self.s_ve)
        c3 = p.emit("vector", lambda e: e.tensor_mul(sq[0:1, :], rstdB[0:1, :], rstdB[0:1, :]),
                    waits=[(self.s_ve, c1)], inc=self.s_ve)
        c4 = p.emit("vector", lambda e: e.tensor_sub(nmrB[0:1, :], nmrB[0:1, :], sq[0:1, :]),
                    waits=[(self.s_ve, c3)], inc=self.s_ve)
    c5 = p.emit("vector", lambda e, eps_in=eps_in:
                e.tensor_scalar_add(nmrB[0:1, :], nmrB[0:1, :], eps_in),
                waits=[(self.s_ve, self.s_ve.n)], inc=self.s_ve)
    c6 = p.emit("scalar", lambda e: e.activation(nmrB[0:1, :], nmrB[0:1, :], AF.Ln),
                waits=[(self.s_ve, c5)], inc=self.s_ac)
    c7 = p.emit("scalar", lambda e: e.activation(sq[0:1, :], nmrB[0:1, :], AF.Exp, scale=-0.5),
                waits=[(self.s_ac, c6)], inc=self.s_ac)
    if not rms:
        c8 = p.emit("vector", lambda e: e.tensor_mul(nmrB[0:1, :], sq[0:1, :], rstdB[0:1, :]),
                    waits=[(self.s_ac, c7)], inc=self.s_ve)
        c9 = p.emit("vector", lambda e: e.tensor_scalar_mul(nmrB[0:1, :], nmrB[0:1, :], -1.0),
                    waits=[(self.s_ve, c8)], inc=self.s_ve)
    for c in range(NT):
        mv = p.emit("tensor", lambda e, c=c: e.matmul(self.fb(c, 0, 512), self.onesf[0:1, :],
                                                      sq[0:1, c * 512:(c + 1) * 512],
                                                      start=True, stop=True),
                    waits=[(self.s_ac, c7), (self.s_ve, self.s_ve.n)] + self.bank_war(c),
                    inc=self.s_mm)
        vv = p.emit("vector", lambda e, c=c: e.tensor_copy(rstdB[:, c * 512:(c + 1) * 512],
                                                           self.fb(c, 0, 512)),
                    waits=[(self.s_mm, mv)], inc=self.s_ve)
        self.bank_read(c, self.s_ve, vv)
    if not rms:
        for c in range(NT):
            mv = p.emit("tensor", lambda e, c=c: e.matmul(self.fb(c, 0, 512), self.onesf[0:1, :],
                                                          nmrB[0:1, c * 512:(c + 1) * 512],
                                                          start=True, stop=True),
                        waits=[(self.s_ve, self.s_ve.n)] + self.bank_war(c), inc=self.s_mm)
            vv = p.emit("vector", lambda e, c=c: e.tensor_copy(sq[:, c * 512:(c + 1) * 512],
                                                               self.fb(c, 0, 512)),
                        waits=[(self.s_mm, mv)], inc=self.s_ve)
            self.bank_read(c, self.s_ve, vv)
    xg_rdy = []
    for dd in range(8):
        u0 = p.emit("vector", lambda e, dd=dd: e.tensor_mul(nmrB[:], xn[dd][:], rstdB[:]),
                    waits=[(self.s_ve, self.s_ve.n), crdy], inc=self.s_ve)
        if not rms:
            p.emit("vector", lambda e: e.tensor_add(nmrB[:], nmrB[:], sq[:]), inc=self.s_ve)
            u2 = p.emit("vector", lambda e, dd=dd:
                        e.tensor_scalar(xg[dd][:], nmrB[:], colsG[:, dd:dd + 1],
                                        colsG[:, 8 + dd:8 + dd + 1], ALU.mult, ALU.add),
                        inc=self.s_ve)
        else:
            u2 = p.emit("vector", lambda e, dd=dd:
                        e.tensor_scalar_mul(xg[dd][:], nmrB[:], colsG[:, dd:dd + 1]),
                        inc=self.s_ve)
        xg_rdy.append((self.s_ve, u2))
    wbuf = p.sbuf(self.name("wb"), (128, 8192), BF16)
    qsg = [p.sbuf(self.name("qg"), (128, T), BF16) for _ in range(2)]
    sg_free = [0, 0]
    for ch in range(4):
        w_r = self.wload(wbuf, W["ipw"][ch], 1024, sem=s_w,
                         waits=[(self.s_mm, self.s_mm.n)])

        def evac(fi, c, ps, mm_pair, ch=ch):
            sl = fi % 2
            extra = [(s_sp[sl], sg_free[sl])] if c == 0 else []
            v = p.emit("vector", lambda e, fi=fi, c=c, ps=ps, sl=sl:
                       e.tensor_copy(qsg[sl][:, c * 512:(c + 1) * 512], ps),
                       waits=[mm_pair] + extra, inc=self.s_ve)
            if c == NT - 1:
                p.dma("sync", mzr[ch * 8 + fi], qsg[sl][:], s_sp[sl],
                      waits=[(self.s_ve, v)])
                sg_free[sl] = s_sp[sl].n
            return (self.s_ve, v)
        self.proj(wbuf, w_r, xg, xg_rdy, 1024, evac)
    p.wait("vector", (s_sp[0], s_sp[0].n), (s_sp[1], s_sp[1].n))
    p.barrier()
    scA.close()

    scB = p.scope()
    cvb = [p.sbuf(self.name("cv"), (128, T), BF16) for _ in range(2)]
    acc = p.sbuf(self.name("acc"), (128, T), F32)
    ctmp = p.sbuf(self.name("ctp"), (128, T), F32)
    xst = [p.sbuf(self.name("xs"), (128, T), BF16) for _ in range(2)]
    cv_free = [0, 0]
    st_free = [0, 0]
    for dd in range(16):
        sl = dd % 2
        lv = p.dma("sync", cvb[sl][:], mzr[dd], s_bl,
                   waits=[(self.s_ve, cv_free[sl])])
        v0 = p.emit("vector", lambda e, sl=sl, dd=dd:
                    e.tensor_scalar_mul(acc[:], cvb[sl][:], colsCW[:, dd * 4 + 3:dd * 4 + 4]),
                    waits=[(s_bl, lv), crdy, (self.s_ac, getattr(self, "_mm_cva", 0))],
                    inc=self.s_ve)
        for k in range(3):
            dlt = 3 - k
            p.emit("vector", lambda e, sl=sl, dd=dd, k=k, dlt=dlt:
                   e.tensor_scalar_mul(ctmp[:, 0:T - dlt], cvb[sl][:, 0:T - dlt],
                                       colsCW[:, dd * 4 + k:dd * 4 + k + 1]),
                   inc=self.s_ve)
            p.emit("vector", lambda e, dlt=dlt:
                   e.tensor_add(acc[:, dlt:T], acc[:, dlt:T], ctmp[:, 0:T - dlt]),
                   inc=self.s_ve)
        vz = p.emit("vector", lambda e: e.memset(ctmp[:, 0:4], 0.0), inc=self.s_ve)
        cv_free[sl] = self.s_ve.n
        aa = p.emit("scalar", lambda e, sl=sl, dd=dd:
                    e.activation(xst[sl][:], acc[:], AF.Silu, bias=colsCB[:, dd:dd + 1]),
                    waits=[(self.s_ve, vz), (s_bl, st_free[sl])], inc=self.s_ac)
        self._mm_cva = aa
        p.dma("sync", mcr[dd], xst[sl][:], s_bl, waits=[(self.s_ac, aa)])
        st_free[sl] = s_bl.n
    p.wait("vector", (s_bl, s_bl.n))
    p.barrier()
    scB.close()

    dblb = p.sbuf(self.name("db"), (64, T), BF16)
    bcb = p.sbuf(self.name("bc"), (2 * S, T), F32)
    scC = p.scope()
    wz = p.sbuf(self.name("wz"), (128, 16 * R), BF16)
    xcb = [p.sbuf(self.name("xc"), (128, T), BF16) for _ in range(2)]
    wz_r = self.wload(wz, W["xpw"], R, KC=16, sem=s_w, waits=[(self.s_mm, self.s_mm.n)])
    xc_mm = [0, 0]
    scC_mm = None
    for kc in range(16):
        sl = kc % 2
        lv = p.dma("sync", xcb[sl][:], mcr[kc], s_bl, waits=[(self.s_mm, xc_mm[sl])])
        for c in range(NT):
            p.emit("tensor", lambda e, kc=kc, c=c, sl=sl:
                   e.matmul(self.psf[0:64, c * 512:(c + 1) * 512],
                            wz[:, kc * R:kc * R + 64],
                            xcb[sl][:, c * 512:(c + 1) * 512],
                            start=(kc == 0), stop=(kc == 15)),
                   waits=[wz_r, (s_bl, lv)] +
                         (self.bank_war(0, 1, 2, 3, 4, 5, 6, 7) if (kc == 0 and c == 0) else []),
                   inc=None)
            bcps = (self.fb(4 + c, 0, 512)[0:2 * S, :] if c < 2 else
                    self.psf2[0:2 * S, (c - 2) * 512:(c - 1) * 512])
            scC_mm = p.emit("tensor", lambda e, kc=kc, c=c, sl=sl, bcps=bcps:
                            e.matmul(bcps, wz[:, kc * R + 64:kc * R + R],
                                     xcb[sl][:, c * 512:(c + 1) * 512],
                                     start=(kc == 0), stop=(kc == 15)),
                            inc=self.s_mm if c == NT - 1 else None)
        xc_mm[sl] = scC_mm
    for c in range(NT):
        v1 = p.emit("vector", lambda e, c=c:
                    e.tensor_copy(dblb[:, c * 512:(c + 1) * 512],
                                  self.psf[0:64, c * 512:(c + 1) * 512]),
                    waits=[(self.s_mm, scC_mm)], inc=self.s_ve)
        self.bank_read(c, self.s_ve, v1)
        bcps = (self.fb(4 + c, 0, 512)[0:2 * S, :] if c < 2 else
                self.psf2[0:2 * S, (c - 2) * 512:(c - 1) * 512])
        v2 = p.emit("vector", lambda e, c=c, bcps=bcps:
                    e.tensor_copy(bcb[:, c * 512:(c + 1) * 512], bcps),
                    inc=self.s_ve)
        self.bank_read(4 + c if c < 2 else 6 + (c - 2), self.s_ve, v2)
    p.wait("vector", (s_bl, s_bl.n))
    p.barrier()
    scC.close()

    scD = p.scope()
    dtwb = p.sbuf(self.name("dw"), (64, 2048), BF16)
    xcd = [p.sbuf(self.name("xd"), (128, T), BF16) for _ in range(2)]
    dst = [p.sbuf(self.name("ds"), (128, T), BF16) for _ in range(2)]
    ust = [p.sbuf(self.name("us"), (128, T), BF16) for _ in range(2)]
    dwv = p.wdma(dtwb[0:64, :], W["dtw"], s_w,
                waits=[(self.s_mm, self.s_mm.n)])
    dw_r = (s_w, dwv)
    spt = p.sbuf(self.name("sp"), (128, 1536), F32)
    st_free = [0, 0]
    dtv = (self.s_ve, self.s_ve.n)
    for fi in range(16):
        sl = fi % 2
        lv = p.dma("sync", xcd[sl][:], mcr[fi], s_bl, waits=[(self.s_ve, st_free[sl])])
        for c in range(NT):
            bk = c % 2
            mv = p.emit("tensor", lambda e, fi=fi, c=c, bk=bk:
                        e.matmul(self.fb(bk, 0, 512), dtwb[0:64, fi * 128:(fi + 1) * 128],
                                 dblb[:, c * 512:(c + 1) * 512],
                                 start=True, stop=True),
                        waits=[dw_r, dtv] + self.bank_war(bk), inc=self.s_mm)
            x0 = spt[:, 0:512]
            mm_ = spt[:, 512:1024]
            e2 = spt[:, 1024:1536]
            h1 = p.emit("vector", lambda e, fi=fi, bk=bk:
                        e.tensor_scalar_add(x0, self.fb(bk, 0, 512),
                                            colsDT[:, fi:fi + 1]),
                        waits=[(self.s_mm, mv), crdy,
                               (self.s_ac, getattr(self, "_mm_spa", 0)),
                               (s_sp[sl], getattr(self, "_mm_dsp", 0))],
                        inc=self.s_ve)
            self.bank_read(bk, self.s_ve, h1)
            h2 = p.emit("vector", lambda e: e.tensor_scalar_max(mm_, x0, 0.0),
                        inc=self.s_ve)
            h3 = p.emit("vector", lambda e: e.tensor_sub(x0, x0, mm_), inc=self.s_ve)
            h4 = p.emit("scalar", lambda e: e.activation(x0, x0, AF.Exp),
                        waits=[(self.s_ve, h3)], inc=self.s_ac)
            h5 = p.emit("scalar", lambda e: e.activation(e2, mm_, AF.Exp, scale=-1.0),
                        inc=self.s_ac)
            h6 = p.emit("vector", lambda e: e.tensor_add(x0, x0, e2),
                        waits=[(self.s_ac, h5)], inc=self.s_ve)
            h7 = p.emit("scalar", lambda e: e.activation(x0, x0, AF.Ln),
                        waits=[(self.s_ve, h6)], inc=self.s_ac)
            aa = p.emit("vector", lambda e, c=c, sl=sl:
                        e.tensor_add(dst[sl][:, c * 512:(c + 1) * 512], x0, mm_),
                        waits=[(self.s_ac, h7)], inc=self.s_ve)
            self._mm_spa = h7
            vv = p.emit("vector", lambda e, c=c, sl=sl:
                        e.tensor_mul(ust[sl][:, c * 512:(c + 1) * 512],
                                     dst[sl][:, c * 512:(c + 1) * 512],
                                     xcd[sl][:, c * 512:(c + 1) * 512]),
                        waits=[(self.s_ve, aa), (s_bl, lv), (s_sp[sl], 1)],
                        inc=self.s_ve)
        p.dma("sync", dtr[fi], dst[sl][:], s_sp[sl], waits=[(self.s_ac, self.s_ac.n)])
        p.dma("sync", dur[fi], ust[sl][:], s_sp[sl], waits=[(self.s_ve, self.s_ve.n)])
        self._mm_dsp = s_sp[sl].n
        st_free[sl] = self.s_ve.n
    p.wait("vector", (s_sp[0], s_sp[0].n), (s_sp[1], s_sp[1].n), (s_bl, s_bl.n))
    p.barrier()
    scD.close()

    scF0 = p.scope()
    brB = p.sbuf(self.name("bB"), (1, T), BF16)
    bst = [p.sbuf(self.name("bs"), (128, T), BF16) for _ in range(2)]
    bcr = scr["bcd"].rearrange("(a p) t -> a p t", p=128)
    bcv = (self.s_ve, self.s_ve.n)
    bst_free = [0, 0]
    for s in range(2 * S):
        sl = s % 2
        for c in range(NT):
            mvb = p.emit("tensor", lambda e, s=s, c=c:
                         e.matmul(self.psf[0:1, 5 * 512:5 * 512 + 512],
                                  self.identf[0:2 * S, s:s + 1],
                                  bcb[:, c * 512:(c + 1) * 512],
                                  start=True, stop=True),
                         waits=[bcv] + self.bank_war(5), inc=self.s_mm)
            vb = p.emit("vector", lambda e, c=c:
                        e.tensor_copy(brB[0:1, c * 512:(c + 1) * 512],
                                      self.psf[0:1, 5 * 512:5 * 512 + 512]),
                        waits=[(self.s_mm, mvb)], inc=self.s_ve)
            self.bank_read(5, self.s_ve, vb)
        brv = (self.s_ve, self.s_ve.n)
        for c in range(NT):
            bk = c % 4
            mv = p.emit("tensor", lambda e, c=c, bk=bk:
                        e.matmul(self.fb(bk, 0, 512), self.ones[0:1, :],
                                 brB[0:1, c * 512:(c + 1) * 512], start=True, stop=True),
                        waits=[brv] + self.bank_war(bk), inc=self.s_mm)
            vv = p.emit("vector", lambda e, c=c, bk=bk, sl=sl:
                        e.tensor_copy(bst[sl][:, c * 512:(c + 1) * 512],
                                      self.fb(bk, 0, 512)),
                        waits=[(self.s_mm, mv), (s_bl, bst_free[sl])], inc=self.s_ve)
            self.bank_read(bk, self.s_ve, vv)
        p.dma("sync", bcr[s], bst[sl][:], s_bl, waits=[(self.s_ve, self.s_ve.n)])
        bst_free[sl] = s_bl.n
    p.wait("vector", (s_bl, s_bl.n))
    p.barrier()
    scF0.close()
    scF = p.scope()
    BLK = 2
    dtt = [p.sbuf(self.name("dt"), (128, T), BF16) for _ in range(BLK)]
    dtu = [p.sbuf(self.name("du"), (128, T), BF16) for _ in range(BLK)]
    yac = [p.sbuf(self.name("ya"), (128, T), F32) for _ in range(BLK)]
    Bbt = [p.sbuf(self.name("Bt"), (128, T), BF16) for _ in range(2)]
    Cbt = [p.sbuf(self.name("Ct"), (128, T), BF16) for _ in range(2)]
    d0b = p.sbuf(self.name("d0"), (128, T), F32)
    d1b = p.sbuf(self.name("d1"), (128, T), F32)
    hb = p.sbuf(self.name("hb"), (128, T), F32)
    tmpb = d1b
    zst = p.sbuf(self.name("zs"), (128, T), BF16)
    bc_free = [0, 0]
    for blk in range(16 // BLK):
        lvv = []
        for j in range(BLK):
            dd = blk * BLK + j
            p.dma("sync", dtt[j][:], dtr[dd], s_bl, waits=[(self.s_ac, getattr(self, "_mm_sc", 0))])
            lv = p.dma("sync", dtu[j][:], dur[dd], s_bl,
                       waits=[(self.s_ve, getattr(self, "_mm_scv", 0))])
            lvv.append(lv)
        blrdy = (s_bl, s_bl.n)
        yz = [p.emit("vector", lambda e, j=j: e.memset(yac[j][:], 0.0),
                     waits=[(self.s_ve, getattr(self, "_mm_scv", 0))], inc=self.s_ve)
              for j in range(BLK)]
        for s in range(S):
            sl = s % 2
            lB = p.dma("sync", Bbt[sl][:], bcr[s], s_bl,
                       waits=[(self.s_ve, bc_free[sl])])
            lC = p.dma("sync", Cbt[sl][:], bcr[S + s], s_bl)
            bcrdy = (s_bl, lC)
            for j in range(BLK):
                dd = blk * BLK + j
                a0 = p.emit("scalar", lambda e, j=j, dd=dd, s=s:
                            e.activation(d0b[:], dtt[j][:], AF.Exp,
                                         scale=colsA[:, dd * S + s:dd * S + s + 1]),
                            waits=[blrdy, ardy, (self.s_ve, getattr(self, "_mm_d0", 0))],
                            inc=self.s_ac)
                p.emit("vector", lambda e, j=j, sl=sl:
                       e.tensor_mul(d1b[:], dtu[j][:], Bbt[sl][:]),
                       waits=[bcrdy, blrdy,
                              (self.s_ve, getattr(self, "_mm_d1", 0))], inc=self.s_ve)
                sc = p.emit("vector", lambda e: e.tensor_tensor_scan(
                    hb[:], d0b[:], d1b[:], 0.0, ALU.mult, ALU.add),
                    waits=[(self.s_ac, a0)], inc=self.s_ve)
                self._mm_d0 = sc
                self._mm_d1 = sc
                p.emit("vector", lambda e, sl=sl:
                       e.tensor_mul(tmpb[:], hb[:], Cbt[sl][:]),
                       waits=[(self.s_ve, sc)], inc=self.s_ve)
                ya = p.emit("vector", lambda e, j=j: e.tensor_add(yac[j][:], yac[j][:], tmpb[:]),
                            waits=[(self.s_ve, yz[j])], inc=self.s_ve)
                self._mm_d1 = ya
            bc_free[sl] = self.s_ve.n
        for j in range(BLK):
            dd = blk * BLK + j
            lv1 = p.dma("sync", dtt[j][:], mcr[dd], s_bl, waits=[(self.s_ac, self.s_ac.n),
                                                                 (self.s_ve, self.s_ve.n)])
            lv2 = p.dma("sync", dtu[j][:], mzr[16 + dd], s_bl)
            v1 = p.emit("vector", lambda e, j=j, dd=dd:
                        e.tensor_scalar_mul(tmpb[:], dtt[j][:], colsD[:, dd:dd + 1]),
                        waits=[(s_bl, lv2), crdy, (self.s_ve, self.s_ve.n)], inc=self.s_ve)
            v2 = p.emit("vector", lambda e, j=j: e.tensor_add(yac[j][:], yac[j][:], tmpb[:]),
                        inc=self.s_ve)
            a1_ = p.emit("scalar", lambda e, j=j: e.activation(d0b[:], dtu[j][:], AF.Silu),
                         waits=[(s_bl, lv2), (self.s_ve, v2)], inc=self.s_ac)
            v3 = p.emit("vector", lambda e, j=j: e.tensor_mul(zst[:], yac[j][:], d0b[:]),
                        waits=[(self.s_ac, a1_)], inc=self.s_ve)
            p.dma("sync", myr[dd], zst[:], s_sp[0], waits=[(self.s_ve, v3)])
            p.wait("vector", (s_sp[0], s_sp[0].n))
            self._mm_sc = self.s_ac.n
            self._mm_scv = self.s_ve.n
    p.wait("vector", (s_sp[0], s_sp[0].n), (s_bl, s_bl.n))
    p.barrier()
    scF.close()

    for half in range(2):
        scG = p.scope()
        wbuf = p.sbuf(self.name("wo"), (128, 8192), BF16)
        ysrc = [p.sbuf(self.name("ys"), (128, T), BF16) for _ in range(8)]
        w_r = self.wload(wbuf, W["opw"][half * 8:(half + 1) * 8], 1024, sem=s_w,
                         waits=[(self.s_mm, self.s_mm.n)])
        yrdy = []
        for j in range(8):
            lv = p.dma("sync", ysrc[j][:], myr[half * 8 + j], s_bl,
                       waits=[(self.s_mm, self.s_mm.n)])
            yrdy.append((s_bl, lv))
        self.proj(wbuf, w_r, ysrc, yrdy, 1024, self.ev_residual(xT))
        p.wait("vector", (s_bl, s_bl.n))
        p.barrier()
        scG.close()
    out_rdy = []
    for dd in range(8):
        v = p.emit("vector", lambda e, dd=dd:
                   e.tensor_add(xT[dd][:], xT[dd][:], xn[dd][:]),
                   waits=[(self.s_ve, self.s_ve.n)], inc=self.s_ve)
        out_rdy.append((self.s_ve, v))
    return [(self.s_ve, self.s_ve.n)] * 8


Mk.sub_mamba = sub_mamba


def host_prep_mamba(inp, pfx, keys, d_state):
    f32 = np.float32
    S = d_state
    R = 64 + 2 * S
    d = {}
    ipw = np.asarray(inp[keys["ipw"]], f32)
    d[pfx + "ipw"] = np.ascontiguousarray(
        ipw.T.reshape(8, 128, 4, 1024).transpose(2, 0, 1, 3))
    xpw = np.asarray(inp[keys["xpw"]], f32)
    d[pfx + "xpw"] = np.ascontiguousarray(xpw.T.reshape(16, 128, R))
    d[pfx + "dtw"] = np.ascontiguousarray(np.asarray(inp[keys["dtw"]], f32).T)
    d[pfx + "dtb"] = np.ascontiguousarray(np.asarray(inp[keys["dtb"]], f32).reshape(16, 128).T)
    d[pfx + "cw"] = np.ascontiguousarray(
        np.asarray(inp[keys["cw"]], f32).reshape(16, 128, 4).transpose(1, 0, 2))
    d[pfx + "cb"] = np.ascontiguousarray(np.asarray(inp[keys["cb"]], f32).reshape(16, 128).T)
    d[pfx + "alog"] = np.ascontiguousarray(
        np.asarray(inp[keys["alog"]], f32).reshape(16, 128, S).transpose(1, 0, 2))
    d[pfx + "dcol"] = np.ascontiguousarray(np.asarray(inp[keys["dcol"]], f32).reshape(16, 128).T)
    opw = np.asarray(inp[keys["opw"]], f32)
    d[pfx + "opw"] = np.ascontiguousarray(opw.T.reshape(16, 128, 1024))
    d[pfx + "g"] = np.ascontiguousarray(np.asarray(inp[keys["g"]], f32).reshape(8, 128).T)
    if "b" in keys:
        d[pfx + "b"] = np.ascontiguousarray(np.asarray(inp[keys["b"]], f32).reshape(8, 128).T)
    return d


def host_prep_l0_attn_a(inp):
    return host_prep_mamba(inp, "l0a_", dict(
        ipw="L0_attn_a_in_proj_w", xpw="L0_attn_a_x_proj_w", dtw="L0_attn_a_dt_proj_w",
        dtb="L0_attn_a_dt_proj_b", cw="L0_attn_a_conv_w", cb="L0_attn_a_conv_b",
        alog="L0_attn_a_A_log", dcol="L0_attn_a_D", opw="L0_attn_a_out_proj_w",
        g="L0_attn_a_ln_gamma", b="L0_attn_a_ln_beta"), 16)


def host_prep_l3_attn_b(inp):
    return host_prep_mamba(inp, "l3b_", dict(
        ipw="L3_attn_b_in_proj_w", xpw="L3_attn_b_x_proj_w", dtw="L3_attn_b_dt_proj_w",
        dtb="L3_attn_b_dt_proj_b", cw="L3_attn_b_conv_w", cb="L3_attn_b_conv_b",
        alog="L3_attn_b_A_log", dcol="L3_attn_b_D_param", opw="L3_attn_b_out_proj_w",
        g="L3_attn_b_norm_w"), 32)


def sub_l1_attn_b(self, xT, x_rdy, W, scr):
    p = self.p
    K = 17
    s_cl = self.spool(0)
    s_cg = self.gpool(0)
    s_w = self.gpool(1)
    s_sp = [self.spool(1), self.spool(2)]
    s_pl = self.spool(3)

    wbuf = p.sbuf(self.name("wb"), (128, 8192), BF16)
    cols = p.sbuf(self.name("cl"), (128, 8), F32)
    p.dma("sync", cols[:, 0:1], W["qg"], s_cl)
    p.dma("sync", cols[:, 1:2], W["qb"], s_cl)
    p.dma("sync", cols[:, 2:3], W["kg"], s_cl)
    p.dma("sync", cols[:, 3:4], W["kb"], s_cl)
    p.dma("sync", cols[0:32, 4:5], W["bwin"], s_cl)
    p.dma("sync", cols[0:32, 5:6], W["affs"], s_cl)
    p.dma("sync", cols[0:32, 6:7], W["affb"], s_cl)
    selw = p.sbuf(self.name("sw"), (32, 8 * 68), F32)
    p.dma("sync", selw[:], W["selw"], s_cl)
    e34 = p.sbuf(self.name("e3"), (128, K * 34), BF16)
    p.wdma(e34[:], W["e34sc"], s_cg)
    sel342 = p.sbuf(self.name("s42"), (34, 2), BF16)
    p.wdma(sel342[:], W["sel342"], s_cg)
    sel234 = p.sbuf(self.name("s24"), (2, 34), F32)
    p.dma("sync", sel234[:], W["sel234"], s_cl)
    selav = p.sbuf(self.name("sav"), (34, K * 128), BF16)
    p.wdma(selav[:], W["selav"], s_cg)
    rel34 = p.sbuf(self.name("r34"), (34, 1), F32)
    p.dma("sync", rel34[:], W["rel34"], s_cl)
    e64 = p.sbuf(self.name("e64"), (128, 2), F32)
    p.dma("sync", e64[:], W["e64b"], s_cl)
    e2f = p.sbuf(self.name("e2f"), (2, 128), F32)
    p.dma("sync", e2f[:], W["e2fb"], s_cl)
    cst = (s_cl, s_cl.n)
    cstg = (s_cg, s_cg.n)

    qsg = [p.sbuf(self.name("qg"), (128, T), BF16) for _ in range(2)]
    sq2 = p.sbuf(self.name("sq"), (128, T), F32)
    wsb = p.sbuf(self.name("ws"), (32, T), F32)
    wwin = p.sbuf(self.name("ww"), (128, 8 * 32), BF16)
    scX = p.scope()
    xn = [p.sbuf(self.name("xn"), (128, T), BF16) for _ in range(8)]
    rdy = self.ln(xT, x_rdy, xn)
    p.wait("tensor", rdy[1])
    xn_rdy = [rdy[0]] * 8
    scP = p.scope()
    r2 = p.sbuf(self.name("r2"), (2, 3 * T), F32)
    rb = p.sbuf(self.name("rb"), (128, T), F32)
    rb2 = p.sbuf(self.name("rb2"), (128, T), F32)
    sg_free = [0, 0]

    def headln_spill(gcol, bcol, dst_d):
        def evac(fi, c, ps, mm_pair):
            sl = fi % 2
            extra = [(s_sp[sl], sg_free[sl])] if c == 0 else []
            a = p.emit("scalar", lambda e, fi=fi, c=c, ps=ps, sl=sl:
                       e.activation(sq2[:, c * 512:(c + 1) * 512], ps, AF.Silu),
                       waits=[mm_pair, (self.s_ve, getattr(self, "_1b_sq", 0))] + extra,
                       inc=self.s_ac)
            if c < NT - 1:
                return (self.s_ac, a)
            m1 = None
            for cc in range(NT):
                m1 = p.emit("tensor", lambda e, cc=cc:
                            e.matmul(self.fb(4, 0, 512)[0:2, :], e64[:],
                                     sq2[:, cc * 512:(cc + 1) * 512],
                                     start=True, stop=True),
                            waits=[(self.s_ac, a), cst] + self.bank_war(4), inc=self.s_mm)
                v1 = p.emit("vector", lambda e, cc=cc:
                            e.tensor_copy(r2[0:2, cc * 512:(cc + 1) * 512],
                                          self.fb(4, 0, 512)[0:2, :]),
                            waits=[(self.s_mm, m1)], inc=self.s_ve)
                self.bank_read(4, self.s_ve, v1)
            aq = p.emit("scalar", lambda e: e.activation(rb[:], sq2[:], AF.Square),
                        waits=[(self.s_ac, a)], inc=self.s_ac)
            for cc in range(NT):
                m1 = p.emit("tensor", lambda e, cc=cc:
                            e.matmul(self.fb(4, 0, 512)[0:2, :], e64[:],
                                     rb[:, cc * 512:(cc + 1) * 512],
                                     start=True, stop=True),
                            waits=[(self.s_ac, aq)] + self.bank_war(4), inc=self.s_mm)
                v1 = p.emit("vector", lambda e, cc=cc:
                            e.tensor_copy(r2[0:2, T + cc * 512:T + (cc + 1) * 512],
                                          self.fb(4, 0, 512)[0:2, :]),
                            waits=[(self.s_mm, m1)], inc=self.s_ve)
                self.bank_read(4, self.s_ve, v1)
            i64 = 1.0 / 64.0
            c1 = p.emit("vector", lambda e: e.tensor_scalar_mul(r2[0:2, 0:T], r2[0:2, 0:T], i64),
                        waits=[(self.s_ve, self.s_ve.n)], inc=self.s_ve)
            c2 = p.emit("vector", lambda e: e.tensor_scalar_mul(r2[0:2, T:2 * T],
                                                                r2[0:2, T:2 * T], i64),
                        inc=self.s_ve)
            c3 = p.emit("vector", lambda e: e.tensor_mul(r2[0:2, 2 * T:3 * T],
                                                         r2[0:2, 0:T], r2[0:2, 0:T]),
                        inc=self.s_ve)
            c4 = p.emit("vector", lambda e: e.tensor_sub(r2[0:2, T:2 * T],
                                                         r2[0:2, T:2 * T],
                                                         r2[0:2, 2 * T:3 * T]),
                        inc=self.s_ve)
            c5 = p.emit("vector", lambda e: e.tensor_scalar_add(r2[0:2, T:2 * T],
                                                                r2[0:2, T:2 * T], 1e-5),
                        inc=self.s_ve)
            c6 = p.emit("scalar", lambda e: e.activation(r2[0:2, T:2 * T],
                                                         r2[0:2, T:2 * T], AF.Ln),
                        waits=[(self.s_ve, c5)], inc=self.s_ac)
            c7 = p.emit("scalar", lambda e: e.activation(r2[0:2, T:2 * T],
                                                         r2[0:2, T:2 * T], AF.Exp, scale=-0.5),
                        waits=[(self.s_ac, c6)], inc=self.s_ac)
            c8 = p.emit("vector", lambda e: e.tensor_mul(r2[0:2, 0:T], r2[0:2, 0:T],
                                                         r2[0:2, T:2 * T]),
                        waits=[(self.s_ac, c7)], inc=self.s_ve)
            c9 = p.emit("vector", lambda e: e.tensor_scalar_mul(r2[0:2, 0:T],
                                                                r2[0:2, 0:T], -1.0),
                        inc=self.s_ve)
            for cc in range(NT):
                m1 = p.emit("tensor", lambda e, cc=cc:
                            e.matmul(self.fb(4, 0, 512), e2f[:],
                                     r2[0:2, T + cc * 512:T + (cc + 1) * 512],
                                     start=True, stop=True),
                            waits=[(self.s_ve, c9)] + self.bank_war(4), inc=self.s_mm)
                v1 = p.emit("vector", lambda e, cc=cc:
                            e.tensor_copy(rb[:, cc * 512:(cc + 1) * 512], self.fb(4, 0, 512)),
                            waits=[(self.s_mm, m1)], inc=self.s_ve)
                self.bank_read(4, self.s_ve, v1)
                m2 = p.emit("tensor", lambda e, cc=cc:
                            e.matmul(self.fb(5, 0, 512), e2f[:],
                                     r2[0:2, cc * 512:(cc + 1) * 512],
                                     start=True, stop=True),
                            waits=[(self.s_ve, c9)] + self.bank_war(5), inc=self.s_mm)
                v2 = p.emit("vector", lambda e, cc=cc:
                            e.tensor_copy(rb2[:, cc * 512:(cc + 1) * 512], self.fb(5, 0, 512)),
                            waits=[(self.s_mm, m2)], inc=self.s_ve)
                self.bank_read(5, self.s_ve, v2)
            u0 = p.emit("vector", lambda e: e.tensor_mul(rb[:], sq2[:], rb[:]),
                        waits=[(self.s_ve, self.s_ve.n)], inc=self.s_ve)
            u1 = p.emit("vector", lambda e: e.tensor_add(rb[:], rb[:], rb2[:]),
                        inc=self.s_ve)
            u2 = p.emit("vector", lambda e, gcol=gcol, bcol=bcol, sl=sl:
                        e.tensor_scalar(qsg[sl][:], rb[:], gcol, bcol, ALU.mult, ALU.add),
                        waits=[cst], inc=self.s_ve)
            self._1b_sq = u2
            p.dma("sync", dst_d[fi * 128:(fi + 1) * 128, :], qsg[sl][:], s_sp[sl],
                  waits=[(self.s_ve, u2)])
            sg_free[sl] = s_sp[sl].n
            return (self.s_ac, a)
        return evac

    def silu_spill(dst_d, roff=0):
        def evac(fi, c, ps, mm_pair):
            sl = fi % 2
            extra = [(s_sp[sl], sg_free[sl])] if c == 0 else []
            a = p.emit("scalar", lambda e, c=c, ps=ps, sl=sl:
                       e.activation(qsg[sl][:, c * 512:(c + 1) * 512], ps, AF.Silu),
                       waits=[mm_pair] + extra, inc=self.s_ac)
            if c == NT - 1:
                p.dma("sync", dst_d[(roff + fi) * 128:(roff + fi + 1) * 128, :],
                      qsg[sl][:], s_sp[sl], waits=[(self.s_ac, a)])
                sg_free[sl] = s_sp[sl].n
            return (self.s_ac, a)
        return evac

    wq_r = self.wload(wbuf, W["wq"], 1024, sem=s_w)
    self.proj(wbuf, wq_r, xn, xn_rdy, 1024,
              headln_spill(cols[:, 0:1], cols[:, 1:2], scr["qd"]), banks=(0, 1, 2, 3))
    wk_r = self.wload(wbuf, W["wkv"][0:8], 1024, sem=s_w, waits=[(self.s_mm, self.s_mm.n)])
    self.proj(wbuf, wk_r, xn, xn_rdy, 1024,
              headln_spill(cols[:, 2:3], cols[:, 3:4], scr["kd"]), banks=(0, 1, 2, 3))
    p.wait("vector", (s_sp[0], s_sp[0].n), (s_sp[1], s_sp[1].n))
    p.barrier()
    scP.close()
    wv_r = self.wload(wbuf, W["wkv"][8:16], 1024, sem=s_w, waits=[(self.s_mm, self.s_mm.n)])
    self.proj(wbuf, wv_r, xn, xn_rdy, 1024, silu_spill(scr["vd"]), banks=(0, 1, 2, 3))
    ww_r = self.wload(wwin, W["wwin"], 32, sem=s_w, waits=[(self.s_mm, self.s_mm.n)])
    for c in range(NT):
        mv = None
        for kc in range(8):
            mv = p.emit("tensor", lambda e, kc=kc, c=c:
                        e.matmul(self.fb(4, 0, 512)[0:32, :], wwin[:, kc * 32:(kc + 1) * 32],
                                 xn[kc][:, c * 512:(c + 1) * 512],
                                 start=(kc == 0), stop=(kc == 7)),
                        waits=[ww_r, xn_rdy[kc]] + (self.bank_war(4) if kc == 0 else []),
                        inc=self.s_mm if kc == 7 else None)
        a1 = p.emit("scalar", lambda e, c=c:
                    e.activation(wsb[:, c * 512:(c + 1) * 512], self.fb(4, 0, 512)[0:32, :],
                                 AF.Silu, bias=cols[0:32, 4:5]),
                    waits=[(self.s_mm, mv), cst], inc=self.s_ac)
        self.bank_read(4, self.s_ac, a1)
    aw = p.emit("scalar", lambda e: e.activation(wsb[:], wsb[:], AF.Sigmoid),
                waits=[(self.s_ac, self.s_ac.n)], inc=self.s_ac)
    vw = p.emit("vector", lambda e: e.tensor_scalar(wsb[:], wsb[:], cols[0:32, 5:6],
                                                    cols[0:32, 6:7], ALU.mult, ALU.add),
                waits=[(self.s_ac, aw), cst], inc=self.s_ve)
    qkv_done = [(s_sp[0], s_sp[0].n), (s_sp[1], s_sp[1].n), (self.s_ve, vw)]
    p.wait("vector", (s_sp[0], s_sp[0].n), (s_sp[1], s_sp[1].n))
    p.barrier()
    scX.close()

    scA = p.scope()
    qTp = p.sbuf(self.name("qp"), (128, T), BF16)
    kTp = p.sbuf(self.name("kp"), (128, T), BF16)
    vTp = p.sbuf(self.name("vp"), (128, T), BF16)
    pb = [p.sbuf(self.name("pb"), (128, T), BF16) for _ in range(2)]
    pensb = p.sbuf(self.name("pn"), (34, T), F32)
    t34 = p.sbuf(self.name("t34"), (34, T), F32)
    esc = p.sbuf(self.name("es"), (34, T), BF16)
    atts = p.sbuf(self.name("at"), (34, T), BF16)
    dnr = p.sbuf(self.name("dn"), (2, T), F32)
    oacc = p.sbuf(self.name("oa"), (128, T), F32)
    tmps = p.sbuf(self.name("tm"), (128, 512), F32)
    SCALE = 64.0 ** (-0.5)
    pair_free = 0
    for ti in range(8):
        ldw = qkv_done + [(self.s_mm, pair_free), (self.s_ac, self.s_ac.n),
                          (self.s_ve, self.s_ve.n)]
        p.dma("sync", qTp[:], scr["qd"][ti * 128:(ti + 1) * 128, :], s_pl, waits=ldw)
        p.dma("sync", kTp[:], scr["kd"][ti * 128:(ti + 1) * 128, :], s_pl, waits=ldw)
        p.dma("sync", vTp[:], scr["vd"][ti * 128:(ti + 1) * 128, :], s_pl, waits=ldw)
        prdy = (s_pl, s_pl.n)
        for c in range(NT):
            m1 = p.emit("tensor", lambda e, ti=ti, c=c:
                        e.matmul(self.psf2[0:34, 0:512], selw[:, ti * 68:ti * 68 + 34],
                                 wsb[0:32, c * 512:(c + 1) * 512], start=True, stop=True),
                        waits=[(self.s_ve, vw)] + self.bank_war(6), inc=self.s_mm)
            v1 = p.emit("vector", lambda e, c=c:
                        e.tensor_scalar_sub(t34[:, c * 512:(c + 1) * 512],
                                            self.psf2[0:34, 0:512], rel34[:, 0:1]),
                        waits=[(self.s_mm, m1), cst,
                               (self.s_ac, getattr(self, "_1b_pen", 0))], inc=self.s_ve)
            self.bank_read(6, self.s_ve, v1)
            m2 = p.emit("tensor", lambda e, ti=ti, c=c:
                        e.matmul(self.psf2[0:34, 512:1024], selw[:, ti * 68 + 34:ti * 68 + 68],
                                 wsb[0:32, c * 512:(c + 1) * 512], start=True, stop=True),
                        waits=[(self.s_ve, vw)] + self.bank_war(7), inc=self.s_mm)
            v2 = p.emit("vector", lambda e, c=c:
                        e.tensor_mul(pensb[:, c * 512:(c + 1) * 512],
                                     t34[:, c * 512:(c + 1) * 512],
                                     self.psf2[0:34, 512:1024]),
                        waits=[(self.s_mm, m2), (self.s_ve, v1)], inc=self.s_ve)
            self.bank_read(7, self.s_ve, v2)
        ap = p.emit("scalar", lambda e: e.activation(pensb[:], pensb[:], AF.Sigmoid),
                    waits=[(self.s_ve, self.s_ve.n)], inc=self.s_ac)
        vp2 = p.emit("vector", lambda e: e.tensor_scalar(pensb[:], pensb[:], 1e4, -1e4,
                                                         ALU.mult, ALU.add),
                     waits=[(self.s_ac, ap)], inc=self.s_ve)
        self._1b_pen = ap
        pen_rdy = (self.s_ve, vp2)
        pb_free = [0, 0]
        for w in range(K):
            dlt = w - 8
            sl = w % 2
            a0 = max(0, -dlt)
            b0 = T - max(0, dlt)
            ve = p.emit("vector", lambda e, sl=sl, a0=a0, b0=b0, dlt=dlt:
                        e.tensor_mul(pb[sl][:, a0:b0], qTp[:, a0:b0],
                                     kTp[:, a0 + dlt:b0 + dlt]),
                        waits=[prdy, (self.s_mm, pb_free[sl])], inc=self.s_ve)
            if a0 > 0:
                ve = p.emit("vector", lambda e, sl=sl, a0=a0:
                            e.memset(pb[sl][:, 0:a0], 0.0), inc=self.s_ve)
            if b0 < T:
                ve = p.emit("vector", lambda e, sl=sl, b0=b0:
                            e.memset(pb[sl][:, b0:T], 0.0), inc=self.s_ve)
            for c in range(NT):
                mv = p.emit("tensor", lambda e, w=w, c=c, sl=sl:
                            e.matmul(self.psf[0:34, c * 512:(c + 1) * 512],
                                     e34[:, w * 34:(w + 1) * 34],
                                     pb[sl][:, c * 512:(c + 1) * 512],
                                     start=(w == 0), stop=(w == K - 1)),
                            waits=[(self.s_ve, ve), cstg] +
                                  (self.bank_war(0, 1, 2, 3) if (w == 0 and c == 0) else []),
                            inc=self.s_mm if c == NT - 1 else None)
            pb_free[sl] = self.s_mm.n
        smm = self.s_mm.n
        for c in range(NT):
            v1 = p.emit("vector", lambda e, c=c:
                        e.tensor_add(t34[:, c * 512:(c + 1) * 512],
                                     pensb[:, c * 512:(c + 1) * 512],
                                     self.psf[0:34, c * 512:(c + 1) * 512]),
                        waits=[(self.s_mm, smm), pen_rdy], inc=self.s_ve)
            self.bank_read(c, self.s_ve, v1)
            a1 = p.emit("scalar", lambda e, c=c:
                        e.activation(esc[:, c * 512:(c + 1) * 512],
                                     t34[:, c * 512:(c + 1) * 512], AF.Exp),
                        waits=[(self.s_ve, v1)], inc=self.s_ac)
        erdy = (self.s_ac, self.s_ac.n)
        for c in range(NT):
            m1 = p.emit("tensor", lambda e, c=c:
                        e.matmul(self.fb(4, 0, 512)[0:2, :], sel342[:],
                                 esc[:, c * 512:(c + 1) * 512], start=True, stop=True),
                        waits=[erdy, cstg] + self.bank_war(4), inc=self.s_mm)
            v1 = p.emit("vector", lambda e, c=c:
                        e.tensor_scalar_add(dnr[0:2, c * 512:(c + 1) * 512],
                                            self.fb(4, 0, 512)[0:2, :], 1e-30),
                        waits=[(self.s_mm, m1)], inc=self.s_ve)
            self.bank_read(4, self.s_ve, v1)
        vr = p.emit("vector", lambda e: e.reciprocal(dnr[0:2, :], dnr[0:2, :]),
                    waits=[(self.s_ve, self.s_ve.n)], inc=self.s_ve)
        for c in range(NT):
            m1 = p.emit("tensor", lambda e, c=c:
                        e.matmul(self.fb(4, 0, 512)[0:34, :], sel234[:],
                                 dnr[0:2, c * 512:(c + 1) * 512], start=True, stop=True),
                        waits=[(self.s_ve, vr)] + self.bank_war(4), inc=self.s_mm)
            v1 = p.emit("vector", lambda e, c=c:
                        e.tensor_mul(atts[:, c * 512:(c + 1) * 512],
                                     esc[:, c * 512:(c + 1) * 512],
                                     self.fb(4, 0, 512)[0:34, :]),
                        waits=[(self.s_mm, m1)], inc=self.s_ve)
            self.bank_read(4, self.s_ve, v1)
        at_rdy = (self.s_ve, self.s_ve.n)
        z0 = p.emit("vector", lambda e: e.memset(oacc[:], 0.0),
                    waits=[(self.s_ac, getattr(self, "_1b_oa", 0))], inc=self.s_ve)
        for w in range(K):
            dlt = w - 8
            for c in range(NT):
                lo = max(c * 512, -dlt)
                hi = min((c + 1) * 512, T - dlt)
                if lo >= hi:
                    continue
                bk = 4 + (w % 2)
                m1 = p.emit("tensor", lambda e, w=w, c=c, bk=bk:
                            e.matmul(self.fb(bk, 0, 512), selav[:, w * 128:(w + 1) * 128],
                                     atts[:, c * 512:(c + 1) * 512], start=True, stop=True),
                            waits=[at_rdy, cstg] + self.bank_war(bk), inc=self.s_mm)
                v1 = p.emit("vector", lambda e, lo=lo, hi=hi, c=c, dlt=dlt, bk=bk:
                            e.tensor_mul(tmps[:, lo - c * 512:hi - c * 512],
                                         vTp[:, lo + dlt:hi + dlt],
                                         self.fb(bk, 0, 512)[:, lo - c * 512:hi - c * 512]),
                            waits=[(self.s_mm, m1), (self.s_ve, self.s_ve.n)], inc=self.s_ve)
                self.bank_read(bk, self.s_ve, v1)
                v2 = p.emit("vector", lambda e, lo=lo, hi=hi:
                            e.tensor_add(oacc[:, lo:hi], oacc[:, lo:hi],
                                         tmps[:, lo - (lo // 512) * 512:
                                              lo - (lo // 512) * 512 + (hi - lo)]),
                            inc=self.s_ve)
        av = p.emit("scalar", lambda e, ti=ti: e.copy(qsg[0][:], oacc[:]),
                    waits=[(self.s_ve, self.s_ve.n), (s_sp[0], sg_free[0])], inc=self.s_ac)
        self._1b_oa = av
        p.dma("sync", scr["od"][ti * 128:(ti + 1) * 128, :], qsg[0][:], s_sp[0],
              waits=[(self.s_ac, av)])
        sg_free[0] = s_sp[0].n
        pair_free = self.s_mm.n

    p.wait("vector", (s_sp[0], s_sp[0].n), (s_pl, s_pl.n))
    p.barrier()
    scA.close()
    wo_r = self.wload(wbuf, W["wout"], 1024, sem=s_w, waits=[(self.s_mm, self.s_mm.n)])
    osrc = [p.sbuf(self.name("os"), (128, T), BF16) for _ in range(8)]
    ordy = []
    for j in range(8):
        lv = p.dma("sync", osrc[j][:], scr["od"][j * 128:(j + 1) * 128, :], s_pl,
                   waits=[(s_sp[0], s_sp[0].n)])
        ordy.append((s_pl, lv))

    def ev_silu_res(fi, c, ps, mm_pair):
        a = p.emit("scalar", lambda e, fi=fi, c=c, ps=ps:
                   e.activation(sq2[:, c * 512:(c + 1) * 512], ps, AF.Silu),
                   waits=[mm_pair, (self.s_ve, getattr(self, "_1b_fr", 0))], inc=self.s_ac)
        v = p.emit("vector", lambda e, fi=fi, c=c:
                   e.tensor_add(xT[fi][:, c * 512:(c + 1) * 512],
                                xT[fi][:, c * 512:(c + 1) * 512],
                                sq2[:, c * 512:(c + 1) * 512]),
                   waits=[(self.s_ac, a)], inc=self.s_ve)
        self._1b_fr = v
        return (self.s_ac, a)
    self.proj(wbuf, wo_r, osrc, ordy, 1024, ev_silu_res, banks=(0, 1, 2, 3))
    p.wait("vector", (self.s_ve, self.s_ve.n))
    return [(self.s_ve, self.s_ve.n)] * 8


Mk.sub_l1_attn_b = sub_l1_attn_b


def host_prep_l1_attn_b(inp):
    f32 = np.float32
    K = 17
    d = {}
    d["l1b_wq"] = np.ascontiguousarray(np.asarray(inp["L1_attn_b_Wq"], f32).reshape(8, 128, 1024))
    d["l1b_wkv"] = np.ascontiguousarray(np.asarray(inp["L1_attn_b_Wkv"], f32)
                                        .reshape(8, 128, 2, 1024).transpose(2, 0, 1, 3)
                                        .reshape(16, 128, 1024))
    d["l1b_wwin"] = np.ascontiguousarray(np.asarray(inp["L1_attn_b_Wwin"], f32).reshape(8, 128, 32))
    d["l1b_wout"] = np.ascontiguousarray(np.asarray(inp["L1_attn_b_Wout"], f32).reshape(8, 128, 1024))
    d["l1b_bwin"] = np.ascontiguousarray(np.asarray(inp["L1_attn_b_bwin"], f32).reshape(32, 1))
    for nm, key in (("qg", "L1_attn_b_q_g"), ("qb", "L1_attn_b_q_b"),
                    ("kg", "L1_attn_b_k_g"), ("kb", "L1_attn_b_k_b")):
        d["l1b_" + nm] = np.ascontiguousarray(np.tile(np.asarray(inp[key], f32), 2).reshape(128, 1))
    SCALE = 64.0 ** (-0.5)
    sw = np.zeros((32, 8, 68), f32)
    for ti in range(8):
        for w in range(K):
            for hh in range(2):
                sw[2 * ti + hh, ti, 2 * w + hh] = 1.0
                sw[16 + 2 * ti + hh, ti, 34 + 2 * w + hh] = 1.0
    d["l1b_selw"] = np.ascontiguousarray(sw.reshape(32, 8 * 68))
    d["l1b_affs"] = np.concatenate([np.full(16, 8.0, f32), np.full(16, 9.5, f32)]).reshape(32, 1)
    d["l1b_affb"] = np.full((32, 1), 0.5, f32)
    e34 = np.zeros((128, K, 34), f32)
    for w in range(K):
        for hh in range(2):
            e34[hh * 64:(hh + 1) * 64, w, 2 * w + hh] = SCALE
    d["l1b_e34sc"] = np.ascontiguousarray(e34.reshape(128, K * 34))
    s342 = np.zeros((34, 2), f32)
    s234 = np.zeros((2, 34), f32)
    for w in range(K):
        for hh in range(2):
            s342[2 * w + hh, hh] = 1.0
            s234[hh, 2 * w + hh] = 1.0
    d["l1b_sel342"] = s342
    d["l1b_sel234"] = s234
    sav = np.zeros((34, K, 128), f32)
    for w in range(K):
        for hh in range(2):
            sav[2 * w + hh, w, hh * 64:(hh + 1) * 64] = 1.0
    d["l1b_selav"] = np.ascontiguousarray(sav.reshape(34, K * 128))
    r34 = np.zeros((34, 1), f32)
    for w in range(K):
        r34[2 * w, 0] = r34[2 * w + 1, 0] = abs(w - 8)
    d["l1b_rel34"] = r34
    e64 = np.zeros((128, 2), f32)
    e64[0:64, 0] = 1.0
    e64[64:128, 1] = 1.0
    d["l1b_e64b"] = e64
    e2 = np.zeros((2, 128), f32)
    e2[0, 0:64] = 1.0
    e2[1, 64:128] = 1.0
    d["l1b_e2fb"] = e2
    return d

def sub_l1_attn_a(self, xT, x_rdy, W, scr):
    p = self.p
    SC = (3.0 * 1024 * 64 * 16) ** (-0.25)
    SC2 = (1024.0 * 1024.0) ** (-0.25)
    s_cl = self.spool(0)
    s_cg = self.gpool(0)
    s_w = self.gpool(1)
    s_so = [self.spool(1), self.spool(2)]
    s_hl = [self.spool(3), self.spool(4)]
    s_hv = [self.gpool(2), self.gpool(3)]
    s_pp = [self.gpool(4), self.gpool(5)]

    xn = [p.sbuf(self.name("xn"), (128, T), BF16) for _ in range(8)]
    rdy = self.ln(xT, x_rdy, xn)
    xn_rdy = [rdy[0]] * 8
    p.wait("tensor", rdy[1])

    wbuf = p.sbuf(self.name("wb"), (128, 8192), BF16)
    stag = [p.sbuf(self.name("st"), (128, T), BF16) for _ in range(2)]
    biasq = p.sbuf(self.name("bq"), (128, 16), F32)
    self.colload(biasq[:, 0:8], W["bq"], s_cl)
    self.colload(biasq[:, 8:16], W["obias"], s_cl)
    utm = p.sbuf(self.name("ut"), (128, 128), BF16)
    p.wdma(utm[:], W["utm"], s_cg)
    ckT = [p.sbuf(self.name("ck"), (128, T), BF16) for _ in range(2)]
    skT = [p.sbuf(self.name("sk"), (128, T), BF16) for _ in range(2)]
    snT = [p.sbuf(self.name("sn"), (128, T), BF16) for _ in range(2)]
    sn2T = [p.sbuf(self.name("s2"), (128, T), BF16) for _ in range(2)]
    for ft in range(2):
        p.wdma(ckT[ft][:], W["coskT"][ft], s_cg)
        p.wdma(skT[ft][:], W["sinkT"][ft], s_cg)
        p.wdma(snT[ft][:], W["sinsT"][ft], s_cg)
        p.wdma(sn2T[ft][:], W["sins2T"][ft], s_cg)
    posT = [p.sbuf(self.name("po"), (128, 256), BF16) for _ in range(2)]
    cst = (s_cl, s_cl.n)
    cstg = (s_cg, s_cg.n)
    stg_free = [0, 0]

    def scratch_proj(w_rdy, bias_col, dst):
        def evac(fi, c, ps, mm_pair):
            sl = fi % 2
            sb = stag[sl]
            if bias_col is not None:
                v = p.emit("vector", lambda e, c=c, ps=ps, sb=sb, fi=fi:
                           e.tensor_scalar(sb[:, c * 512:(c + 1) * 512], ps, SC,
                                           biasq[:, fi:fi + 1], ALU.mult, ALU.add),
                           waits=[mm_pair, cst, (s_so[sl], stg_free[sl])],
                           inc=self.s_ve)
            else:
                v = p.emit("vector", lambda e, c=c, ps=ps, sb=sb:
                           e.tensor_scalar_mul(sb[:, c * 512:(c + 1) * 512], ps, SC),
                           waits=[mm_pair, (s_so[sl], stg_free[sl])], inc=self.s_ve)
            if c == NT - 1:
                p.dma("sync", dst[fi * 128:(fi + 1) * 128, :], sb[:],
                      s_so[sl], waits=[(self.s_ve, v)])
                stg_free[sl] = s_so[sl].n
            return (self.s_ve, v)
        return evac

    wq_r = self.wload(wbuf, W["wq"], D, sem=s_w)
    self.proj(wbuf, wq_r, xn, xn_rdy, D, scratch_proj(wq_r, biasq, scr["qd"]))
    wk_r = self.wload(wbuf, W["wk"], D, sem=s_w, waits=[(self.s_mm, self.s_mm.n)])
    self.proj(wbuf, wk_r, xn, xn_rdy, D, scratch_proj(wk_r, None, scr["kd"]))
    wv_r = self.wload(wbuf, W["wv"], D, sem=s_w, waits=[(self.s_mm, self.s_mm.n)])
    self.proj(wbuf, wv_r, xn, xn_rdy, D, scratch_proj(wv_r, None, scr["vd"]))
    qkv_done = [(s_so[0], s_so[0].n), (s_so[1], s_so[1].n)]

    qb = [p.sbuf(self.name("qh"), (128, T), BF16) for _ in range(2)]
    kb = [p.sbuf(self.name("kh"), (128, T), BF16) for _ in range(2)]
    vb = [p.sbuf(self.name("vh"), (128, T), BF16) for _ in range(2)]
    vtm = p.sbuf(self.name("vt"), (128, T), BF16)
    expT = p.sbuf(self.name("ex"), (128, T), BF16)
    Ub = stag
    Vb = [wbuf[:, 0:2048], wbuf[:, 2048:4096]]
    o_h = p.sbuf(self.name("oh"), (128, 1024), F32)
    oT = xn
    mcols = p.sbuf(self.name("mc"), (128, 16), F32)
    gcols = p.sbuf(self.name("gc"), (128, 16), F32)
    srow = self.n_tm[0]
    csc = self.n_tm[1][0:1, 0:256]
    oaccB = self.n_tm[1]
    ncB = p.sbuf(self.name("nc"), (128, 1), F32)

    pair_rdy = []
    pos_rdy = []
    pair_free = [0, 0]

    def load_pair(pr):
        b = pr % 2
        w = qkv_done + [(self.s_mm, pair_free[b])]
        p.dma("sync", qb[b][:], scr["qd"][pr * 128:(pr + 1) * 128, :], s_hl[b], waits=w)
        v0 = p.dma("sync", kb[b][:], scr["kd"][pr * 128:(pr + 1) * 128, :], s_hl[b], waits=w)
        v = p.dma("gpsimd", vb[b][:], scr["vd"][pr * 128:(pr + 1) * 128, :], s_hv[b], waits=w)
        v2 = p.wdma(posT[b][:], W["posT"][pr], s_pp[b],
                   waits=[(self.s_mm, pair_free[b])])
        pair_rdy.append([(s_hl[b], v0), (s_hv[b], v)])
        pos_rdy.append((s_pp[b], v2))

    load_pair(0)
    oT_w = [0] * 8
    uv_free = 0
    for pr in range(8):
        b = pr % 2
        if pr < 7:
            load_pair(pr + 1)
        prdy1, prdy2 = pair_rdy[pr]
        posr = pos_rdy[pr]
        for tt in range(16):
            bk_t = 4 + (tt % 2)
            mmv = p.emit("tensor", lambda e, tt=tt, b=b, bk_t=bk_t:
                         e.matmul(self.fb(bk_t, 0, 128),
                                  vb[b][:, tt * 128:(tt + 1) * 128], self.identb[:],
                                  start=True, stop=True),
                         waits=[prdy2] + self.bank_war(bk_t), inc=self.s_mm)
            v = p.emit("vector", lambda e, tt=tt, bk_t=bk_t:
                       e.tensor_copy(vtm[:, tt * 128:(tt + 1) * 128], self.fb(bk_t, 0, 128)),
                       waits=[(self.s_mm, mmv)], inc=self.s_ve)
            self.bank_read(bk_t, self.s_ve, v)
        vtm_rdy = (self.s_ve, self.s_ve.n)

        for hh in range(2):
            h = pr * 2 + hh
            r0 = hh * 64
            for ft in range(2):
                for c in range(4):
                    bk = c % 2
                    mg = p.emit("tensor", lambda e, ft=ft, c=c, b=b, r0=r0, bk=bk:
                                e.matmul(self.fb(bk, 0, 512),
                                         posT[b][r0:r0 + 64, ft * 128:(ft + 1) * 128],
                                         qb[b][r0:r0 + 64, c * 512:(c + 1) * 512],
                                         start=True, stop=True),
                                waits=[posr, prdy1] + self.bank_war(bk), inc=self.s_mm)
                    u1 = p.emit("vector", lambda e, ft=ft, c=c, bk=bk:
                                e.tensor_mul(Ub[ft][:, c * 512:(c + 1) * 512],
                                             self.fb(bk, 0, 512),
                                             snT[ft][:, c * 512:(c + 1) * 512]),
                                waits=[(self.s_mm, mg), cstg,
                                       (self.s_mm, uv_free)], inc=self.s_ve)
                    u2 = p.emit("vector", lambda e, ft=ft, c=c, bk=bk:
                                e.tensor_mul(Vb[ft][:, c * 512:(c + 1) * 512],
                                             self.fb(bk, 0, 512),
                                             sn2T[ft][:, c * 512:(c + 1) * 512]),
                                inc=self.s_ve)
                    self.bank_read(bk, self.s_ve, u2)
            uv_rdy = (self.s_ve, self.s_ve.n)
            for tqt in range(16):
                for c in range(4):
                    p.emit("tensor", lambda e, tqt=tqt, c=c, b=b, r0=r0:
                           e.matmul(self.fb(c, 0, 512),
                                    qb[b][r0:r0 + 64, tqt * 128:(tqt + 1) * 128],
                                    kb[b][r0:r0 + 64, c * 512:(c + 1) * 512],
                                    start=True, stop=False),
                           waits=[prdy1] + self.bank_war(c))
                    for ft in range(2):
                        p.emit("tensor", lambda e, tqt=tqt, c=c, ft=ft:
                               e.matmul(self.fb(c, 0, 512),
                                        Ub[ft][:, tqt * 128:(tqt + 1) * 128],
                                        ckT[ft][:, c * 512:(c + 1) * 512],
                                        start=False, stop=False),
                               waits=[uv_rdy, cstg])
                        p.emit("tensor", lambda e, tqt=tqt, c=c, ft=ft:
                               e.matmul(self.fb(c, 0, 512),
                                        Vb[ft][:, tqt * 128:(tqt + 1) * 128],
                                        skT[ft][:, c * 512:(c + 1) * 512],
                                        start=False, stop=(ft == 1)),
                               inc=self.s_mm if (ft == 1 and c == 3) else None)
                mmv = self.s_mm.n
                v = p.emit("vector", lambda e, tqt=tqt:
                           e.reduce_max(mcols[:, tqt:tqt + 1], self.psf[:, 0:T], axis=AX.X),
                           waits=[(self.s_mm, mmv)], inc=self.s_ve)
                for bk_i in range(4):
                    self.bank_read(bk_i, self.s_ve, v)
            e1 = p.emit("vector", lambda e: e.reduce_max(gcols[:, 0:1], mcols[:], axis=AX.X),
                        waits=[(self.s_ve, self.s_ve.n)], inc=self.s_ve)
            e2 = p.emit("tensor", lambda e: e.transpose(self.psf[0:1, 2048:2176], gcols[:, 0:1],
                                                        self.identf[:]),
                        waits=[(self.s_ve, e1)] + self.bank_war(4), inc=self.s_mm)
            e3 = p.emit("vector", lambda e: e.tensor_copy(csc[0:1, 4:132], self.psf[0:1, 2048:2176]),
                        waits=[(self.s_mm, e2)], inc=self.s_ve)
            self.bank_read(4, self.s_ve, e3)
            e4 = p.emit("vector", lambda e: e.reduce_max(csc[0:1, 0:1], csc[0:1, 4:132], axis=AX.X),
                        waits=[(self.s_ve, e3)], inc=self.s_ve)
            e5 = p.emit("vector", lambda e: e.tensor_scalar_mul(csc[0:1, 1:2], csc[0:1, 0:1], -0.125),
                        waits=[(self.s_ve, e4)], inc=self.s_ve)
            e6 = p.emit("tensor", lambda e: e.matmul(self.fb(4, 0, 1), self.onesf[0:1, :],
                                                     csc[0:1, 1:2], start=True, stop=True),
                        waits=[(self.s_ve, e5)] + self.bank_war(4), inc=self.s_mm)
            e7 = p.emit("vector", lambda e: e.tensor_copy(ncB[:], self.fb(4, 0, 1)),
                        waits=[(self.s_mm, e6)], inc=self.s_ve)
            self.bank_read(4, self.s_ve, e7)
            ncB_rdy = (self.s_ve, e7)

            for tkt in range(16):
                c_lo = tkt // 4
                for c in range(c_lo, 4):
                    bk_b = 4 + (c % 2)
                    p.emit("tensor", lambda e, tkt=tkt, c=c, b=b, r0=r0, bk_b=bk_b:
                           e.matmul(self.fb(bk_b, 0, 512),
                                    kb[b][r0:r0 + 64, tkt * 128:(tkt + 1) * 128],
                                    qb[b][r0:r0 + 64, c * 512:(c + 1) * 512],
                                    start=True, stop=False),
                           waits=[prdy1] + self.bank_war(bk_b))
                    for ft in range(2):
                        p.emit("tensor", lambda e, tkt=tkt, c=c, ft=ft, bk_b=bk_b:
                               e.matmul(self.fb(bk_b, 0, 512),
                                        ckT[ft][:, tkt * 128:(tkt + 1) * 128],
                                        Ub[ft][:, c * 512:(c + 1) * 512],
                                        start=False, stop=False),
                               waits=[uv_rdy])
                        mmv = p.emit("tensor", lambda e, tkt=tkt, c=c, ft=ft, bk_b=bk_b:
                                     e.matmul(self.fb(bk_b, 0, 512),
                                              skT[ft][:, tkt * 128:(tkt + 1) * 128],
                                              Vb[ft][:, c * 512:(c + 1) * 512],
                                              start=False, stop=(ft == 1)),
                                     inc=self.s_mm if ft == 1 else None)
                    a = p.emit("scalar", lambda e, c=c, bk_b=bk_b:
                               e.activation(expT[:, c * 512:(c + 1) * 512],
                                            self.fb(bk_b, 0, 512), AF.Exp,
                                            bias=ncB[:], scale=0.125),
                               waits=[(self.s_mm, mmv), ncB_rdy,
                                      (self.s_mm, getattr(self, "_1a_ex", 0))],
                               inc=self.s_ac)
                    self.bank_read(bk_b, self.s_ac, a)
                    if c == c_lo:
                        dq = tkt * 128 - c * 512
                        a = p.emit("vector", lambda e, c=c, dq=dq:
                                   e.tensor_mul(expT[:, c * 512 + dq:c * 512 + dq + 128],
                                                expT[:, c * 512 + dq:c * 512 + dq + 128],
                                                utm[:]),
                                   waits=[(self.s_ac, a), cstg], inc=self.s_ve)
                        if dq > 0:
                            a = p.emit("vector", lambda e, c=c, dq=dq:
                                       e.memset(expT[:, c * 512:c * 512 + dq], 0.0),
                                       inc=self.s_ve)
                exp_rdy = (self.s_ve, self.s_ve.n)
                exp_rdy2 = (self.s_ac, self.s_ac.n)
                for c in range(c_lo, 4):
                    p.emit("tensor", lambda e, tkt=tkt, c=c:
                           e.matmul(self.psf[0:1, c * 512:(c + 1) * 512],
                                    self.ones[:, 0:1], expT[:, c * 512:(c + 1) * 512],
                                    start=(tkt == 0), stop=(tkt == (4 * c + 3)),
                                    ),
                           waits=[exp_rdy, exp_rdy2] +
                                 (self.bank_war(0, 1, 2, 3) if (tkt == 0 and c == 0) else []))
                for tqt in range(tkt, 16):
                    p.emit("tensor", lambda e, tkt=tkt, tqt=tqt, hh=hh:
                           e.matmul(self.psf2[:, tqt * 64:(tqt + 1) * 64],
                                    expT[:, tqt * 128:(tqt + 1) * 128],
                                    vtm[:, tkt * 128 + hh * 64:tkt * 128 + hh * 64 + 64],
                                    start=True, stop=True),
                           waits=[exp_rdy, exp_rdy2, vtm_rdy] + self.bank_war(6, 7),
                           inc=self.s_mm if tqt == 15 else None)
                mvav = self.s_mm.n
                if tkt == 0:
                    vav = p.emit("vector", lambda e: e.tensor_copy(
                        oaccB[:, 0:1024], self.psf2[:, 0:1024]),
                        waits=[(self.s_mm, mvav)], inc=self.s_ve)
                else:
                    vav = p.emit("vector", lambda e, tkt=tkt: e.tensor_add(
                        oaccB[:, tkt * 64:1024], oaccB[:, tkt * 64:1024],
                        self.psf2[:, tkt * 64:1024]),
                        waits=[(self.s_mm, mvav)], inc=self.s_ve)
                self.bank_read(6, self.s_ve, vav)
                self.bank_read(7, self.s_ve, vav)
                self._1a_ex = self.s_mm.n
            av_done = self.s_mm.n
            av_done_v = self.s_ve.n
            p.emit("vector", lambda e: e.tensor_copy(srow[0:1, 0:1024], self.psf[0:1, 0:1024]),
                   waits=[(self.s_mm, av_done)], inc=self.s_ve)
            g1 = p.emit("vector", lambda e: e.tensor_copy(srow[64:65, 0:1024], self.psf[0:1, 1024:2048]),
                        inc=self.s_ve)
            for bk_i in range(4):
                self.bank_read(bk_i, self.s_ve, g1)
            p.emit("vector", lambda e: e.tensor_scalar_add(srow[0:1, 0:1024], srow[0:1, 0:1024], 1e-30),
                   waits=[(self.s_ve, g1)], inc=self.s_ve)
            g2 = p.emit("vector", lambda e: e.tensor_scalar_add(srow[64:65, 0:1024], srow[64:65, 0:1024], 1e-30),
                        inc=self.s_ve)
            for tqt in range(16):
                bk_t = 4 + (tqt % 2)
                r_s = 64 * (tqt // 8)
                c_s = (tqt % 8) * 128
                mmv = p.emit("tensor", lambda e, bk_t=bk_t, r_s=r_s, c_s=c_s:
                             e.matmul(self.fb(bk_t, 0, 1),
                                      srow[r_s:r_s + 1, c_s:c_s + 128],
                                      self.identf[r_s:r_s + 1, r_s:r_s + 1],
                                      start=True, stop=True),
                             waits=[(self.s_ve, g2)] + self.bank_war(bk_t), inc=self.s_mm)
                v = p.emit("vector", lambda e, tqt=tqt, bk_t=bk_t:
                           e.tensor_copy(gcols[:, tqt:tqt + 1], self.fb(bk_t, 0, 1)),
                           waits=[(self.s_mm, mmv)], inc=self.s_ve)
                self.bank_read(bk_t, self.s_ve, v)
            g3 = p.emit("vector", lambda e: e.reciprocal(gcols[:], gcols[:]),
                        waits=[(self.s_ve, self.s_ve.n)], inc=self.s_ve)
            for tqt in range(16):
                p.emit("vector", lambda e, tqt=tqt:
                       e.tensor_scalar_mul(o_h[:, tqt * 64:(tqt + 1) * 64],
                                           oaccB[:, tqt * 64:(tqt + 1) * 64],
                                           gcols[:, tqt:tqt + 1]),
                       waits=[(self.s_ve, g3), (self.s_ve, av_done_v)], inc=self.s_ve)
            ohv = self.s_ve.n
            dd0 = h // 2
            rr = (h % 2) * 64
            for tqt in range(16):
                bk_t = 4 + (tqt % 2)
                mmv = p.emit("tensor", lambda e, tqt=tqt, bk_t=bk_t:
                             e.transpose(self.psf[0:64, bk_t * 512:bk_t * 512 + 128],
                                         o_h[:, tqt * 64:(tqt + 1) * 64],
                                         self.identf[:]),
                             waits=[(self.s_ve, ohv)] + self.bank_war(bk_t), inc=self.s_mm)
                a = p.emit("scalar", lambda e, tqt=tqt, bk_t=bk_t, dd0=dd0, rr=rr:
                           e.copy(oT[dd0][rr:rr + 64, tqt * 128:(tqt + 1) * 128],
                                  self.psf[0:64, bk_t * 512:bk_t * 512 + 128]),
                           waits=[(self.s_mm, mmv)], inc=self.s_ac)
                self.bank_read(bk_t, self.s_ac, a)
            oT_w[dd0] = self.s_ac.n
            uv_free = self.s_mm.n
        pair_free[b] = self.s_mm.n

    wo_r = self.wload(wbuf, W["wo"], D, sem=s_w, waits=[(self.s_mm, self.s_mm.n)])
    oT_rdy = [(self.s_ac, oT_w[dd]) for dd in range(8)]

    def ev_out(fi, c, ps, mm_pair):
        v = p.emit("vector", lambda e, fi=fi, c=c, ps=ps:
                   e.tensor_scalar(stag[0][:, c * 512:(c + 1) * 512], ps, SC2,
                                   biasq[:, 8 + fi:8 + fi + 1], ALU.mult, ALU.add),
                   waits=[mm_pair, cst, (self.s_ve, getattr(self, "_1a_fo", 0))],
                   inc=self.s_ve)
        v2 = p.emit("vector", lambda e, fi=fi, c=c:
                    e.tensor_add(xT[fi][:, c * 512:(c + 1) * 512],
                                 xT[fi][:, c * 512:(c + 1) * 512],
                                 stag[0][:, c * 512:(c + 1) * 512]),
                    inc=self.s_ve)
        self._1a_fo = v2
        return (self.s_ve, v)
    self.proj(wbuf, wo_r, oT, oT_rdy, D, ev_out, banks=(0, 1, 2, 3))
    return [(self.s_ve, self.s_ve.n)] * 8


Mk.sub_l1_attn_a = sub_l1_attn_a


def host_prep_l1_attn_a(inp):
    f32 = np.float32
    d = {}
    qkv = np.asarray(inp["L1_attn_a_qkv"], f32).reshape(1024, 3, 1024)
    for i, nm in enumerate(("wq", "wk", "wv")):
        d["l1a_" + nm] = np.ascontiguousarray(qkv[:, i, :].reshape(8, 128, 1024))
    d["l1a_wo"] = np.ascontiguousarray(
        np.asarray(inp["L1_attn_a_out_kernel"], f32).reshape(8, 128, 1024))
    d["l1a_bq"] = np.ascontiguousarray(
        np.asarray(inp["L1_attn_a_q_bias"], f32).reshape(1024).reshape(8, 128).T)
    d["l1a_obias"] = np.ascontiguousarray(
        np.asarray(inp["L1_attn_a_out_bias"], f32).reshape(8, 128).T)
    pos = np.asarray(inp["L1_attn_a_positional"], f32)
    pT = np.zeros((8, 128, 256), f32)
    for pr in range(8):
        for hhh in range(2):
            pT[pr, hhh * 64:(hhh + 1) * 64, :] = pos[:, 2 * pr + hhh, :].T
    d["l1a_posT"] = pT
    FR, MP = 256, 10000.0
    t = np.arange(T, dtype=np.float64)
    idx = np.arange(FR)
    om = MP ** (-(idx // 2 * 2) / FR)
    ang = t[:, None] * om[None, :]
    even = (idx % 2 == 0)
    c0 = np.sqrt(2.0) * FR ** (-0.5)
    sins = np.where(even, np.sin(ang), np.cos(ang)) * c0
    sins2 = np.where(even, -np.cos(ang), np.sin(ang)) * c0
    cosk = np.cos(ang)
    sink = np.sin(ang)
    pack = lambda a: np.ascontiguousarray(a.T.reshape(2, 128, T).astype(f32))
    d["l1a_sinsT"] = pack(sins)
    d["l1a_sins2T"] = pack(sins2)
    d["l1a_coskT"] = pack(cosk)
    d["l1a_sinkT"] = pack(sink)
    jj = np.arange(128)[:, None]
    ii = np.arange(128)[None, :]
    d["l1a_utm"] = (jj <= ii).astype(f32)
    return d
def _rotary_tables(window):
    half = np.arange(0, 64, 2, dtype=np.float64)
    inv_freq = 1.0 / 10000.0 ** (half / 64)
    t = np.arange(T, dtype=np.float64)
    freqs = t[:, None] * inv_freq[None, :]
    freqs = np.concatenate([freqs, freqs], axis=-1)
    base = (half + 0.4 * 64) / (1.4 * 64)
    power = (t - T // 2) / float(window // 2)
    scale = base[None, :] ** power[:, None]
    scale = np.concatenate([scale, scale], axis=-1)
    cos, sin = np.cos(freqs), np.sin(freqs)
    cq = (cos * scale).T.astype(np.float32)
    sq = (sin * scale).T.astype(np.float32)
    ck = (cos / scale).T.astype(np.float32)
    sk = (sin / scale).T.astype(np.float32)
    rep = lambda a: np.ascontiguousarray(np.tile(a, (2, 1)))
    return rep(cq), rep(sq), rep(ck), rep(sk)


def host_prep_local(inp, pfx, keys, depth, heads, window):
    f32 = np.float32
    INNER = heads * 64
    NI = INNER // 128
    d = {}
    cq, sq, ck, sk = _rotary_tables(window)
    d[pfx + "cosq"], d[pfx + "sinq"] = cq, sq
    d[pfx + "cosk"], d[pfx + "sink"] = ck, sk
    rot = np.zeros((64, 64), f32)
    rot[0:32, 32:64] = -np.eye(32, dtype=f32)
    rot[32:64, 0:32] = np.eye(32, dtype=f32)
    P = np.zeros((128, 128), f32)
    P[0:64, 0:64] = rot.T
    P[64:128, 64:128] = rot.T
    d[pfx + "rotP"] = P
    e2 = np.zeros((2, 128), f32)
    e2[0, 0:64] = 1.0
    e2[1, 64:128] = 1.0
    d[pfx + "e2f"] = e2
    e64 = np.zeros((128, 2), f32)
    e64[0:64, 0] = 1.0
    e64[64:128, 1] = 1.0
    d[pfx + "e64"] = e64
    WT = window // 128
    NEGM = -1e5
    p_ = np.arange(128)[:, None]
    u = np.arange((WT - 1) * 128 + 2 * window)[None, :] - (WT - 1) * 128
    badA = (u > p_ + window) | (u < p_)
    d[pfx + "mskA"] = np.where(badA, NEGM, 0.0).astype(f32)
    u = np.arange((2 * WT - 1) * 128 + window)[None, :] - (2 * WT - 1) * 128
    badT = (u < p_ - window) | (u > p_)
    d[pfx + "mskT"] = np.where(badT, NEGM, 0.0).astype(f32)

    wqkv = np.asarray(inp[keys["wqkv"]], f32)
    wo = np.asarray(inp[keys["wo"]], f32)
    qs = np.asarray(inp[keys["qs"]], f32)
    ks = np.asarray(inp[keys["ks"]], f32)
    lg = np.asarray(inp[keys["lg"]], f32)
    lb = np.asarray(inp[keys["lb"]], f32)
    w1 = np.asarray(inp[keys["w1"]], f32)
    b1 = np.asarray(inp[keys["b1"]], f32)
    w2 = np.asarray(inp[keys["w2"]], f32)
    b2 = np.asarray(inp[keys["b2"]], f32)
    for l in range(depth):
        q = wqkv[l]
        d[f"{pfx}{l}_wqkvT"] = np.ascontiguousarray(
            q.reshape(8, 128, 3, INNER).transpose(2, 0, 1, 3))
        d[f"{pfx}{l}_qs"] = np.ascontiguousarray(np.tile(qs[l], 2).reshape(128, 1))
        d[f"{pfx}{l}_ks"] = np.ascontiguousarray(np.tile(ks[l], 2).reshape(128, 1))
        d[f"{pfx}{l}_woT"] = np.ascontiguousarray(wo[l].reshape(NI, 128, 1024))
        d[f"{pfx}{l}_lg"] = np.ascontiguousarray(lg[l].reshape(8, 128).T)
        d[f"{pfx}{l}_lb"] = np.ascontiguousarray(lb[l].reshape(8, 128).T)
        w1a = w1[l][:, :2730]
        w1g = w1[l][:, 2730:]
        pad = np.zeros((1024, 86), f32)
        w1a = np.concatenate([w1a, pad], 1)
        w1g = np.concatenate([w1g, pad], 1)
        d[f"{pfx}{l}_w1a"] = np.ascontiguousarray(
            w1a.reshape(8, 128, 22, 128).transpose(2, 1, 0, 3))
        d[f"{pfx}{l}_w1g"] = np.ascontiguousarray(
            w1g.reshape(8, 128, 22, 128).transpose(2, 1, 0, 3))
        b1a = np.concatenate([b1[l][:2730], np.zeros(86, f32)]).reshape(22, 128).T
        b1g = np.concatenate([b1[l][2730:], np.zeros(86, f32)]).reshape(22, 128).T
        d[f"{pfx}{l}_b1a"] = np.ascontiguousarray(b1a)
        d[f"{pfx}{l}_b1g"] = np.ascontiguousarray(b1g)
        w2p = np.concatenate([w2[l], np.zeros((86, 1024), f32)], 0)
        d[f"{pfx}{l}_w2T"] = np.ascontiguousarray(
            w2p.reshape(22, 128, 8, 128).transpose(2, 1, 0, 3))
        d[f"{pfx}{l}_b2"] = np.ascontiguousarray(b2[l].reshape(8, 128).T)
    return d


def host_prep_l0_ffn(inp):
    return host_prep_local(inp, "l0f_", dict(
        wqkv="L0_ffn_wqkv", wo="L0_ffn_wo", qs="L0_ffn_q_scale", ks="L0_ffn_k_scale",
        lg="L0_ffn_ln_g", lb="L0_ffn_ln_b", w1="L0_ffn_w1", b1="L0_ffn_b1",
        w2="L0_ffn_w2", b2="L0_ffn_b2"), 2, 8, 512)


def host_prep_l3_ffn(inp):
    return host_prep_local(inp, "l3f_", dict(
        wqkv="L3_ffn_wqkv", wo="L3_ffn_wo", qs="L3_ffn_q_scale", ks="L3_ffn_k_scale",
        lg="L3_ffn_ln_g", lb="L3_ffn_ln_b", w1="L3_ffn_w1", b1="L3_ffn_b1",
        w2="L3_ffn_w2", b2="L3_ffn_b2"), 2, 8, 512)


def host_prep_l2_ffn(inp):
    return host_prep_local(inp, "l2f_", dict(
        wqkv="L2_ffn_Wqkv", wo="L2_ffn_Wo", qs="L2_ffn_q_scale", ks="L2_ffn_k_scale",
        lg="L2_ffn_ln_g", lb="L2_ffn_ln_b", w1="L2_ffn_W1", b1="L2_ffn_b1",
        w2="L2_ffn_W2", b2="L2_ffn_b2"), 6, 16, 256)



CHAIN = [
    ("mamba", dict(d_state=16, rms=False), "l0a_", host_prep_l0_attn_a),
    ("local", dict(depth=2, heads=8, window=512), "l0f_", host_prep_l0_ffn),
    ("l0ab", {}, "l0ab_", host_prep_l0_attn_b),
    ("local", dict(depth=2, heads=8, window=512), "l0f_", None),
    ("l1aa", {}, "l1a_", host_prep_l1_attn_a),
    ("l1f", {}, "l1f_", host_prep_l1_ffn),
    ("l1ab", {}, "l1b_", host_prep_l1_attn_b),
    ("l1f", {}, "l1f_", None),
    ("l2aa", {}, "l2a_", host_prep_l2_attn_a),
    ("local", dict(depth=6, heads=16, window=256), "l2f_", host_prep_l2_ffn),
    ("l2ab", {}, "l2b_", host_prep_l2_attn_b),
    ("local", dict(depth=6, heads=16, window=256), "l2f_", None),
    ("l3aa", {}, "l3a_", host_prep_l3_attn_a),
    ("local", dict(depth=2, heads=8, window=512), "l3f_", None),
    ("mamba", dict(d_state=32, rms=True), "l3b_", host_prep_l3_attn_b),
    ("local", dict(depth=2, heads=8, window=512), "l3f_", None),
]
CHAIN_PREPS = {"l0f_": host_prep_l0_ffn, "l3f_": host_prep_l3_ffn}


def _build_sub_program(kind, kw, pfx, host):
    nc = bass.Bass("TRN2", target_bir_lowering=False)
    x_d = nc.dram_tensor("x", [D, T], F32, kind="ExternalInput").ap()
    id_d = nc.dram_tensor("c_ident", [128, 128], F32, kind="ExternalInput").ap()
    on_d = nc.dram_tensor("c_ones", [128, 128], F32, kind="ExternalInput").ap()
    W = {}
    wkeys = []
    for k, arr in host.items():
        if not k.startswith(pfx):
            continue
        W[k[len(pfx):]] = nc.dram_tensor(k, list(arr.shape), F32,
                                         kind="ExternalInput").ap()
        wkeys.append(k)
    xo_d = nc.dram_tensor("xo", [D, T], F32, kind="ExternalOutput").ap()
    mk = Mk(nc)
    p = mk.p
    xT = [p.sbuf(f"xT{i}", (128, T), F32) for i in range(8)]
    mk.load_consts(id_d, on_d)
    for i in range(8):
        p.dma("sync", xT[i][:], x_d[i * 128:(i + 1) * 128, :], mk.s_ld)
    xrdy = [(mk.s_ld, mk.s_ld.n)] * 8
    scr = None
    if kind != "l1f":
        scr = {k: nc.dram_tensor("scr_" + k, [D, T], BF16, kind="Internal").ap()
               for k in ("qd", "kd", "vd")}
        scr["kd32"] = nc.dram_tensor("scr_kd32", [D, T], F32, kind="Internal").ap()
        scr["xspill"] = nc.dram_tensor("scr_xsp", [D, T], F32, kind="Internal").ap()
        if kind == "local":
            ni = kw["heads"] * 64 // 128
            scr["vd2"] = nc.dram_tensor("scr_vd2", [ni, 128, T], BF16, kind="Internal").ap()
        if kind == "l2aa":
            scr["kd2"] = nc.dram_tensor("scr_kd2", [8, 128, T], BF16, kind="Internal").ap()
        if kind == "l1ab":
            scr["od"] = nc.dram_tensor("scr_od", [1024, T], BF16, kind="Internal").ap()
        if kind == "mamba":
            scr["mzd"] = nc.dram_tensor("scr_mzd", [4096, T], BF16, kind="Internal").ap()
            scr["bcd"] = nc.dram_tensor("scr_bcd", [8192, T], BF16, kind="Internal").ap()
            for nm in ("mcd", "dtd", "dtu2", "myd"):
                scr[nm] = nc.dram_tensor("scr_" + nm, [2048, T], BF16, kind="Internal").ap()
    if kind == "mamba":
        rdy = mk.sub_mamba(xT, xrdy, W, scr, kw["d_state"], kw["rms"], pfx)
    elif kind == "local":
        rdy = mk.sub_local_ffn(xT, xrdy, W, scr, kw["depth"], kw["heads"],
                               kw["window"], pfx)
    elif kind == "l0ab":
        rdy = mk.sub_l0_attn_b(xT, xrdy, W, scr)
    elif kind == "l1aa":
        rdy = mk.sub_l1_attn_a(xT, xrdy, W, scr)
    elif kind == "l1f":
        rdy = mk.sub_l1_ffn(xT, xrdy, W)
    elif kind == "l1ab":
        rdy = mk.sub_l1_attn_b(xT, xrdy, W, scr)
    elif kind == "l2aa":
        rdy = mk.sub_l2_attn_a(xT, xrdy, W, scr)
    elif kind == "l2ab":
        rdy = mk.sub_l2_attn_b(xT, xrdy, W, scr)
    elif kind == "l3aa":
        rdy = mk.sub_l3_attn_a(xT, xrdy, W, scr)
    else:
        raise ValueError(kind)
    st = p.sem("stout")
    for i in range(8):
        p.dma("gpsimd", xo_d[i * 128:(i + 1) * 128, :], xT[i][:], st, waits=rdy)
    p.wait("gpsimd", (st, st.n))
    p.flush()
    return nc, wkeys


def kernel(**inputs):
    return kernel_fused(inputs)


def kernel_unfused(**inputs):
    host = {}
    done = set()
    for kind, kw, pfx, prep in CHAIN:
        if prep is None:
            prep = CHAIN_PREPS.get(pfx)
        if prep is not None and pfx not in done:
            host.update(prep(inputs))
            done.add(pfx)
    x = np.asarray(inputs["L0_attn_a_x"], np.float32)
    cur = [np.ascontiguousarray(x[c].T) for c in range(NCORES)]
    ident = np.eye(128, dtype=np.float32)
    ones = np.ones((128, 128), np.float32)
    progs = {}
    for kind, kw, pfx, _prep in CHAIN:
        key = (kind, pfx, tuple(sorted(kw.items())))
        if key not in progs:
            progs[key] = _build_sub_program(kind, kw, pfx, host)
        nc, wkeys = progs[key]
        base = {k: host[k] for k in wkeys}
        base["c_ident"] = ident
        base["c_ones"] = ones
        im = [dict(base, x=cur[c]) for c in range(NCORES)]
        res = run_bass_kernel_spmd(nc, im, core_ids=list(range(NCORES)))
        cur = [np.ascontiguousarray(np.asarray(res.results[c]["xo"]))
               for c in range(NCORES)]
    return np.stack([cur[c].T for c in range(NCORES)]).astype(np.float32)


def _build_fused_program(host):
    nc = bass.Bass("TRN2", target_bir_lowering=False)
    x_d = nc.dram_tensor("x", [D, T], F32, kind="ExternalInput").ap()
    id_d = nc.dram_tensor("c_ident", [128, 128], F32, kind="ExternalInput").ap()
    on_d = nc.dram_tensor("c_ones", [128, 128], F32, kind="ExternalInput").ap()
    W = {}
    for k, arr in host.items():
        W[k] = nc.dram_tensor(k, list(arr.shape), F32, kind="ExternalInput").ap()
    xo_d = nc.dram_tensor("xo", [D, T], F32, kind="ExternalOutput").ap()
    mk = Mk(nc)
    p = mk.p
    xT = [p.sbuf(f"xT{i}", (128, T), F32) for i in range(8)]
    mk.load_consts(id_d, on_d)
    for i in range(8):
        p.dma("sync", xT[i][:], x_d[i * 128:(i + 1) * 128, :], mk.s_ld)
    xrdy = [(mk.s_ld, mk.s_ld.n)] * 8
    scr = {k: nc.dram_tensor("scr_" + k, [D, T], BF16, kind="Internal").ap()
           for k in ("qd", "kd", "vd", "od")}
    scr["kd32"] = nc.dram_tensor("scr_kd32", [D, T], F32, kind="Internal").ap()
    scr["xspill"] = nc.dram_tensor("scr_xsp", [D, T], F32, kind="Internal").ap()
    scr["vd2"] = nc.dram_tensor("scr_vd2", [8, 128, T], BF16, kind="Internal").ap()
    scr["kd2"] = scr["vd2"]
    scr["mzd"] = nc.dram_tensor("scr_mzd", [4096, T], BF16, kind="Internal").ap()
    scr["bcd"] = nc.dram_tensor("scr_bcd", [8192, T], BF16, kind="Internal").ap()
    for nm in ("mcd", "dtd", "dtu2", "myd"):
        scr[nm] = nc.dram_tensor("scr_" + nm, [2048, T], BF16, kind="Internal").ap()

    def bf16_keys(kind, kw, pfx):
        if kind == "local":
            ks = ["cosq", "sinq", "cosk", "sink", "rotP", "mskA", "mskT"]
            for l in range(kw["depth"]):
                ks += [f"{l}_{n}" for n in ("wqkvT", "woT", "w1a", "w1g", "w2T")]
        elif kind == "mamba":
            ks = ["ipw", "xpw", "dtw", "opw"]
        elif kind == "l0ab":
            ks = ["WqT", "WkT", "WvT", "WoT"]
        elif kind == "l1aa":
            ks = ["wq", "wk", "wv", "wo", "posT", "utm", "coskT", "sinkT",
                  "sinsT", "sins2T"]
        elif kind == "l1f":
            ks = ["w1T", "w2T", "h1T", "h2T", "w3T"]
        elif kind == "l1ab":
            ks = ["wq", "wkv", "wwin", "wout", "e34sc", "sel342", "selav"]
        elif kind == "l2aa":
            ks = ["wqT", "wkT", "wvT", "woT", "utm"]
        elif kind == "l2ab":
            ks = ["wqT", "wkT", "wvT", "wgT", "outw", "e2"]
        elif kind == "l3aa":
            ks = ["WrT", "WkT", "WvT", "WoT"]
        else:
            ks = []
        return [pfx + k for k in ks]

    allb = []
    for kind, kw, pfx, _prep in CHAIN:
        for k in bf16_keys(kind, kw, pfx):
            if k not in allb:
                assert k in host, k
                allb.append(k)
    p.w_bf16 = True
    p.cur_label = "S_cast"
    Wb = {}
    s_ca = p.sem("wcast")
    csc = p.scope()
    stf = [p.sbuf("cstf%d" % i, (128, 2048), F32) for i in range(2)]
    stb = [p.sbuf("cstb%d" % i, (128, 2048), BF16) for i in range(2)]
    slot_rd = [0, 0]
    slot_st = [0, 0]
    csi = 0
    for key in allb:
        shape = list(host[key].shape)
        ne = 1
        for s in shape:
            ne *= s
        r = 128
        while ne % r:
            r //= 2
        C = ne // r
        axes = " ".join("abcdef"[i] for i in range(len(shape)))
        flat_in = W[key].rearrange(f"{axes} -> ({axes})").rearrange(
            "(p q) -> p q", p=r)
        bt = nc.dram_tensor("wb16_" + key, shape, BF16, kind="Internal").ap()
        flat_out = bt.rearrange(f"{axes} -> ({axes})").rearrange(
            "(p q) -> p q", p=r)
        off = 0
        while off < C:
            cw = min(2048, C - off)
            sl = csi % 2
            lv = p.dma("sync", stf[sl][0:r, 0:cw], flat_in[:, off:off + cw],
                       s_ca, waits=[(self_sve := p.sem("ve"), slot_rd[sl])])
            v = p.emit("vector", lambda e, sl=sl, r=r, cw=cw:
                       e.tensor_copy(stb[sl][0:r, 0:cw], stf[sl][0:r, 0:cw]),
                       waits=[(s_ca, lv), (s_ca, slot_st[sl])], inc=p.sem("ve"))
            p.dma("sync", flat_out[:, off:off + cw], stb[sl][0:r, 0:cw],
                  s_ca, waits=[(p.sem("ve"), v)])
            slot_rd[sl] = v
            slot_st[sl] = s_ca.n
            csi += 1
            off += cw
        Wb[key] = bt
    p.wait("sync", (s_ca, s_ca.n))
    p.wait("vector", (s_ca, s_ca.n))
    p.barrier()
    csc.close()
    p.cur_label = None

    def subW(pfx):
        d = {k[len(pfx):]: v for k, v in W.items() if k.startswith(pfx)}
        for k, v in Wb.items():
            if k.startswith(pfx):
                d[k[len(pfx):]] = v
        return d

    for si, (kind, kw, pfx, _prep) in enumerate(CHAIN):
        sc = p.scope()
        p.cur_label = "S%02d_%s%s" % (si, kind, pfx)
        Ws = subW(pfx)
        if kind == "mamba":
            mk.sub_mamba(xT, xrdy, Ws, scr, kw["d_state"], kw["rms"], pfx)
        elif kind == "local":
            mk._lf_escfree = [0, 0]
            mk.sub_local_ffn(xT, xrdy, Ws, scr, kw["depth"], kw["heads"],
                             kw["window"], pfx)
        elif kind == "l0ab":
            mk.sub_l0_attn_b(xT, xrdy, Ws, scr)
        elif kind == "l1aa":
            mk.sub_l1_attn_a(xT, xrdy, Ws, scr)
        elif kind == "l1f":
            mk.sub_l1_ffn(xT, xrdy, Ws)
        elif kind == "l1ab":
            mk.sub_l1_attn_b(xT, xrdy, Ws, scr)
        elif kind == "l2aa":
            mk.sub_l2_attn_a(xT, xrdy, Ws, scr)
        elif kind == "l2ab":
            mk.sub_l2_attn_b(xT, xrdy, Ws, scr)
        elif kind == "l3aa":
            mk.sub_l3_attn_a(xT, xrdy, Ws, scr)
        drains = [(s, s.n) for s in p._sems.values() if s.dma and s.n > 0]
        for eng in ("sync", "gpsimd", "vector"):
            p.wait(eng, *drains)
        p.barrier()
        sc.close()
        p.cur_label = None
        xrdy = [(p.sem("_barrier"), p.sem("_barrier").n)] * 8
    st = p.sem("stout")
    for i in range(8):
        p.dma("gpsimd", xo_d[i * 128:(i + 1) * 128, :], xT[i][:], st, waits=xrdy)
    p.wait("gpsimd", (st, st.n))
    p.flush()
    return nc


def kernel_fused(inputs):
    host = {}
    done = set()
    for kind, kw, pfx, prep in CHAIN:
        if prep is None:
            prep = CHAIN_PREPS.get(pfx)
        if prep is not None and pfx not in done:
            host.update(prep(inputs))
            done.add(pfx)
    x = np.asarray(inputs["L0_attn_a_x"], np.float32)
    ident = np.eye(128, dtype=np.float32)
    ones = np.ones((128, 128), np.float32)
    nc = _build_fused_program(host)
    base = dict(host)
    base["c_ident"] = ident
    base["c_ones"] = ones
    im = [dict(base, x=np.ascontiguousarray(x[c].T)) for c in range(NCORES)]
    res = run_bass_kernel_spmd(nc, im, core_ids=list(range(NCORES)))
    out = np.stack([np.asarray(res.results[c]["xo"]).T for c in range(NCORES)])
    return out.astype(np.float32)
